# version 02
# baseline (speedup 1.0000x reference)
import contextlib
import numpy as np
import concourse.bass as bass
import concourse.mybir as mybir
from concourse.bass_utils import run_bass_kernel_spmd

F32 = mybir.dt.float32
F32R = mybir.dt.float32r
BF16 = mybir.dt.bfloat16
I16 = mybir.dt.int16
U16 = mybir.dt.uint16
AF = mybir.ActivationFunctionType
ALU = mybir.AluOpType
AX = mybir.AxisListType

NCORES = 8
B = 2
S = 2048
D = 1024
DF = 3072
NH = 16
TOK = B * S
QMAX = 32767.0
EPS = 1e-5

ENGINES = ("sync", "gpsimd", "tensor", "vector", "scalar")


class Prog:

    def __init__(self, nc, ctx):
        self.nc = nc
        self.ctx = ctx
        self.ph = ctx.enter_context(nc.semaphore("phase"))
        self.phases = []
        self.sems = {}

    def sem(self, name):
        if name not in self.sems:
            h = self.ctx.enter_context(self.nc.semaphore(name))
            self.sems[name] = [h, 0]
        return self.sems[name]

    def phase(self):
        d = {e: [] for e in ENGINES}
        self.phases.append(d)
        return d

    def emit(self, block):
        nph = len(self.phases)

        def mk(ename):
            def body(eng):
                for i, ph in enumerate(self.phases):
                    if i > 0:
                        eng.wait_ge(self.ph, 5 * i)
                    for fn in ph[ename]:
                        fn(eng)
                    eng.drain().then_inc(self.ph, 1)
            return body

        for e in ENGINES:
            getattr(block, e)(mk(e))
        return nph



def build_a1():
    nc = bass.Bass("TRN2", target_bir_lowering=False,
                   detect_race_conditions=False)
    x = nc.dram_tensor("x", [B, S, D], F32, kind="ExternalInput").ap()
    conv_w = nc.dram_tensor("conv_w", [D, 9], F32, kind="ExternalInput").ap()
    pe_w = nc.dram_tensor("pe_w", [D, D], F32, kind="ExternalInput").ap()
    qkv_w = nc.dram_tensor("qkv_w", [3 * D, D], F32, kind="ExternalInput").ap()
    qkvT = nc.dram_tensor("qkvT", [B, 3 * D, S], BF16, kind="ExternalOutput").ap()
    qkT_h = nc.dram_tensor("qkT_h", [128, 24 * 1024], F32R, kind="Internal").ap()
    qkT_l = nc.dram_tensor("qkT_l", [128, 24 * 1024], F32R, kind="Internal").ap()

    BT = 512
    NBLK = S // BT

    with contextlib.ExitStack() as ctx:
        E = ctx.enter_context
        wtmp = E(nc.sbuf_tensor("wtmp", [128, 256], F32))
        wth = E(nc.sbuf_tensor("wth", [128, 256], F32R))
        wtl = E(nc.sbuf_tensor("wtl", [128, 256], F32R))
        ident = E(nc.sbuf_tensor("ident", [128, 128], F32))
        cw = E(nc.sbuf_tensor("cw", [128, 8 * 9], F32))
        zcol = E(nc.sbuf_tensor("zcol", [128, 8], F32))
        epsc = E(nc.sbuf_tensor("epsc", [128, 1], F32))
        wpeh = E(nc.sbuf_tensor("wpeh", [128, 8 * 1024], F32R))
        wpel = E(nc.sbuf_tensor("wpel", [128, 8 * 1024], F32R))
        wqh = E(nc.sbuf_tensor("wqh", [128, 2 * 1024], F32R))
        wql = E(nc.sbuf_tensor("wql", [128, 2 * 1024], F32R))
        lnT = E(nc.sbuf_tensor("lnT", [128, 2 * 8 * BT], F32))
        haloL = E(nc.sbuf_tensor("haloL", [128, 8 * NBLK], F32))
        haloF = E(nc.sbuf_tensor("haloF", [128, 8 * NBLK], F32))
        xtm = E(nc.sbuf_tensor("xtm", [128, 2 * D], F32))
        sq = E(nc.sbuf_tensor("sq", [128, D], F32))
        stats = E(nc.sbuf_tensor("stats", [128, 16], F32))
        xcf = E(nc.sbuf_tensor("xcf", [128, BT], F32))
        xch = E(nc.sbuf_tensor("xch", [128, 8 * BT], F32R))
        xcl = E(nc.sbuf_tensor("xcl", [128, 8 * BT], F32R))
        x2t = E(nc.sbuf_tensor("x2t", [128, BT], F32))
        x2h = E(nc.sbuf_tensor("x2h", [128, 8 * BT], F32R))
        x2l = E(nc.sbuf_tensor("x2l", [128, 8 * BT], F32R))
        qstage = E(nc.sbuf_tensor("qstage", [128, 2 * BT], BF16))
        psA = E(nc.psum_tensor("psA", [128, 512], F32))
        psB = E(nc.psum_tensor("psB", [128, 512], F32))
        psT = E(nc.psum_tensor("psT", [128, 128], F32))
        psT2 = E(nc.psum_tensor("psT2", [128, 128], F32))
        block = E(nc.Block())

        prog = Prog(nc, ctx)
        W = lambda name: prog.sem(name)
        def WNOW(name):
            s = prog.sem(name)
            v = s[1]
            return lambda eng, _h=s[0], _v=v: eng.wait_ge(_h, _v)

        def INC(name, n=1):
            s = prog.sem(name)
            s[1] += n
            return s[0], s[1]

        ph0 = prog.phase()

        def p0(g):
            h, _ = INC("g0", 4)
            g.memset(ident[:], 0.0).then_inc(h, 1)
            g.wait_ge(h, 1)
            g.affine_select(out=ident[:], in_=ident[:], compare_op=ALU.not_equal,
                            fill=1.0, base=0, pattern=[[-1, 128]],
                            channel_multiplier=1).then_inc(h, 1)
            g.memset(zcol[:], 0.0).then_inc(h, 1)
            g.memset(epsc[:], EPS).then_inc(h, 1)
            g.wait_ge(h, 4)
        ph0["gpsimd"].append(p0)

        def p0s(sync):
            h, v = INC("cwld", 16)
            sync.dma_start(out=cw.rearrange("p (c n) -> p c n", c=8),
                           in_=conv_w.rearrange("(c p) n -> p c n", p=128)
                           ).then_inc(h, 16)
            sync.wait_ge(h, v)
        ph0["sync"].append(p0s)

        ph1 = prog.phase()
        tiles = []
        for of in range(8):
            for k in range(8):
                tiles.append((pe_w, of, k, True))
        for of in range(24):
            for k in range(8):
                tiles.append((qkv_w, of, k, False))

        for i, (src, of, k, is_pe) in enumerate(tiles):
            buf = (i % 2) * 128
            ldn = "a_ld%d" % (i % 2)
            h_ld, v_ld = INC(ldn, 16)

            def load(eng, i=i, src=src, of=of, k=k, buf=buf, h=h_ld):
                if i >= 2:
                    eng.wait_ge(prog.sems["a_tp"][0], i - 1)
                eng.dma_start(out=wtmp[:, buf:buf + 128],
                              in_=src[of * 128:of * 128 + 128,
                                      k * 128:k * 128 + 128]).then_inc(h, 16)
            ph1["sync"].append(load)

            h_tp, v_tp = INC("a_tp", 1)

            def tpose(eng, i=i, buf=buf, v_ld=v_ld, h=h_tp, ldn=ldn):
                eng.wait_ge(prog.sems[ldn][0], v_ld)
                if i >= 2:
                    eng.wait_ge(prog.sems["a_vp"][0], i - 1)
                p = psT if i % 2 == 0 else psT2
                eng.transpose(p[:], wtmp[:, buf:buf + 128], ident[:]).then_inc(h, 1)
            ph1["tensor"].append(tpose)

            h_vp, v_vp = INC("a_vp", 1)
            col = of * 1024 + k * 128
            if is_pe:
                def split(eng, i=i, v_tp=v_tp, col=col, h=h_vp):
                    eng.wait_ge(prog.sems["a_tp"][0], v_tp)
                    p = psT if i % 2 == 0 else psT2
                    eng.tensor_copy(wpeh[:, col:col + 128], p[:])
                    eng.tensor_tensor(wpel[:, col:col + 128], p[:],
                                      wpeh[:, col:col + 128],
                                      ALU.subtract).then_inc(h, 1)
                ph1["vector"].append(split)
            else:
                gpn = "a_gp%d" % (i % 2)
                v_gp_need = prog.sem(gpn)[1]

                def split(eng, i=i, v_tp=v_tp, buf=buf, h=h_vp, gpn=gpn,
                          v_gp_need=v_gp_need):
                    eng.wait_ge(prog.sems["a_tp"][0], v_tp)
                    if v_gp_need > 0:
                        eng.wait_ge(prog.sems[gpn][0], v_gp_need)
                    p = psT if i % 2 == 0 else psT2
                    eng.tensor_copy(wth[:, buf:buf + 128], p[:])
                    eng.tensor_tensor(wtl[:, buf:buf + 128], p[:],
                                      wth[:, buf:buf + 128],
                                      ALU.subtract).then_inc(h, 1)
                ph1["vector"].append(split)

                h_gp, v_gp = INC(gpn, 32)

                def store(eng, v_vp=v_vp, buf=buf, col=col, h=h_gp):
                    eng.wait_ge(prog.sems["a_vp"][0], v_vp)
                    eng.dma_start(out=qkT_h[:, col:col + 128],
                                  in_=wth[:, buf:buf + 128]).then_inc(h, 16)
                    eng.dma_start(out=qkT_l[:, col:col + 128],
                                  in_=wtl[:, buf:buf + 128]).then_inc(h, 16)
                ph1["gpsimd"].append(store)
        _snap = {n: prog.sems[n][1] for n in ("a_ld0", "a_ld1", "a_gp0", "a_gp1")
                 if n in prog.sems}

        def p1_sync_fin(eng, _snap=_snap):
            for n in ("a_ld0", "a_ld1"):
                if n in _snap:
                    eng.wait_ge(prog.sems[n][0], _snap[n])
        ph1["sync"].append(p1_sync_fin)

        def p1_gp_fin(eng, _snap=_snap):
            for n in ("a_gp0", "a_gp1"):
                if n in _snap:
                    eng.wait_ge(prog.sems[n][0], _snap[n])
        ph1["gpsimd"].append(p1_gp_fin)

        xslot = [0]

        def emit_ln_block(b, blk):
            ph = prog.phase()
            slot = (blk % 2) * 8 * BT
            for i4 in range(BT // 128):
                t0 = blk * BT + i4 * 128
                n_x = xslot[0]
                xslot[0] += 1
                buf = (n_x % 2) * D
                ldn = "x_ld%d" % (n_x % 2)
                h_ld, v_ld = INC(ldn, 16)
                v_tp_need = max(prog.sem("x_tp")[1] - 8, 0)

                def load(eng, b=b, t0=t0, buf=buf, h=h_ld, v_tp_need=v_tp_need):
                    if v_tp_need > 0:
                        eng.wait_ge(prog.sems["x_tp"][0], v_tp_need)
                    eng.dma_start(out=xtm[:, buf:buf + D],
                                  in_=x[b, t0:t0 + 128, :]).then_inc(h, 16)
                ph["sync"].append(load)

                h_sp, v_sp = INC("x_sp", 1)

                v_vp_need = max(prog.sem("x_vp")[1] - 1, 0)

                def dosq(eng, v_ld=v_ld, buf=buf, h=h_sp, v_vp_need=v_vp_need,
                         ldn=ldn):
                    eng.wait_ge(prog.sems[ldn][0], v_ld)
                    if v_vp_need > 0:
                        eng.wait_ge(prog.sems["x_vp"][0], v_vp_need)
                    eng.activation(sq[:], xtm[:, buf:buf + D],
                                   AF.Square).then_inc(h, 1)
                ph["scalar"].append(dosq)

                h_vp, v_vp = INC("x_vp", 1)

                def dostats(eng, v_sp=v_sp, v_ld=v_ld, buf=buf, h=h_vp, ldn=ldn):
                    eng.wait_ge(prog.sems[ldn][0], v_ld)
                    eng.reduce_sum(stats[:, 0:1], xtm[:, buf:buf + D], AX.X)
                    eng.wait_ge(prog.sems["x_sp"][0], v_sp)
                    eng.reduce_sum(stats[:, 1:2], sq[:], AX.X)
                    eng.drain()
                    eng.tensor_scalar(stats[:, 2:3], stats[:, 0:1],
                                      1.0 / D, None, ALU.mult)
                    eng.tensor_scalar(stats[:, 3:4], stats[:, 1:2],
                                      1.0 / D, None, ALU.mult)
                    eng.drain()
                    eng.tensor_tensor(stats[:, 4:5], stats[:, 2:3],
                                      stats[:, 2:3], ALU.mult)
                    eng.drain()
                    eng.tensor_tensor(stats[:, 5:6], stats[:, 3:4],
                                      stats[:, 4:5], ALU.subtract)
                    eng.drain().then_inc(h, 1)
                ph["vector"].append(dostats)

                h_sp2, v_sp2 = INC("x_sp", 1)

                def dosqrt(eng, v_vp=v_vp, h=h_sp2):
                    eng.wait_ge(prog.sems["x_vp"][0], v_vp)
                    eng.activation(stats[:, 6:7], stats[:, 5:6], AF.Sqrt,
                                   bias=epsc[:]).then_inc(h, 1)
                ph["scalar"].append(dosqrt)

                h_vp2, v_vp2 = INC("x_vp", 1)

                def doapply(eng, v_sp2=v_sp2, buf=buf, h=h_vp2):
                    eng.wait_ge(prog.sems["x_sp"][0], v_sp2)
                    eng.reciprocal(stats[:, 7:8], stats[:, 6:7])
                    eng.drain()
                    eng.tensor_tensor(stats[:, 8:9], stats[:, 2:3],
                                      stats[:, 7:8], ALU.mult)
                    eng.drain()
                    eng.tensor_scalar(stats[:, 9:10], stats[:, 8:9],
                                      -1.0, None, ALU.mult)
                    eng.drain()
                    eng.tensor_scalar(xtm[:, buf:buf + D], xtm[:, buf:buf + D],
                                      stats[:, 7:8], stats[:, 9:10],
                                      ALU.mult, ALU.add).then_inc(h, 1)
                ph["vector"].append(doapply)

                for c in range(8):
                    h_tp, v_tp = INC("x_tp", 1)

                    def tpose(eng, v_vp2=v_vp2, buf=buf, c=c, h=h_tp,
                              ntp=prog.sems["x_tp"][1]):
                        eng.wait_ge(prog.sems["x_vp"][0], v_vp2)
                        if ntp >= 3:
                            eng.wait_ge(prog.sems["x_ev"][0], ntp - 2)
                        p = psT if ntp % 2 == 1 else psT2
                        eng.transpose(p[:], xtm[:, buf + c * 128:buf + c * 128 + 128],
                                      ident[:]).then_inc(h, 1)
                    prog.sem("x_ev")
                    ph["tensor"].append(tpose)

                    h_ev, v_ev = INC("x_ev", 1)
                    dst = slot + c * BT + i4 * 128

                    def evac(eng, v_tp=v_tp, dst=dst, h=h_ev,
                             ntp=prog.sems["x_tp"][1]):
                        eng.wait_ge(prog.sems["x_tp"][0], v_tp)
                        p = psT if ntp % 2 == 1 else psT2
                        eng.tensor_copy(lnT[:, dst:dst + 128], p[:]).then_inc(h, 1)
                    ph["vector"].append(evac)

            def halos(eng, b=b, blk=blk, slot=slot):
                for c in range(8):
                    eng.tensor_copy(haloL[:, c * NBLK + blk:c * NBLK + blk + 1],
                                    lnT[:, slot + c * BT + BT - 1:
                                        slot + c * BT + BT])
                    eng.tensor_copy(haloF[:, c * NBLK + blk:c * NBLK + blk + 1],
                                    lnT[:, slot + c * BT:slot + c * BT + 1])
            ph["vector"].append(halos)

            _snap = {n: prog.sems[n][1] for n in ("x_ld0", "x_ld1")
                     if n in prog.sems}

            def lnfin(eng, _snap=_snap):
                for n, v in _snap.items():
                    eng.wait_ge(prog.sems[n][0], v)
            ph["sync"].append(lnfin)

        def emit_main_block(b, blk):
            slot = (blk % 2) * 8 * BT
            t0 = blk * BT

            ph = prog.phase()

            def conv(eng, b=b, blk=blk, slot=slot):
                for c in range(8):
                    w0 = cw[:, 9 * c + 1:9 * c + 2]
                    w1 = cw[:, 9 * c + 4:9 * c + 5]
                    w2 = cw[:, 9 * c + 7:9 * c + 8]
                    mid = lnT[:, slot + c * BT:slot + c * BT + BT]
                    eng.tensor_scalar(xcf[:], mid, w1, None, ALU.mult)
                    if blk == 0:
                        eng.scalar_tensor_tensor(
                            xcf[:, 1:BT], lnT[:, slot + c * BT:slot + c * BT + BT - 1],
                            w0, xcf[:, 1:BT], ALU.mult, ALU.add)
                    else:
                        eng.scalar_tensor_tensor(
                            xcf[:, 0:1], haloL[:, c * NBLK + blk - 1:c * NBLK + blk],
                            w0, xcf[:, 0:1], ALU.mult, ALU.add)
                        eng.scalar_tensor_tensor(
                            xcf[:, 1:BT], lnT[:, slot + c * BT:slot + c * BT + BT - 1],
                            w0, xcf[:, 1:BT], ALU.mult, ALU.add)
                    if blk == NBLK - 1:
                        eng.scalar_tensor_tensor(
                            xcf[:, 0:BT - 1], lnT[:, slot + c * BT + 1:slot + c * BT + BT],
                            w2, xcf[:, 0:BT - 1], ALU.mult, ALU.add)
                    else:
                        eng.scalar_tensor_tensor(
                            xcf[:, 0:BT - 1], lnT[:, slot + c * BT + 1:slot + c * BT + BT],
                            w2, xcf[:, 0:BT - 1], ALU.mult, ALU.add)
                        eng.scalar_tensor_tensor(
                            xcf[:, BT - 1:BT], haloF[:, c * NBLK + blk + 1:c * NBLK + blk + 2],
                            w2, xcf[:, BT - 1:BT], ALU.mult, ALU.add)
                    eng.tensor_copy(xch[:, c * BT:c * BT + BT], xcf[:])
                    eng.tensor_tensor(xcl[:, c * BT:c * BT + BT], xcf[:],
                                      xch[:, c * BT:c * BT + BT], ALU.subtract)
            ph["vector"].append(conv)

            ph = prog.phase()
            for of in range(8):
                bank = psA if of % 2 == 0 else psB
                h_mm, v_mm = INC("m_mm", 1)

                def pemm(eng, of=of, bank=bank, h=h_mm, nmm=prog.sems["m_mm"][1]):
                    if nmm >= 3:
                        eng.wait_ge(prog.sems["m_ev"][0], nmm - 2)
                    for k in range(8):
                        wcol = of * 1024 + k * 128
                        eng.matmul(bank[:, 0:BT], wpeh[:, wcol:wcol + 128],
                                   xch[:, k * BT:k * BT + BT],
                                   start=(k == 0), stop=False)
                        eng.matmul(bank[:, 0:BT], wpeh[:, wcol:wcol + 128],
                                   xcl[:, k * BT:k * BT + BT],
                                   start=False, stop=False)
                        mm = eng.matmul(bank[:, 0:BT], wpel[:, wcol:wcol + 128],
                                        xch[:, k * BT:k * BT + BT],
                                        start=False, stop=(k == 7))
                        if k == 7:
                            mm.then_inc(h, 1)
                prog.sem("m_ev")
                ph["tensor"].append(pemm)

                h_ev, v_ev = INC("m_ev", 1)

                def peev(eng, of=of, bank=bank, v_mm=v_mm, h=h_ev):
                    eng.wait_ge(prog.sems["m_mm"][0], v_mm)
                    eng.tensor_tensor(x2t[:], bank[:, 0:BT],
                                      xch[:, of * BT:of * BT + BT], ALU.add)
                    eng.tensor_tensor(x2t[:], x2t[:],
                                      xcl[:, of * BT:of * BT + BT], ALU.add)
                    eng.tensor_copy(x2h[:, of * BT:of * BT + BT], x2t[:])
                    eng.tensor_tensor(x2l[:, of * BT:of * BT + BT], x2t[:],
                                      x2h[:, of * BT:of * BT + BT],
                                      ALU.subtract).then_inc(h, 1)
                ph["vector"].append(peev)

            ph = prog.phase()
            for of in range(24):
                buf = (of % 2) * 1024
                wsn = "q_ws%d" % (of % 2)
                v_need = prog.sem("q_mm")[1] - 1 if of >= 2 else 0
                h_ws, v_ws = INC(wsn, 32)

                def wload(eng, of=of, buf=buf, h=h_ws, v_need=v_need):
                    if v_need > 0:
                        eng.wait_ge(prog.sems["q_mm"][0], v_need)
                    eng.dma_start(out=wqh[:, buf:buf + 1024],
                                  in_=qkT_h[:, of * 1024:of * 1024 + 1024]).then_inc(h, 16)
                    eng.dma_start(out=wql[:, buf:buf + 1024],
                                  in_=qkT_l[:, of * 1024:of * 1024 + 1024]).then_inc(h, 16)
                ph["sync"].append(wload)

                bank = psA if of % 2 == 0 else psB
                h_mm, v_mm = INC("q_mm", 1)

                def qmm(eng, of=of, buf=buf, bank=bank, v_ws=v_ws,
                        h=h_mm, nmm=prog.sems["q_mm"][1], wsn=wsn):
                    eng.wait_ge(prog.sems[wsn][0], v_ws)
                    if nmm >= 3:
                        eng.wait_ge(prog.sems["q_ev"][0], nmm - 2)
                    for k in range(8):
                        wcol = buf + k * 128
                        eng.matmul(bank[:, 0:BT], wqh[:, wcol:wcol + 128],
                                   x2h[:, k * BT:k * BT + BT],
                                   start=(k == 0), stop=False)
                        eng.matmul(bank[:, 0:BT], wqh[:, wcol:wcol + 128],
                                   x2l[:, k * BT:k * BT + BT],
                                   start=False, stop=False)
                        mm = eng.matmul(bank[:, 0:BT], wql[:, wcol:wcol + 128],
                                        x2h[:, k * BT:k * BT + BT],
                                        start=False, stop=(k == 7))
                        if k == 7:
                            mm.then_inc(h, 1)
                prog.sem("q_ev")
                prog.sem("q_st")
                ph["tensor"].append(qmm)

                h_ev, v_ev = INC("q_ev", 1)
                nev = prog.sems["q_ev"][1]
                stn = "q_st%d" % (nev % 2)
                v_st_need = prog.sem(stn)[1]

                def qev(eng, of=of, bank=bank, v_mm=v_mm, h=h_ev,
                        nev=nev, stn=stn, v_st_need=v_st_need):
                    eng.wait_ge(prog.sems["q_mm"][0], v_mm)
                    if v_st_need > 0:
                        eng.wait_ge(prog.sems[stn][0], v_st_need)
                    qb = (nev % 2) * BT
                    eng.tensor_copy(qstage[:, qb:qb + BT],
                                    bank[:, 0:BT]).then_inc(h, 1)
                ph["vector"].append(qev)

                h_st, v_st = INC(stn, 16)

                def qst(eng, b=b, of=of, t0=t0, v_ev=v_ev, h=h_st,
                        nev=nev):
                    eng.wait_ge(prog.sems["q_ev"][0], v_ev)
                    qb = (nev % 2) * BT
                    eng.dma_start(out=qkvT[b, of * 128:of * 128 + 128,
                                           t0:t0 + BT],
                                  in_=qstage[:, qb:qb + BT]).then_inc(h, 16)
                ph["gpsimd"].append(qst)

            _snap = {n: prog.sems[n][1] for n in
                     ("q_ws0", "q_ws1", "q_st0", "q_st1") if n in prog.sems}

            def qsync_fin(eng, _snap=_snap):
                for n in ("q_ws0", "q_ws1"):
                    if n in _snap:
                        eng.wait_ge(prog.sems[n][0], _snap[n])
            ph["sync"].append(qsync_fin)

            def qgp_fin(eng, _snap=_snap):
                for n in ("q_st0", "q_st1"):
                    if n in _snap:
                        eng.wait_ge(prog.sems[n][0], _snap[n])
            ph["gpsimd"].append(qgp_fin)

        for b in range(B):
            emit_ln_block(b, 0)
            emit_ln_block(b, 1)
            emit_main_block(b, 0)
            emit_ln_block(b, 2)
            emit_main_block(b, 1)
            emit_ln_block(b, 3)
            emit_main_block(b, 2)
            emit_main_block(b, 3)

        prog.emit(block)
    return nc




def build_a2():
    nc = bass.Bass("TRN2", target_bir_lowering=False,
                   detect_race_conditions=False)
    qkvT = nc.dram_tensor("qkvT", [B, 3 * D, S], BF16, kind="ExternalInput").ap()
    onT = nc.dram_tensor("onT", [B, D, S], BF16, kind="ExternalOutput").ap()

    NP = 8
    NBLK = 4
    BQ = 512
    NCH = 16

    with contextlib.ExitStack() as ctx:
        E = ctx.enter_context
        ident = E(nc.sbuf_tensor("ident", [128, 128], BF16))
        identf = E(nc.sbuf_tensor("identf", [128, 128], F32))
        onesr = E(nc.sbuf_tensor("onesr", [1, 128], F32))
        indic = E(nc.sbuf_tensor("indic", [128, 2], BF16))
        qplain = E(nc.sbuf_tensor("qplain", [128, 8 * 256], BF16))
        kplain = E(nc.sbuf_tensor("kplain", [128, 8 * 256], BF16))
        vplain = E(nc.sbuf_tensor("vplain", [128, 8 * 256], BF16))
        qtr = E(nc.sbuf_tensor("qtr", [128, S], BF16))
        ktr = E(nc.sbuf_tensor("ktr", [128, S], BF16))
        vht = E(nc.sbuf_tensor("vht", [64, S], BF16))
        vre = E(nc.sbuf_tensor("vre", [128, 2 * NCH * 65], BF16))
        nstat = E(nc.sbuf_tensor("nstat", [1, 16], F32))
        nstat2 = E(nc.sbuf_tensor("nstat2", [128, 4], F32))
        biasA = E(nc.sbuf_tensor("biasA", [128, 1], F32))
        biasB = E(nc.sbuf_tensor("biasB", [128, 1], F32))
        pt = E(nc.sbuf_tensor("pt", [128, 4 * 512], BF16))
        rsum = E(nc.sbuf_tensor("rsum", [1, 512], F32))
        bsum = E(nc.sbuf_tensor("bsum", [64, 2 * 512], F32))
        onst = E(nc.sbuf_tensor("onst", [128, 2 * 512], BF16))
        psQ = [E(nc.psum_tensor("psQ%d" % i, [128, 512], F32)) for i in range(4)]
        psVA = E(nc.psum_tensor("psVA", [128, 512], F32))
        psVB = E(nc.psum_tensor("psVB", [128, 512], F32))
        psX = E(nc.psum_tensor("psX", [128, 512], F32))
        psVT = E(nc.psum_tensor("psVT", [128, 128], BF16))
        block = E(nc.Block())

        prog = Prog(nc, ctx)

        def INC(name, n=1):
            s = prog.sem(name)
            s[1] += n
            return s[0], s[1]

        ph0 = prog.phase()

        def p0(g):
            h, _ = INC("g0", 6)
            g.memset(ident[:], 0.0).then_inc(h, 1)
            g.wait_ge(h, 1)
            g.affine_select(out=ident[:], in_=ident[:], compare_op=ALU.not_equal,
                            fill=1.0, base=0, pattern=[[-1, 128]],
                            channel_multiplier=1).then_inc(h, 1)
            g.memset(onesr[:], 1.0).then_inc(h, 1)
            hf, _ = INC("g2", 2)
            g.memset(identf[:], 0.0).then_inc(hf, 1)
            g.wait_ge(hf, 1)
            g.affine_select(out=identf[:], in_=identf[:], compare_op=ALU.not_equal,
                            fill=1.0, base=0, pattern=[[-1, 128]],
                            channel_multiplier=1).then_inc(hf, 1)
            g.wait_ge(hf, 2)
            g.memset(indic[:], 0.0).then_inc(h, 1)
            g.wait_ge(h, 4)
            g.memset(indic[0:64, 0:1], 1.0).then_inc(h, 1)
            g.memset(indic[64:128, 1:2], 1.0).then_inc(h, 1)
            g.wait_ge(h, 6)
            h2, _ = INC("g1", 1)
            g.memset(vre.rearrange("p (s c) -> p s c", c=65)[:, :, 64:65],
                     1.0).then_inc(h2, 1)
            g.wait_ge(h2, 1)
        ph0["gpsimd"].append(p0)

        def emit_pair(b, u):
            ph = prog.phase()
            for name, buf, base in (("q", qplain, 0), ("k", kplain, D),
                                    ("v", vplain, 2 * D)):
                for c8 in range(8):
                    h_ld, v_ld = INC("r_ld", 16)

                    def ld(eng, b=b, u=u, c8=c8, buf=buf, base=base, h=h_ld):
                        eng.dma_start(
                            out=buf[:, c8 * 256:c8 * 256 + 256],
                            in_=qkvT[b, base + c8 * 128: base + c8 * 128 + 128,
                                     256 * u:256 * u + 256]).then_inc(h, 16)
                    ph["sync"].append(ld)
            v_ld_all = prog.sems["r_ld"][1]

            h_re, v_re = INC("r_re", 1)

            def rearr(eng, v_ld_all=v_ld_all, h=h_re):
                eng.wait_ge(prog.sems["r_ld"][0], v_ld_all)
                for dst, srcb in ((qtr, qplain), (ktr, kplain)):
                    for hh in range(2):
                        for g in range(16):
                            eng.tensor_copy(
                                dst.rearrange("p (t g) -> p t g", g=16)
                                   [hh * 64:hh * 64 + 64, :, g],
                                srcb[(g % 2) * 64:(g % 2) * 64 + 64,
                                     (g // 2) * 256 + hh * 128:
                                     (g // 2) * 256 + hh * 128 + 128])
                eng.drain().then_inc(h, 1)
            ph["vector"].append(rearr)

            for hh in range(2):
                h_vh, v_vh = INC("r_vh", 1)

                v_ev_need = prog.sem("r_ev")[1]

                def vre_build(eng, hh=hh, v_ld_all=v_ld_all, h=h_vh,
                              v_ev_need=v_ev_need):
                    eng.wait_ge(prog.sems["r_ld"][0], v_ld_all)
                    if hh == 1:
                        eng.wait_ge(prog.sems["r_ev"][0], v_ev_need)
                    for g in range(16):
                        eng.tensor_copy(
                            vht.rearrange("p (t g) -> p t g", g=16)[:, :, g],
                            vplain[(g % 2) * 64:(g % 2) * 64 + 64,
                                   (g // 2) * 256 + hh * 128:
                                   (g // 2) * 256 + hh * 128 + 128])
                    eng.drain().then_inc(h, 1)
                ph["vector"].append(vre_build)

                for c in range(NCH):
                    h_tp, v_tp = INC("r_tp", 1)

                    def vtp(eng, hh=hh, c=c, v_vh=v_vh, h=h_tp,
                            ntp=prog.sems["r_tp"][1]):
                        eng.wait_ge(prog.sems["r_vh"][0], v_vh)
                        if ntp >= 2:
                            eng.wait_ge(prog.sems["r_ev"][0], ntp - 1)
                        p = psVT
                        eng.transpose(p[:, 0:64], vht[:, c * 128:c * 128 + 128],
                                      ident[0:64, 0:64]).then_inc(h, 1)
                    prog.sem("r_ev")
                    ph["tensor"].append(vtp)

                    h_ev, v_ev = INC("r_ev", 1)
                    col = (hh * NCH + c) * 65

                    def vev(eng, v_tp=v_tp, col=col, h=h_ev):
                        eng.wait_ge(prog.sems["r_tp"][0], v_tp)
                        eng.tensor_copy(vre[:, col:col + 64],
                                        psVT[:, 0:64]).then_inc(h, 1)
                    ph["vector"].append(vev)

            h_cm, v_cm = INC("r_cm", 1)
            v_re_c = prog.sems["r_re"][1]

            def colmax(eng, v_re_c=v_re_c, h=h_cm):
                eng.wait_ge(prog.sems["r_re"][0], v_re_c)
                eng.tensor_reduce(nstat2[:, 0:1], qtr[:], AX.X, ALU.max,
                                  apply_absolute_value=True)
                eng.tensor_reduce(nstat2[:, 1:2], ktr[:], AX.X, ALU.max,
                                  apply_absolute_value=True)
                eng.drain()
                eng.tensor_tensor(nstat2[:, 2:3], nstat2[:, 0:1],
                                  nstat2[:, 1:2], ALU.max)
                eng.drain().then_inc(h, 1)
            ph["vector"].append(colmax)

            h_ct, v_ct = INC("r_ct", 1)

            def coltp(eng, v_cm=v_cm, h=h_ct):
                eng.wait_ge(prog.sems["r_cm"][0], v_cm)
                eng.transpose(psX[0:1, 0:128], nstat2[:, 2:3],
                              identf[:]).then_inc(h, 1)
            ph["tensor"].append(coltp)

            h_nr, v_nr = INC("r_nr", 1)

            def headmax(eng, v_ct=v_ct, h=h_nr):
                eng.wait_ge(prog.sems["r_ct"][0], v_ct)
                eng.tensor_reduce(nstat[:, 0:1], psX[0:1, 0:64], AX.X, ALU.max)
                eng.tensor_reduce(nstat[:, 1:2], psX[0:1, 64:128], AX.X, ALU.max)
                eng.drain()
                eng.tensor_tensor(nstat[:, 2:3], nstat[:, 0:1], nstat[:, 0:1],
                                  ALU.mult)
                eng.tensor_tensor(nstat[:, 3:4], nstat[:, 1:2], nstat[:, 1:2],
                                  ALU.mult)
                eng.drain()
                eng.tensor_scalar(nstat[:, 4:5], nstat[:, 2:3], -8.0, None,
                                  ALU.mult)
                eng.tensor_scalar(nstat[:, 5:6], nstat[:, 3:4], -8.0, None,
                                  ALU.mult)
                eng.drain().then_inc(h, 1)
            ph["vector"].append(headmax)

            h_bc, v_bc = INC("r_bc", 1)

            def bcb(eng, v_nr=v_nr, h=h_bc):
                eng.wait_ge(prog.sems["r_nr"][0], v_nr)
                eng.matmul(psX[:, 0:1], onesr[:], nstat[0:1, 4:5],
                           start=True, stop=True)
                eng.matmul(psX[:, 1:2], onesr[:], nstat[0:1, 5:6],
                           start=True, stop=True).then_inc(h, 1)
            ph["tensor"].append(bcb)

            h_bv, v_bv = INC("r_bv", 1)

            def bcv(eng, v_bc=v_bc, h=h_bv):
                eng.wait_ge(prog.sems["r_bc"][0], v_bc)
                eng.tensor_copy(biasA[:], psX[:, 0:1])
                eng.tensor_copy(biasB[:], psX[:, 1:2])
                eng.drain().then_inc(h, 1)
            ph["vector"].append(bcv)

            def ld_fin(eng, v=v_ld_all):
                eng.wait_ge(prog.sems["r_ld"][0], v)
            ph["sync"].append(ld_fin)

            ph = prog.phase()
            for blk in range(NBLK):
                q0 = blk * BQ
                for c in range(NCH):
                    bankA = psQ[(c % 2) * 2]
                    bankB = psQ[(c % 2) * 2 + 1]
                    h_qk, v_qk = INC("t_qk", 1)

                    def qk(eng, c=c, q0=q0, bankA=bankA, bankB=bankB, h=h_qk,
                           nqk=prog.sems["t_qk"][1]):
                        if nqk >= 3:
                            eng.wait_ge(prog.sems["t_ex"][0], nqk - 2)
                        eng.matmul(bankA[:], ktr[0:64, c * 128:c * 128 + 128],
                                   qtr[0:64, q0:q0 + BQ], start=True, stop=True)
                        eng.matmul(bankB[:], ktr[64:128, c * 128:c * 128 + 128],
                                   qtr[64:128, q0:q0 + BQ], start=True,
                                   stop=True).then_inc(h, 1)
                    prog.sem("t_ex")
                    ph["tensor"].append(qk)

                    h_ex, v_ex = INC("t_ex", 1)
                    slotA = (c % 2) * 512
                    slotB = 1024 + (c % 2) * 512

                    def ex(eng, v_qk=v_qk, bankA=bankA, bankB=bankB,
                           slotA=slotA, slotB=slotB, h=h_ex,
                           nex=prog.sems["t_ex"][1]):
                        eng.wait_ge(prog.sems["t_qk"][0], v_qk)
                        if nex >= 3:
                            eng.wait_ge(prog.sems["t_av"][0], nex - 2)
                        eng.activation(pt[:, slotA:slotA + 512], bankA[:],
                                       AF.Exp, bias=biasA[:], scale=0.125)
                        eng.activation(pt[:, slotB:slotB + 512], bankB[:],
                                       AF.Exp, bias=biasB[:],
                                       scale=0.125).then_inc(h, 1)
                    prog.sem("t_av")
                    ph["scalar"].append(ex)

                    h_av, v_av = INC("t_av", 1)
                    v_nrm_need = prog.sem("t_n1")[1] if c == 0 else 0

                    def av(eng, c=c, v_ex=v_ex, slotA=slotA, slotB=slotB,
                           h=h_av, v_nrm_need=v_nrm_need):
                        eng.wait_ge(prog.sems["t_ex"][0], v_ex)
                        if c == 0 and v_nrm_need > 0:
                            eng.wait_ge(prog.sems["t_n1"][0], v_nrm_need)
                        eng.matmul(psVA[0:65, :], vre[:, c * 65:c * 65 + 65],
                                   pt[:, slotA:slotA + 512],
                                   start=(c == 0), stop=(c == NCH - 1))
                        mm = eng.matmul(psVB[0:65, :],
                                        vre[:, (NCH + c) * 65:(NCH + c) * 65 + 65],
                                        pt[:, slotB:slotB + 512],
                                        start=(c == 0), stop=(c == NCH - 1))
                        if c == NCH - 1:
                            mm.then_inc(h, 1)
                        else:
                            eng.nop().then_inc(h, 1)
                    ph["tensor"].append(av)

                v_av_end = prog.sems["t_av"][1]
                h_n1, v_n1 = INC("t_n1", 1)

                def norm1(eng, v_av_end=v_av_end, h=h_n1):
                    eng.wait_ge(prog.sems["t_av"][0], v_av_end)
                    eng.reciprocal(rsum[:], psVA[64:65, :])
                    eng.drain().then_inc(h, 1)
                ph["vector"].append(norm1)

                h_b1, v_b1 = INC("t_b1", 1)

                def bc1(eng, v_n1=v_n1, h=h_b1):
                    eng.wait_ge(prog.sems["t_n1"][0], v_n1)
                    eng.matmul(psX[0:64, :], onesr[0:1, 0:64], rsum[:],
                               start=True, stop=True).then_inc(h, 1)
                ph["tensor"].append(bc1)

                h_n2, v_n2 = INC("t_n1", 1)

                def norm2(eng, v_b1=v_b1, h=h_n2):
                    eng.wait_ge(prog.sems["t_b1"][0], v_b1)
                    eng.tensor_copy(bsum[:, 0:512], psX[0:64, :])
                    eng.drain()
                    eng.reciprocal(rsum[:], psVB[64:65, :])
                    eng.drain().then_inc(h, 1)
                ph["vector"].append(norm2)

                h_b2, v_b2 = INC("t_b1", 1)

                def bc2(eng, v_n2=v_n2, h=h_b2):
                    eng.wait_ge(prog.sems["t_n1"][0], v_n2)
                    eng.matmul(psX[0:64, :], onesr[0:1, 0:64], rsum[:],
                               start=True, stop=True).then_inc(h, 1)
                ph["tensor"].append(bc2)

                h_n3, v_n3 = INC("t_n1", 1)
                stn = "t_st%d" % (blk % 2)
                v_st_need = prog.sem(stn)[1]

                def norm3(eng, blk=blk, v_b2=v_b2, h=h_n3, v_st_need=v_st_need,
                          stn=stn):
                    eng.wait_ge(prog.sems["t_b1"][0], v_b2)
                    if v_st_need > 0:
                        eng.wait_ge(prog.sems[stn][0], v_st_need)
                    eng.tensor_copy(bsum[:, 512:1024], psX[0:64, :])
                    eng.drain()
                    ob = (blk % 2) * 512
                    eng.tensor_tensor(onst[0:64, ob:ob + 512], psVA[0:64, :],
                                      bsum[:, 0:512], ALU.mult)
                    eng.tensor_tensor(onst[64:128, ob:ob + 512], psVB[0:64, :],
                                      bsum[:, 512:1024], ALU.mult)
                    eng.drain().then_inc(h, 1)
                ph["vector"].append(norm3)

                h_st, v_st = INC(stn, 16)

                def store(eng, b=b, u=u, q0=q0, blk=blk, v_n3=v_n3, h=h_st):
                    eng.wait_ge(prog.sems["t_n1"][0], v_n3)
                    ob = (blk % 2) * 512
                    eng.dma_start(out=onT[b, u * 128:u * 128 + 128, q0:q0 + BQ],
                                  in_=onst[:, ob:ob + 512]).then_inc(h, 16)
                ph["gpsimd"].append(store)

            _snap = {n: prog.sems[n][1] for n in ("t_st0", "t_st1")
                     if n in prog.sems}

            def gp_fin(eng, _snap=_snap):
                for n, v in _snap.items():
                    eng.wait_ge(prog.sems[n][0], v)
            ph["gpsimd"].append(gp_fin)

        for b in range(B):
            for u in range(NP):
                emit_pair(b, u)

        prog.emit(block)
    return nc


_A2 = None


def _get_a2():
    global _A2
    if _A2 is None:
        _A2 = build_a2()
    return _A2


def run_a2(qkvT_list):
    nc = _get_a2()
    in_maps = [{"qkvT": q} for q in qkvT_list]
    res = run_bass_kernel_spmd(nc, in_maps, core_ids=list(range(NCORES)))
    return [r["onT"] for r in res.results]




def build_a3():
    nc = bass.Bass("TRN2", target_bir_lowering=False,
                   detect_race_conditions=False)
    onT = nc.dram_tensor("onT", [B, D, S], BF16, kind="ExternalInput").ap()
    x = nc.dram_tensor("x", [B, S, D], F32, kind="ExternalInput").ap()
    out_w = nc.dram_tensor("out_w", [D, D], F32, kind="ExternalInput").ap()
    xpT = nc.dram_tensor("xpT", [B, D, S], F32, kind="ExternalOutput").ap()
    m1 = nc.dram_tensor("m1", [1, 1], F32, kind="ExternalOutput").ap()
    woT_h = nc.dram_tensor("woT_h", [128, 8 * D], F32R, kind="Internal").ap()
    woT_l = nc.dram_tensor("woT_l", [128, 8 * D], F32R, kind="Internal").ap()

    BT = 512
    NBLK = S // BT

    with contextlib.ExitStack() as ctx:
        E = ctx.enter_context
        wtmp = E(nc.sbuf_tensor("wtmp", [128, 256], F32))
        wth = E(nc.sbuf_tensor("wth", [128, 256], F32R))
        wtl = E(nc.sbuf_tensor("wtl", [128, 256], F32R))
        identf = E(nc.sbuf_tensor("identf", [128, 128], F32))
        onescol = E(nc.sbuf_tensor("onescol", [128, 1], F32))
        onesr = E(nc.sbuf_tensor("onesr", [1, 128], F32))
        epsc = E(nc.sbuf_tensor("epsc", [1, 1], F32))
        obuf = E(nc.sbuf_tensor("obuf", [128, 8 * BT], BF16))
        orr = E(nc.sbuf_tensor("orr", [128, 8 * BT], F32R))
        xtm = E(nc.sbuf_tensor("xtm", [128, 2 * D], F32))
        xT = E(nc.sbuf_tensor("xT", [128, 8 * BT], F32))
        xp = E(nc.sbuf_tensor("xp", [128, 8 * BT], F32))
        sqb = E(nc.sbuf_tensor("sqb", [128, 2 * BT], F32))
        wsh = E(nc.sbuf_tensor("wsh", [128, 2 * 1024], F32R))
        wsl = E(nc.sbuf_tensor("wsl", [128, 2 * 1024], F32R))
        rows = E(nc.sbuf_tensor("rows", [1, 16 * BT], F32))
        mub = E(nc.sbuf_tensor("mub", [128, BT], F32))
        rsb = E(nc.sbuf_tensor("rsb", [128, BT], F32))
        zb = E(nc.sbuf_tensor("zb", [128, BT], F32))
        macc = E(nc.sbuf_tensor("macc", [128, 1], F32))
        mcol = E(nc.sbuf_tensor("mcol", [128, 1], F32))
        m1s = E(nc.sbuf_tensor("m1s", [1, 1], F32))
        psA = E(nc.psum_tensor("psA", [128, 512], F32))
        psB = E(nc.psum_tensor("psB", [128, 512], F32))
        psT = E(nc.psum_tensor("psT", [128, 128], F32))
        psT2 = E(nc.psum_tensor("psT2", [128, 128], F32))
        psS = E(nc.psum_tensor("psS", [1, 512], F32))
        psS2 = E(nc.psum_tensor("psS2", [1, 512], F32))
        psX = E(nc.psum_tensor("psX", [128, 512], F32))
        block = E(nc.Block())

        prog = Prog(nc, ctx)

        def INC(name, n=1):
            s = prog.sem(name)
            s[1] += n
            return s[0], s[1]

        ph0 = prog.phase()

        def p0(g):
            h, _ = INC("g0", 5)
            g.memset(identf[:], 0.0).then_inc(h, 1)
            g.wait_ge(h, 1)
            g.affine_select(out=identf[:], in_=identf[:], compare_op=ALU.not_equal,
                            fill=1.0, base=0, pattern=[[-1, 128]],
                            channel_multiplier=1).then_inc(h, 1)
            g.memset(onescol[:], 1.0).then_inc(h, 1)
            g.memset(onesr[:], 1.0).then_inc(h, 1)
            g.memset(epsc[:], EPS).then_inc(h, 1)
            g.wait_ge(h, 5)
            h2, _ = INC("g1", 1)
            g.memset(macc[:], 0.0).then_inc(h2, 1)
            g.wait_ge(h2, 1)
        ph0["gpsimd"].append(p0)

        ph1 = prog.phase()
        for i in range(64):
            of, k = divmod(i, 8)
            buf = (i % 2) * 128
            ldn = "a_ld%d" % (i % 2)
            h_ld, v_ld = INC(ldn, 16)

            def load(eng, i=i, of=of, k=k, buf=buf, h=h_ld):
                if i >= 2:
                    eng.wait_ge(prog.sems["a_tp"][0], i - 1)
                eng.dma_start(out=wtmp[:, buf:buf + 128],
                              in_=out_w[of * 128:of * 128 + 128,
                                        k * 128:k * 128 + 128]).then_inc(h, 16)
            ph1["sync"].append(load)

            h_tp, v_tp = INC("a_tp", 1)

            def tpose(eng, i=i, buf=buf, v_ld=v_ld, h=h_tp, ldn=ldn):
                eng.wait_ge(prog.sems[ldn][0], v_ld)
                if i >= 2:
                    eng.wait_ge(prog.sems["a_vp"][0], i - 1)
                p = psT if i % 2 == 0 else psT2
                eng.transpose(p[:], wtmp[:, buf:buf + 128],
                              identf[:]).then_inc(h, 1)
            ph1["tensor"].append(tpose)

            gpn = "a_gp%d" % (i % 2)
            v_gp_need = prog.sem(gpn)[1]
            h_vp, v_vp = INC("a_vp", 1)

            def split(eng, i=i, v_tp=v_tp, buf=buf, h=h_vp, gpn=gpn,
                      v_gp_need=v_gp_need):
                eng.wait_ge(prog.sems["a_tp"][0], v_tp)
                if v_gp_need > 0:
                    eng.wait_ge(prog.sems[gpn][0], v_gp_need)
                p = psT if i % 2 == 0 else psT2
                eng.tensor_copy(wth[:, buf:buf + 128], p[:])
                eng.tensor_tensor(wtl[:, buf:buf + 128], p[:],
                                  wth[:, buf:buf + 128],
                                  ALU.subtract).then_inc(h, 1)
            ph1["vector"].append(split)

            h_gp, v_gp = INC(gpn, 32)
            col = of * 1024 + k * 128

            def store(eng, v_vp=v_vp, buf=buf, col=col, h=h_gp):
                eng.wait_ge(prog.sems["a_vp"][0], v_vp)
                eng.dma_start(out=woT_h[:, col:col + 128],
                              in_=wth[:, buf:buf + 128]).then_inc(h, 16)
                eng.dma_start(out=woT_l[:, col:col + 128],
                              in_=wtl[:, buf:buf + 128]).then_inc(h, 16)
            ph1["gpsimd"].append(store)

        _snap = {n: prog.sems[n][1] for n in ("a_ld0", "a_ld1", "a_gp0", "a_gp1")
                 if n in prog.sems}

        def p1fin_s(eng, _snap=_snap):
            for n in ("a_ld0", "a_ld1"):
                if n in _snap:
                    eng.wait_ge(prog.sems[n][0], _snap[n])
        ph1["sync"].append(p1fin_s)

        def p1fin_g(eng, _snap=_snap):
            for n in ("a_gp0", "a_gp1"):
                if n in _snap:
                    eng.wait_ge(prog.sems[n][0], _snap[n])
        ph1["gpsimd"].append(p1fin_g)

        def emit_block(b, blk):
            t0 = blk * BT
            ph = prog.phase()
            for c in range(8):
                h_ld, v_ld = INC("o_ld", 16)

                def ldo(eng, b=b, c=c, t0=t0, h=h_ld):
                    eng.dma_start(out=obuf[:, c * BT:c * BT + BT],
                                  in_=onT[b, c * 128:c * 128 + 128,
                                          t0:t0 + BT]).then_inc(h, 16)
                ph["sync"].append(ldo)
            v_old = prog.sems["o_ld"][1]

            h_or, v_or = INC("o_or", 1)

            def ocvt(eng, v_old=v_old, h=h_or):
                eng.wait_ge(prog.sems["o_ld"][0], v_old)
                for c in range(8):
                    eng.tensor_copy(orr[:, c * BT:c * BT + BT],
                                    obuf[:, c * BT:c * BT + BT])
                eng.drain().then_inc(h, 1)
            ph["vector"].append(ocvt)

            for i4 in range(4):
                xldn = "x_ld%d" % (i4 % 2)
                h_xl, v_xl = INC(xldn, 16)
                v_tp_need = max(prog.sem("x_tp")[1] - 8, 0)

                def ldx(eng, b=b, t0=t0, i4=i4, h=h_xl, v_tp_need=v_tp_need):
                    if v_tp_need > 0:
                        eng.wait_ge(prog.sems["x_tp"][0], v_tp_need)
                    eng.dma_start(out=xtm[:, (i4 % 2) * D:(i4 % 2) * D + D],
                                  in_=x[b, t0 + i4 * 128:t0 + i4 * 128 + 128, :]
                                  ).then_inc(h, 16)
                ph["sync"].append(ldx)

                for c in range(8):
                    h_tp, v_tp = INC("x_tp", 1)

                    def xtp(eng, i4=i4, c=c, v_xl=v_xl, h=h_tp, xldn=xldn,
                            ntp=prog.sems["x_tp"][1]):
                        eng.wait_ge(prog.sems[xldn][0], v_xl)
                        if ntp >= 3:
                            eng.wait_ge(prog.sems["x_ev"][0], ntp - 2)
                        p = psT if ntp % 2 == 1 else psT2
                        eng.transpose(p[:], xtm[:, (i4 % 2) * D + c * 128:
                                                (i4 % 2) * D + c * 128 + 128],
                                      identf[:]).then_inc(h, 1)
                    prog.sem("x_ev")
                    ph["tensor"].append(xtp)

                    h_ev, v_ev = INC("x_ev", 1)
                    dst = c * BT + i4 * 128

                    def xev(eng, v_tp=v_tp, dst=dst, h=h_ev,
                            ntp=prog.sems["x_tp"][1]):
                        eng.wait_ge(prog.sems["x_tp"][0], v_tp)
                        p = psT if ntp % 2 == 1 else psT2
                        eng.tensor_copy(xT[:, dst:dst + 128], p[:]).then_inc(h, 1)
                    ph["vector"].append(xev)

            for of in range(8):
                buf = (of % 2) * 1024
                wsn = "w_ws%d" % (of % 2)
                v_need = prog.sem("w_mm")[1] - 1 if of >= 2 else 0
                h_ws, v_ws = INC(wsn, 32)

                def wload(eng, of=of, buf=buf, h=h_ws, v_need=v_need):
                    if v_need > 0:
                        eng.wait_ge(prog.sems["w_mm"][0], v_need)
                    eng.dma_start(out=wsh[:, buf:buf + 1024],
                                  in_=woT_h[:, of * 1024:of * 1024 + 1024]
                                  ).then_inc(h, 16)
                    eng.dma_start(out=wsl[:, buf:buf + 1024],
                                  in_=woT_l[:, of * 1024:of * 1024 + 1024]
                                  ).then_inc(h, 16)
                ph["sync"].append(wload)

                bank = psA if of % 2 == 0 else psB
                h_mm, v_mm = INC("w_mm", 1)
                v_or_c = prog.sems["o_or"][1]

                def wmm(eng, of=of, buf=buf, bank=bank, v_ws=v_ws, h=h_mm,
                        wsn=wsn, v_or_c=v_or_c, nmm=prog.sems["w_mm"][1]):
                    eng.wait_ge(prog.sems[wsn][0], v_ws)
                    eng.wait_ge(prog.sems["o_or"][0], v_or_c)
                    if nmm >= 3:
                        eng.wait_ge(prog.sems["w_ev"][0], nmm - 2)
                    for k in range(8):
                        wcol = buf + k * 128
                        eng.matmul(bank[:, 0:BT], wsh[:, wcol:wcol + 128],
                                   orr[:, k * BT:k * BT + BT],
                                   start=(k == 0), stop=False)
                        mm = eng.matmul(bank[:, 0:BT], wsl[:, wcol:wcol + 128],
                                        orr[:, k * BT:k * BT + BT],
                                        start=False, stop=(k == 7))
                        if k == 7:
                            mm.then_inc(h, 1)
                prog.sem("w_ev")
                ph["tensor"].append(wmm)

                v_xev_all = prog.sems["x_ev"][1]
                h_ev, v_ev = INC("w_ev", 1)

                def wev(eng, of=of, bank=bank, v_mm=v_mm, v_xev_all=v_xev_all,
                        h=h_ev):
                    eng.wait_ge(prog.sems["w_mm"][0], v_mm)
                    eng.wait_ge(prog.sems["x_ev"][0], v_xev_all)
                    eng.tensor_tensor(xp[:, of * BT:of * BT + BT], bank[:, 0:BT],
                                      xT[:, of * BT:of * BT + BT],
                                      ALU.add).then_inc(h, 1)
                ph["vector"].append(wev)

                h_sq, v_sq = INC("o_sq", 1)

                def dosq(eng, of=of, v_ev=v_ev, h=h_sq):
                    eng.wait_ge(prog.sems["w_ev"][0], v_ev)
                    eng.activation(sqb[:, (of % 2) * BT:(of % 2) * BT + BT],
                                   xp[:, of * BT:of * BT + BT],
                                   AF.Square).then_inc(h, 1)
                ph["scalar"].append(dosq)

                h_sm, v_sm = INC("o_sm", 1)

                def statmm(eng, of=of, v_sq=v_sq, h=h_sm):
                    eng.wait_ge(prog.sems["o_sq"][0], v_sq)
                    eng.matmul(psS[0:1, 0:BT], onescol[:],
                               xp[:, of * BT:of * BT + BT],
                               start=(of == 0), stop=(of == 7))
                    mm = eng.matmul(psS2[0:1, 0:BT], onescol[:],
                                    sqb[:, (of % 2) * BT:(of % 2) * BT + BT],
                                    start=(of == 0), stop=(of == 7))
                    if of == 7:
                        mm.then_inc(h, 1)
                    else:
                        eng.nop().then_inc(h, 1)
                ph["tensor"].append(statmm)

                stn = "o_st%d" % (of % 2)
                h_st, v_st = INC(stn, 16)

                def stx(eng, b=b, of=of, t0=t0, v_ev=v_ev, h=h_st):
                    eng.wait_ge(prog.sems["w_ev"][0], v_ev)
                    eng.dma_start(out=xpT[b, of * 128:of * 128 + 128, t0:t0 + BT],
                                  in_=xp[:, of * BT:of * BT + BT]).then_inc(h, 16)
                ph["gpsimd"].append(stx)

            v_sm_all = prog.sems["o_sm"][1]
            h_r1, v_r1 = INC("o_r1", 1)

            def rowstats(eng, v_sm_all=v_sm_all, h=h_r1):
                eng.wait_ge(prog.sems["o_sm"][0], v_sm_all)
                eng.tensor_scalar(rows[:, 0:BT], psS[0:1, 0:BT], 1.0 / D, None,
                                  ALU.mult)
                eng.tensor_scalar(rows[:, BT:2 * BT], psS2[0:1, 0:BT], 1.0 / D,
                                  None, ALU.mult)
                eng.drain()
                eng.tensor_tensor(rows[:, 2 * BT:3 * BT], rows[:, 0:BT],
                                  rows[:, 0:BT], ALU.mult)
                eng.drain()
                eng.tensor_tensor(rows[:, 3 * BT:4 * BT], rows[:, BT:2 * BT],
                                  rows[:, 2 * BT:3 * BT], ALU.subtract)
                eng.drain().then_inc(h, 1)
            ph["vector"].append(rowstats)

            h_r2, v_r2 = INC("o_r2", 1)

            def rowsqrt(eng, v_r1=v_r1, h=h_r2):
                eng.wait_ge(prog.sems["o_r1"][0], v_r1)
                eng.activation(rows[:, 4 * BT:5 * BT], rows[:, 3 * BT:4 * BT],
                               AF.Sqrt,
                               bias=epsc[:].broadcast_to((1, 1))).then_inc(h, 1)
            ph["scalar"].append(rowsqrt)

            h_r3, v_r3 = INC("o_r1", 1)

            def rowrecip(eng, v_r2=v_r2, h=h_r3):
                eng.wait_ge(prog.sems["o_r2"][0], v_r2)
                eng.reciprocal(rows[:, 5 * BT:6 * BT], rows[:, 4 * BT:5 * BT])
                eng.drain()
                eng.tensor_tensor(rows[:, 6 * BT:7 * BT], rows[:, 4 * BT:5 * BT],
                                  rows[:, 5 * BT:6 * BT], ALU.mult)
                eng.drain()
                eng.tensor_scalar(rows[:, 6 * BT:7 * BT], rows[:, 6 * BT:7 * BT],
                                  -1.0, 2.0, ALU.mult, ALU.add)
                eng.drain()
                eng.tensor_tensor(rows[:, 5 * BT:6 * BT], rows[:, 5 * BT:6 * BT],
                                  rows[:, 6 * BT:7 * BT], ALU.mult)
                eng.drain().then_inc(h, 1)
            ph["vector"].append(rowrecip)

            h_bc, v_bc = INC("o_bc", 1)

            def statbc(eng, v_r3=v_r3, h=h_bc):
                eng.wait_ge(prog.sems["o_r1"][0], v_r3)
                eng.matmul(psX[:, 0:BT], onesr[:], rows[:, 0:BT],
                           start=True, stop=True).then_inc(h, 1)
            ph["tensor"].append(statbc)

            h_bv, v_bv = INC("o_r1", 1)

            def statbv(eng, v_bc=v_bc, h=h_bv):
                eng.wait_ge(prog.sems["o_bc"][0], v_bc)
                eng.tensor_copy(mub[:], psX[:, 0:BT])
                eng.drain().then_inc(h, 1)
            ph["vector"].append(statbv)

            h_bc2, v_bc2 = INC("o_bc", 1)

            def statbc2(eng, v_bv=v_bv, h=h_bc2):
                eng.wait_ge(prog.sems["o_r1"][0], v_bv)
                eng.matmul(psX[:, 0:BT], onesr[:], rows[:, 5 * BT:6 * BT],
                           start=True, stop=True).then_inc(h, 1)
            ph["tensor"].append(statbc2)

            h_bv2, v_bv2 = INC("o_r1", 1)

            def statbv2(eng, v_bc2=v_bc2, h=h_bv2):
                eng.wait_ge(prog.sems["o_bc"][0], v_bc2)
                eng.tensor_copy(rsb[:], psX[:, 0:BT])
                eng.drain().then_inc(h, 1)
            ph["vector"].append(statbv2)

            h_mx, v_mx = INC("o_r1", 1)

            def lnmax(eng, v_bv2=v_bv2, h=h_mx):
                eng.wait_ge(prog.sems["o_r1"][0], v_bv2)
                for c in range(8):
                    eng.tensor_tensor(zb[:], xp[:, c * BT:c * BT + BT], mub[:],
                                      ALU.subtract)
                    eng.drain()
                    eng.tensor_tensor(zb[:], zb[:], rsb[:], ALU.mult)
                    eng.drain()
                    eng.tensor_reduce(mcol[:], zb[:], AX.X, ALU.max,
                                      apply_absolute_value=True)
                    eng.drain()
                    eng.tensor_tensor(macc[:], macc[:], mcol[:], ALU.max)
                    eng.drain()
                eng.nop().then_inc(h, 1)
            ph["vector"].append(lnmax)

            _snap = {n: prog.sems[n][1] for n in
                     ("o_ld", "x_ld0", "x_ld1", "w_ws0", "w_ws1") if n in prog.sems}

            def sfin(eng, _snap=_snap):
                for n, v in _snap.items():
                    eng.wait_ge(prog.sems[n][0], v)
            ph["sync"].append(sfin)

            _snap2 = {n: prog.sems[n][1] for n in ("o_st0", "o_st1")
                      if n in prog.sems}

            def gfin(eng, _snap2=_snap2):
                for n, v in _snap2.items():
                    eng.wait_ge(prog.sems[n][0], v)
            ph["gpsimd"].append(gfin)

        for b in range(B):
            for blk in range(NBLK):
                emit_block(b, blk)

        phf = prog.phase()
        h_ftp, v_ftp = INC("f_tp", 1)
        h_fvp, v_fvp = INC("f_vp", 1)
        h_fst, v_fst = INC("f_st", 16)

        def fin_t(eng, h=h_ftp):
            eng.transpose(psT[0:1, 0:128], macc[:], identf[:]).then_inc(h, 1)
        phf["tensor"].append(fin_t)

        def fin_v(eng, h=h_fvp, v_ftp=v_ftp):
            eng.wait_ge(prog.sems["f_tp"][0], v_ftp)
            eng.tensor_reduce(m1s[:], psT[0:1, 0:128], AX.X, ALU.max)
            eng.drain().then_inc(h, 1)
        phf["vector"].append(fin_v)

        def fin_g(eng, h=h_fst, v_fvp=v_fvp, v_fst=v_fst):
            eng.wait_ge(prog.sems["f_vp"][0], v_fvp)
            eng.dma_start(out=m1[:], in_=m1s[:]).then_inc(h, 16)
            eng.wait_ge(h, v_fst)
        phf["gpsimd"].append(fin_g)

        prog.emit(block)
    return nc



def build_b1():
    nc = bass.Bass("TRN2", target_bir_lowering=False,
                   detect_race_conditions=False)
    xpT = nc.dram_tensor("xpT", [B, D, S], F32, kind="ExternalInput").ap()
    m1a = nc.dram_tensor("m1a", [1, 8], F32, kind="ExternalInput").ap()
    w_up = nc.dram_tensor("w_up", [DF, D], F32, kind="ExternalInput").ap()
    w_gate = nc.dram_tensor("w_gate", [DF, D], F32, kind="ExternalInput").ap()
    up_raw = nc.dram_tensor("up_raw", [DF, TOK], F32, kind="ExternalOutput").ap()
    gt_raw = nc.dram_tensor("gt_raw", [DF, TOK], F32, kind="ExternalOutput").ap()
    m2 = nc.dram_tensor("m2", [1, 1], F32, kind="ExternalOutput").ap()
    m3 = nc.dram_tensor("m3", [1, 1], F32, kind="ExternalOutput").ap()
    upT_h = nc.dram_tensor("upT_h", [128, 24 * 1024], F32R, kind="Internal").ap()
    upT_l = nc.dram_tensor("upT_l", [128, 24 * 1024], F32R, kind="Internal").ap()
    gtT_h = nc.dram_tensor("gtT_h", [128, 24 * 1024], F32R, kind="Internal").ap()
    gtT_l = nc.dram_tensor("gtT_l", [128, 24 * 1024], F32R, kind="Internal").ap()

    BT = 512
    TB = 1024
    NSL = TB // BT

    with contextlib.ExitStack() as ctx:
        E = ctx.enter_context
        identf = E(nc.sbuf_tensor("identf", [128, 128], F32))
        onescol = E(nc.sbuf_tensor("onescol", [128, 1], F32))
        onesr = E(nc.sbuf_tensor("onesr", [1, 128], F32))
        epsc = E(nc.sbuf_tensor("epsc", [1, 1], F32))
        sc = E(nc.sbuf_tensor("sc", [1, 16], F32))
        s1col = E(nc.sbuf_tensor("s1col", [128, 1], F32))
        wbuf = E(nc.sbuf_tensor("wbuf", [128, 2 * 1024], F32))
        wg = E(nc.sbuf_tensor("wg", [128, 2 * 32], F32))
        wq = E(nc.sbuf_tensor("wq", [128, 1024], I16))
        wdq = E(nc.sbuf_tensor("wdq", [128, 2 * 1024], F32))
        wth = E(nc.sbuf_tensor("wth", [128, 256], F32R))
        wtl = E(nc.sbuf_tensor("wtl", [128, 256], F32R))
        xpb = E(nc.sbuf_tensor("xpb", [128, 8 * BT], F32))
        sqb = E(nc.sbuf_tensor("sqb", [128, 2 * BT], F32))
        rows = E(nc.sbuf_tensor("rows", [1, 16 * BT], F32))
        mub = E(nc.sbuf_tensor("mub", [128, BT], F32))
        rsb = E(nc.sbuf_tensor("rsb", [128, BT], F32))
        zb = E(nc.sbuf_tensor("zb", [128, BT], F32))
        yqi = E(nc.sbuf_tensor("yqi", [128, BT], I16))
        yqh = E(nc.sbuf_tensor("yqh", [128, 8 * TB], F32R))
        yql = E(nc.sbuf_tensor("yql", [128, 8 * TB], F32R))
        wsh = E(nc.sbuf_tensor("wsh", [128, 2 * 1024], F32R))
        wsl = E(nc.sbuf_tensor("wsl", [128, 2 * 1024], F32R))
        est = E(nc.sbuf_tensor("est", [128, 4 * BT], F32))
        macc = E(nc.sbuf_tensor("macc", [128, 2], F32))
        mcol = E(nc.sbuf_tensor("mcol", [128, 1], F32))
        mrow = E(nc.sbuf_tensor("mrow", [1, 2], F32))
        psA = E(nc.psum_tensor("psA", [128, 512], F32))
        psB = E(nc.psum_tensor("psB", [128, 512], F32))
        psC = E(nc.psum_tensor("psC", [128, 512], F32))
        psD = E(nc.psum_tensor("psD", [128, 512], F32))
        psT = E(nc.psum_tensor("psT", [128, 128], F32))
        psT2 = E(nc.psum_tensor("psT2", [128, 128], F32))
        psS = E(nc.psum_tensor("psS", [1, 512], F32))
        psS2 = E(nc.psum_tensor("psS2", [1, 512], F32))
        block = E(nc.Block())

        prog = Prog(nc, ctx)

        def INC(name, n=1):
            s = prog.sem(name)
            s[1] += n
            return s[0], s[1]

        ph0 = prog.phase()

        def p0(g):
            h, _ = INC("g0", 6)
            g.memset(identf[:], 0.0).then_inc(h, 1)
            g.wait_ge(h, 1)
            g.affine_select(out=identf[:], in_=identf[:], compare_op=ALU.not_equal,
                            fill=1.0, base=0, pattern=[[-1, 128]],
                            channel_multiplier=1).then_inc(h, 1)
            g.memset(onescol[:], 1.0).then_inc(h, 1)
            g.memset(onesr[:], 1.0).then_inc(h, 1)
            g.memset(epsc[:], EPS).then_inc(h, 1)
            g.memset(macc[:], 0.0).then_inc(h, 1)
            g.wait_ge(h, 6)
        ph0["gpsimd"].append(p0)

        def p0s(sync):
            h, v = INC("s_ld", 16)
            sync.dma_start(out=sc[:, 8:16], in_=m1a[:]).then_inc(h, 16)
            sync.wait_ge(h, v)
        ph0["sync"].append(p0s)

        ph0b = prog.phase()
        h_svp1, v_svp1 = INC("s_vp", 1)
        h_stp1, v_stp1 = INC("s_tp", 1)
        h_svp2, v_svp2 = INC("s_vp", 1)

        def p0v(eng, h=h_svp1):
            eng.tensor_reduce(sc[:, 0:1], sc[:, 8:16], AX.X, ALU.max)
            eng.drain()
            eng.tensor_scalar(sc[:, 1:2], sc[:, 0:1], 1.0 / QMAX, 1e-12,
                              ALU.mult, ALU.max)
            eng.drain()
            eng.reciprocal(sc[:, 2:3], sc[:, 1:2])
            eng.drain()
            eng.tensor_tensor(sc[:, 3:4], sc[:, 1:2], sc[:, 2:3], ALU.mult)
            eng.drain()
            eng.tensor_scalar(sc[:, 3:4], sc[:, 3:4], -1.0, 2.0, ALU.mult,
                              ALU.add)
            eng.drain()
            eng.tensor_tensor(sc[:, 2:3], sc[:, 2:3], sc[:, 3:4], ALU.mult)
            eng.drain().then_inc(h, 1)
        ph0b["vector"].append(p0v)

        def p0t(eng, h=h_stp1, v_svp1=v_svp1):
            eng.wait_ge(prog.sems["s_vp"][0], v_svp1)
            eng.matmul(psT[:, 0:1], onesr[:], sc[0:1, 1:2],
                       start=True, stop=True).then_inc(h, 1)
        ph0b["tensor"].append(p0t)

        def p0v2(eng, h=h_svp2, v_stp1=v_stp1):
            eng.wait_ge(prog.sems["s_tp"][0], v_stp1)
            eng.tensor_copy(s1col[:], psT[:, 0:1])
            eng.drain().then_inc(h, 1)
        ph0b["vector"].append(p0v2)

        ph1 = prog.phase()
        wplan = []
        for wi, (wsrc, dsth, dstl) in enumerate(
                ((w_up, upT_h, upT_l), (w_gate, gtT_h, gtT_l))):
            for r in range(24):
                wplan.append((wsrc, dsth, dstl, r))
        for i, (wsrc, dsth, dstl, r) in enumerate(wplan):
            buf = (i % 2) * 1024
            ldn = "b_ld%d" % (i % 2)
            h_ld, v_ld = INC(ldn, 16)

            def load(eng, i=i, wsrc=wsrc, r=r, buf=buf, h=h_ld):
                if i >= 2:
                    eng.wait_ge(prog.sems["b_dq"][0], i - 1)
                eng.dma_start(out=wbuf[:, buf:buf + 1024],
                              in_=wsrc[r * 128:r * 128 + 128, :]).then_inc(h, 16)
            ph1["sync"].append(load)

            h_dq, v_dq = INC("b_dq", 1)

            def dequant(eng, i=i, buf=buf, v_ld=v_ld, h=h_dq, ldn=ldn):
                eng.wait_ge(prog.sems[ldn][0], v_ld)
                if i >= 2:
                    eng.wait_ge(prog.sems["b_tp"][0], (i - 1) * 8)
                gb = (i % 2) * 32
                wb3 = wbuf[:, buf:buf + 1024].rearrange("p (g c) -> p g c", g=32)
                eng.tensor_reduce(
                    wg[:, gb:gb + 32].rearrange("p (g o) -> p g o", o=1),
                    wb3, AX.X, ALU.max, apply_absolute_value=True)
                eng.drain()
                eng.tensor_scalar(wg[:, gb:gb + 32], wg[:, gb:gb + 32],
                                  1.0 / 7.0, 1e-12, ALU.mult, ALU.max)
                eng.drain()
                eng.reciprocal(zb[:, 0:32], wg[:, gb:gb + 32])
                eng.drain()
                eng.tensor_tensor(zb[:, 32:64], wg[:, gb:gb + 32], zb[:, 0:32],
                                  ALU.mult)
                eng.drain()
                eng.tensor_scalar(zb[:, 32:64], zb[:, 32:64], -1.0, 2.0,
                                  ALU.mult, ALU.add)
                eng.drain()
                eng.tensor_tensor(zb[:, 0:32], zb[:, 0:32], zb[:, 32:64],
                                  ALU.mult)
                eng.drain()
                eng.tensor_tensor(
                    wdq[:, buf:buf + 1024].rearrange("p (g c) -> p g c", g=32),
                    wb3,
                    zb[:, 0:32].rearrange("p (g o) -> p g o", o=1).broadcast_to(
                        (128, 32, 32)), ALU.mult)
                eng.drain()
                eng.tensor_copy(wq[:], wdq[:, buf:buf + 1024])
                eng.drain()
                eng.tensor_scalar(wq[:], wq[:], 7.0, -8.0, ALU.min, ALU.max)
                eng.drain()
                eng.tensor_tensor(
                    wdq[:, buf:buf + 1024].rearrange("p (g c) -> p g c", g=32),
                    wq.rearrange("p (g c) -> p g c", g=32),
                    wg[:, gb:gb + 32].rearrange("p (g o) -> p g o",
                                                o=1).broadcast_to((128, 32, 32)),
                    ALU.mult)
                eng.drain()
                eng.tensor_scalar(wdq[:, buf:buf + 1024], wdq[:, buf:buf + 1024],
                                  s1col[:], None, ALU.mult)
                eng.drain().then_inc(h, 1)
            prog.sem("b_tp")
            ph1["vector"].append(dequant)

            for k in range(8):
                h_tp, v_tp = INC("b_tp", 1)

                def tpose(eng, k=k, buf=buf, v_dq=v_dq, h=h_tp,
                          ntp=prog.sems["b_tp"][1]):
                    eng.wait_ge(prog.sems["b_dq"][0], v_dq)
                    if ntp >= 3:
                        eng.wait_ge(prog.sems["b_sp"][0], ntp - 2)
                    p = psT if ntp % 2 == 1 else psT2
                    eng.transpose(p[:], wdq[:, buf + k * 128:buf + k * 128 + 128],
                                  identf[:]).then_inc(h, 1)
                prog.sem("b_sp")
                ph1["tensor"].append(tpose)

                sbuf2 = (prog.sems["b_sp"][1] % 2) * 128
                gpn = "b_gp%d" % (prog.sems["b_sp"][1] % 2)
                v_gp_need = prog.sem(gpn)[1]
                h_sp, v_sp = INC("b_sp", 1)

                def split(eng, v_tp=v_tp, sbuf2=sbuf2, h=h_sp, gpn=gpn,
                          v_gp_need=v_gp_need, ntp=prog.sems["b_tp"][1]):
                    eng.wait_ge(prog.sems["b_tp"][0], v_tp)
                    if v_gp_need > 0:
                        eng.wait_ge(prog.sems[gpn][0], v_gp_need)
                    p = psT if ntp % 2 == 1 else psT2
                    eng.tensor_copy(wth[:, sbuf2:sbuf2 + 128], p[:])
                    eng.tensor_tensor(wtl[:, sbuf2:sbuf2 + 128], p[:],
                                      wth[:, sbuf2:sbuf2 + 128],
                                      ALU.subtract).then_inc(h, 1)
                ph1["vector"].append(split)

                h_gp, v_gp = INC(gpn, 32)
                col = r * 1024 + k * 128

                def store(eng, v_sp=v_sp, sbuf2=sbuf2, col=col, h=h_gp,
                          dsth=dsth, dstl=dstl):
                    eng.wait_ge(prog.sems["b_sp"][0], v_sp)
                    eng.dma_start(out=dsth[:, col:col + 128],
                                  in_=wth[:, sbuf2:sbuf2 + 128]).then_inc(h, 16)
                    eng.dma_start(out=dstl[:, col:col + 128],
                                  in_=wtl[:, sbuf2:sbuf2 + 128]).then_inc(h, 16)
                ph["gpsimd"].append(store) if False else ph1["gpsimd"].append(store)

        _snap = {n: prog.sems[n][1] for n in
                 ("b_ld0", "b_ld1", "b_gp0", "b_gp1") if n in prog.sems}

        def p1fin_s(eng, _snap=_snap):
            for n in ("b_ld0", "b_ld1"):
                if n in _snap:
                    eng.wait_ge(prog.sems[n][0], _snap[n])
        ph1["sync"].append(p1fin_s)

        def p1fin_g(eng, _snap=_snap):
            for n in ("b_gp0", "b_gp1"):
                if n in _snap:
                    eng.wait_ge(prog.sems[n][0], _snap[n])
        ph1["gpsimd"].append(p1fin_g)

        def emit_xq(tb):
            for sl in range(NSL):
                t0 = tb * TB + sl * BT
                b, s0 = divmod(t0, S)
                ph = prog.phase()
                for c in range(8):
                    h_ld, v_ld = INC("q_ld", 16)

                    def ldx(eng, b=b, s0=s0, c=c, h=h_ld):
                        eng.dma_start(out=xpb[:, c * BT:c * BT + BT],
                                      in_=xpT[b, c * 128:c * 128 + 128,
                                              s0:s0 + BT]).then_inc(h, 16)
                    ph["sync"].append(ldx)
                v_ld_all = prog.sems["q_ld"][1]

                for c in range(8):
                    h_sq, v_sq = INC("q_sq", 1)

                    def dosq(eng, c=c, v_ld_all=v_ld_all, h=h_sq):
                        eng.wait_ge(prog.sems["q_ld"][0], v_ld_all)
                        eng.activation(sqb[:, (c % 2) * BT:(c % 2) * BT + BT],
                                       xpb[:, c * BT:c * BT + BT],
                                       AF.Square).then_inc(h, 1)
                    ph["scalar"].append(dosq)

                    h_sm, v_sm = INC("q_sm", 1)

                    def statmm(eng, c=c, v_sq=v_sq, v_ld_all=v_ld_all, h=h_sm):
                        eng.wait_ge(prog.sems["q_sq"][0], v_sq)
                        eng.matmul(psS[0:1, 0:BT], onescol[:],
                                   xpb[:, c * BT:c * BT + BT],
                                   start=(c == 0), stop=(c == 7))
                        mm = eng.matmul(psS2[0:1, 0:BT], onescol[:],
                                        sqb[:, (c % 2) * BT:(c % 2) * BT + BT],
                                        start=(c == 0), stop=(c == 7))
                        if c == 7:
                            mm.then_inc(h, 1)
                        else:
                            eng.nop().then_inc(h, 1)
                    ph["tensor"].append(statmm)

                v_sm_all = prog.sems["q_sm"][1]
                h_r1, v_r1 = INC("q_r1", 1)

                def rowstats(eng, v_sm_all=v_sm_all, h=h_r1):
                    eng.wait_ge(prog.sems["q_sm"][0], v_sm_all)
                    eng.tensor_scalar(rows[:, 0:BT], psS[0:1, 0:BT], 1.0 / D,
                                      None, ALU.mult)
                    eng.tensor_scalar(rows[:, BT:2 * BT], psS2[0:1, 0:BT],
                                      1.0 / D, None, ALU.mult)
                    eng.drain()
                    eng.tensor_tensor(rows[:, 2 * BT:3 * BT], rows[:, 0:BT],
                                      rows[:, 0:BT], ALU.mult)
                    eng.drain()
                    eng.tensor_tensor(rows[:, 3 * BT:4 * BT], rows[:, BT:2 * BT],
                                      rows[:, 2 * BT:3 * BT], ALU.subtract)
                    eng.drain().then_inc(h, 1)
                ph["vector"].append(rowstats)

                h_r2, v_r2 = INC("q_r2", 1)

                def rowsqrt(eng, v_r1=v_r1, h=h_r2):
                    eng.wait_ge(prog.sems["q_r1"][0], v_r1)
                    eng.activation(rows[:, 4 * BT:5 * BT], rows[:, 3 * BT:4 * BT],
                                   AF.Sqrt, bias=epsc[:]).then_inc(h, 1)
                ph["scalar"].append(rowsqrt)

                h_r3, v_r3 = INC("q_r1", 1)

                def rowrecip(eng, v_r2=v_r2, h=h_r3):
                    eng.wait_ge(prog.sems["q_r2"][0], v_r2)
                    eng.reciprocal(rows[:, 5 * BT:6 * BT], rows[:, 4 * BT:5 * BT])
                    eng.drain()
                    eng.tensor_tensor(rows[:, 6 * BT:7 * BT],
                                      rows[:, 4 * BT:5 * BT],
                                      rows[:, 5 * BT:6 * BT], ALU.mult)
                    eng.drain()
                    eng.tensor_scalar(rows[:, 6 * BT:7 * BT],
                                      rows[:, 6 * BT:7 * BT], -1.0, 2.0,
                                      ALU.mult, ALU.add)
                    eng.drain()
                    eng.tensor_tensor(rows[:, 5 * BT:6 * BT],
                                      rows[:, 5 * BT:6 * BT],
                                      rows[:, 6 * BT:7 * BT], ALU.mult)
                    eng.drain()
                    eng.tensor_tensor(rows[:, 5 * BT:6 * BT],
                                      rows[:, 5 * BT:6 * BT],
                                      sc[0:1, 2:3].broadcast_to((1, BT)),
                                      ALU.mult)
                    eng.drain()
                    eng.tensor_tensor(rows[:, 7 * BT:8 * BT], rows[:, 0:BT],
                                      rows[:, 5 * BT:6 * BT], ALU.mult)
                    eng.drain()
                    eng.tensor_scalar(rows[:, 7 * BT:8 * BT],
                                      rows[:, 7 * BT:8 * BT], -1.0, None,
                                      ALU.mult)
                    eng.drain().then_inc(h, 1)
                ph["vector"].append(rowrecip)

                h_bc, v_bc = INC("q_bc", 1)

                def statbc(eng, v_r3=v_r3, h=h_bc):
                    eng.wait_ge(prog.sems["q_r1"][0], v_r3)
                    eng.matmul(psC[:, 0:BT], onesr[:], rows[:, 5 * BT:6 * BT],
                               start=True, stop=True)
                    eng.matmul(psD[:, 0:BT], onesr[:], rows[:, 7 * BT:8 * BT],
                               start=True, stop=True).then_inc(h, 1)
                ph["tensor"].append(statbc)

                h_bv, v_bv = INC("q_r1", 1)

                def statbv(eng, v_bc=v_bc, h=h_bv):
                    eng.wait_ge(prog.sems["q_bc"][0], v_bc)
                    eng.tensor_copy(rsb[:], psC[:, 0:BT])
                    eng.tensor_copy(mub[:], psD[:, 0:BT])
                    eng.drain().then_inc(h, 1)
                ph["vector"].append(statbv)

                h_yq, v_yq = INC("q_r1", 1)

                def doyq(eng, sl=sl, v_bv=v_bv, h=h_yq):
                    eng.wait_ge(prog.sems["q_r1"][0], v_bv)
                    for c in range(8):
                        eng.tensor_tensor(zb[:], xpb[:, c * BT:c * BT + BT],
                                          rsb[:], ALU.mult)
                        eng.drain()
                        eng.tensor_tensor(zb[:], zb[:], mub[:], ALU.add)
                        eng.drain()
                        eng.tensor_copy(yqi[:], zb[:])
                        eng.drain()
                        dst = c * TB + sl * BT
                        eng.tensor_copy(yqh[:, dst:dst + BT], yqi[:])
                        eng.drain()
                        eng.tensor_tensor(yql[:, dst:dst + BT], yqi[:],
                                          yqh[:, dst:dst + BT], ALU.subtract)
                        eng.drain()
                    eng.nop().then_inc(h, 1)
                ph["vector"].append(doyq)

                _snap = {"q_ld": prog.sems["q_ld"][1]}

                def sfin(eng, _snap=_snap):
                    eng.wait_ge(prog.sems["q_ld"][0], _snap["q_ld"])
                ph["sync"].append(sfin)

        def emit_gemm(tb, which):
            srch = upT_h if which == 0 else gtT_h
            srcl = upT_l if which == 0 else gtT_l
            dst = up_raw if which == 0 else gt_raw
            ph = prog.phase()
            for of in range(24):
                buf = (of % 2) * 1024
                wsn = "g_ws%d" % (of % 2)
                v_need = prog.sem("g_mm")[1] - 1 if of >= 2 else 0
                h_ws, v_ws = INC(wsn, 32)

                def wload(eng, of=of, buf=buf, h=h_ws, v_need=v_need,
                          srch=srch, srcl=srcl):
                    if v_need > 0:
                        eng.wait_ge(prog.sems["g_mm"][0], v_need)
                    eng.dma_start(out=wsh[:, buf:buf + 1024],
                                  in_=srch[:, of * 1024:of * 1024 + 1024]
                                  ).then_inc(h, 16)
                    eng.dma_start(out=wsl[:, buf:buf + 1024],
                                  in_=srcl[:, of * 1024:of * 1024 + 1024]
                                  ).then_inc(h, 16)
                ph["sync"].append(wload)

                banks = (psA, psB) if of % 2 == 0 else (psC, psD)
                h_mm, v_mm = INC("g_mm", 1)

                def gmm(eng, of=of, buf=buf, banks=banks, v_ws=v_ws, h=h_mm,
                        wsn=wsn, nmm=prog.sems["g_mm"][1]):
                    eng.wait_ge(prog.sems[wsn][0], v_ws)
                    if nmm >= 3:
                        eng.wait_ge(prog.sems["g_ev"][0], (nmm - 2) * 2)
                    for k in range(8):
                        wcol = buf + k * 128
                        for sl in range(NSL):
                            bank = banks[sl]
                            eng.matmul(bank[:, 0:BT], wsh[:, wcol:wcol + 128],
                                       yqh[:, k * TB + sl * BT:
                                           k * TB + sl * BT + BT],
                                       start=(k == 0), stop=False)
                            eng.matmul(bank[:, 0:BT], wsh[:, wcol:wcol + 128],
                                       yql[:, k * TB + sl * BT:
                                           k * TB + sl * BT + BT],
                                       start=False, stop=False)
                            mm = eng.matmul(bank[:, 0:BT],
                                            wsl[:, wcol:wcol + 128],
                                            yqh[:, k * TB + sl * BT:
                                                k * TB + sl * BT + BT],
                                            start=False, stop=(k == 7))
                            if k == 7 and sl == NSL - 1:
                                mm.then_inc(h, 1)
                prog.sem("g_ev")
                ph["tensor"].append(gmm)

                for sl in range(NSL):
                    h_ev, v_ev = INC("g_ev", 1)
                    stn = "g_st%d" % (prog.sems["g_ev"][1] % 4)
                    v_st_need = prog.sem(stn)[1]

                    def gev(eng, of=of, sl=sl, banks=banks, v_mm=v_mm, h=h_ev,
                            stn=stn, v_st_need=v_st_need,
                            nev=prog.sems["g_ev"][1], which=which):
                        eng.wait_ge(prog.sems["g_mm"][0], v_mm)
                        if v_st_need > 0:
                            eng.wait_ge(prog.sems[stn][0], v_st_need)
                        bank = banks[sl]
                        eb = (nev % 4) * BT
                        eng.tensor_copy(est[:, eb:eb + BT], bank[:, 0:BT])
                        eng.drain()
                        eng.tensor_reduce(mcol[:], est[:, eb:eb + BT], AX.X,
                                          ALU.max, apply_absolute_value=True)
                        eng.drain()
                        eng.tensor_tensor(macc[:, which:which + 1],
                                          macc[:, which:which + 1], mcol[:],
                                          ALU.max)
                        eng.drain().then_inc(h, 1)
                    ph["vector"].append(gev)

                    h_st, v_st = INC(stn, 16)

                    def gst(eng, tb=tb, of=of, sl=sl, v_ev=v_ev, h=h_st,
                            dst=dst, nev=prog.sems["g_ev"][1]):
                        eng.wait_ge(prog.sems["g_ev"][0], v_ev)
                        eb = (nev % 4) * BT
                        eng.dma_start(out=dst[of * 128:of * 128 + 128,
                                              tb * TB + sl * BT:
                                              tb * TB + sl * BT + BT],
                                      in_=est[:, eb:eb + BT]).then_inc(h, 16)
                    ph["gpsimd"].append(gst)

            _snap = {n: prog.sems[n][1] for n in
                     ("g_ws0", "g_ws1") if n in prog.sems}

            def sfin(eng, _snap=_snap):
                for n, v in _snap.items():
                    eng.wait_ge(prog.sems[n][0], v)
            ph["sync"].append(sfin)

            _snap2 = {n: prog.sems[n][1] for n in
                      ("g_st0", "g_st1", "g_st2", "g_st3") if n in prog.sems}

            def gfin(eng, _snap2=_snap2):
                for n, v in _snap2.items():
                    eng.wait_ge(prog.sems[n][0], v)
            ph["gpsimd"].append(gfin)

        for tb in range(TOK // TB):
            emit_xq(tb)
            emit_gemm(tb, 0)
            emit_gemm(tb, 1)

        phf = prog.phase()
        h_ftp, v_ftp = INC("f_tp", 1)
        h_fvp, v_fvp = INC("f_vp", 1)
        h_fst, v_fst = INC("f_st", 32)

        def fin_t(eng, h=h_ftp):
            eng.transpose(psT[0:1, 0:128], macc[:, 0:1], identf[:])
            eng.transpose(psT2[0:1, 0:128], macc[:, 1:2],
                          identf[:]).then_inc(h, 1)
        phf["tensor"].append(fin_t)

        def fin_v(eng, h=h_fvp, v_ftp=v_ftp):
            eng.wait_ge(prog.sems["f_tp"][0], v_ftp)
            eng.tensor_reduce(mrow[:, 0:1], psT[0:1, 0:128], AX.X, ALU.max)
            eng.tensor_reduce(mrow[:, 1:2], psT2[0:1, 0:128], AX.X, ALU.max)
            eng.drain().then_inc(h, 1)
        phf["vector"].append(fin_v)

        def fin_g(eng, h=h_fst, v_fvp=v_fvp, v_fst=v_fst):
            eng.wait_ge(prog.sems["f_vp"][0], v_fvp)
            eng.dma_start(out=m2[:], in_=mrow[:, 0:1]).then_inc(h, 16)
            eng.dma_start(out=m3[:], in_=mrow[:, 1:2]).then_inc(h, 16)
            eng.wait_ge(h, v_fst)
        phf["gpsimd"].append(fin_g)

        prog.emit(block)
    return nc




def _v_newton_recip(eng, out_ap, in_ap, t_ap):
    eng.reciprocal(out_ap, in_ap)
    eng.drain()
    eng.tensor_tensor(t_ap, in_ap, out_ap, ALU.mult)
    eng.drain()
    eng.tensor_scalar(t_ap, t_ap, -1.0, 2.0, ALU.mult, ALU.add)
    eng.drain()
    eng.tensor_tensor(out_ap, out_ap, t_ap, ALU.mult)
    eng.drain()



def build_b2():
    nc = bass.Bass("TRN2", target_bir_lowering=False,
                   detect_race_conditions=False)
    gt_raw = nc.dram_tensor("gt_raw", [DF, TOK], F32, kind="ExternalInput").ap()
    m3a = nc.dram_tensor("m3a", [1, 8], F32, kind="ExternalInput").ap()
    ab = nc.dram_tensor("ab", [DF, TOK], F32, kind="ExternalOutput").ap()
    m4 = nc.dram_tensor("m4", [1, 1], F32, kind="ExternalOutput").ap()

    W = 2048
    NT = (DF // 128) * (TOK // W)

    with contextlib.ExitStack() as ctx:
        E = ctx.enter_context
        identf = E(nc.sbuf_tensor("identf", [128, 128], F32))
        onesr = E(nc.sbuf_tensor("onesr", [1, 128], F32))
        sc = E(nc.sbuf_tensor("sc", [1, 16], F32))
        r3col = E(nc.sbuf_tensor("r3col", [128, 1], F32))
        s3col = E(nc.sbuf_tensor("s3col", [128, 1], F32))
        gbuf = E(nc.sbuf_tensor("gbuf", [128, 2 * W], F32))
        abuf = E(nc.sbuf_tensor("abuf", [128, 2 * W], F32))
        ai = E(nc.sbuf_tensor("ai", [128, W], I16))
        bu = E(nc.sbuf_tensor("bu", [128, W], U16))
        sg = E(nc.sbuf_tensor("sg", [128, W], F32))
        macc = E(nc.sbuf_tensor("macc", [128, 1], F32))
        mcol = E(nc.sbuf_tensor("mcol", [128, 1], F32))
        m4s = E(nc.sbuf_tensor("m4s", [1, 1], F32))
        psT = E(nc.psum_tensor("psT", [128, 128], F32))
        block = E(nc.Block())
        prog = Prog(nc, ctx)

        def INC(name, n=1):
            s = prog.sem(name)
            s[1] += n
            return s[0], s[1]

        ph0 = prog.phase()
        h_sld, v_sld = INC("s_ld", 16)

        def p0g(g):
            h, _ = INC("g0", 4)
            g.memset(identf[:], 0.0).then_inc(h, 1)
            g.wait_ge(h, 1)
            g.affine_select(out=identf[:], in_=identf[:], compare_op=ALU.not_equal,
                            fill=1.0, base=0, pattern=[[-1, 128]],
                            channel_multiplier=1).then_inc(h, 1)
            g.memset(onesr[:], 1.0).then_inc(h, 1)
            g.memset(macc[:], 0.0).then_inc(h, 1)
            g.wait_ge(h, 4)
        ph0["gpsimd"].append(p0g)

        def p0s(sync, h=h_sld, v=v_sld):
            sync.dma_start(out=sc[:, 8:16], in_=m3a[:]).then_inc(h, 16)
            sync.wait_ge(h, v)
        ph0["sync"].append(p0s)

        ph0b = prog.phase()
        h_sv, v_sv = INC("s_vp", 1)
        h_st2, v_st2 = INC("s_tp", 1)
        h_sv2, v_sv2 = INC("s_vp", 1)

        def p0v(eng, h=h_sv):
            eng.tensor_reduce(sc[:, 0:1], sc[:, 8:16], AX.X, ALU.max)
            eng.drain()
            eng.tensor_scalar(sc[:, 1:2], sc[:, 0:1], 1.0 / QMAX, 1e-12,
                              ALU.mult, ALU.max)
            eng.drain()
            _v_newton_recip(eng, sc[:, 2:3], sc[:, 1:2], sc[:, 3:4])
            eng.nop().then_inc(h, 1)
        ph0b["vector"].append(p0v)

        def p0t(eng, h=h_st2, v_sv=v_sv):
            eng.wait_ge(prog.sems["s_vp"][0], v_sv)
            eng.matmul(psT[:, 0:1], onesr[:], sc[0:1, 2:3], start=True, stop=True)
            eng.matmul(psT[:, 1:2], onesr[:], sc[0:1, 1:2], start=True,
                       stop=True).then_inc(h, 1)
        ph0b["tensor"].append(p0t)

        def p0v2(eng, h=h_sv2, v=v_st2):
            eng.wait_ge(prog.sems["s_tp"][0], v)
            eng.tensor_copy(r3col[:], psT[:, 0:1])
            eng.tensor_copy(s3col[:], psT[:, 1:2])
            eng.drain().then_inc(h, 1)
        ph0b["vector"].append(p0v2)

        ph = prog.phase()
        for i in range(NT):
            r, cb = divmod(i, TOK // W)
            buf = (i % 2) * W
            ldn = "e_ld%d" % (i % 2)
            v_va_need = max(prog.sem("e_va")[1] - 1, 0)
            h_ld, v_ld = INC(ldn, 16)

            def load(eng, r=r, cb=cb, buf=buf, h=h_ld, v_va_need=v_va_need):
                if v_va_need > 0:
                    eng.wait_ge(prog.sems["e_va"][0], v_va_need)
                eng.dma_start(out=gbuf[:, buf:buf + W],
                              in_=gt_raw[r * 128:r * 128 + 128,
                                         cb * W:cb * W + W]).then_inc(h, 16)
            ph["sync"].append(load)

            v_sg_need = prog.sem("e_sg")[1]
            h_va, v_va = INC("e_va", 1)

            def va(eng, buf=buf, v_ld=v_ld, h=h_va, ldn=ldn,
                   v_sg_need=v_sg_need):
                eng.wait_ge(prog.sems[ldn][0], v_ld)
                if v_sg_need > 0:
                    eng.wait_ge(prog.sems["e_sg"][0], v_sg_need)
                eng.tensor_scalar(ai[:], gbuf[:, buf:buf + W], r3col[:], None,
                                  ALU.mult)
                eng.drain().then_inc(h, 1)
            ph["vector"].append(va)

            h_sg, v_sg = INC("e_sg", 1)

            def dosig(eng, v_va=v_va, h=h_sg):
                eng.wait_ge(prog.sems["e_va"][0], v_va)
                eng.activation(sg[:], ai[:], AF.Sigmoid,
                               scale=s3col[:]).then_inc(h, 1)
            ph["scalar"].append(dosig)

            stn = "e_st%d" % (i % 2)
            v_st_need = prog.sem(stn)[1]
            h_vb, v_vb = INC("e_vb", 1)

            def vb(eng, buf=buf, v_sg=v_sg, h=h_vb, stn=stn,
                   v_st_need=v_st_need):
                eng.wait_ge(prog.sems["e_sg"][0], v_sg)
                if v_st_need > 0:
                    eng.wait_ge(prog.sems[stn][0], v_st_need)
                eng.tensor_scalar(bu[:], sg[:], 65536.0, None, ALU.mult)
                eng.drain()
                eng.tensor_tensor(abuf[:, buf:buf + W], ai[:], bu[:], ALU.mult)
                eng.drain()
                eng.tensor_reduce(mcol[:], abuf[:, buf:buf + W], AX.X, ALU.max,
                                  apply_absolute_value=True)
                eng.drain()
                eng.tensor_tensor(macc[:], macc[:], mcol[:], ALU.max)
                eng.drain().then_inc(h, 1)
            ph["vector"].append(vb)

            h_stt, v_stt = INC(stn, 16)

            def store(eng, r=r, cb=cb, buf=buf, v_vb=v_vb, h=h_stt):
                eng.wait_ge(prog.sems["e_vb"][0], v_vb)
                eng.dma_start(out=ab[r * 128:r * 128 + 128, cb * W:cb * W + W],
                              in_=abuf[:, buf:buf + W]).then_inc(h, 16)
            ph["gpsimd"].append(store)

        _snap = {n: prog.sems[n][1] for n in ("e_ld0", "e_ld1") if n in prog.sems}

        def sfin(eng, _snap=_snap):
            for n, v in _snap.items():
                eng.wait_ge(prog.sems[n][0], v)
        ph["sync"].append(sfin)

        _snap2 = {n: prog.sems[n][1] for n in ("e_st0", "e_st1") if n in prog.sems}

        def gfin(eng, _snap2=_snap2):
            for n, v in _snap2.items():
                eng.wait_ge(prog.sems[n][0], v)
        ph["gpsimd"].append(gfin)

        phf = prog.phase()
        h_ftp, v_ftp = INC("f_tp", 1)
        h_fvp, v_fvp = INC("f_vp", 1)
        h_fst, v_fst = INC("f_st", 16)

        def fin_t(eng, h=h_ftp):
            eng.transpose(psT[0:1, 0:128], macc[:], identf[:]).then_inc(h, 1)
        phf["tensor"].append(fin_t)

        def fin_v(eng, h=h_fvp, v_ftp=v_ftp):
            eng.wait_ge(prog.sems["f_tp"][0], v_ftp)
            eng.tensor_reduce(m4s[:], psT[0:1, 0:128], AX.X, ALU.max)
            eng.drain().then_inc(h, 1)
        phf["vector"].append(fin_v)

        def fin_g(eng, h=h_fst, v_fvp=v_fvp, v_fst=v_fst):
            eng.wait_ge(prog.sems["f_vp"][0], v_fvp)
            eng.dma_start(out=m4[:], in_=m4s[:]).then_inc(h, 16)
            eng.wait_ge(h, v_fst)
        phf["gpsimd"].append(fin_g)

        prog.emit(block)
    return nc



def build_b3():
    nc = bass.Bass("TRN2", target_bir_lowering=False,
                   detect_race_conditions=False)
    abt = nc.dram_tensor("ab", [DF, TOK], F32, kind="ExternalInput").ap()
    up_raw = nc.dram_tensor("up_raw", [DF, TOK], F32, kind="ExternalInput").ap()
    m2a = nc.dram_tensor("m2a", [1, 8], F32, kind="ExternalInput").ap()
    m3a = nc.dram_tensor("m3a", [1, 8], F32, kind="ExternalInput").ap()
    m4a = nc.dram_tensor("m4a", [1, 8], F32, kind="ExternalInput").ap()
    cu = nc.dram_tensor("cu", [DF, TOK], F32, kind="ExternalOutput").ap()
    m5 = nc.dram_tensor("m5", [1, 1], F32, kind="ExternalOutput").ap()

    W = 2048
    NT = (DF // 128) * (TOK // W)

    with contextlib.ExitStack() as ctx:
        E = ctx.enter_context
        identf = E(nc.sbuf_tensor("identf", [128, 128], F32))
        onesr = E(nc.sbuf_tensor("onesr", [1, 128], F32))
        sc = E(nc.sbuf_tensor("sc", [1, 40], F32))
        fccol = E(nc.sbuf_tensor("fccol", [128, 1], F32))
        r2col = E(nc.sbuf_tensor("r2col", [128, 1], F32))
        abuf = E(nc.sbuf_tensor("abuf", [128, 2 * W], F32))
        ubuf = E(nc.sbuf_tensor("ubuf", [128, 2 * W], F32))
        cbuf = E(nc.sbuf_tensor("cbuf", [128, 2 * W], F32))
        ci = E(nc.sbuf_tensor("ci", [128, W], I16))
        ui = E(nc.sbuf_tensor("ui", [128, W], I16))
        macc = E(nc.sbuf_tensor("macc", [128, 1], F32))
        mcol = E(nc.sbuf_tensor("mcol", [128, 1], F32))
        m5s = E(nc.sbuf_tensor("m5s", [1, 1], F32))
        psT = E(nc.psum_tensor("psT", [128, 128], F32))
        block = E(nc.Block())
        prog = Prog(nc, ctx)

        def INC(name, n=1):
            s = prog.sem(name)
            s[1] += n
            return s[0], s[1]

        ph0 = prog.phase()
        h_sld, v_sld = INC("s_ld", 48)

        def p0g(g):
            h, _ = INC("g0", 4)
            g.memset(identf[:], 0.0).then_inc(h, 1)
            g.wait_ge(h, 1)
            g.affine_select(out=identf[:], in_=identf[:], compare_op=ALU.not_equal,
                            fill=1.0, base=0, pattern=[[-1, 128]],
                            channel_multiplier=1).then_inc(h, 1)
            g.memset(onesr[:], 1.0).then_inc(h, 1)
            g.memset(macc[:], 0.0).then_inc(h, 1)
            g.wait_ge(h, 4)
        ph0["gpsimd"].append(p0g)

        def p0s(sync, h=h_sld, v=v_sld):
            sync.dma_start(out=sc[:, 8:16], in_=m2a[:]).then_inc(h, 16)
            sync.dma_start(out=sc[:, 16:24], in_=m3a[:]).then_inc(h, 16)
            sync.dma_start(out=sc[:, 24:32], in_=m4a[:]).then_inc(h, 16)
            sync.wait_ge(h, v)
        ph0["sync"].append(p0s)

        ph0b = prog.phase()
        h_sv, v_sv = INC("s_vp", 1)
        h_st2, v_st2 = INC("s_tp", 1)
        h_sv2, v_sv2 = INC("s_vp", 1)

        def p0v(eng, h=h_sv):
            eng.tensor_reduce(sc[:, 0:1], sc[:, 8:16], AX.X, ALU.max)
            eng.tensor_reduce(sc[:, 1:2], sc[:, 16:24], AX.X, ALU.max)
            eng.tensor_reduce(sc[:, 2:3], sc[:, 24:32], AX.X, ALU.max)
            eng.drain()
            eng.tensor_scalar(sc[:, 3:4], sc[:, 0:1], 1.0 / QMAX, 1e-12,
                              ALU.mult, ALU.max)
            eng.tensor_scalar(sc[:, 4:5], sc[:, 1:2], 1.0 / QMAX, 1e-12,
                              ALU.mult, ALU.max)
            eng.drain()
            _v_newton_recip(eng, sc[:, 5:6], sc[:, 3:4], sc[:, 6:7])
            eng.tensor_scalar(sc[:, 7:8], sc[:, 4:5], 1.0 / 65536.0, None,
                              ALU.mult)
            eng.drain()
            eng.tensor_tensor(sc[:, 32:33], sc[:, 7:8], sc[:, 2:3], ALU.mult)
            eng.drain()
            eng.tensor_scalar(sc[:, 33:34], sc[:, 32:33], 1.0 / QMAX, 1e-12,
                              ALU.mult, ALU.max)
            eng.drain()
            _v_newton_recip(eng, sc[:, 34:35], sc[:, 33:34], sc[:, 6:7])
            eng.tensor_tensor(sc[:, 35:36], sc[:, 7:8], sc[:, 34:35],
                              ALU.mult)
            eng.drain().then_inc(h, 1)
        ph0b["vector"].append(p0v)

        def p0t(eng, h=h_st2, v_sv=v_sv):
            eng.wait_ge(prog.sems["s_vp"][0], v_sv)
            eng.matmul(psT[:, 0:1], onesr[:], sc[0:1, 35:36], start=True,
                       stop=True)
            eng.matmul(psT[:, 1:2], onesr[:], sc[0:1, 5:6], start=True,
                       stop=True).then_inc(h, 1)
        ph0b["tensor"].append(p0t)

        def p0v2(eng, h=h_sv2, v=v_st2):
            eng.wait_ge(prog.sems["s_tp"][0], v)
            eng.tensor_copy(fccol[:], psT[:, 0:1])
            eng.tensor_copy(r2col[:], psT[:, 1:2])
            eng.drain().then_inc(h, 1)
        ph0b["vector"].append(p0v2)

        ph = prog.phase()
        for i in range(NT):
            r, cb = divmod(i, TOK // W)
            buf = (i % 2) * W
            ldn = "e_ld%d" % (i % 2)
            v_vb_need = max(prog.sem("e_vb")[1] - 1, 0)
            h_ld, v_ld = INC(ldn, 32)

            def load(eng, r=r, cb=cb, buf=buf, h=h_ld, v_vb_need=v_vb_need):
                if v_vb_need > 0:
                    eng.wait_ge(prog.sems["e_vb"][0], v_vb_need)
                eng.dma_start(out=abuf[:, buf:buf + W],
                              in_=abt[r * 128:r * 128 + 128,
                                      cb * W:cb * W + W]).then_inc(h, 16)
                eng.dma_start(out=ubuf[:, buf:buf + W],
                              in_=up_raw[r * 128:r * 128 + 128,
                                         cb * W:cb * W + W]).then_inc(h, 16)
            ph["sync"].append(load)

            stn = "e_st%d" % (i % 2)
            v_st_need = prog.sem(stn)[1]
            h_vb, v_vb = INC("e_vb", 1)

            def vb(eng, buf=buf, v_ld=v_ld, h=h_vb, ldn=ldn, stn=stn,
                   v_st_need=v_st_need):
                eng.wait_ge(prog.sems[ldn][0], v_ld)
                if v_st_need > 0:
                    eng.wait_ge(prog.sems[stn][0], v_st_need)
                eng.tensor_scalar(ci[:], abuf[:, buf:buf + W], fccol[:], None,
                                  ALU.mult)
                eng.tensor_scalar(ui[:], ubuf[:, buf:buf + W], r2col[:], None,
                                  ALU.mult)
                eng.drain()
                eng.tensor_tensor(cbuf[:, buf:buf + W], ci[:], ui[:], ALU.mult)
                eng.drain()
                eng.tensor_reduce(mcol[:], cbuf[:, buf:buf + W], AX.X, ALU.max,
                                  apply_absolute_value=True)
                eng.drain()
                eng.tensor_tensor(macc[:], macc[:], mcol[:], ALU.max)
                eng.drain().then_inc(h, 1)
            ph["vector"].append(vb)

            h_stt, v_stt = INC(stn, 16)

            def store(eng, r=r, cb=cb, buf=buf, v_vb=v_vb, h=h_stt):
                eng.wait_ge(prog.sems["e_vb"][0], v_vb)
                eng.dma_start(out=cu[r * 128:r * 128 + 128, cb * W:cb * W + W],
                              in_=cbuf[:, buf:buf + W]).then_inc(h, 16)
            ph["gpsimd"].append(store)

        _snap = {n: prog.sems[n][1] for n in ("e_ld0", "e_ld1") if n in prog.sems}

        def sfin(eng, _snap=_snap):
            for n, v in _snap.items():
                eng.wait_ge(prog.sems[n][0], v)
        ph["sync"].append(sfin)

        _snap2 = {n: prog.sems[n][1] for n in ("e_st0", "e_st1") if n in prog.sems}

        def gfin(eng, _snap2=_snap2):
            for n, v in _snap2.items():
                eng.wait_ge(prog.sems[n][0], v)
        ph["gpsimd"].append(gfin)

        phf = prog.phase()
        h_ftp, v_ftp = INC("f_tp", 1)
        h_fvp, v_fvp = INC("f_vp", 1)
        h_fst, v_fst = INC("f_st", 16)

        def fin_t(eng, h=h_ftp):
            eng.transpose(psT[0:1, 0:128], macc[:], identf[:]).then_inc(h, 1)
        phf["tensor"].append(fin_t)

        def fin_v(eng, h=h_fvp, v_ftp=v_ftp):
            eng.wait_ge(prog.sems["f_tp"][0], v_ftp)
            eng.tensor_reduce(m5s[:], psT[0:1, 0:128], AX.X, ALU.max)
            eng.drain().then_inc(h, 1)
        phf["vector"].append(fin_v)

        def fin_g(eng, h=h_fst, v_fvp=v_fvp, v_fst=v_fst):
            eng.wait_ge(prog.sems["f_vp"][0], v_fvp)
            eng.dma_start(out=m5[:], in_=m5s[:]).then_inc(h, 16)
            eng.wait_ge(h, v_fst)
        phf["gpsimd"].append(fin_g)

        prog.emit(block)
    return nc



def build_b4():
    nc = bass.Bass("TRN2", target_bir_lowering=False,
                   detect_race_conditions=False)
    cut = nc.dram_tensor("cu", [DF, TOK], F32, kind="ExternalInput").ap()
    xpT = nc.dram_tensor("xpT", [B, D, S], F32, kind="ExternalInput").ap()
    w_down = nc.dram_tensor("w_down", [D, DF], F32, kind="ExternalInput").ap()
    m2a = nc.dram_tensor("m2a", [1, 8], F32, kind="ExternalInput").ap()
    m3a = nc.dram_tensor("m3a", [1, 8], F32, kind="ExternalInput").ap()
    m4a = nc.dram_tensor("m4a", [1, 8], F32, kind="ExternalInput").ap()
    m5a = nc.dram_tensor("m5a", [1, 8], F32, kind="ExternalInput").ap()
    out = nc.dram_tensor("out", [B, S, D], F32, kind="ExternalOutput").ap()
    dnT_h = nc.dram_tensor("dnT_h", [128, 8 * DF], F32R, kind="Internal").ap()
    dnT_l = nc.dram_tensor("dnT_l", [128, 8 * DF], F32R, kind="Internal").ap()

    BT = 512
    NSL = TOK // BT

    with contextlib.ExitStack() as ctx:
        E = ctx.enter_context
        identf = E(nc.sbuf_tensor("identf", [128, 128], F32))
        onesr = E(nc.sbuf_tensor("onesr", [1, 128], F32))
        sc = E(nc.sbuf_tensor("sc", [1, 48], F32))
        fdcol = E(nc.sbuf_tensor("fdcol", [128, 1], F32))
        s5col = E(nc.sbuf_tensor("s5col", [128, 1], F32))
        wbuf = E(nc.sbuf_tensor("wbuf", [128, 2 * DF], F32))
        wg = E(nc.sbuf_tensor("wg", [128, 96], F32))
        wq = E(nc.sbuf_tensor("wq", [128, DF], I16))
        wdq = E(nc.sbuf_tensor("wdq", [128, DF], F32))
        wz = E(nc.sbuf_tensor("wz", [128, 192], F32))
        wth = E(nc.sbuf_tensor("wth", [128, 256], F32R))
        wtl = E(nc.sbuf_tensor("wtl", [128, 256], F32R))
        cub = E(nc.sbuf_tensor("cub", [128, 2 * BT], F32))
        di = E(nc.sbuf_tensor("di", [128, BT], I16))
        dh = E(nc.sbuf_tensor("dh", [128, 24 * BT], F32R))
        dl = E(nc.sbuf_tensor("dl", [128, 24 * BT], F32R))
        wsh = E(nc.sbuf_tensor("wsh", [128, 2 * DF], F32R))
        wsl = E(nc.sbuf_tensor("wsl", [128, 2 * DF], F32R))
        xpb = E(nc.sbuf_tensor("xpb", [128, 2 * BT], F32))
        fo = E(nc.sbuf_tensor("fo", [128, BT], F32))
        ost = E(nc.sbuf_tensor("ost", [128, 2 * BT], F32))
        psA = E(nc.psum_tensor("psA", [128, 512], F32))
        psB = E(nc.psum_tensor("psB", [128, 512], F32))
        psT = E(nc.psum_tensor("psT", [128, 128], F32))
        psT2 = E(nc.psum_tensor("psT2", [128, 128], F32))
        block = E(nc.Block())
        prog = Prog(nc, ctx)

        def INC(name, n=1):
            s = prog.sem(name)
            s[1] += n
            return s[0], s[1]

        ph0 = prog.phase()
        h_sld, v_sld = INC("s_ld", 64)

        def p0g(g):
            h, _ = INC("g0", 3)
            g.memset(identf[:], 0.0).then_inc(h, 1)
            g.wait_ge(h, 1)
            g.affine_select(out=identf[:], in_=identf[:], compare_op=ALU.not_equal,
                            fill=1.0, base=0, pattern=[[-1, 128]],
                            channel_multiplier=1).then_inc(h, 1)
            g.memset(onesr[:], 1.0).then_inc(h, 1)
            g.wait_ge(h, 3)
        ph0["gpsimd"].append(p0g)

        def p0s(sync, h=h_sld, v=v_sld):
            sync.dma_start(out=sc[:, 8:16], in_=m2a[:]).then_inc(h, 16)
            sync.dma_start(out=sc[:, 16:24], in_=m3a[:]).then_inc(h, 16)
            sync.dma_start(out=sc[:, 24:32], in_=m4a[:]).then_inc(h, 16)
            sync.dma_start(out=sc[:, 32:40], in_=m5a[:]).then_inc(h, 16)
            sync.wait_ge(h, v)
        ph0["sync"].append(p0s)

        ph0b = prog.phase()
        h_sv, v_sv = INC("s_vp", 1)
        h_st2, v_st2 = INC("s_tp", 1)
        h_sv2, v_sv2 = INC("s_vp", 1)

        def p0v(eng, h=h_sv):
            eng.tensor_reduce(sc[:, 0:1], sc[:, 8:16], AX.X, ALU.max)
            eng.tensor_reduce(sc[:, 1:2], sc[:, 16:24], AX.X, ALU.max)
            eng.tensor_reduce(sc[:, 2:3], sc[:, 24:32], AX.X, ALU.max)
            eng.tensor_reduce(sc[:, 3:4], sc[:, 32:40], AX.X, ALU.max)
            eng.drain()
            eng.tensor_scalar(sc[:, 4:5], sc[:, 0:1], 1.0 / QMAX, 1e-12,
                              ALU.mult, ALU.max)
            eng.tensor_scalar(sc[:, 5:6], sc[:, 1:2], 1.0 / QMAX, 1e-12,
                              ALU.mult, ALU.max)
            eng.drain()
            eng.tensor_scalar(sc[:, 6:7], sc[:, 5:6], 1.0 / 65536.0, None,
                              ALU.mult)
            eng.drain()
            eng.tensor_tensor(sc[:, 40:41], sc[:, 6:7], sc[:, 2:3], ALU.mult)
            eng.drain()
            eng.tensor_scalar(sc[:, 41:42], sc[:, 40:41], 1.0 / QMAX, 1e-12,
                              ALU.mult, ALU.max)
            eng.drain()
            eng.tensor_tensor(sc[:, 42:43], sc[:, 41:42], sc[:, 4:5],
                              ALU.mult)
            eng.drain()
            eng.tensor_tensor(sc[:, 43:44], sc[:, 42:43], sc[:, 3:4], ALU.mult)
            eng.drain()
            eng.tensor_scalar(sc[:, 44:45], sc[:, 43:44], 1.0 / QMAX, 1e-12,
                              ALU.mult, ALU.max)
            eng.drain()
            _v_newton_recip(eng, sc[:, 45:46], sc[:, 44:45], sc[:, 7:8])
            eng.tensor_tensor(sc[:, 46:47], sc[:, 42:43], sc[:, 45:46],
                              ALU.mult)
            eng.drain().then_inc(h, 1)
        ph0b["vector"].append(p0v)

        def p0t(eng, h=h_st2, v_sv=v_sv):
            eng.wait_ge(prog.sems["s_vp"][0], v_sv)
            eng.matmul(psT[:, 0:1], onesr[:], sc[0:1, 46:47], start=True,
                       stop=True)
            eng.matmul(psT[:, 1:2], onesr[:], sc[0:1, 44:45], start=True,
                       stop=True).then_inc(h, 1)
        ph0b["tensor"].append(p0t)

        def p0v2(eng, h=h_sv2, v=v_st2):
            eng.wait_ge(prog.sems["s_tp"][0], v)
            eng.tensor_copy(fdcol[:], psT[:, 0:1])
            eng.tensor_copy(s5col[:], psT[:, 1:2])
            eng.drain().then_inc(h, 1)
        ph0b["vector"].append(p0v2)

        ph1 = prog.phase()
        for i in range(8):
            buf = (i % 2) * DF
            ldn = "b_ld%d" % (i % 2)
            h_ld, v_ld = INC(ldn, 16)

            def load(eng, i=i, buf=buf, h=h_ld):
                if i >= 2:
                    eng.wait_ge(prog.sems["b_dq"][0], i - 1)
                eng.dma_start(out=wbuf[:, buf:buf + DF],
                              in_=w_down[i * 128:i * 128 + 128, :]).then_inc(h, 16)
            ph1["sync"].append(load)

            h_dq, v_dq = INC("b_dq", 1)

            def dequant(eng, i=i, buf=buf, v_ld=v_ld, h=h_dq, ldn=ldn):
                eng.wait_ge(prog.sems[ldn][0], v_ld)
                if i >= 2:
                    eng.wait_ge(prog.sems["b_tp"][0], (i - 1) * 24)
                wb3 = wbuf[:, buf:buf + DF].rearrange("p (g c) -> p g c", g=96)
                eng.tensor_reduce(
                    wg[:].rearrange("p (g o) -> p g o", o=1),
                    wb3, AX.X, ALU.max, apply_absolute_value=True)
                eng.drain()
                eng.tensor_scalar(wg[:], wg[:], 1.0 / 7.0, 1e-12,
                                  ALU.mult, ALU.max)
                eng.drain()
                eng.reciprocal(wz[:, 0:96], wg[:])
                eng.drain()
                eng.tensor_tensor(wz[:, 96:192], wg[:], wz[:, 0:96], ALU.mult)
                eng.drain()
                eng.tensor_scalar(wz[:, 96:192], wz[:, 96:192], -1.0, 2.0,
                                  ALU.mult, ALU.add)
                eng.drain()
                eng.tensor_tensor(wz[:, 0:96], wz[:, 0:96], wz[:, 96:192],
                                  ALU.mult)
                eng.drain()
                eng.tensor_tensor(
                    wdq[:].rearrange("p (g c) -> p g c", g=96), wb3,
                    wz[:, 0:96].rearrange("p (g o) -> p g o", o=1).broadcast_to(
                        (128, 96, 32)), ALU.mult)
                eng.drain()
                eng.tensor_copy(wq[:], wdq[:])
                eng.drain()
                eng.tensor_scalar(wq[:], wq[:], 7.0, -8.0, ALU.min, ALU.max)
                eng.drain()
                eng.tensor_tensor(
                    wdq[:].rearrange("p (g c) -> p g c", g=96),
                    wq.rearrange("p (g c) -> p g c", g=96),
                    wg[:].rearrange("p (g o) -> p g o", o=1).broadcast_to(
                        (128, 96, 32)), ALU.mult)
                eng.drain()
                eng.tensor_scalar(wdq[:], wdq[:], s5col[:], None, ALU.mult)
                eng.drain().then_inc(h, 1)
            prog.sem("b_tp")
            ph1["vector"].append(dequant)

            for k in range(24):
                h_tp, v_tp = INC("b_tp", 1)

                def tpose(eng, k=k, v_dq=v_dq, h=h_tp,
                          ntp=prog.sems["b_tp"][1]):
                    eng.wait_ge(prog.sems["b_dq"][0], v_dq)
                    if ntp >= 3:
                        eng.wait_ge(prog.sems["b_sp"][0], ntp - 2)
                    p = psT if ntp % 2 == 1 else psT2
                    eng.transpose(p[:], wdq[:, k * 128:k * 128 + 128],
                                  identf[:]).then_inc(h, 1)
                prog.sem("b_sp")
                ph1["tensor"].append(tpose)

                sbuf2 = (prog.sems["b_sp"][1] % 2) * 128
                gpn = "b_gp%d" % (prog.sems["b_sp"][1] % 2)
                v_gp_need = prog.sem(gpn)[1]
                h_sp, v_sp = INC("b_sp", 1)

                def split(eng, v_tp=v_tp, sbuf2=sbuf2, h=h_sp, gpn=gpn,
                          v_gp_need=v_gp_need, ntp=prog.sems["b_tp"][1]):
                    eng.wait_ge(prog.sems["b_tp"][0], v_tp)
                    if v_gp_need > 0:
                        eng.wait_ge(prog.sems[gpn][0], v_gp_need)
                    p = psT if ntp % 2 == 1 else psT2
                    eng.tensor_copy(wth[:, sbuf2:sbuf2 + 128], p[:])
                    eng.tensor_tensor(wtl[:, sbuf2:sbuf2 + 128], p[:],
                                      wth[:, sbuf2:sbuf2 + 128],
                                      ALU.subtract).then_inc(h, 1)
                ph1["vector"].append(split)

                h_gp, v_gp = INC(gpn, 32)
                col = i * DF + k * 128

                def store(eng, v_sp=v_sp, sbuf2=sbuf2, col=col, h=h_gp):
                    eng.wait_ge(prog.sems["b_sp"][0], v_sp)
                    eng.dma_start(out=dnT_h[:, col:col + 128],
                                  in_=wth[:, sbuf2:sbuf2 + 128]).then_inc(h, 16)
                    eng.dma_start(out=dnT_l[:, col:col + 128],
                                  in_=wtl[:, sbuf2:sbuf2 + 128]).then_inc(h, 16)
                ph1["gpsimd"].append(store)

        _snap = {n: prog.sems[n][1] for n in
                 ("b_ld0", "b_ld1", "b_gp0", "b_gp1") if n in prog.sems}

        def p1fin_s(eng, _snap=_snap):
            for n in ("b_ld0", "b_ld1"):
                if n in _snap:
                    eng.wait_ge(prog.sems[n][0], _snap[n])
        ph1["sync"].append(p1fin_s)

        def p1fin_g(eng, _snap=_snap):
            for n in ("b_gp0", "b_gp1"):
                if n in _snap:
                    eng.wait_ge(prog.sems[n][0], _snap[n])
        ph1["gpsimd"].append(p1fin_g)

        def emit_slice(sl):
            t0 = sl * BT
            b, s0 = divmod(t0, S)
            ph = prog.phase()
            for c in range(24):
                cbuf = (c % 2) * BT
                ldn = "c_ld%d" % (c % 2)
                v_dc_need = max(prog.sem("c_dc")[1] - 1, 0)
                h_ld, v_ld = INC(ldn, 16)

                def ldc(eng, c=c, t0=t0, cbuf=cbuf, h=h_ld,
                        v_dc_need=v_dc_need):
                    if v_dc_need > 0:
                        eng.wait_ge(prog.sems["c_dc"][0], v_dc_need)
                    eng.dma_start(out=cub[:, cbuf:cbuf + BT],
                                  in_=cut[c * 128:c * 128 + 128, t0:t0 + BT]
                                  ).then_inc(h, 16)
                ph["sync"].append(ldc)

                h_dc, v_dc = INC("c_dc", 1)

                def dcvt(eng, c=c, cbuf=cbuf, v_ld=v_ld, h=h_dc, ldn=ldn):
                    eng.wait_ge(prog.sems[ldn][0], v_ld)
                    eng.tensor_scalar(di[:], cub[:, cbuf:cbuf + BT], fdcol[:],
                                      None, ALU.mult)
                    eng.drain()
                    eng.tensor_copy(dh[:, c * BT:c * BT + BT], di[:])
                    eng.drain()
                    eng.tensor_tensor(dl[:, c * BT:c * BT + BT], di[:],
                                      dh[:, c * BT:c * BT + BT],
                                      ALU.subtract).then_inc(h, 1)
                ph["vector"].append(dcvt)
            v_dc_all = prog.sems["c_dc"][1]

            for r in range(8):
                wb = (r % 2) * DF
                wsn = "c_ws%d" % (r % 2)
                v_need = prog.sem("c_mm")[1] - 1 if r >= 2 else 0
                h_ws, v_ws = INC(wsn, 32)

                def wload(eng, r=r, wb=wb, h=h_ws, v_need=v_need):
                    if v_need > 0:
                        eng.wait_ge(prog.sems["c_mm"][0], v_need)
                    eng.dma_start(out=wsh[:, wb:wb + DF],
                                  in_=dnT_h[:, r * DF:r * DF + DF]).then_inc(h, 16)
                    eng.dma_start(out=wsl[:, wb:wb + DF],
                                  in_=dnT_l[:, r * DF:r * DF + DF]).then_inc(h, 16)
                ph["sync"].append(wload)

                xldn = "x_ld%d" % (r % 2)
                v_ev_need = max(prog.sem("c_ev")[1] - 1, 0)
                h_xl, v_xl = INC(xldn, 16)

                def ldxp(eng, r=r, b=b, s0=s0, h=h_xl, v_ev_need=v_ev_need):
                    if v_ev_need > 0:
                        eng.wait_ge(prog.sems["c_ev"][0], v_ev_need)
                    eng.dma_start(out=xpb[:, (r % 2) * BT:(r % 2) * BT + BT],
                                  in_=xpT[b, r * 128:r * 128 + 128, s0:s0 + BT]
                                  ).then_inc(h, 16)
                ph["sync"].append(ldxp)

                bank = psA if r % 2 == 0 else psB
                h_mm, v_mm = INC("c_mm", 1)

                def cmm(eng, r=r, wb=wb, bank=bank, v_ws=v_ws, h=h_mm,
                        wsn=wsn, v_dc_all=v_dc_all, nmm=prog.sems["c_mm"][1]):
                    eng.wait_ge(prog.sems[wsn][0], v_ws)
                    eng.wait_ge(prog.sems["c_dc"][0], v_dc_all)
                    if nmm >= 3:
                        eng.wait_ge(prog.sems["c_ev"][0], nmm - 2)
                    for k in range(24):
                        wcol = wb + k * 128
                        eng.matmul(bank[:, 0:BT], wsh[:, wcol:wcol + 128],
                                   dh[:, k * BT:k * BT + BT],
                                   start=(k == 0), stop=False)
                        eng.matmul(bank[:, 0:BT], wsh[:, wcol:wcol + 128],
                                   dl[:, k * BT:k * BT + BT],
                                   start=False, stop=False)
                        mm = eng.matmul(bank[:, 0:BT], wsl[:, wcol:wcol + 128],
                                        dh[:, k * BT:k * BT + BT],
                                        start=False, stop=(k == 23))
                        if k == 23:
                            mm.then_inc(h, 1)
                prog.sem("c_ev")
                ph["tensor"].append(cmm)

                h_ev, v_ev = INC("c_ev", 1)

                def cev(eng, r=r, bank=bank, v_mm=v_mm, v_xl=v_xl, h=h_ev,
                        xldn=xldn):
                    eng.wait_ge(prog.sems["c_mm"][0], v_mm)
                    eng.wait_ge(prog.sems[xldn][0], v_xl)
                    eng.tensor_tensor(fo[:], bank[:, 0:BT],
                                      xpb[:, (r % 2) * BT:(r % 2) * BT + BT],
                                      ALU.add)
                    eng.drain().then_inc(h, 1)
                ph["vector"].append(cev)

                for j in range(4):
                    h_tp, v_tp = INC("c_tp", 1)

                    def ftp(eng, j=j, v_ev=v_ev, h=h_tp,
                            ntp=prog.sems["c_tp"][1]):
                        eng.wait_ge(prog.sems["c_ev"][0], v_ev)
                        if ntp >= 3:
                            eng.wait_ge(prog.sems["c_fe"][0], ntp - 2)
                        p = psT if ntp % 2 == 1 else psT2
                        eng.transpose(p[:], fo[:, j * 128:j * 128 + 128],
                                      identf[:]).then_inc(h, 1)
                    prog.sem("c_fe")
                    ph["tensor"].append(ftp)

                    ssl = (prog.sem("c_fe")[1] % 4) * 128
                    stn = "c_st%d" % (prog.sem("c_fe")[1] % 4)
                    v_st_need = prog.sem(stn)[1]
                    h_fe, v_fe = INC("c_fe", 1)

                    def fev(eng, v_tp=v_tp, ssl=ssl, h=h_fe, stn=stn,
                            v_st_need=v_st_need, ntp=prog.sems["c_tp"][1]):
                        eng.wait_ge(prog.sems["c_tp"][0], v_tp)
                        if v_st_need > 0:
                            eng.wait_ge(prog.sems[stn][0], v_st_need)
                        p = psT if ntp % 2 == 1 else psT2
                        eng.tensor_copy(ost[:, ssl:ssl + 128],
                                        p[:]).then_inc(h, 1)
                    ph["vector"].append(fev)

                    h_st, v_st = INC(stn, 16)

                    def stf(eng, r=r, j=j, b=b, s0=s0, ssl=ssl, v_fe=v_fe,
                            h=h_st):
                        eng.wait_ge(prog.sems["c_fe"][0], v_fe)
                        eng.dma_start(
                            out=out[b, s0 + j * 128:s0 + j * 128 + 128,
                                    r * 128:r * 128 + 128],
                            in_=ost[:, ssl:ssl + 128]).then_inc(h, 16)
                    ph["gpsimd"].append(stf)

            _snap = {n: prog.sems[n][1] for n in
                     ("c_ld0", "c_ld1", "c_ws0", "c_ws1", "x_ld0", "x_ld1")
                     if n in prog.sems}

            def sfin(eng, _snap=_snap):
                for n, v in _snap.items():
                    eng.wait_ge(prog.sems[n][0], v)
            ph["sync"].append(sfin)

            _snap2 = {n: prog.sems[n][1] for n in
                      ("c_st0", "c_st1", "c_st2", "c_st3") if n in prog.sems}

            def gfin(eng, _snap2=_snap2):
                for n, v in _snap2.items():
                    eng.wait_ge(prog.sems[n][0], v)
            ph["gpsimd"].append(gfin)

        for sl in range(NSL):
            emit_slice(sl)

        prog.emit(block)
    return nc



_CACHE = {}


def _get(name, builder):
    if name not in _CACHE:
        _CACHE[name] = builder()
    return _CACHE[name]


def kernel(**inputs):
    xs = np.ascontiguousarray(inputs["L0_attn_x"], dtype=np.float32)
    conv = np.ascontiguousarray(
        inputs["L0_attn_conv_w"], dtype=np.float32).reshape(D, 9)
    pe = np.ascontiguousarray(inputs["L0_attn_pe_w"], dtype=np.float32)
    qkv_w = np.ascontiguousarray(inputs["L0_attn_qkv_w"], dtype=np.float32)
    out_w = np.ascontiguousarray(inputs["L0_attn_out_w"], dtype=np.float32)
    w_gate = np.ascontiguousarray(inputs["L0_ffn_w_gate"], dtype=np.float32)
    w_up = np.ascontiguousarray(inputs["L0_ffn_w_up"], dtype=np.float32)
    w_down = np.ascontiguousarray(inputs["L0_ffn_w_down"], dtype=np.float32)
    cores = list(range(NCORES))

    def run(name, builder, per_core):
        nc = _get(name, builder)
        return run_bass_kernel_spmd(nc, per_core, core_ids=cores).results

    r1 = run("a1", build_a1,
             [{"x": xs[B * c:B * c + B], "conv_w": conv, "pe_w": pe,
               "qkv_w": qkv_w} for c in cores])
    r2 = run("a2", build_a2, [{"qkvT": r1[c]["qkvT"]} for c in cores])
    r3 = run("a3", build_a3,
             [{"onT": r2[c]["onT"], "x": xs[B * c:B * c + B], "out_w": out_w}
              for c in cores])
    m1a = np.concatenate([r3[c]["m1"] for c in cores], axis=1)
    r4 = run("b1", build_b1,
             [{"xpT": r3[c]["xpT"], "m1a": m1a, "w_up": w_up,
               "w_gate": w_gate} for c in cores])
    m2a = np.concatenate([r4[c]["m2"] for c in cores], axis=1)
    m3a = np.concatenate([r4[c]["m3"] for c in cores], axis=1)
    r5 = run("b2", build_b2,
             [{"gt_raw": r4[c]["gt_raw"], "m3a": m3a} for c in cores])
    m4a = np.concatenate([r5[c]["m4"] for c in cores], axis=1)
    r6 = run("b3", build_b3,
             [{"ab": r5[c]["ab"], "up_raw": r4[c]["up_raw"], "m2a": m2a,
               "m3a": m3a, "m4a": m4a} for c in cores])
    m5a = np.concatenate([r6[c]["m5"] for c in cores], axis=1)
    r7 = run("b4", build_b4,
             [{"cu": r6[c]["cu"], "xpT": r3[c]["xpT"], "w_down": w_down,
               "m2a": m2a, "m3a": m3a, "m4a": m4a, "m5a": m5a}
              for c in cores])
    return np.concatenate([r7[c]["out"] for c in cores], axis=0)




# version 04
# speedup vs baseline: 1.0014x; 1.0014x over previous
import contextlib
import numpy as np
import concourse.bass as bass
import concourse.mybir as mybir
from concourse.bass_utils import run_bass_kernel_spmd

F32 = mybir.dt.float32
F32R = mybir.dt.float32r
BF16 = mybir.dt.bfloat16
I16 = mybir.dt.int16
U16 = mybir.dt.uint16
AF = mybir.ActivationFunctionType
ALU = mybir.AluOpType
AX = mybir.AxisListType

NCORES = 8
B = 2
S = 2048
D = 1024
DF = 3072
NH = 16
TOK = B * S
QMAX = 32767.0
EPS = 1e-5

ENGINES = ("sync", "gpsimd", "tensor", "vector", "scalar")


class Prog:

    def __init__(self, nc, ctx):
        self.nc = nc
        self.ctx = ctx
        self.ph = ctx.enter_context(nc.semaphore("phase"))
        self.phases = []
        self.sems = {}

    def sem(self, name):
        if name not in self.sems:
            h = self.ctx.enter_context(self.nc.semaphore(name))
            self.sems[name] = [h, 0]
        return self.sems[name]

    def phase(self):
        d = {e: [] for e in ENGINES}
        self.phases.append(d)
        return d

    def emit(self, block):
        nph = len(self.phases)

        def mk(ename):
            def body(eng):
                for i, ph in enumerate(self.phases):
                    if i > 0:
                        eng.wait_ge(self.ph, 5 * i)
                    for fn in ph[ename]:
                        fn(eng)
                    eng.drain().then_inc(self.ph, 1)
            return body

        for e in ENGINES:
            getattr(block, e)(mk(e))
        return nph



def build_a1():
    nc = bass.Bass("TRN2", target_bir_lowering=False,
                   detect_race_conditions=False)
    x = nc.dram_tensor("x", [B, S, D], F32, kind="ExternalInput").ap()
    conv_w = nc.dram_tensor("conv_w", [D, 9], F32, kind="ExternalInput").ap()
    pe_w = nc.dram_tensor("pe_w", [D, D], F32, kind="ExternalInput").ap()
    qkv_w = nc.dram_tensor("qkv_w", [3 * D, D], F32, kind="ExternalInput").ap()
    qkvT = nc.dram_tensor("qkvT", [B, 3 * D, S], BF16, kind="ExternalOutput").ap()
    qkT_h = nc.dram_tensor("qkT_h", [128, 24 * 1024], F32R, kind="Internal").ap()
    qkT_l = nc.dram_tensor("qkT_l", [128, 24 * 1024], F32R, kind="Internal").ap()

    BT = 512
    NBLK = S // BT

    with contextlib.ExitStack() as ctx:
        E = ctx.enter_context
        wtmp = E(nc.sbuf_tensor("wtmp", [128, 256], F32))
        wth = E(nc.sbuf_tensor("wth", [128, 256], F32R))
        wtl = E(nc.sbuf_tensor("wtl", [128, 256], F32R))
        ident = E(nc.sbuf_tensor("ident", [128, 128], F32))
        cw = E(nc.sbuf_tensor("cw", [128, 8 * 9], F32))
        zcol = E(nc.sbuf_tensor("zcol", [128, 8], F32))
        epsc = E(nc.sbuf_tensor("epsc", [128, 1], F32))
        wpeh = E(nc.sbuf_tensor("wpeh", [128, 8 * 1024], F32R))
        wpel = E(nc.sbuf_tensor("wpel", [128, 8 * 1024], F32R))
        wqh = E(nc.sbuf_tensor("wqh", [128, 2 * 1024], F32R))
        wql = E(nc.sbuf_tensor("wql", [128, 2 * 1024], F32R))
        lnT = E(nc.sbuf_tensor("lnT", [128, 2 * 8 * BT], F32))
        haloL = E(nc.sbuf_tensor("haloL", [128, 8 * NBLK], F32))
        haloF = E(nc.sbuf_tensor("haloF", [128, 8 * NBLK], F32))
        xtm = E(nc.sbuf_tensor("xtm", [128, 2 * D], F32))
        sq = E(nc.sbuf_tensor("sq", [128, D], F32))
        stats = E(nc.sbuf_tensor("stats", [128, 16], F32))
        xcf = E(nc.sbuf_tensor("xcf", [128, BT], F32))
        xch = E(nc.sbuf_tensor("xch", [128, 8 * BT], F32R))
        xcl = E(nc.sbuf_tensor("xcl", [128, 8 * BT], F32R))
        x2t = E(nc.sbuf_tensor("x2t", [128, BT], F32))
        x2h = E(nc.sbuf_tensor("x2h", [128, 8 * BT], F32R))
        x2l = E(nc.sbuf_tensor("x2l", [128, 8 * BT], F32R))
        qstage = E(nc.sbuf_tensor("qstage", [128, 2 * BT], BF16))
        psA = E(nc.psum_tensor("psA", [128, 512], F32))
        psB = E(nc.psum_tensor("psB", [128, 512], F32))
        psT = E(nc.psum_tensor("psT", [128, 128], F32))
        psT2 = E(nc.psum_tensor("psT2", [128, 128], F32))
        block = E(nc.Block())

        prog = Prog(nc, ctx)
        W = lambda name: prog.sem(name)
        def WNOW(name):
            s = prog.sem(name)
            v = s[1]
            return lambda eng, _h=s[0], _v=v: eng.wait_ge(_h, _v)

        def INC(name, n=1):
            s = prog.sem(name)
            s[1] += n
            return s[0], s[1]

        ph0 = prog.phase()

        def p0(g):
            h, _ = INC("g0", 4)
            g.memset(ident[:], 0.0).then_inc(h, 1)
            g.wait_ge(h, 1)
            g.affine_select(out=ident[:], in_=ident[:], compare_op=ALU.not_equal,
                            fill=1.0, base=0, pattern=[[-1, 128]],
                            channel_multiplier=1).then_inc(h, 1)
            g.memset(zcol[:], 0.0).then_inc(h, 1)
            g.memset(epsc[:], EPS).then_inc(h, 1)
            g.wait_ge(h, 4)
        ph0["gpsimd"].append(p0)

        def p0s(sync):
            h, v = INC("cwld", 16)
            sync.dma_start(out=cw.rearrange("p (c n) -> p c n", c=8),
                           in_=conv_w.rearrange("(c p) n -> p c n", p=128)
                           ).then_inc(h, 16)
            sync.wait_ge(h, v)
        ph0["sync"].append(p0s)

        ph1 = prog.phase()
        tiles = []
        for of in range(8):
            for k in range(8):
                tiles.append((pe_w, of, k, True))
        for of in range(24):
            for k in range(8):
                tiles.append((qkv_w, of, k, False))

        for i, (src, of, k, is_pe) in enumerate(tiles):
            buf = (i % 2) * 128
            ldn = "a_ld%d" % (i % 2)
            h_ld, v_ld = INC(ldn, 16)

            def load(eng, i=i, src=src, of=of, k=k, buf=buf, h=h_ld):
                if i >= 2:
                    eng.wait_ge(prog.sems["a_tp"][0], i - 1)
                eng.dma_start(out=wtmp[:, buf:buf + 128],
                              in_=src[of * 128:of * 128 + 128,
                                      k * 128:k * 128 + 128]).then_inc(h, 16)
            ph1["sync"].append(load)

            h_tp, v_tp = INC("a_tp", 1)

            def tpose(eng, i=i, buf=buf, v_ld=v_ld, h=h_tp, ldn=ldn):
                eng.wait_ge(prog.sems[ldn][0], v_ld)
                if i >= 2:
                    eng.wait_ge(prog.sems["a_vp"][0], i - 1)
                p = psT if i % 2 == 0 else psT2
                eng.transpose(p[:], wtmp[:, buf:buf + 128], ident[:]).then_inc(h, 1)
            ph1["tensor"].append(tpose)

            h_vp, v_vp = INC("a_vp", 1)
            col = of * 1024 + k * 128
            if is_pe:
                def split(eng, i=i, v_tp=v_tp, col=col, h=h_vp):
                    eng.wait_ge(prog.sems["a_tp"][0], v_tp)
                    p = psT if i % 2 == 0 else psT2
                    eng.tensor_copy(wpeh[:, col:col + 128], p[:])
                    eng.tensor_tensor(wpel[:, col:col + 128], p[:],
                                      wpeh[:, col:col + 128],
                                      ALU.subtract).then_inc(h, 1)
                ph1["vector"].append(split)
            else:
                gpn = "a_gp%d" % (i % 2)
                v_gp_need = prog.sem(gpn)[1]

                def split(eng, i=i, v_tp=v_tp, buf=buf, h=h_vp, gpn=gpn,
                          v_gp_need=v_gp_need):
                    eng.wait_ge(prog.sems["a_tp"][0], v_tp)
                    if v_gp_need > 0:
                        eng.wait_ge(prog.sems[gpn][0], v_gp_need)
                    p = psT if i % 2 == 0 else psT2
                    eng.tensor_copy(wth[:, buf:buf + 128], p[:])
                    eng.tensor_tensor(wtl[:, buf:buf + 128], p[:],
                                      wth[:, buf:buf + 128],
                                      ALU.subtract).then_inc(h, 1)
                ph1["vector"].append(split)

                h_gp, v_gp = INC(gpn, 32)

                def store(eng, v_vp=v_vp, buf=buf, col=col, h=h_gp):
                    eng.wait_ge(prog.sems["a_vp"][0], v_vp)
                    eng.dma_start(out=qkT_h[:, col:col + 128],
                                  in_=wth[:, buf:buf + 128]).then_inc(h, 16)
                    eng.dma_start(out=qkT_l[:, col:col + 128],
                                  in_=wtl[:, buf:buf + 128]).then_inc(h, 16)
                ph1["gpsimd"].append(store)
        _snap = {n: prog.sems[n][1] for n in ("a_ld0", "a_ld1", "a_gp0", "a_gp1")
                 if n in prog.sems}

        def p1_sync_fin(eng, _snap=_snap):
            for n in ("a_ld0", "a_ld1"):
                if n in _snap:
                    eng.wait_ge(prog.sems[n][0], _snap[n])
        ph1["sync"].append(p1_sync_fin)

        def p1_gp_fin(eng, _snap=_snap):
            for n in ("a_gp0", "a_gp1"):
                if n in _snap:
                    eng.wait_ge(prog.sems[n][0], _snap[n])
        ph1["gpsimd"].append(p1_gp_fin)

        xslot = [0]

        def emit_ln_block(b, blk):
            ph = prog.phase()
            slot = (blk % 2) * 8 * BT
            for i4 in range(BT // 128):
                t0 = blk * BT + i4 * 128
                n_x = xslot[0]
                xslot[0] += 1
                buf = (n_x % 2) * D
                ldn = "x_ld%d" % (n_x % 2)
                h_ld, v_ld = INC(ldn, 16)
                v_tp_need = max(prog.sem("x_tp")[1] - 8, 0)

                def load(eng, b=b, t0=t0, buf=buf, h=h_ld, v_tp_need=v_tp_need):
                    if v_tp_need > 0:
                        eng.wait_ge(prog.sems["x_tp"][0], v_tp_need)
                    eng.dma_start(out=xtm[:, buf:buf + D],
                                  in_=x[b, t0:t0 + 128, :]).then_inc(h, 16)
                ph["sync"].append(load)

                h_sp, v_sp = INC("x_sp", 1)

                v_vp_need = max(prog.sem("x_vp")[1] - 1, 0)

                def dosq(eng, v_ld=v_ld, buf=buf, h=h_sp, v_vp_need=v_vp_need,
                         ldn=ldn):
                    eng.wait_ge(prog.sems[ldn][0], v_ld)
                    if v_vp_need > 0:
                        eng.wait_ge(prog.sems["x_vp"][0], v_vp_need)
                    eng.activation(sq[:], xtm[:, buf:buf + D],
                                   AF.Square).then_inc(h, 1)
                ph["scalar"].append(dosq)

                h_vp, v_vp = INC("x_vp", 1)

                def dostats(eng, v_sp=v_sp, v_ld=v_ld, buf=buf, h=h_vp, ldn=ldn):
                    eng.wait_ge(prog.sems[ldn][0], v_ld)
                    eng.reduce_sum(stats[:, 0:1], xtm[:, buf:buf + D], AX.X)
                    eng.wait_ge(prog.sems["x_sp"][0], v_sp)
                    eng.reduce_sum(stats[:, 1:2], sq[:], AX.X)
                    eng.drain()
                    eng.tensor_scalar(stats[:, 2:3], stats[:, 0:1],
                                      1.0 / D, None, ALU.mult)
                    eng.tensor_scalar(stats[:, 3:4], stats[:, 1:2],
                                      1.0 / D, None, ALU.mult)
                    eng.drain()
                    eng.tensor_tensor(stats[:, 4:5], stats[:, 2:3],
                                      stats[:, 2:3], ALU.mult)
                    eng.drain()
                    eng.tensor_tensor(stats[:, 5:6], stats[:, 3:4],
                                      stats[:, 4:5], ALU.subtract)
                    eng.drain().then_inc(h, 1)
                ph["vector"].append(dostats)

                h_sp2, v_sp2 = INC("x_sp", 1)

                def dosqrt(eng, v_vp=v_vp, h=h_sp2):
                    eng.wait_ge(prog.sems["x_vp"][0], v_vp)
                    eng.activation(stats[:, 6:7], stats[:, 5:6], AF.Sqrt,
                                   bias=epsc[:]).then_inc(h, 1)
                ph["scalar"].append(dosqrt)

                h_vp2, v_vp2 = INC("x_vp", 1)

                def doapply(eng, v_sp2=v_sp2, buf=buf, h=h_vp2):
                    eng.wait_ge(prog.sems["x_sp"][0], v_sp2)
                    eng.reciprocal(stats[:, 7:8], stats[:, 6:7])
                    eng.drain()
                    eng.tensor_tensor(stats[:, 8:9], stats[:, 2:3],
                                      stats[:, 7:8], ALU.mult)
                    eng.drain()
                    eng.tensor_scalar(stats[:, 9:10], stats[:, 8:9],
                                      -1.0, None, ALU.mult)
                    eng.drain()
                    eng.tensor_scalar(xtm[:, buf:buf + D], xtm[:, buf:buf + D],
                                      stats[:, 7:8], stats[:, 9:10],
                                      ALU.mult, ALU.add).then_inc(h, 1)
                ph["vector"].append(doapply)

                for c in range(8):
                    h_tp, v_tp = INC("x_tp", 1)

                    def tpose(eng, v_vp2=v_vp2, buf=buf, c=c, h=h_tp,
                              ntp=prog.sems["x_tp"][1]):
                        eng.wait_ge(prog.sems["x_vp"][0], v_vp2)
                        if ntp >= 3:
                            eng.wait_ge(prog.sems["x_ev"][0], ntp - 2)
                        p = psT if ntp % 2 == 1 else psT2
                        eng.transpose(p[:], xtm[:, buf + c * 128:buf + c * 128 + 128],
                                      ident[:]).then_inc(h, 1)
                    prog.sem("x_ev")
                    ph["tensor"].append(tpose)

                    h_ev, v_ev = INC("x_ev", 1)
                    dst = slot + c * BT + i4 * 128

                    def evac(eng, v_tp=v_tp, dst=dst, h=h_ev,
                             ntp=prog.sems["x_tp"][1]):
                        eng.wait_ge(prog.sems["x_tp"][0], v_tp)
                        p = psT if ntp % 2 == 1 else psT2
                        eng.tensor_copy(lnT[:, dst:dst + 128], p[:]).then_inc(h, 1)
                    ph["vector"].append(evac)

            def halos(eng, b=b, blk=blk, slot=slot):
                for c in range(8):
                    eng.tensor_copy(haloL[:, c * NBLK + blk:c * NBLK + blk + 1],
                                    lnT[:, slot + c * BT + BT - 1:
                                        slot + c * BT + BT])
                    eng.tensor_copy(haloF[:, c * NBLK + blk:c * NBLK + blk + 1],
                                    lnT[:, slot + c * BT:slot + c * BT + 1])
            ph["vector"].append(halos)

            _snap = {n: prog.sems[n][1] for n in ("x_ld0", "x_ld1")
                     if n in prog.sems}

            def lnfin(eng, _snap=_snap):
                for n, v in _snap.items():
                    eng.wait_ge(prog.sems[n][0], v)
            ph["sync"].append(lnfin)

        def emit_main_block(b, blk):
            slot = (blk % 2) * 8 * BT
            t0 = blk * BT

            ph = prog.phase()

            def conv(eng, b=b, blk=blk, slot=slot):
                for c in range(8):
                    w0 = cw[:, 9 * c + 1:9 * c + 2]
                    w1 = cw[:, 9 * c + 4:9 * c + 5]
                    w2 = cw[:, 9 * c + 7:9 * c + 8]
                    mid = lnT[:, slot + c * BT:slot + c * BT + BT]
                    eng.tensor_scalar(xcf[:], mid, w1, None, ALU.mult)
                    if blk == 0:
                        eng.scalar_tensor_tensor(
                            xcf[:, 1:BT], lnT[:, slot + c * BT:slot + c * BT + BT - 1],
                            w0, xcf[:, 1:BT], ALU.mult, ALU.add)
                    else:
                        eng.scalar_tensor_tensor(
                            xcf[:, 0:1], haloL[:, c * NBLK + blk - 1:c * NBLK + blk],
                            w0, xcf[:, 0:1], ALU.mult, ALU.add)
                        eng.scalar_tensor_tensor(
                            xcf[:, 1:BT], lnT[:, slot + c * BT:slot + c * BT + BT - 1],
                            w0, xcf[:, 1:BT], ALU.mult, ALU.add)
                    if blk == NBLK - 1:
                        eng.scalar_tensor_tensor(
                            xcf[:, 0:BT - 1], lnT[:, slot + c * BT + 1:slot + c * BT + BT],
                            w2, xcf[:, 0:BT - 1], ALU.mult, ALU.add)
                    else:
                        eng.scalar_tensor_tensor(
                            xcf[:, 0:BT - 1], lnT[:, slot + c * BT + 1:slot + c * BT + BT],
                            w2, xcf[:, 0:BT - 1], ALU.mult, ALU.add)
                        eng.scalar_tensor_tensor(
                            xcf[:, BT - 1:BT], haloF[:, c * NBLK + blk + 1:c * NBLK + blk + 2],
                            w2, xcf[:, BT - 1:BT], ALU.mult, ALU.add)
                    eng.tensor_copy(xch[:, c * BT:c * BT + BT], xcf[:])
                    eng.tensor_tensor(xcl[:, c * BT:c * BT + BT], xcf[:],
                                      xch[:, c * BT:c * BT + BT], ALU.subtract)
            ph["vector"].append(conv)

            ph = prog.phase()
            for of in range(8):
                bank = psA if of % 2 == 0 else psB
                h_mm, v_mm = INC("m_mm", 1)

                def pemm(eng, of=of, bank=bank, h=h_mm, nmm=prog.sems["m_mm"][1]):
                    if nmm >= 3:
                        eng.wait_ge(prog.sems["m_ev"][0], nmm - 2)
                    for k in range(8):
                        wcol = of * 1024 + k * 128
                        eng.matmul(bank[:, 0:BT], wpeh[:, wcol:wcol + 128],
                                   xch[:, k * BT:k * BT + BT],
                                   start=(k == 0), stop=False)
                        eng.matmul(bank[:, 0:BT], wpeh[:, wcol:wcol + 128],
                                   xcl[:, k * BT:k * BT + BT],
                                   start=False, stop=False)
                        mm = eng.matmul(bank[:, 0:BT], wpel[:, wcol:wcol + 128],
                                        xch[:, k * BT:k * BT + BT],
                                        start=False, stop=(k == 7))
                        if k == 7:
                            mm.then_inc(h, 1)
                prog.sem("m_ev")
                ph["tensor"].append(pemm)

                h_ev, v_ev = INC("m_ev", 1)

                def peev(eng, of=of, bank=bank, v_mm=v_mm, h=h_ev):
                    eng.wait_ge(prog.sems["m_mm"][0], v_mm)
                    eng.tensor_tensor(x2t[:], bank[:, 0:BT],
                                      xch[:, of * BT:of * BT + BT], ALU.add)
                    eng.tensor_tensor(x2t[:], x2t[:],
                                      xcl[:, of * BT:of * BT + BT], ALU.add)
                    eng.tensor_copy(x2h[:, of * BT:of * BT + BT], x2t[:])
                    eng.tensor_tensor(x2l[:, of * BT:of * BT + BT], x2t[:],
                                      x2h[:, of * BT:of * BT + BT],
                                      ALU.subtract).then_inc(h, 1)
                ph["vector"].append(peev)

            ph = prog.phase()
            for of in range(24):
                buf = (of % 2) * 1024
                wsn = "q_ws%d" % (of % 2)
                v_need = prog.sem("q_mm")[1] - 1 if of >= 2 else 0
                h_ws, v_ws = INC(wsn, 32)

                def wload(eng, of=of, buf=buf, h=h_ws, v_need=v_need):
                    if v_need > 0:
                        eng.wait_ge(prog.sems["q_mm"][0], v_need)
                    eng.dma_start(out=wqh[:, buf:buf + 1024],
                                  in_=qkT_h[:, of * 1024:of * 1024 + 1024]).then_inc(h, 16)
                    eng.dma_start(out=wql[:, buf:buf + 1024],
                                  in_=qkT_l[:, of * 1024:of * 1024 + 1024]).then_inc(h, 16)
                ph["sync"].append(wload)

                bank = psA if of % 2 == 0 else psB
                h_mm, v_mm = INC("q_mm", 1)

                def qmm(eng, of=of, buf=buf, bank=bank, v_ws=v_ws,
                        h=h_mm, nmm=prog.sems["q_mm"][1], wsn=wsn):
                    eng.wait_ge(prog.sems[wsn][0], v_ws)
                    if nmm >= 3:
                        eng.wait_ge(prog.sems["q_ev"][0], nmm - 2)
                    for k in range(8):
                        wcol = buf + k * 128
                        eng.matmul(bank[:, 0:BT], wqh[:, wcol:wcol + 128],
                                   x2h[:, k * BT:k * BT + BT],
                                   start=(k == 0), stop=False)
                        eng.matmul(bank[:, 0:BT], wqh[:, wcol:wcol + 128],
                                   x2l[:, k * BT:k * BT + BT],
                                   start=False, stop=False)
                        mm = eng.matmul(bank[:, 0:BT], wql[:, wcol:wcol + 128],
                                        x2h[:, k * BT:k * BT + BT],
                                        start=False, stop=(k == 7))
                        if k == 7:
                            mm.then_inc(h, 1)
                prog.sem("q_ev")
                prog.sem("q_st")
                ph["tensor"].append(qmm)

                h_ev, v_ev = INC("q_ev", 1)
                nev = prog.sems["q_ev"][1]
                stn = "q_st%d" % (nev % 2)
                v_st_need = prog.sem(stn)[1]

                def qev(eng, of=of, bank=bank, v_mm=v_mm, h=h_ev,
                        nev=nev, stn=stn, v_st_need=v_st_need):
                    eng.wait_ge(prog.sems["q_mm"][0], v_mm)
                    if v_st_need > 0:
                        eng.wait_ge(prog.sems[stn][0], v_st_need)
                    qb = (nev % 2) * BT
                    eng.tensor_copy(qstage[:, qb:qb + BT],
                                    bank[:, 0:BT]).then_inc(h, 1)
                ph["vector"].append(qev)

                h_st, v_st = INC(stn, 16)

                def qst(eng, b=b, of=of, t0=t0, v_ev=v_ev, h=h_st,
                        nev=nev):
                    eng.wait_ge(prog.sems["q_ev"][0], v_ev)
                    qb = (nev % 2) * BT
                    eng.dma_start(out=qkvT[b, of * 128:of * 128 + 128,
                                           t0:t0 + BT],
                                  in_=qstage[:, qb:qb + BT]).then_inc(h, 16)
                ph["gpsimd"].append(qst)

            _snap = {n: prog.sems[n][1] for n in
                     ("q_ws0", "q_ws1", "q_st0", "q_st1") if n in prog.sems}

            def qsync_fin(eng, _snap=_snap):
                for n in ("q_ws0", "q_ws1"):
                    if n in _snap:
                        eng.wait_ge(prog.sems[n][0], _snap[n])
            ph["sync"].append(qsync_fin)

            def qgp_fin(eng, _snap=_snap):
                for n in ("q_st0", "q_st1"):
                    if n in _snap:
                        eng.wait_ge(prog.sems[n][0], _snap[n])
            ph["gpsimd"].append(qgp_fin)

        for b in range(B):
            emit_ln_block(b, 0)
            emit_ln_block(b, 1)
            emit_main_block(b, 0)
            emit_ln_block(b, 2)
            emit_main_block(b, 1)
            emit_ln_block(b, 3)
            emit_main_block(b, 2)
            emit_main_block(b, 3)

        prog.emit(block)
    return nc




def build_a2():
    nc = bass.Bass("TRN2", target_bir_lowering=False,
                   detect_race_conditions=False)
    qkvT = nc.dram_tensor("qkvT", [B, 3 * D, S], BF16, kind="ExternalInput").ap()
    onT = nc.dram_tensor("onT", [B, D, S], BF16, kind="ExternalOutput").ap()

    NP = 8
    NBLK = 4
    BQ = 512
    NCH = 16

    with contextlib.ExitStack() as ctx:
        E = ctx.enter_context
        ident = E(nc.sbuf_tensor("ident", [128, 128], BF16))
        identf = E(nc.sbuf_tensor("identf", [128, 128], F32))
        onesr = E(nc.sbuf_tensor("onesr", [1, 128], F32))
        indic = E(nc.sbuf_tensor("indic", [128, 2], BF16))
        qplain = E(nc.sbuf_tensor("qplain", [128, 8 * 256], BF16))
        kplain = E(nc.sbuf_tensor("kplain", [128, 8 * 256], BF16))
        vplain = E(nc.sbuf_tensor("vplain", [128, 8 * 256], BF16))
        qtr = E(nc.sbuf_tensor("qtr", [128, S], BF16))
        ktr = E(nc.sbuf_tensor("ktr", [128, S], BF16))
        vht = E(nc.sbuf_tensor("vht", [64, S], BF16))
        vre = E(nc.sbuf_tensor("vre", [128, 2 * NCH * 65], BF16))
        nstat = E(nc.sbuf_tensor("nstat", [1, 16], F32))
        nstat2 = E(nc.sbuf_tensor("nstat2", [128, 4], F32))
        biasA = E(nc.sbuf_tensor("biasA", [128, 1], F32))
        biasB = E(nc.sbuf_tensor("biasB", [128, 1], F32))
        pt = E(nc.sbuf_tensor("pt", [128, 4 * 512], BF16))
        rsum = E(nc.sbuf_tensor("rsum", [1, 512], F32))
        bsum = E(nc.sbuf_tensor("bsum", [64, 2 * 512], F32))
        onst = E(nc.sbuf_tensor("onst", [128, 2 * 512], BF16))
        psQ = [E(nc.psum_tensor("psQ%d" % i, [128, 512], F32)) for i in range(4)]
        psVA = E(nc.psum_tensor("psVA", [128, 512], F32))
        psVB = E(nc.psum_tensor("psVB", [128, 512], F32))
        psX = E(nc.psum_tensor("psX", [128, 512], F32))
        psVT = E(nc.psum_tensor("psVT", [128, 128], BF16))
        block = E(nc.Block())

        prog = Prog(nc, ctx)

        def INC(name, n=1):
            s = prog.sem(name)
            s[1] += n
            return s[0], s[1]

        ph0 = prog.phase()

        def p0(g):
            h, _ = INC("g0", 6)
            g.memset(ident[:], 0.0).then_inc(h, 1)
            g.wait_ge(h, 1)
            g.affine_select(out=ident[:], in_=ident[:], compare_op=ALU.not_equal,
                            fill=1.0, base=0, pattern=[[-1, 128]],
                            channel_multiplier=1).then_inc(h, 1)
            g.memset(onesr[:], 1.0).then_inc(h, 1)
            hf, _ = INC("g2", 2)
            g.memset(identf[:], 0.0).then_inc(hf, 1)
            g.wait_ge(hf, 1)
            g.affine_select(out=identf[:], in_=identf[:], compare_op=ALU.not_equal,
                            fill=1.0, base=0, pattern=[[-1, 128]],
                            channel_multiplier=1).then_inc(hf, 1)
            g.wait_ge(hf, 2)
            g.memset(indic[:], 0.0).then_inc(h, 1)
            g.wait_ge(h, 4)
            g.memset(indic[0:64, 0:1], 1.0).then_inc(h, 1)
            g.memset(indic[64:128, 1:2], 1.0).then_inc(h, 1)
            g.wait_ge(h, 6)
            h2, _ = INC("g1", 1)
            g.memset(vre.rearrange("p (s c) -> p s c", c=65)[:, :, 64:65],
                     1.0).then_inc(h2, 1)
            g.wait_ge(h2, 1)
        ph0["gpsimd"].append(p0)

        def emit_pair(b, u):
            ph = prog.phase()
            for name, buf, base in (("q", qplain, 0), ("k", kplain, D),
                                    ("v", vplain, 2 * D)):
                for c8 in range(8):
                    h_ld, v_ld = INC("r_ld", 16)

                    def ld(eng, b=b, u=u, c8=c8, buf=buf, base=base, h=h_ld):
                        eng.dma_start(
                            out=buf[:, c8 * 256:c8 * 256 + 256],
                            in_=qkvT[b, base + c8 * 128: base + c8 * 128 + 128,
                                     256 * u:256 * u + 256]).then_inc(h, 16)
                    ph["sync"].append(ld)
            v_ld_all = prog.sems["r_ld"][1]

            h_re, v_re = INC("r_re", 1)

            def rearr(eng, v_ld_all=v_ld_all, h=h_re):
                eng.wait_ge(prog.sems["r_ld"][0], v_ld_all)
                for dst, srcb in ((qtr, qplain), (ktr, kplain)):
                    for hh in range(2):
                        for g in range(16):
                            eng.tensor_copy(
                                dst.rearrange("p (t g) -> p t g", g=16)
                                   [hh * 64:hh * 64 + 64, :, g],
                                srcb[(g % 2) * 64:(g % 2) * 64 + 64,
                                     (g // 2) * 256 + hh * 128:
                                     (g // 2) * 256 + hh * 128 + 128])
                eng.drain().then_inc(h, 1)
            ph["vector"].append(rearr)

            for hh in range(2):
                h_vh, v_vh = INC("r_vh", 1)

                v_ev_need = prog.sem("r_ev")[1]

                def vre_build(eng, hh=hh, v_ld_all=v_ld_all, h=h_vh,
                              v_ev_need=v_ev_need):
                    eng.wait_ge(prog.sems["r_ld"][0], v_ld_all)
                    if hh == 1:
                        eng.wait_ge(prog.sems["r_ev"][0], v_ev_need)
                    for g in range(16):
                        eng.tensor_copy(
                            vht.rearrange("p (t g) -> p t g", g=16)[:, :, g],
                            vplain[(g % 2) * 64:(g % 2) * 64 + 64,
                                   (g // 2) * 256 + hh * 128:
                                   (g // 2) * 256 + hh * 128 + 128])
                    eng.drain().then_inc(h, 1)
                ph["vector"].append(vre_build)

                for c in range(NCH):
                    h_tp, v_tp = INC("r_tp", 1)

                    def vtp(eng, hh=hh, c=c, v_vh=v_vh, h=h_tp,
                            ntp=prog.sems["r_tp"][1]):
                        eng.wait_ge(prog.sems["r_vh"][0], v_vh)
                        if ntp >= 2:
                            eng.wait_ge(prog.sems["r_ev"][0], ntp - 1)
                        p = psVT
                        eng.transpose(p[:, 0:64], vht[:, c * 128:c * 128 + 128],
                                      ident[0:64, 0:64]).then_inc(h, 1)
                    prog.sem("r_ev")
                    ph["tensor"].append(vtp)

                    h_ev, v_ev = INC("r_ev", 1)
                    col = (hh * NCH + c) * 65

                    def vev(eng, v_tp=v_tp, col=col, h=h_ev):
                        eng.wait_ge(prog.sems["r_tp"][0], v_tp)
                        eng.tensor_copy(vre[:, col:col + 64],
                                        psVT[:, 0:64]).then_inc(h, 1)
                    ph["vector"].append(vev)

            h_cm, v_cm = INC("r_cm", 1)
            v_re_c = prog.sems["r_re"][1]

            def colmax(eng, v_re_c=v_re_c, h=h_cm):
                eng.wait_ge(prog.sems["r_re"][0], v_re_c)
                eng.tensor_reduce(nstat2[:, 0:1], qtr[:], AX.X, ALU.max,
                                  apply_absolute_value=True)
                eng.tensor_reduce(nstat2[:, 1:2], ktr[:], AX.X, ALU.max,
                                  apply_absolute_value=True)
                eng.drain()
                eng.tensor_tensor(nstat2[:, 2:3], nstat2[:, 0:1],
                                  nstat2[:, 1:2], ALU.max)
                eng.drain().then_inc(h, 1)
            ph["vector"].append(colmax)

            h_ct, v_ct = INC("r_ct", 1)

            def coltp(eng, v_cm=v_cm, h=h_ct):
                eng.wait_ge(prog.sems["r_cm"][0], v_cm)
                eng.transpose(psX[0:1, 0:128], nstat2[:, 2:3],
                              identf[:]).then_inc(h, 1)
            ph["tensor"].append(coltp)

            h_nr, v_nr = INC("r_nr", 1)

            def headmax(eng, v_ct=v_ct, h=h_nr):
                eng.wait_ge(prog.sems["r_ct"][0], v_ct)
                eng.tensor_reduce(nstat[:, 0:1], psX[0:1, 0:64], AX.X, ALU.max)
                eng.tensor_reduce(nstat[:, 1:2], psX[0:1, 64:128], AX.X, ALU.max)
                eng.drain()
                eng.tensor_tensor(nstat[:, 2:3], nstat[:, 0:1], nstat[:, 0:1],
                                  ALU.mult)
                eng.tensor_tensor(nstat[:, 3:4], nstat[:, 1:2], nstat[:, 1:2],
                                  ALU.mult)
                eng.drain()
                eng.tensor_scalar(nstat[:, 4:5], nstat[:, 2:3], -8.0, None,
                                  ALU.mult)
                eng.tensor_scalar(nstat[:, 5:6], nstat[:, 3:4], -8.0, None,
                                  ALU.mult)
                eng.drain().then_inc(h, 1)
            ph["vector"].append(headmax)

            h_bc, v_bc = INC("r_bc", 1)

            def bcb(eng, v_nr=v_nr, h=h_bc):
                eng.wait_ge(prog.sems["r_nr"][0], v_nr)
                eng.matmul(psX[:, 0:1], onesr[:], nstat[0:1, 4:5],
                           start=True, stop=True)
                eng.matmul(psX[:, 1:2], onesr[:], nstat[0:1, 5:6],
                           start=True, stop=True).then_inc(h, 1)
            ph["tensor"].append(bcb)

            h_bv, v_bv = INC("r_bv", 1)

            def bcv(eng, v_bc=v_bc, h=h_bv):
                eng.wait_ge(prog.sems["r_bc"][0], v_bc)
                eng.tensor_copy(biasA[:], psX[:, 0:1])
                eng.tensor_copy(biasB[:], psX[:, 1:2])
                eng.drain().then_inc(h, 1)
            ph["vector"].append(bcv)

            def ld_fin(eng, v=v_ld_all):
                eng.wait_ge(prog.sems["r_ld"][0], v)
            ph["sync"].append(ld_fin)

            ph = prog.phase()
            for blk in range(NBLK):
                q0 = blk * BQ
                for c in range(NCH):
                    bankA = psQ[(c % 2) * 2]
                    bankB = psQ[(c % 2) * 2 + 1]
                    h_qk, v_qk = INC("t_qk", 1)

                    def qk(eng, c=c, q0=q0, bankA=bankA, bankB=bankB, h=h_qk,
                           nqk=prog.sems["t_qk"][1]):
                        if nqk >= 3:
                            eng.wait_ge(prog.sems["t_ex"][0], nqk - 2)
                        eng.matmul(bankA[:], ktr[0:64, c * 128:c * 128 + 128],
                                   qtr[0:64, q0:q0 + BQ], start=True, stop=True)
                        eng.matmul(bankB[:], ktr[64:128, c * 128:c * 128 + 128],
                                   qtr[64:128, q0:q0 + BQ], start=True,
                                   stop=True).then_inc(h, 1)
                    prog.sem("t_ex")
                    ph["tensor"].append(qk)

                    h_ex, v_ex = INC("t_ex", 1)
                    slotA = (c % 2) * 512
                    slotB = 1024 + (c % 2) * 512

                    def ex(eng, v_qk=v_qk, bankA=bankA, bankB=bankB,
                           slotA=slotA, slotB=slotB, h=h_ex,
                           nex=prog.sems["t_ex"][1]):
                        eng.wait_ge(prog.sems["t_qk"][0], v_qk)
                        if nex >= 3:
                            eng.wait_ge(prog.sems["t_av"][0], nex - 2)
                        eng.activation(pt[:, slotA:slotA + 512], bankA[:],
                                       AF.Exp, bias=biasA[:], scale=0.125)
                        eng.activation(pt[:, slotB:slotB + 512], bankB[:],
                                       AF.Exp, bias=biasB[:],
                                       scale=0.125).then_inc(h, 1)
                    prog.sem("t_av")
                    ph["scalar"].append(ex)

                    h_av, v_av = INC("t_av", 1)
                    v_nrm_need = prog.sem("t_n1")[1] if c == 0 else 0

                    def av(eng, c=c, v_ex=v_ex, slotA=slotA, slotB=slotB,
                           h=h_av, v_nrm_need=v_nrm_need):
                        eng.wait_ge(prog.sems["t_ex"][0], v_ex)
                        if c == 0 and v_nrm_need > 0:
                            eng.wait_ge(prog.sems["t_n1"][0], v_nrm_need)
                        eng.matmul(psVA[0:65, :], vre[:, c * 65:c * 65 + 65],
                                   pt[:, slotA:slotA + 512],
                                   start=(c == 0), stop=(c == NCH - 1))
                        eng.matmul(psVB[0:65, :],
                                   vre[:, (NCH + c) * 65:(NCH + c) * 65 + 65],
                                   pt[:, slotB:slotB + 512],
                                   start=(c == 0),
                                   stop=(c == NCH - 1)).then_inc(h, 1)
                    ph["tensor"].append(av)

                v_av_end = prog.sems["t_av"][1]
                h_n1, v_n1 = INC("t_n1", 1)

                def norm1(eng, v_av_end=v_av_end, h=h_n1):
                    eng.wait_ge(prog.sems["t_av"][0], v_av_end)
                    eng.reciprocal(rsum[:], psVA[64:65, :])
                    eng.drain().then_inc(h, 1)
                ph["vector"].append(norm1)

                h_b1, v_b1 = INC("t_b1", 1)

                def bc1(eng, v_n1=v_n1, h=h_b1):
                    eng.wait_ge(prog.sems["t_n1"][0], v_n1)
                    eng.matmul(psX[0:64, :], onesr[0:1, 0:64], rsum[:],
                               start=True, stop=True).then_inc(h, 1)
                ph["tensor"].append(bc1)

                h_n2, v_n2 = INC("t_n1", 1)

                def norm2(eng, v_b1=v_b1, h=h_n2):
                    eng.wait_ge(prog.sems["t_b1"][0], v_b1)
                    eng.tensor_copy(bsum[:, 0:512], psX[0:64, :])
                    eng.drain()
                    eng.reciprocal(rsum[:], psVB[64:65, :])
                    eng.drain().then_inc(h, 1)
                ph["vector"].append(norm2)

                h_b2, v_b2 = INC("t_b1", 1)

                def bc2(eng, v_n2=v_n2, h=h_b2):
                    eng.wait_ge(prog.sems["t_n1"][0], v_n2)
                    eng.matmul(psX[0:64, :], onesr[0:1, 0:64], rsum[:],
                               start=True, stop=True).then_inc(h, 1)
                ph["tensor"].append(bc2)

                h_n3, v_n3 = INC("t_n1", 1)
                stn = "t_st%d" % (blk % 2)
                v_st_need = prog.sem(stn)[1]

                def norm3(eng, blk=blk, v_b2=v_b2, h=h_n3, v_st_need=v_st_need,
                          stn=stn):
                    eng.wait_ge(prog.sems["t_b1"][0], v_b2)
                    if v_st_need > 0:
                        eng.wait_ge(prog.sems[stn][0], v_st_need)
                    eng.tensor_copy(bsum[:, 512:1024], psX[0:64, :])
                    eng.drain()
                    ob = (blk % 2) * 512
                    eng.tensor_tensor(onst[0:64, ob:ob + 512], psVA[0:64, :],
                                      bsum[:, 0:512], ALU.mult)
                    eng.tensor_tensor(onst[64:128, ob:ob + 512], psVB[0:64, :],
                                      bsum[:, 512:1024], ALU.mult)
                    eng.drain().then_inc(h, 1)
                ph["vector"].append(norm3)

                h_st, v_st = INC(stn, 16)

                def store(eng, b=b, u=u, q0=q0, blk=blk, v_n3=v_n3, h=h_st):
                    eng.wait_ge(prog.sems["t_n1"][0], v_n3)
                    ob = (blk % 2) * 512
                    eng.dma_start(out=onT[b, u * 128:u * 128 + 128, q0:q0 + BQ],
                                  in_=onst[:, ob:ob + 512]).then_inc(h, 16)
                ph["gpsimd"].append(store)

            _snap = {n: prog.sems[n][1] for n in ("t_st0", "t_st1")
                     if n in prog.sems}

            def gp_fin(eng, _snap=_snap):
                for n, v in _snap.items():
                    eng.wait_ge(prog.sems[n][0], v)
            ph["gpsimd"].append(gp_fin)

        for b in range(B):
            for u in range(NP):
                emit_pair(b, u)

        prog.emit(block)
    return nc


_A2 = None


def _get_a2():
    global _A2
    if _A2 is None:
        _A2 = build_a2()
    return _A2


def run_a2(qkvT_list):
    nc = _get_a2()
    in_maps = [{"qkvT": q} for q in qkvT_list]
    res = run_bass_kernel_spmd(nc, in_maps, core_ids=list(range(NCORES)))
    return [r["onT"] for r in res.results]




def build_a3():
    nc = bass.Bass("TRN2", target_bir_lowering=False,
                   detect_race_conditions=False)
    onT = nc.dram_tensor("onT", [B, D, S], BF16, kind="ExternalInput").ap()
    x = nc.dram_tensor("x", [B, S, D], F32, kind="ExternalInput").ap()
    out_w = nc.dram_tensor("out_w", [D, D], F32, kind="ExternalInput").ap()
    xpT = nc.dram_tensor("xpT", [B, D, S], F32, kind="ExternalOutput").ap()
    m1 = nc.dram_tensor("m1", [1, 1], F32, kind="ExternalOutput").ap()
    woT_h = nc.dram_tensor("woT_h", [128, 8 * D], F32R, kind="Internal").ap()
    woT_l = nc.dram_tensor("woT_l", [128, 8 * D], F32R, kind="Internal").ap()

    BT = 512
    NBLK = S // BT

    with contextlib.ExitStack() as ctx:
        E = ctx.enter_context
        wtmp = E(nc.sbuf_tensor("wtmp", [128, 256], F32))
        wth = E(nc.sbuf_tensor("wth", [128, 256], F32R))
        wtl = E(nc.sbuf_tensor("wtl", [128, 256], F32R))
        identf = E(nc.sbuf_tensor("identf", [128, 128], F32))
        onescol = E(nc.sbuf_tensor("onescol", [128, 1], F32))
        onesr = E(nc.sbuf_tensor("onesr", [1, 128], F32))
        epsc = E(nc.sbuf_tensor("epsc", [1, 1], F32))
        obuf = E(nc.sbuf_tensor("obuf", [128, 8 * BT], BF16))
        orr = E(nc.sbuf_tensor("orr", [128, 8 * BT], F32R))
        xtm = E(nc.sbuf_tensor("xtm", [128, 2 * D], F32))
        xT = E(nc.sbuf_tensor("xT", [128, 8 * BT], F32))
        xp = E(nc.sbuf_tensor("xp", [128, 8 * BT], F32))
        sqb = E(nc.sbuf_tensor("sqb", [128, 2 * BT], F32))
        wsh = E(nc.sbuf_tensor("wsh", [128, 2 * 1024], F32R))
        wsl = E(nc.sbuf_tensor("wsl", [128, 2 * 1024], F32R))
        rows = E(nc.sbuf_tensor("rows", [1, 16 * BT], F32))
        mub = E(nc.sbuf_tensor("mub", [128, BT], F32))
        rsb = E(nc.sbuf_tensor("rsb", [128, BT], F32))
        zb = E(nc.sbuf_tensor("zb", [128, BT], F32))
        macc = E(nc.sbuf_tensor("macc", [128, 1], F32))
        mcol = E(nc.sbuf_tensor("mcol", [128, 1], F32))
        m1s = E(nc.sbuf_tensor("m1s", [1, 1], F32))
        psA = E(nc.psum_tensor("psA", [128, 512], F32))
        psB = E(nc.psum_tensor("psB", [128, 512], F32))
        psT = E(nc.psum_tensor("psT", [128, 128], F32))
        psT2 = E(nc.psum_tensor("psT2", [128, 128], F32))
        psS = E(nc.psum_tensor("psS", [1, 512], F32))
        psS2 = E(nc.psum_tensor("psS2", [1, 512], F32))
        psX = E(nc.psum_tensor("psX", [128, 512], F32))
        block = E(nc.Block())

        prog = Prog(nc, ctx)

        def INC(name, n=1):
            s = prog.sem(name)
            s[1] += n
            return s[0], s[1]

        ph0 = prog.phase()

        def p0(g):
            h, _ = INC("g0", 5)
            g.memset(identf[:], 0.0).then_inc(h, 1)
            g.wait_ge(h, 1)
            g.affine_select(out=identf[:], in_=identf[:], compare_op=ALU.not_equal,
                            fill=1.0, base=0, pattern=[[-1, 128]],
                            channel_multiplier=1).then_inc(h, 1)
            g.memset(onescol[:], 1.0).then_inc(h, 1)
            g.memset(onesr[:], 1.0).then_inc(h, 1)
            g.memset(epsc[:], EPS).then_inc(h, 1)
            g.wait_ge(h, 5)
            h2, _ = INC("g1", 1)
            g.memset(macc[:], 0.0).then_inc(h2, 1)
            g.wait_ge(h2, 1)
        ph0["gpsimd"].append(p0)

        ph1 = prog.phase()
        for i in range(64):
            of, k = divmod(i, 8)
            buf = (i % 2) * 128
            ldn = "a_ld%d" % (i % 2)
            h_ld, v_ld = INC(ldn, 16)

            def load(eng, i=i, of=of, k=k, buf=buf, h=h_ld):
                if i >= 2:
                    eng.wait_ge(prog.sems["a_tp"][0], i - 1)
                eng.dma_start(out=wtmp[:, buf:buf + 128],
                              in_=out_w[of * 128:of * 128 + 128,
                                        k * 128:k * 128 + 128]).then_inc(h, 16)
            ph1["sync"].append(load)

            h_tp, v_tp = INC("a_tp", 1)

            def tpose(eng, i=i, buf=buf, v_ld=v_ld, h=h_tp, ldn=ldn):
                eng.wait_ge(prog.sems[ldn][0], v_ld)
                if i >= 2:
                    eng.wait_ge(prog.sems["a_vp"][0], i - 1)
                p = psT if i % 2 == 0 else psT2
                eng.transpose(p[:], wtmp[:, buf:buf + 128],
                              identf[:]).then_inc(h, 1)
            ph1["tensor"].append(tpose)

            gpn = "a_gp%d" % (i % 2)
            v_gp_need = prog.sem(gpn)[1]
            h_vp, v_vp = INC("a_vp", 1)

            def split(eng, i=i, v_tp=v_tp, buf=buf, h=h_vp, gpn=gpn,
                      v_gp_need=v_gp_need):
                eng.wait_ge(prog.sems["a_tp"][0], v_tp)
                if v_gp_need > 0:
                    eng.wait_ge(prog.sems[gpn][0], v_gp_need)
                p = psT if i % 2 == 0 else psT2
                eng.tensor_copy(wth[:, buf:buf + 128], p[:])
                eng.tensor_tensor(wtl[:, buf:buf + 128], p[:],
                                  wth[:, buf:buf + 128],
                                  ALU.subtract).then_inc(h, 1)
            ph1["vector"].append(split)

            h_gp, v_gp = INC(gpn, 32)
            col = of * 1024 + k * 128

            def store(eng, v_vp=v_vp, buf=buf, col=col, h=h_gp):
                eng.wait_ge(prog.sems["a_vp"][0], v_vp)
                eng.dma_start(out=woT_h[:, col:col + 128],
                              in_=wth[:, buf:buf + 128]).then_inc(h, 16)
                eng.dma_start(out=woT_l[:, col:col + 128],
                              in_=wtl[:, buf:buf + 128]).then_inc(h, 16)
            ph1["gpsimd"].append(store)

        _snap = {n: prog.sems[n][1] for n in ("a_ld0", "a_ld1", "a_gp0", "a_gp1")
                 if n in prog.sems}

        def p1fin_s(eng, _snap=_snap):
            for n in ("a_ld0", "a_ld1"):
                if n in _snap:
                    eng.wait_ge(prog.sems[n][0], _snap[n])
        ph1["sync"].append(p1fin_s)

        def p1fin_g(eng, _snap=_snap):
            for n in ("a_gp0", "a_gp1"):
                if n in _snap:
                    eng.wait_ge(prog.sems[n][0], _snap[n])
        ph1["gpsimd"].append(p1fin_g)

        def emit_block(b, blk):
            t0 = blk * BT
            ph = prog.phase()
            for c in range(8):
                h_ld, v_ld = INC("o_ld", 16)

                def ldo(eng, b=b, c=c, t0=t0, h=h_ld):
                    eng.dma_start(out=obuf[:, c * BT:c * BT + BT],
                                  in_=onT[b, c * 128:c * 128 + 128,
                                          t0:t0 + BT]).then_inc(h, 16)
                ph["sync"].append(ldo)
            v_old = prog.sems["o_ld"][1]

            h_or, v_or = INC("o_or", 1)

            def ocvt(eng, v_old=v_old, h=h_or):
                eng.wait_ge(prog.sems["o_ld"][0], v_old)
                for c in range(8):
                    eng.tensor_copy(orr[:, c * BT:c * BT + BT],
                                    obuf[:, c * BT:c * BT + BT])
                eng.drain().then_inc(h, 1)
            ph["vector"].append(ocvt)

            for i4 in range(4):
                xldn = "x_ld%d" % (i4 % 2)
                h_xl, v_xl = INC(xldn, 16)
                v_tp_need = max(prog.sem("x_tp")[1] - 8, 0)

                def ldx(eng, b=b, t0=t0, i4=i4, h=h_xl, v_tp_need=v_tp_need):
                    if v_tp_need > 0:
                        eng.wait_ge(prog.sems["x_tp"][0], v_tp_need)
                    eng.dma_start(out=xtm[:, (i4 % 2) * D:(i4 % 2) * D + D],
                                  in_=x[b, t0 + i4 * 128:t0 + i4 * 128 + 128, :]
                                  ).then_inc(h, 16)
                ph["sync"].append(ldx)

                for c in range(8):
                    h_tp, v_tp = INC("x_tp", 1)

                    def xtp(eng, i4=i4, c=c, v_xl=v_xl, h=h_tp, xldn=xldn,
                            ntp=prog.sems["x_tp"][1]):
                        eng.wait_ge(prog.sems[xldn][0], v_xl)
                        if ntp >= 3:
                            eng.wait_ge(prog.sems["x_ev"][0], ntp - 2)
                        p = psT if ntp % 2 == 1 else psT2
                        eng.transpose(p[:], xtm[:, (i4 % 2) * D + c * 128:
                                                (i4 % 2) * D + c * 128 + 128],
                                      identf[:]).then_inc(h, 1)
                    prog.sem("x_ev")
                    ph["tensor"].append(xtp)

                    h_ev, v_ev = INC("x_ev", 1)
                    dst = c * BT + i4 * 128

                    def xev(eng, v_tp=v_tp, dst=dst, h=h_ev,
                            ntp=prog.sems["x_tp"][1]):
                        eng.wait_ge(prog.sems["x_tp"][0], v_tp)
                        p = psT if ntp % 2 == 1 else psT2
                        eng.tensor_copy(xT[:, dst:dst + 128], p[:]).then_inc(h, 1)
                    ph["vector"].append(xev)

            for of in range(8):
                buf = (of % 2) * 1024
                wsn = "w_ws%d" % (of % 2)
                v_need = prog.sem("w_mm")[1] - 1 if of >= 2 else 0
                h_ws, v_ws = INC(wsn, 32)

                def wload(eng, of=of, buf=buf, h=h_ws, v_need=v_need):
                    if v_need > 0:
                        eng.wait_ge(prog.sems["w_mm"][0], v_need)
                    eng.dma_start(out=wsh[:, buf:buf + 1024],
                                  in_=woT_h[:, of * 1024:of * 1024 + 1024]
                                  ).then_inc(h, 16)
                    eng.dma_start(out=wsl[:, buf:buf + 1024],
                                  in_=woT_l[:, of * 1024:of * 1024 + 1024]
                                  ).then_inc(h, 16)
                ph["sync"].append(wload)

                bank = psA if of % 2 == 0 else psB
                h_mm, v_mm = INC("w_mm", 1)
                v_or_c = prog.sems["o_or"][1]

                def wmm(eng, of=of, buf=buf, bank=bank, v_ws=v_ws, h=h_mm,
                        wsn=wsn, v_or_c=v_or_c, nmm=prog.sems["w_mm"][1]):
                    eng.wait_ge(prog.sems[wsn][0], v_ws)
                    eng.wait_ge(prog.sems["o_or"][0], v_or_c)
                    if nmm >= 3:
                        eng.wait_ge(prog.sems["w_ev"][0], nmm - 2)
                    for k in range(8):
                        wcol = buf + k * 128
                        eng.matmul(bank[:, 0:BT], wsh[:, wcol:wcol + 128],
                                   orr[:, k * BT:k * BT + BT],
                                   start=(k == 0), stop=False)
                        mm = eng.matmul(bank[:, 0:BT], wsl[:, wcol:wcol + 128],
                                        orr[:, k * BT:k * BT + BT],
                                        start=False, stop=(k == 7))
                        if k == 7:
                            mm.then_inc(h, 1)
                prog.sem("w_ev")
                ph["tensor"].append(wmm)

                v_xev_all = prog.sems["x_ev"][1]
                h_ev, v_ev = INC("w_ev", 1)

                def wev(eng, of=of, bank=bank, v_mm=v_mm, v_xev_all=v_xev_all,
                        h=h_ev):
                    eng.wait_ge(prog.sems["w_mm"][0], v_mm)
                    eng.wait_ge(prog.sems["x_ev"][0], v_xev_all)
                    eng.tensor_tensor(xp[:, of * BT:of * BT + BT], bank[:, 0:BT],
                                      xT[:, of * BT:of * BT + BT],
                                      ALU.add).then_inc(h, 1)
                ph["vector"].append(wev)

                v_sm_need = max(prog.sem("o_sm")[1] - 1, 0)
                h_sq, v_sq = INC("o_sq", 1)

                def dosq(eng, of=of, v_ev=v_ev, h=h_sq, v_sm_need=v_sm_need):
                    eng.wait_ge(prog.sems["w_ev"][0], v_ev)
                    if of >= 2 and v_sm_need > 0:
                        eng.wait_ge(prog.sems["o_sm"][0], v_sm_need)
                    eng.activation(sqb[:, (of % 2) * BT:(of % 2) * BT + BT],
                                   xp[:, of * BT:of * BT + BT],
                                   AF.Square).then_inc(h, 1)
                ph["scalar"].append(dosq)

                h_sm, v_sm = INC("o_sm", 1)

                def statmm(eng, of=of, v_sq=v_sq, h=h_sm):
                    eng.wait_ge(prog.sems["o_sq"][0], v_sq)
                    eng.matmul(psS[0:1, 0:BT], onescol[:],
                               xp[:, of * BT:of * BT + BT],
                               start=(of == 0), stop=(of == 7))
                    eng.matmul(psS2[0:1, 0:BT], onescol[:],
                               sqb[:, (of % 2) * BT:(of % 2) * BT + BT],
                               start=(of == 0), stop=(of == 7)).then_inc(h, 1)
                ph["tensor"].append(statmm)

                stn = "o_st%d" % (of % 2)
                h_st, v_st = INC(stn, 16)

                def stx(eng, b=b, of=of, t0=t0, v_ev=v_ev, h=h_st):
                    eng.wait_ge(prog.sems["w_ev"][0], v_ev)
                    eng.dma_start(out=xpT[b, of * 128:of * 128 + 128, t0:t0 + BT],
                                  in_=xp[:, of * BT:of * BT + BT]).then_inc(h, 16)
                ph["gpsimd"].append(stx)

            v_sm_all = prog.sems["o_sm"][1]
            h_r1, v_r1 = INC("o_r1", 1)

            def rowstats(eng, v_sm_all=v_sm_all, h=h_r1):
                eng.wait_ge(prog.sems["o_sm"][0], v_sm_all)
                eng.tensor_scalar(rows[:, 0:BT], psS[0:1, 0:BT], 1.0 / D, None,
                                  ALU.mult)
                eng.tensor_scalar(rows[:, BT:2 * BT], psS2[0:1, 0:BT], 1.0 / D,
                                  None, ALU.mult)
                eng.drain()
                eng.tensor_tensor(rows[:, 2 * BT:3 * BT], rows[:, 0:BT],
                                  rows[:, 0:BT], ALU.mult)
                eng.drain()
                eng.tensor_tensor(rows[:, 3 * BT:4 * BT], rows[:, BT:2 * BT],
                                  rows[:, 2 * BT:3 * BT], ALU.subtract)
                eng.drain().then_inc(h, 1)
            ph["vector"].append(rowstats)

            h_r2, v_r2 = INC("o_r2", 1)

            def rowsqrt(eng, v_r1=v_r1, h=h_r2):
                eng.wait_ge(prog.sems["o_r1"][0], v_r1)
                eng.activation(rows[:, 4 * BT:5 * BT], rows[:, 3 * BT:4 * BT],
                               AF.Sqrt,
                               bias=epsc[:].broadcast_to((1, 1))).then_inc(h, 1)
            ph["scalar"].append(rowsqrt)

            h_r3, v_r3 = INC("o_r1", 1)

            def rowrecip(eng, v_r2=v_r2, h=h_r3):
                eng.wait_ge(prog.sems["o_r2"][0], v_r2)
                eng.reciprocal(rows[:, 5 * BT:6 * BT], rows[:, 4 * BT:5 * BT])
                eng.drain()
                eng.tensor_tensor(rows[:, 6 * BT:7 * BT], rows[:, 4 * BT:5 * BT],
                                  rows[:, 5 * BT:6 * BT], ALU.mult)
                eng.drain()
                eng.tensor_scalar(rows[:, 6 * BT:7 * BT], rows[:, 6 * BT:7 * BT],
                                  -1.0, 2.0, ALU.mult, ALU.add)
                eng.drain()
                eng.tensor_tensor(rows[:, 5 * BT:6 * BT], rows[:, 5 * BT:6 * BT],
                                  rows[:, 6 * BT:7 * BT], ALU.mult)
                eng.drain().then_inc(h, 1)
            ph["vector"].append(rowrecip)

            h_bc, v_bc = INC("o_bc", 1)

            def statbc(eng, v_r3=v_r3, h=h_bc):
                eng.wait_ge(prog.sems["o_r1"][0], v_r3)
                eng.matmul(psX[:, 0:BT], onesr[:], rows[:, 0:BT],
                           start=True, stop=True).then_inc(h, 1)
            ph["tensor"].append(statbc)

            h_bv, v_bv = INC("o_r1", 1)

            def statbv(eng, v_bc=v_bc, h=h_bv):
                eng.wait_ge(prog.sems["o_bc"][0], v_bc)
                eng.tensor_copy(mub[:], psX[:, 0:BT])
                eng.drain().then_inc(h, 1)
            ph["vector"].append(statbv)

            h_bc2, v_bc2 = INC("o_bc", 1)

            def statbc2(eng, v_bv=v_bv, h=h_bc2):
                eng.wait_ge(prog.sems["o_r1"][0], v_bv)
                eng.matmul(psX[:, 0:BT], onesr[:], rows[:, 5 * BT:6 * BT],
                           start=True, stop=True).then_inc(h, 1)
            ph["tensor"].append(statbc2)

            h_bv2, v_bv2 = INC("o_r1", 1)

            def statbv2(eng, v_bc2=v_bc2, h=h_bv2):
                eng.wait_ge(prog.sems["o_bc"][0], v_bc2)
                eng.tensor_copy(rsb[:], psX[:, 0:BT])
                eng.drain().then_inc(h, 1)
            ph["vector"].append(statbv2)

            h_mx, v_mx = INC("o_r1", 1)

            def lnmax(eng, v_bv2=v_bv2, h=h_mx):
                eng.wait_ge(prog.sems["o_r1"][0], v_bv2)
                for c in range(8):
                    eng.tensor_tensor(zb[:], xp[:, c * BT:c * BT + BT], mub[:],
                                      ALU.subtract)
                    eng.drain()
                    eng.tensor_tensor(zb[:], zb[:], rsb[:], ALU.mult)
                    eng.drain()
                    eng.tensor_reduce(mcol[:], zb[:], AX.X, ALU.max,
                                      apply_absolute_value=True)
                    eng.drain()
                    eng.tensor_tensor(macc[:], macc[:], mcol[:], ALU.max)
                    eng.drain()
                eng.nop().then_inc(h, 1)
            ph["vector"].append(lnmax)

            _snap = {n: prog.sems[n][1] for n in
                     ("o_ld", "x_ld0", "x_ld1", "w_ws0", "w_ws1") if n in prog.sems}

            def sfin(eng, _snap=_snap):
                for n, v in _snap.items():
                    eng.wait_ge(prog.sems[n][0], v)
            ph["sync"].append(sfin)

            _snap2 = {n: prog.sems[n][1] for n in ("o_st0", "o_st1")
                      if n in prog.sems}

            def gfin(eng, _snap2=_snap2):
                for n, v in _snap2.items():
                    eng.wait_ge(prog.sems[n][0], v)
            ph["gpsimd"].append(gfin)

        for b in range(B):
            for blk in range(NBLK):
                emit_block(b, blk)

        phf = prog.phase()
        h_ftp, v_ftp = INC("f_tp", 1)
        h_fvp, v_fvp = INC("f_vp", 1)
        h_fst, v_fst = INC("f_st", 16)

        def fin_t(eng, h=h_ftp):
            eng.transpose(psT[0:1, 0:128], macc[:], identf[:]).then_inc(h, 1)
        phf["tensor"].append(fin_t)

        def fin_v(eng, h=h_fvp, v_ftp=v_ftp):
            eng.wait_ge(prog.sems["f_tp"][0], v_ftp)
            eng.tensor_reduce(m1s[:], psT[0:1, 0:128], AX.X, ALU.max)
            eng.drain().then_inc(h, 1)
        phf["vector"].append(fin_v)

        def fin_g(eng, h=h_fst, v_fvp=v_fvp, v_fst=v_fst):
            eng.wait_ge(prog.sems["f_vp"][0], v_fvp)
            eng.dma_start(out=m1[:], in_=m1s[:]).then_inc(h, 16)
            eng.wait_ge(h, v_fst)
        phf["gpsimd"].append(fin_g)

        prog.emit(block)
    return nc



_B1_SCRATCH_KIND = "Internal"
_B1_DEBUG_YQ = False


def build_b1():
    nc = bass.Bass("TRN2", target_bir_lowering=False,
                   detect_race_conditions=False)
    xpT = nc.dram_tensor("xpT", [B, D, S], F32, kind="ExternalInput").ap()
    m1a = nc.dram_tensor("m1a", [1, 8], F32, kind="ExternalInput").ap()
    w_up = nc.dram_tensor("w_up", [DF, D], F32, kind="ExternalInput").ap()
    w_gate = nc.dram_tensor("w_gate", [DF, D], F32, kind="ExternalInput").ap()
    up_raw = nc.dram_tensor("up_raw", [DF, TOK], F32, kind="ExternalOutput").ap()
    gt_raw = nc.dram_tensor("gt_raw", [DF, TOK], F32, kind="ExternalOutput").ap()
    m2 = nc.dram_tensor("m2", [1, 1], F32, kind="ExternalOutput").ap()
    m3 = nc.dram_tensor("m3", [1, 1], F32, kind="ExternalOutput").ap()
    if _B1_DEBUG_YQ:
        yqd = nc.dram_tensor("yqd", [128, 8 * 1024], F32,
                             kind="ExternalOutput").ap()
    upT_h = nc.dram_tensor("upT_h", [128, 24 * 1024], F32R, kind=_B1_SCRATCH_KIND).ap()
    upT_l = nc.dram_tensor("upT_l", [128, 24 * 1024], F32R, kind=_B1_SCRATCH_KIND).ap()
    gtT_h = nc.dram_tensor("gtT_h", [128, 24 * 1024], F32R, kind="Internal").ap()
    gtT_l = nc.dram_tensor("gtT_l", [128, 24 * 1024], F32R, kind="Internal").ap()

    BT = 512
    TB = 1024
    NSL = TB // BT

    with contextlib.ExitStack() as ctx:
        E = ctx.enter_context
        identf = E(nc.sbuf_tensor("identf", [128, 128], F32))
        onescol = E(nc.sbuf_tensor("onescol", [128, 1], F32))
        onesr = E(nc.sbuf_tensor("onesr", [1, 128], F32))
        epsc = E(nc.sbuf_tensor("epsc", [1, 1], F32))
        sc = E(nc.sbuf_tensor("sc", [1, 16], F32))
        s1col = E(nc.sbuf_tensor("s1col", [128, 1], F32))
        wbuf = E(nc.sbuf_tensor("wbuf", [128, 2 * 1024], F32))
        wg = E(nc.sbuf_tensor("wg", [128, 2 * 32], F32))
        wq = E(nc.sbuf_tensor("wq", [128, 1024], I16))
        wdq = E(nc.sbuf_tensor("wdq", [128, 2 * 1024], F32))
        wth = E(nc.sbuf_tensor("wth", [128, 256], F32R))
        wtl = E(nc.sbuf_tensor("wtl", [128, 256], F32R))
        xpb = E(nc.sbuf_tensor("xpb", [128, 8 * BT], F32))
        sqb = E(nc.sbuf_tensor("sqb", [128, 2 * BT], F32))
        rows = E(nc.sbuf_tensor("rows", [1, 16 * BT], F32))
        mub = E(nc.sbuf_tensor("mub", [128, BT], F32))
        rsb = E(nc.sbuf_tensor("rsb", [128, BT], F32))
        zb = E(nc.sbuf_tensor("zb", [128, BT], F32))
        yqi = E(nc.sbuf_tensor("yqi", [128, BT], I16))
        yqh = E(nc.sbuf_tensor("yqh", [128, 8 * TB], F32R))
        yql = E(nc.sbuf_tensor("yql", [128, 8 * TB], F32R))
        wsh = E(nc.sbuf_tensor("wsh", [128, 2 * 1024], F32R))
        wsl = E(nc.sbuf_tensor("wsl", [128, 2 * 1024], F32R))
        est = E(nc.sbuf_tensor("est", [128, 4 * BT], F32))
        macc = E(nc.sbuf_tensor("macc", [128, 2], F32))
        mcol = E(nc.sbuf_tensor("mcol", [128, 1], F32))
        mrow = E(nc.sbuf_tensor("mrow", [1, 2], F32))
        psA = E(nc.psum_tensor("psA", [128, 512], F32))
        psB = E(nc.psum_tensor("psB", [128, 512], F32))
        psC = E(nc.psum_tensor("psC", [128, 512], F32))
        psD = E(nc.psum_tensor("psD", [128, 512], F32))
        psT = E(nc.psum_tensor("psT", [128, 128], F32))
        psT2 = E(nc.psum_tensor("psT2", [128, 128], F32))
        psS = E(nc.psum_tensor("psS", [1, 512], F32))
        psS2 = E(nc.psum_tensor("psS2", [1, 512], F32))
        block = E(nc.Block())

        prog = Prog(nc, ctx)

        def INC(name, n=1):
            s = prog.sem(name)
            s[1] += n
            return s[0], s[1]

        ph0 = prog.phase()

        def p0(g):
            h, _ = INC("g0", 6)
            g.memset(identf[:], 0.0).then_inc(h, 1)
            g.wait_ge(h, 1)
            g.affine_select(out=identf[:], in_=identf[:], compare_op=ALU.not_equal,
                            fill=1.0, base=0, pattern=[[-1, 128]],
                            channel_multiplier=1).then_inc(h, 1)
            g.memset(onescol[:], 1.0).then_inc(h, 1)
            g.memset(onesr[:], 1.0).then_inc(h, 1)
            g.memset(epsc[:], EPS).then_inc(h, 1)
            g.memset(macc[:], 0.0).then_inc(h, 1)
            g.wait_ge(h, 6)
        ph0["gpsimd"].append(p0)

        def p0s(sync):
            h, v = INC("s_ld", 16)
            sync.dma_start(out=sc[:, 8:16], in_=m1a[:]).then_inc(h, 16)
            sync.wait_ge(h, v)
        ph0["sync"].append(p0s)

        ph0b = prog.phase()
        h_svp1, v_svp1 = INC("s_vp", 1)
        h_stp1, v_stp1 = INC("s_tp", 1)
        h_svp2, v_svp2 = INC("s_vp", 1)

        def p0v(eng, h=h_svp1):
            eng.tensor_reduce(sc[:, 0:1], sc[:, 8:16], AX.X, ALU.max)
            eng.drain()
            eng.tensor_scalar(sc[:, 1:2], sc[:, 0:1], 1.0 / QMAX, 1e-12,
                              ALU.mult, ALU.max)
            eng.drain()
            eng.reciprocal(sc[:, 2:3], sc[:, 1:2])
            eng.drain()
            eng.tensor_tensor(sc[:, 3:4], sc[:, 1:2], sc[:, 2:3], ALU.mult)
            eng.drain()
            eng.tensor_scalar(sc[:, 3:4], sc[:, 3:4], -1.0, 2.0, ALU.mult,
                              ALU.add)
            eng.drain()
            eng.tensor_tensor(sc[:, 2:3], sc[:, 2:3], sc[:, 3:4], ALU.mult)
            eng.drain().then_inc(h, 1)
        ph0b["vector"].append(p0v)

        def p0t(eng, h=h_stp1, v_svp1=v_svp1):
            eng.wait_ge(prog.sems["s_vp"][0], v_svp1)
            eng.matmul(psT[:, 0:1], onesr[:], sc[0:1, 1:2],
                       start=True, stop=True).then_inc(h, 1)
        ph0b["tensor"].append(p0t)

        def p0v2(eng, h=h_svp2, v_stp1=v_stp1):
            eng.wait_ge(prog.sems["s_tp"][0], v_stp1)
            eng.tensor_copy(s1col[:], psT[:, 0:1])
            eng.drain().then_inc(h, 1)
        ph0b["vector"].append(p0v2)

        ph1 = prog.phase()
        wplan = []
        for wi, (wsrc, dsth, dstl) in enumerate(
                ((w_up, upT_h, upT_l), (w_gate, gtT_h, gtT_l))):
            for r in range(24):
                wplan.append((wsrc, dsth, dstl, r))
        for i, (wsrc, dsth, dstl, r) in enumerate(wplan):
            buf = (i % 2) * 1024
            ldn = "b_ld%d" % (i % 2)
            h_ld, v_ld = INC(ldn, 16)

            def load(eng, i=i, wsrc=wsrc, r=r, buf=buf, h=h_ld):
                if i >= 2:
                    eng.wait_ge(prog.sems["b_dq"][0], i - 1)
                eng.dma_start(out=wbuf[:, buf:buf + 1024],
                              in_=wsrc[r * 128:r * 128 + 128, :]).then_inc(h, 16)
            ph1["sync"].append(load)

            h_dq, v_dq = INC("b_dq", 1)

            def dequant(eng, i=i, buf=buf, v_ld=v_ld, h=h_dq, ldn=ldn):
                eng.wait_ge(prog.sems[ldn][0], v_ld)
                if i >= 2:
                    eng.wait_ge(prog.sems["b_tp"][0], (i - 1) * 8)
                gb = (i % 2) * 32
                wb3 = wbuf[:, buf:buf + 1024].rearrange("p (g c) -> p g c", g=32)
                eng.tensor_reduce(
                    wg[:, gb:gb + 32].rearrange("p (g o) -> p g o", o=1),
                    wb3, AX.X, ALU.max, apply_absolute_value=True)
                eng.drain()
                eng.tensor_scalar(wg[:, gb:gb + 32], wg[:, gb:gb + 32],
                                  1.0 / 7.0, 1e-12, ALU.mult, ALU.max)
                eng.drain()
                eng.reciprocal(zb[:, 0:32], wg[:, gb:gb + 32])
                eng.drain()
                eng.tensor_tensor(zb[:, 32:64], wg[:, gb:gb + 32], zb[:, 0:32],
                                  ALU.mult)
                eng.drain()
                eng.tensor_scalar(zb[:, 32:64], zb[:, 32:64], -1.0, 2.0,
                                  ALU.mult, ALU.add)
                eng.drain()
                eng.tensor_tensor(zb[:, 0:32], zb[:, 0:32], zb[:, 32:64],
                                  ALU.mult)
                eng.drain()
                eng.tensor_tensor(
                    wdq[:, buf:buf + 1024].rearrange("p (g c) -> p g c", g=32),
                    wb3,
                    zb[:, 0:32].rearrange("p (g o) -> p g o", o=1).broadcast_to(
                        (128, 32, 32)), ALU.mult)
                eng.drain()
                eng.tensor_copy(wq[:], wdq[:, buf:buf + 1024])
                eng.drain()
                eng.tensor_scalar(wq[:], wq[:], 7.0, -8.0, ALU.min, ALU.max)
                eng.drain()
                eng.tensor_tensor(
                    wdq[:, buf:buf + 1024].rearrange("p (g c) -> p g c", g=32),
                    wq.rearrange("p (g c) -> p g c", g=32),
                    wg[:, gb:gb + 32].rearrange("p (g o) -> p g o",
                                                o=1).broadcast_to((128, 32, 32)),
                    ALU.mult)
                eng.drain()
                eng.tensor_scalar(wdq[:, buf:buf + 1024], wdq[:, buf:buf + 1024],
                                  s1col[:], None, ALU.mult)
                eng.drain().then_inc(h, 1)
            prog.sem("b_tp")
            ph1["vector"].append(dequant)

            for k in range(8):
                h_tp, v_tp = INC("b_tp", 1)

                def tpose(eng, k=k, buf=buf, v_dq=v_dq, h=h_tp,
                          ntp=prog.sems["b_tp"][1]):
                    eng.wait_ge(prog.sems["b_dq"][0], v_dq)
                    if ntp >= 3:
                        eng.wait_ge(prog.sems["b_sp"][0], ntp - 2)
                    p = psT if ntp % 2 == 1 else psT2
                    eng.transpose(p[:], wdq[:, buf + k * 128:buf + k * 128 + 128],
                                  identf[:]).then_inc(h, 1)
                prog.sem("b_sp")
                ph1["tensor"].append(tpose)

                sbuf2 = (prog.sems["b_sp"][1] % 2) * 128
                gpn = "b_gp%d" % (prog.sems["b_sp"][1] % 2)
                v_gp_need = prog.sem(gpn)[1]
                h_sp, v_sp = INC("b_sp", 1)

                def split(eng, v_tp=v_tp, sbuf2=sbuf2, h=h_sp, gpn=gpn,
                          v_gp_need=v_gp_need, ntp=prog.sems["b_tp"][1]):
                    eng.wait_ge(prog.sems["b_tp"][0], v_tp)
                    if v_gp_need > 0:
                        eng.wait_ge(prog.sems[gpn][0], v_gp_need)
                    p = psT if ntp % 2 == 1 else psT2
                    eng.tensor_copy(wth[:, sbuf2:sbuf2 + 128], p[:])
                    eng.tensor_tensor(wtl[:, sbuf2:sbuf2 + 128], p[:],
                                      wth[:, sbuf2:sbuf2 + 128],
                                      ALU.subtract).then_inc(h, 1)
                ph1["vector"].append(split)

                h_gp, v_gp = INC(gpn, 32)
                col = r * 1024 + k * 128

                def store(eng, v_sp=v_sp, sbuf2=sbuf2, col=col, h=h_gp,
                          dsth=dsth, dstl=dstl):
                    eng.wait_ge(prog.sems["b_sp"][0], v_sp)
                    eng.dma_start(out=dsth[:, col:col + 128],
                                  in_=wth[:, sbuf2:sbuf2 + 128]).then_inc(h, 16)
                    eng.dma_start(out=dstl[:, col:col + 128],
                                  in_=wtl[:, sbuf2:sbuf2 + 128]).then_inc(h, 16)
                ph["gpsimd"].append(store) if False else ph1["gpsimd"].append(store)

        _snap = {n: prog.sems[n][1] for n in
                 ("b_ld0", "b_ld1", "b_gp0", "b_gp1") if n in prog.sems}

        def p1fin_s(eng, _snap=_snap):
            for n in ("b_ld0", "b_ld1"):
                if n in _snap:
                    eng.wait_ge(prog.sems[n][0], _snap[n])
        ph1["sync"].append(p1fin_s)

        def p1fin_g(eng, _snap=_snap):
            for n in ("b_gp0", "b_gp1"):
                if n in _snap:
                    eng.wait_ge(prog.sems[n][0], _snap[n])
        ph1["gpsimd"].append(p1fin_g)

        def emit_xq(tb):
            for sl in range(NSL):
                t0 = tb * TB + sl * BT
                b, s0 = divmod(t0, S)
                ph = prog.phase()
                for c in range(8):
                    h_ld, v_ld = INC("q_ld", 16)

                    def ldx(eng, b=b, s0=s0, c=c, h=h_ld):
                        eng.dma_start(out=xpb[:, c * BT:c * BT + BT],
                                      in_=xpT[b, c * 128:c * 128 + 128,
                                              s0:s0 + BT]).then_inc(h, 16)
                    ph["sync"].append(ldx)
                v_ld_all = prog.sems["q_ld"][1]

                for c in range(8):
                    v_sm_need = max(prog.sem("q_sm")[1] - 1, 0)
                    h_sq, v_sq = INC("q_sq", 1)

                    def dosq(eng, c=c, v_ld_all=v_ld_all, h=h_sq,
                             v_sm_need=v_sm_need):
                        eng.wait_ge(prog.sems["q_ld"][0], v_ld_all)
                        if c >= 2 and v_sm_need > 0:
                            eng.wait_ge(prog.sems["q_sm"][0], v_sm_need)
                        eng.activation(sqb[:, (c % 2) * BT:(c % 2) * BT + BT],
                                       xpb[:, c * BT:c * BT + BT],
                                       AF.Square).then_inc(h, 1)
                    ph["scalar"].append(dosq)

                    h_sm, v_sm = INC("q_sm", 1)

                    def statmm(eng, c=c, v_sq=v_sq, v_ld_all=v_ld_all, h=h_sm):
                        eng.wait_ge(prog.sems["q_sq"][0], v_sq)
                        eng.matmul(psS[0:1, 0:BT], onescol[:],
                                   xpb[:, c * BT:c * BT + BT],
                                   start=(c == 0), stop=(c == 7))
                        eng.matmul(psS2[0:1, 0:BT], onescol[:],
                                   sqb[:, (c % 2) * BT:(c % 2) * BT + BT],
                                   start=(c == 0), stop=(c == 7)).then_inc(h, 1)
                    ph["tensor"].append(statmm)

                v_sm_all = prog.sems["q_sm"][1]
                h_r1, v_r1 = INC("q_r1", 1)

                def rowstats(eng, v_sm_all=v_sm_all, h=h_r1):
                    eng.wait_ge(prog.sems["q_sm"][0], v_sm_all)
                    eng.tensor_scalar(rows[:, 0:BT], psS[0:1, 0:BT], 1.0 / D,
                                      None, ALU.mult)
                    eng.tensor_scalar(rows[:, BT:2 * BT], psS2[0:1, 0:BT],
                                      1.0 / D, None, ALU.mult)
                    eng.drain()
                    eng.tensor_tensor(rows[:, 2 * BT:3 * BT], rows[:, 0:BT],
                                      rows[:, 0:BT], ALU.mult)
                    eng.drain()
                    eng.tensor_tensor(rows[:, 3 * BT:4 * BT], rows[:, BT:2 * BT],
                                      rows[:, 2 * BT:3 * BT], ALU.subtract)
                    eng.drain().then_inc(h, 1)
                ph["vector"].append(rowstats)

                h_r2, v_r2 = INC("q_r2", 1)

                def rowsqrt(eng, v_r1=v_r1, h=h_r2):
                    eng.wait_ge(prog.sems["q_r1"][0], v_r1)
                    eng.activation(rows[:, 4 * BT:5 * BT], rows[:, 3 * BT:4 * BT],
                                   AF.Sqrt, bias=epsc[:]).then_inc(h, 1)
                ph["scalar"].append(rowsqrt)

                h_r3, v_r3 = INC("q_r1", 1)

                def rowrecip(eng, v_r2=v_r2, h=h_r3):
                    eng.wait_ge(prog.sems["q_r2"][0], v_r2)
                    eng.reciprocal(rows[:, 5 * BT:6 * BT], rows[:, 4 * BT:5 * BT])
                    eng.drain()
                    eng.tensor_tensor(rows[:, 6 * BT:7 * BT],
                                      rows[:, 4 * BT:5 * BT],
                                      rows[:, 5 * BT:6 * BT], ALU.mult)
                    eng.drain()
                    eng.tensor_scalar(rows[:, 6 * BT:7 * BT],
                                      rows[:, 6 * BT:7 * BT], -1.0, 2.0,
                                      ALU.mult, ALU.add)
                    eng.drain()
                    eng.tensor_tensor(rows[:, 5 * BT:6 * BT],
                                      rows[:, 5 * BT:6 * BT],
                                      rows[:, 6 * BT:7 * BT], ALU.mult)
                    eng.drain()
                    eng.tensor_tensor(rows[:, 5 * BT:6 * BT],
                                      rows[:, 5 * BT:6 * BT],
                                      sc[0:1, 2:3].broadcast_to((1, BT)),
                                      ALU.mult)
                    eng.drain()
                    eng.tensor_tensor(rows[:, 7 * BT:8 * BT], rows[:, 0:BT],
                                      rows[:, 5 * BT:6 * BT], ALU.mult)
                    eng.drain()
                    eng.tensor_scalar(rows[:, 7 * BT:8 * BT],
                                      rows[:, 7 * BT:8 * BT], -1.0, None,
                                      ALU.mult)
                    eng.drain().then_inc(h, 1)
                ph["vector"].append(rowrecip)

                h_bc, v_bc = INC("q_bc", 1)

                def statbc(eng, v_r3=v_r3, h=h_bc):
                    eng.wait_ge(prog.sems["q_r1"][0], v_r3)
                    eng.matmul(psC[:, 0:BT], onesr[:], rows[:, 5 * BT:6 * BT],
                               start=True, stop=True)
                    eng.matmul(psD[:, 0:BT], onesr[:], rows[:, 7 * BT:8 * BT],
                               start=True, stop=True).then_inc(h, 1)
                ph["tensor"].append(statbc)

                h_bv, v_bv = INC("q_r1", 1)

                def statbv(eng, v_bc=v_bc, h=h_bv):
                    eng.wait_ge(prog.sems["q_bc"][0], v_bc)
                    eng.tensor_copy(rsb[:], psC[:, 0:BT])
                    eng.tensor_copy(mub[:], psD[:, 0:BT])
                    eng.drain().then_inc(h, 1)
                ph["vector"].append(statbv)

                h_yq, v_yq = INC("q_r1", 1)

                def doyq(eng, sl=sl, v_bv=v_bv, h=h_yq):
                    eng.wait_ge(prog.sems["q_r1"][0], v_bv)
                    for c in range(8):
                        eng.tensor_tensor(zb[:], xpb[:, c * BT:c * BT + BT],
                                          rsb[:], ALU.mult)
                        eng.drain()
                        eng.tensor_tensor(zb[:], zb[:], mub[:], ALU.add)
                        eng.drain()
                        eng.tensor_copy(yqi[:], zb[:])
                        eng.drain()
                        dst = c * TB + sl * BT
                        eng.tensor_copy(yqh[:, dst:dst + BT], yqi[:])
                        eng.drain()
                        eng.tensor_tensor(yql[:, dst:dst + BT], yqi[:],
                                          yqh[:, dst:dst + BT], ALU.subtract)
                        eng.drain()
                    eng.nop().then_inc(h, 1)
                ph["vector"].append(doyq)

                _snap = {"q_ld": prog.sems["q_ld"][1]}

                def sfin(eng, _snap=_snap):
                    eng.wait_ge(prog.sems["q_ld"][0], _snap["q_ld"])
                ph["sync"].append(sfin)

        def emit_gemm(tb, which):
            srch = upT_h if which == 0 else gtT_h
            srcl = upT_l if which == 0 else gtT_l
            dst = up_raw if which == 0 else gt_raw
            ph = prog.phase()
            for of in range(24):
                buf = (of % 2) * 1024
                wsn = "g_ws%d" % (of % 2)
                v_need = prog.sem("g_mm")[1] - 1 if of >= 2 else 0
                h_ws, v_ws = INC(wsn, 32)

                def wload(eng, of=of, buf=buf, h=h_ws, v_need=v_need,
                          srch=srch, srcl=srcl):
                    if v_need > 0:
                        eng.wait_ge(prog.sems["g_mm"][0], v_need)
                    eng.dma_start(out=wsh[:, buf:buf + 1024],
                                  in_=srch[:, of * 1024:of * 1024 + 1024]
                                  ).then_inc(h, 16)
                    eng.dma_start(out=wsl[:, buf:buf + 1024],
                                  in_=srcl[:, of * 1024:of * 1024 + 1024]
                                  ).then_inc(h, 16)
                ph["sync"].append(wload)

                banks = (psA, psB) if of % 2 == 0 else (psC, psD)
                h_mm, v_mm = INC("g_mm", 1)

                def gmm(eng, of=of, buf=buf, banks=banks, v_ws=v_ws, h=h_mm,
                        wsn=wsn, nmm=prog.sems["g_mm"][1]):
                    eng.wait_ge(prog.sems[wsn][0], v_ws)
                    if nmm >= 3:
                        eng.wait_ge(prog.sems["g_ev"][0], (nmm - 2) * 2)
                    for k in range(8):
                        wcol = buf + k * 128
                        for sl in range(NSL):
                            bank = banks[sl]
                            eng.matmul(bank[:, 0:BT], wsh[:, wcol:wcol + 128],
                                       yqh[:, k * TB + sl * BT:
                                           k * TB + sl * BT + BT],
                                       start=(k == 0), stop=False)
                            eng.matmul(bank[:, 0:BT], wsh[:, wcol:wcol + 128],
                                       yql[:, k * TB + sl * BT:
                                           k * TB + sl * BT + BT],
                                       start=False, stop=False)
                            mm = eng.matmul(bank[:, 0:BT],
                                            wsl[:, wcol:wcol + 128],
                                            yqh[:, k * TB + sl * BT:
                                                k * TB + sl * BT + BT],
                                            start=False, stop=(k == 7))
                            if k == 7 and sl == NSL - 1:
                                mm.then_inc(h, 1)
                prog.sem("g_ev")
                ph["tensor"].append(gmm)

                for sl in range(NSL):
                    h_ev, v_ev = INC("g_ev", 1)
                    stn = "g_st%d" % (prog.sems["g_ev"][1] % 4)
                    v_st_need = prog.sem(stn)[1]

                    def gev(eng, of=of, sl=sl, banks=banks, v_mm=v_mm, h=h_ev,
                            stn=stn, v_st_need=v_st_need,
                            nev=prog.sems["g_ev"][1], which=which):
                        eng.wait_ge(prog.sems["g_mm"][0], v_mm)
                        if v_st_need > 0:
                            eng.wait_ge(prog.sems[stn][0], v_st_need)
                        bank = banks[sl]
                        eb = (nev % 4) * BT
                        eng.tensor_copy(est[:, eb:eb + BT], bank[:, 0:BT])
                        eng.drain()
                        eng.tensor_reduce(mcol[:], est[:, eb:eb + BT], AX.X,
                                          ALU.max, apply_absolute_value=True)
                        eng.drain()
                        eng.tensor_tensor(macc[:, which:which + 1],
                                          macc[:, which:which + 1], mcol[:],
                                          ALU.max)
                        eng.drain().then_inc(h, 1)
                    ph["vector"].append(gev)

                    h_st, v_st = INC(stn, 16)

                    def gst(eng, tb=tb, of=of, sl=sl, v_ev=v_ev, h=h_st,
                            dst=dst, nev=prog.sems["g_ev"][1]):
                        eng.wait_ge(prog.sems["g_ev"][0], v_ev)
                        eb = (nev % 4) * BT
                        eng.dma_start(out=dst[of * 128:of * 128 + 128,
                                              tb * TB + sl * BT:
                                              tb * TB + sl * BT + BT],
                                      in_=est[:, eb:eb + BT]).then_inc(h, 16)
                    ph["gpsimd"].append(gst)

            _snap = {n: prog.sems[n][1] for n in
                     ("g_ws0", "g_ws1") if n in prog.sems}

            def sfin(eng, _snap=_snap):
                for n, v in _snap.items():
                    eng.wait_ge(prog.sems[n][0], v)
            ph["sync"].append(sfin)

            _snap2 = {n: prog.sems[n][1] for n in
                      ("g_st0", "g_st1", "g_st2", "g_st3") if n in prog.sems}

            def gfin(eng, _snap2=_snap2):
                for n, v in _snap2.items():
                    eng.wait_ge(prog.sems[n][0], v)
            ph["gpsimd"].append(gfin)

        for tb in range(TOK // TB):
            emit_xq(tb)
            emit_gemm(tb, 0)
            emit_gemm(tb, 1)

        if _B1_DEBUG_YQ:
            phd = prog.phase()
            h_dv, v_dv = INC("d_vp", 1)
            h_dg, v_dg = INC("d_gp", 8 * 16)

            def dump_v(eng, h=h_dv):
                for c in range(8):
                    eng.tensor_tensor(xpb[:, 0:1024] if False else
                                      yqh[:, c * 1024:c * 1024 + 1024],
                                      yqh[:, c * 1024:c * 1024 + 1024],
                                      yql[:, c * 1024:c * 1024 + 1024],
                                      ALU.add)
                eng.drain().then_inc(h, 1)
            phd["vector"].append(dump_v)

            def dump_g(eng, h=h_dg, v_dv=v_dv, v_dg=v_dg):
                eng.wait_ge(prog.sems["d_vp"][0], v_dv)
                for c in range(8):
                    eng.dma_start(out=yqd[:, c * 1024:c * 1024 + 1024],
                                  in_=yqh[:, c * 1024:c * 1024 + 1024]
                                  ).then_inc(h, 16)
                eng.wait_ge(h, v_dg)
            phd["gpsimd"].append(dump_g)

        phf = prog.phase()
        h_ftp, v_ftp = INC("f_tp", 1)
        h_fvp, v_fvp = INC("f_vp", 1)
        h_fst, v_fst = INC("f_st", 32)

        def fin_t(eng, h=h_ftp):
            eng.transpose(psT[0:1, 0:128], macc[:, 0:1], identf[:])
            eng.transpose(psT2[0:1, 0:128], macc[:, 1:2],
                          identf[:]).then_inc(h, 1)
        phf["tensor"].append(fin_t)

        def fin_v(eng, h=h_fvp, v_ftp=v_ftp):
            eng.wait_ge(prog.sems["f_tp"][0], v_ftp)
            eng.tensor_reduce(mrow[:, 0:1], psT[0:1, 0:128], AX.X, ALU.max)
            eng.tensor_reduce(mrow[:, 1:2], psT2[0:1, 0:128], AX.X, ALU.max)
            eng.drain().then_inc(h, 1)
        phf["vector"].append(fin_v)

        def fin_g(eng, h=h_fst, v_fvp=v_fvp, v_fst=v_fst):
            eng.wait_ge(prog.sems["f_vp"][0], v_fvp)
            eng.dma_start(out=m2[:], in_=mrow[:, 0:1]).then_inc(h, 16)
            eng.dma_start(out=m3[:], in_=mrow[:, 1:2]).then_inc(h, 16)
            eng.wait_ge(h, v_fst)
        phf["gpsimd"].append(fin_g)

        prog.emit(block)
    return nc




def _v_newton_recip(eng, out_ap, in_ap, t_ap):
    eng.reciprocal(out_ap, in_ap)
    eng.drain()
    eng.tensor_tensor(t_ap, in_ap, out_ap, ALU.mult)
    eng.drain()
    eng.tensor_scalar(t_ap, t_ap, -1.0, 2.0, ALU.mult, ALU.add)
    eng.drain()
    eng.tensor_tensor(out_ap, out_ap, t_ap, ALU.mult)
    eng.drain()



def build_b2():
    nc = bass.Bass("TRN2", target_bir_lowering=False,
                   detect_race_conditions=False)
    gt_raw = nc.dram_tensor("gt_raw", [DF, TOK], F32, kind="ExternalInput").ap()
    m3a = nc.dram_tensor("m3a", [1, 8], F32, kind="ExternalInput").ap()
    ab = nc.dram_tensor("ab", [DF, TOK], F32, kind="ExternalOutput").ap()
    m4 = nc.dram_tensor("m4", [1, 1], F32, kind="ExternalOutput").ap()

    W = 2048
    NT = (DF // 128) * (TOK // W)

    with contextlib.ExitStack() as ctx:
        E = ctx.enter_context
        identf = E(nc.sbuf_tensor("identf", [128, 128], F32))
        onesr = E(nc.sbuf_tensor("onesr", [1, 128], F32))
        sc = E(nc.sbuf_tensor("sc", [1, 16], F32))
        r3col = E(nc.sbuf_tensor("r3col", [128, 1], F32))
        s3col = E(nc.sbuf_tensor("s3col", [128, 1], F32))
        gbuf = E(nc.sbuf_tensor("gbuf", [128, 2 * W], F32))
        abuf = E(nc.sbuf_tensor("abuf", [128, 2 * W], F32))
        ai = E(nc.sbuf_tensor("ai", [128, W], I16))
        bu = E(nc.sbuf_tensor("bu", [128, W], U16))
        sg = E(nc.sbuf_tensor("sg", [128, W], F32))
        macc = E(nc.sbuf_tensor("macc", [128, 1], F32))
        mcol = E(nc.sbuf_tensor("mcol", [128, 1], F32))
        m4s = E(nc.sbuf_tensor("m4s", [1, 1], F32))
        psT = E(nc.psum_tensor("psT", [128, 128], F32))
        block = E(nc.Block())
        prog = Prog(nc, ctx)

        def INC(name, n=1):
            s = prog.sem(name)
            s[1] += n
            return s[0], s[1]

        ph0 = prog.phase()
        h_sld, v_sld = INC("s_ld", 16)

        def p0g(g):
            h, _ = INC("g0", 4)
            g.memset(identf[:], 0.0).then_inc(h, 1)
            g.wait_ge(h, 1)
            g.affine_select(out=identf[:], in_=identf[:], compare_op=ALU.not_equal,
                            fill=1.0, base=0, pattern=[[-1, 128]],
                            channel_multiplier=1).then_inc(h, 1)
            g.memset(onesr[:], 1.0).then_inc(h, 1)
            g.memset(macc[:], 0.0).then_inc(h, 1)
            g.wait_ge(h, 4)
        ph0["gpsimd"].append(p0g)

        def p0s(sync, h=h_sld, v=v_sld):
            sync.dma_start(out=sc[:, 8:16], in_=m3a[:]).then_inc(h, 16)
            sync.wait_ge(h, v)
        ph0["sync"].append(p0s)

        ph0b = prog.phase()
        h_sv, v_sv = INC("s_vp", 1)
        h_st2, v_st2 = INC("s_tp", 1)
        h_sv2, v_sv2 = INC("s_vp", 1)

        def p0v(eng, h=h_sv):
            eng.tensor_reduce(sc[:, 0:1], sc[:, 8:16], AX.X, ALU.max)
            eng.drain()
            eng.tensor_scalar(sc[:, 1:2], sc[:, 0:1], 1.0 / QMAX, 1e-12,
                              ALU.mult, ALU.max)
            eng.drain()
            _v_newton_recip(eng, sc[:, 2:3], sc[:, 1:2], sc[:, 3:4])
            eng.nop().then_inc(h, 1)
        ph0b["vector"].append(p0v)

        def p0t(eng, h=h_st2, v_sv=v_sv):
            eng.wait_ge(prog.sems["s_vp"][0], v_sv)
            eng.matmul(psT[:, 0:1], onesr[:], sc[0:1, 2:3], start=True, stop=True)
            eng.matmul(psT[:, 1:2], onesr[:], sc[0:1, 1:2], start=True,
                       stop=True).then_inc(h, 1)
        ph0b["tensor"].append(p0t)

        def p0v2(eng, h=h_sv2, v=v_st2):
            eng.wait_ge(prog.sems["s_tp"][0], v)
            eng.tensor_copy(r3col[:], psT[:, 0:1])
            eng.tensor_copy(s3col[:], psT[:, 1:2])
            eng.drain().then_inc(h, 1)
        ph0b["vector"].append(p0v2)

        ph = prog.phase()
        for i in range(NT):
            r, cb = divmod(i, TOK // W)
            buf = (i % 2) * W
            ldn = "e_ld%d" % (i % 2)
            v_va_need = max(prog.sem("e_va")[1] - 1, 0)
            h_ld, v_ld = INC(ldn, 16)

            def load(eng, r=r, cb=cb, buf=buf, h=h_ld, v_va_need=v_va_need):
                if v_va_need > 0:
                    eng.wait_ge(prog.sems["e_va"][0], v_va_need)
                eng.dma_start(out=gbuf[:, buf:buf + W],
                              in_=gt_raw[r * 128:r * 128 + 128,
                                         cb * W:cb * W + W]).then_inc(h, 16)
            ph["sync"].append(load)

            v_vb_prev = prog.sem("e_vb")[1]
            h_va, v_va = INC("e_va", 1)

            def va(eng, buf=buf, v_ld=v_ld, h=h_va, ldn=ldn,
                   v_vb_prev=v_vb_prev):
                eng.wait_ge(prog.sems[ldn][0], v_ld)
                if v_vb_prev > 0:
                    eng.wait_ge(prog.sems["e_vb"][0], v_vb_prev)
                eng.tensor_scalar(ai[:], gbuf[:, buf:buf + W], r3col[:], None,
                                  ALU.mult)
                eng.drain().then_inc(h, 1)
            ph["vector"].append(va)

            h_sg, v_sg = INC("e_sg", 1)

            def dosig(eng, v_va=v_va, h=h_sg, v_vb_prev=v_vb_prev):
                eng.wait_ge(prog.sems["e_va"][0], v_va)
                if v_vb_prev > 0:
                    eng.wait_ge(prog.sems["e_vb"][0], v_vb_prev)
                eng.activation(sg[:], ai[:], AF.Sigmoid,
                               scale=s3col[:]).then_inc(h, 1)
            ph["scalar"].append(dosig)

            stn = "e_st%d" % (i % 2)
            v_st_need = prog.sem(stn)[1]
            h_vb, v_vb = INC("e_vb", 1)

            def vb(eng, buf=buf, v_sg=v_sg, h=h_vb, stn=stn,
                   v_st_need=v_st_need):
                eng.wait_ge(prog.sems["e_sg"][0], v_sg)
                if v_st_need > 0:
                    eng.wait_ge(prog.sems[stn][0], v_st_need)
                eng.tensor_scalar(bu[:], sg[:], 65536.0, None, ALU.mult)
                eng.drain()
                eng.tensor_tensor(abuf[:, buf:buf + W], ai[:], bu[:], ALU.mult)
                eng.drain()
                eng.tensor_reduce(mcol[:], abuf[:, buf:buf + W], AX.X, ALU.max,
                                  apply_absolute_value=True)
                eng.drain()
                eng.tensor_tensor(macc[:], macc[:], mcol[:], ALU.max)
                eng.drain().then_inc(h, 1)
            ph["vector"].append(vb)

            h_stt, v_stt = INC(stn, 16)

            def store(eng, r=r, cb=cb, buf=buf, v_vb=v_vb, h=h_stt):
                eng.wait_ge(prog.sems["e_vb"][0], v_vb)
                eng.dma_start(out=ab[r * 128:r * 128 + 128, cb * W:cb * W + W],
                              in_=abuf[:, buf:buf + W]).then_inc(h, 16)
            ph["gpsimd"].append(store)

        _snap = {n: prog.sems[n][1] for n in ("e_ld0", "e_ld1") if n in prog.sems}

        def sfin(eng, _snap=_snap):
            for n, v in _snap.items():
                eng.wait_ge(prog.sems[n][0], v)
        ph["sync"].append(sfin)

        _snap2 = {n: prog.sems[n][1] for n in ("e_st0", "e_st1") if n in prog.sems}

        def gfin(eng, _snap2=_snap2):
            for n, v in _snap2.items():
                eng.wait_ge(prog.sems[n][0], v)
        ph["gpsimd"].append(gfin)

        phf = prog.phase()
        h_ftp, v_ftp = INC("f_tp", 1)
        h_fvp, v_fvp = INC("f_vp", 1)
        h_fst, v_fst = INC("f_st", 16)

        def fin_t(eng, h=h_ftp):
            eng.transpose(psT[0:1, 0:128], macc[:], identf[:]).then_inc(h, 1)
        phf["tensor"].append(fin_t)

        def fin_v(eng, h=h_fvp, v_ftp=v_ftp):
            eng.wait_ge(prog.sems["f_tp"][0], v_ftp)
            eng.tensor_reduce(m4s[:], psT[0:1, 0:128], AX.X, ALU.max)
            eng.drain().then_inc(h, 1)
        phf["vector"].append(fin_v)

        def fin_g(eng, h=h_fst, v_fvp=v_fvp, v_fst=v_fst):
            eng.wait_ge(prog.sems["f_vp"][0], v_fvp)
            eng.dma_start(out=m4[:], in_=m4s[:]).then_inc(h, 16)
            eng.wait_ge(h, v_fst)
        phf["gpsimd"].append(fin_g)

        prog.emit(block)
    return nc



def build_b3():
    nc = bass.Bass("TRN2", target_bir_lowering=False,
                   detect_race_conditions=False)
    abt = nc.dram_tensor("ab", [DF, TOK], F32, kind="ExternalInput").ap()
    up_raw = nc.dram_tensor("up_raw", [DF, TOK], F32, kind="ExternalInput").ap()
    m2a = nc.dram_tensor("m2a", [1, 8], F32, kind="ExternalInput").ap()
    m3a = nc.dram_tensor("m3a", [1, 8], F32, kind="ExternalInput").ap()
    m4a = nc.dram_tensor("m4a", [1, 8], F32, kind="ExternalInput").ap()
    cu = nc.dram_tensor("cu", [DF, TOK], F32, kind="ExternalOutput").ap()
    m5 = nc.dram_tensor("m5", [1, 1], F32, kind="ExternalOutput").ap()

    W = 2048
    NT = (DF // 128) * (TOK // W)

    with contextlib.ExitStack() as ctx:
        E = ctx.enter_context
        identf = E(nc.sbuf_tensor("identf", [128, 128], F32))
        onesr = E(nc.sbuf_tensor("onesr", [1, 128], F32))
        sc = E(nc.sbuf_tensor("sc", [1, 40], F32))
        fccol = E(nc.sbuf_tensor("fccol", [128, 1], F32))
        r2col = E(nc.sbuf_tensor("r2col", [128, 1], F32))
        abuf = E(nc.sbuf_tensor("abuf", [128, 2 * W], F32))
        ubuf = E(nc.sbuf_tensor("ubuf", [128, 2 * W], F32))
        cbuf = E(nc.sbuf_tensor("cbuf", [128, 2 * W], F32))
        ci = E(nc.sbuf_tensor("ci", [128, W], I16))
        ui = E(nc.sbuf_tensor("ui", [128, W], I16))
        macc = E(nc.sbuf_tensor("macc", [128, 1], F32))
        mcol = E(nc.sbuf_tensor("mcol", [128, 1], F32))
        m5s = E(nc.sbuf_tensor("m5s", [1, 1], F32))
        psT = E(nc.psum_tensor("psT", [128, 128], F32))
        block = E(nc.Block())
        prog = Prog(nc, ctx)

        def INC(name, n=1):
            s = prog.sem(name)
            s[1] += n
            return s[0], s[1]

        ph0 = prog.phase()
        h_sld, v_sld = INC("s_ld", 48)

        def p0g(g):
            h, _ = INC("g0", 4)
            g.memset(identf[:], 0.0).then_inc(h, 1)
            g.wait_ge(h, 1)
            g.affine_select(out=identf[:], in_=identf[:], compare_op=ALU.not_equal,
                            fill=1.0, base=0, pattern=[[-1, 128]],
                            channel_multiplier=1).then_inc(h, 1)
            g.memset(onesr[:], 1.0).then_inc(h, 1)
            g.memset(macc[:], 0.0).then_inc(h, 1)
            g.wait_ge(h, 4)
        ph0["gpsimd"].append(p0g)

        def p0s(sync, h=h_sld, v=v_sld):
            sync.dma_start(out=sc[:, 8:16], in_=m2a[:]).then_inc(h, 16)
            sync.dma_start(out=sc[:, 16:24], in_=m3a[:]).then_inc(h, 16)
            sync.dma_start(out=sc[:, 24:32], in_=m4a[:]).then_inc(h, 16)
            sync.wait_ge(h, v)
        ph0["sync"].append(p0s)

        ph0b = prog.phase()
        h_sv, v_sv = INC("s_vp", 1)
        h_st2, v_st2 = INC("s_tp", 1)
        h_sv2, v_sv2 = INC("s_vp", 1)

        def p0v(eng, h=h_sv):
            eng.tensor_reduce(sc[:, 0:1], sc[:, 8:16], AX.X, ALU.max)
            eng.tensor_reduce(sc[:, 1:2], sc[:, 16:24], AX.X, ALU.max)
            eng.tensor_reduce(sc[:, 2:3], sc[:, 24:32], AX.X, ALU.max)
            eng.drain()
            eng.tensor_scalar(sc[:, 3:4], sc[:, 0:1], 1.0 / QMAX, 1e-12,
                              ALU.mult, ALU.max)
            eng.tensor_scalar(sc[:, 4:5], sc[:, 1:2], 1.0 / QMAX, 1e-12,
                              ALU.mult, ALU.max)
            eng.drain()
            _v_newton_recip(eng, sc[:, 5:6], sc[:, 3:4], sc[:, 6:7])
            eng.tensor_scalar(sc[:, 7:8], sc[:, 4:5], 1.0 / 65536.0, None,
                              ALU.mult)
            eng.drain()
            eng.tensor_tensor(sc[:, 32:33], sc[:, 7:8], sc[:, 2:3], ALU.mult)
            eng.drain()
            eng.tensor_scalar(sc[:, 33:34], sc[:, 32:33], 1.0 / QMAX, 1e-12,
                              ALU.mult, ALU.max)
            eng.drain()
            _v_newton_recip(eng, sc[:, 34:35], sc[:, 33:34], sc[:, 6:7])
            eng.tensor_tensor(sc[:, 35:36], sc[:, 7:8], sc[:, 34:35],
                              ALU.mult)
            eng.drain().then_inc(h, 1)
        ph0b["vector"].append(p0v)

        def p0t(eng, h=h_st2, v_sv=v_sv):
            eng.wait_ge(prog.sems["s_vp"][0], v_sv)
            eng.matmul(psT[:, 0:1], onesr[:], sc[0:1, 35:36], start=True,
                       stop=True)
            eng.matmul(psT[:, 1:2], onesr[:], sc[0:1, 5:6], start=True,
                       stop=True).then_inc(h, 1)
        ph0b["tensor"].append(p0t)

        def p0v2(eng, h=h_sv2, v=v_st2):
            eng.wait_ge(prog.sems["s_tp"][0], v)
            eng.tensor_copy(fccol[:], psT[:, 0:1])
            eng.tensor_copy(r2col[:], psT[:, 1:2])
            eng.drain().then_inc(h, 1)
        ph0b["vector"].append(p0v2)

        ph = prog.phase()
        for i in range(NT):
            r, cb = divmod(i, TOK // W)
            buf = (i % 2) * W
            ldn = "e_ld%d" % (i % 2)
            v_vb_need = max(prog.sem("e_vb")[1] - 1, 0)
            h_ld, v_ld = INC(ldn, 32)

            def load(eng, r=r, cb=cb, buf=buf, h=h_ld, v_vb_need=v_vb_need):
                if v_vb_need > 0:
                    eng.wait_ge(prog.sems["e_vb"][0], v_vb_need)
                eng.dma_start(out=abuf[:, buf:buf + W],
                              in_=abt[r * 128:r * 128 + 128,
                                      cb * W:cb * W + W]).then_inc(h, 16)
                eng.dma_start(out=ubuf[:, buf:buf + W],
                              in_=up_raw[r * 128:r * 128 + 128,
                                         cb * W:cb * W + W]).then_inc(h, 16)
            ph["sync"].append(load)

            stn = "e_st%d" % (i % 2)
            v_st_need = prog.sem(stn)[1]
            h_vb, v_vb = INC("e_vb", 1)

            def vb(eng, buf=buf, v_ld=v_ld, h=h_vb, ldn=ldn, stn=stn,
                   v_st_need=v_st_need):
                eng.wait_ge(prog.sems[ldn][0], v_ld)
                if v_st_need > 0:
                    eng.wait_ge(prog.sems[stn][0], v_st_need)
                eng.tensor_scalar(ci[:], abuf[:, buf:buf + W], fccol[:], None,
                                  ALU.mult)
                eng.tensor_scalar(ui[:], ubuf[:, buf:buf + W], r2col[:], None,
                                  ALU.mult)
                eng.drain()
                eng.tensor_tensor(cbuf[:, buf:buf + W], ci[:], ui[:], ALU.mult)
                eng.drain()
                eng.tensor_reduce(mcol[:], cbuf[:, buf:buf + W], AX.X, ALU.max,
                                  apply_absolute_value=True)
                eng.drain()
                eng.tensor_tensor(macc[:], macc[:], mcol[:], ALU.max)
                eng.drain().then_inc(h, 1)
            ph["vector"].append(vb)

            h_stt, v_stt = INC(stn, 16)

            def store(eng, r=r, cb=cb, buf=buf, v_vb=v_vb, h=h_stt):
                eng.wait_ge(prog.sems["e_vb"][0], v_vb)
                eng.dma_start(out=cu[r * 128:r * 128 + 128, cb * W:cb * W + W],
                              in_=cbuf[:, buf:buf + W]).then_inc(h, 16)
            ph["gpsimd"].append(store)

        _snap = {n: prog.sems[n][1] for n in ("e_ld0", "e_ld1") if n in prog.sems}

        def sfin(eng, _snap=_snap):
            for n, v in _snap.items():
                eng.wait_ge(prog.sems[n][0], v)
        ph["sync"].append(sfin)

        _snap2 = {n: prog.sems[n][1] for n in ("e_st0", "e_st1") if n in prog.sems}

        def gfin(eng, _snap2=_snap2):
            for n, v in _snap2.items():
                eng.wait_ge(prog.sems[n][0], v)
        ph["gpsimd"].append(gfin)

        phf = prog.phase()
        h_ftp, v_ftp = INC("f_tp", 1)
        h_fvp, v_fvp = INC("f_vp", 1)
        h_fst, v_fst = INC("f_st", 16)

        def fin_t(eng, h=h_ftp):
            eng.transpose(psT[0:1, 0:128], macc[:], identf[:]).then_inc(h, 1)
        phf["tensor"].append(fin_t)

        def fin_v(eng, h=h_fvp, v_ftp=v_ftp):
            eng.wait_ge(prog.sems["f_tp"][0], v_ftp)
            eng.tensor_reduce(m5s[:], psT[0:1, 0:128], AX.X, ALU.max)
            eng.drain().then_inc(h, 1)
        phf["vector"].append(fin_v)

        def fin_g(eng, h=h_fst, v_fvp=v_fvp, v_fst=v_fst):
            eng.wait_ge(prog.sems["f_vp"][0], v_fvp)
            eng.dma_start(out=m5[:], in_=m5s[:]).then_inc(h, 16)
            eng.wait_ge(h, v_fst)
        phf["gpsimd"].append(fin_g)

        prog.emit(block)
    return nc



def build_b4():
    nc = bass.Bass("TRN2", target_bir_lowering=False,
                   detect_race_conditions=False)
    cut = nc.dram_tensor("cu", [DF, TOK], F32, kind="ExternalInput").ap()
    xpT = nc.dram_tensor("xpT", [B, D, S], F32, kind="ExternalInput").ap()
    w_down = nc.dram_tensor("w_down", [D, DF], F32, kind="ExternalInput").ap()
    m2a = nc.dram_tensor("m2a", [1, 8], F32, kind="ExternalInput").ap()
    m3a = nc.dram_tensor("m3a", [1, 8], F32, kind="ExternalInput").ap()
    m4a = nc.dram_tensor("m4a", [1, 8], F32, kind="ExternalInput").ap()
    m5a = nc.dram_tensor("m5a", [1, 8], F32, kind="ExternalInput").ap()
    out = nc.dram_tensor("out", [B, S, D], F32, kind="ExternalOutput").ap()
    dnT_h = nc.dram_tensor("dnT_h", [128, 8 * DF], F32R, kind="Internal").ap()
    dnT_l = nc.dram_tensor("dnT_l", [128, 8 * DF], F32R, kind="Internal").ap()

    BT = 512
    NSL = TOK // BT

    with contextlib.ExitStack() as ctx:
        E = ctx.enter_context
        identf = E(nc.sbuf_tensor("identf", [128, 128], F32))
        onesr = E(nc.sbuf_tensor("onesr", [1, 128], F32))
        sc = E(nc.sbuf_tensor("sc", [1, 48], F32))
        fdcol = E(nc.sbuf_tensor("fdcol", [128, 1], F32))
        s5col = E(nc.sbuf_tensor("s5col", [128, 1], F32))
        wbuf = E(nc.sbuf_tensor("wbuf", [128, 2 * DF], F32))
        wg = E(nc.sbuf_tensor("wg", [128, 96], F32))
        wq = E(nc.sbuf_tensor("wq", [128, DF], I16))
        wdq = E(nc.sbuf_tensor("wdq", [128, DF], F32))
        wz = E(nc.sbuf_tensor("wz", [128, 192], F32))
        wth = E(nc.sbuf_tensor("wth", [128, 256], F32R))
        wtl = E(nc.sbuf_tensor("wtl", [128, 256], F32R))
        cub = E(nc.sbuf_tensor("cub", [128, 2 * BT], F32))
        di = E(nc.sbuf_tensor("di", [128, BT], I16))
        dh = E(nc.sbuf_tensor("dh", [128, 24 * BT], F32R))
        dl = E(nc.sbuf_tensor("dl", [128, 24 * BT], F32R))
        wsh = E(nc.sbuf_tensor("wsh", [128, 2 * DF], F32R))
        wsl = E(nc.sbuf_tensor("wsl", [128, 2 * DF], F32R))
        xpb = E(nc.sbuf_tensor("xpb", [128, 2 * BT], F32))
        fo = E(nc.sbuf_tensor("fo", [128, BT], F32))
        ost = E(nc.sbuf_tensor("ost", [128, 2 * BT], F32))
        psA = E(nc.psum_tensor("psA", [128, 512], F32))
        psB = E(nc.psum_tensor("psB", [128, 512], F32))
        psT = E(nc.psum_tensor("psT", [128, 128], F32))
        psT2 = E(nc.psum_tensor("psT2", [128, 128], F32))
        block = E(nc.Block())
        prog = Prog(nc, ctx)

        def INC(name, n=1):
            s = prog.sem(name)
            s[1] += n
            return s[0], s[1]

        ph0 = prog.phase()
        h_sld, v_sld = INC("s_ld", 64)

        def p0g(g):
            h, _ = INC("g0", 3)
            g.memset(identf[:], 0.0).then_inc(h, 1)
            g.wait_ge(h, 1)
            g.affine_select(out=identf[:], in_=identf[:], compare_op=ALU.not_equal,
                            fill=1.0, base=0, pattern=[[-1, 128]],
                            channel_multiplier=1).then_inc(h, 1)
            g.memset(onesr[:], 1.0).then_inc(h, 1)
            g.wait_ge(h, 3)
        ph0["gpsimd"].append(p0g)

        def p0s(sync, h=h_sld, v=v_sld):
            sync.dma_start(out=sc[:, 8:16], in_=m2a[:]).then_inc(h, 16)
            sync.dma_start(out=sc[:, 16:24], in_=m3a[:]).then_inc(h, 16)
            sync.dma_start(out=sc[:, 24:32], in_=m4a[:]).then_inc(h, 16)
            sync.dma_start(out=sc[:, 32:40], in_=m5a[:]).then_inc(h, 16)
            sync.wait_ge(h, v)
        ph0["sync"].append(p0s)

        ph0b = prog.phase()
        h_sv, v_sv = INC("s_vp", 1)
        h_st2, v_st2 = INC("s_tp", 1)
        h_sv2, v_sv2 = INC("s_vp", 1)

        def p0v(eng, h=h_sv):
            eng.tensor_reduce(sc[:, 0:1], sc[:, 8:16], AX.X, ALU.max)
            eng.tensor_reduce(sc[:, 1:2], sc[:, 16:24], AX.X, ALU.max)
            eng.tensor_reduce(sc[:, 2:3], sc[:, 24:32], AX.X, ALU.max)
            eng.tensor_reduce(sc[:, 3:4], sc[:, 32:40], AX.X, ALU.max)
            eng.drain()
            eng.tensor_scalar(sc[:, 4:5], sc[:, 0:1], 1.0 / QMAX, 1e-12,
                              ALU.mult, ALU.max)
            eng.tensor_scalar(sc[:, 5:6], sc[:, 1:2], 1.0 / QMAX, 1e-12,
                              ALU.mult, ALU.max)
            eng.drain()
            eng.tensor_scalar(sc[:, 6:7], sc[:, 5:6], 1.0 / 65536.0, None,
                              ALU.mult)
            eng.drain()
            eng.tensor_tensor(sc[:, 40:41], sc[:, 6:7], sc[:, 2:3], ALU.mult)
            eng.drain()
            eng.tensor_scalar(sc[:, 41:42], sc[:, 40:41], 1.0 / QMAX, 1e-12,
                              ALU.mult, ALU.max)
            eng.drain()
            eng.tensor_tensor(sc[:, 42:43], sc[:, 41:42], sc[:, 4:5],
                              ALU.mult)
            eng.drain()
            eng.tensor_tensor(sc[:, 43:44], sc[:, 42:43], sc[:, 3:4], ALU.mult)
            eng.drain()
            eng.tensor_scalar(sc[:, 44:45], sc[:, 43:44], 1.0 / QMAX, 1e-12,
                              ALU.mult, ALU.max)
            eng.drain()
            _v_newton_recip(eng, sc[:, 45:46], sc[:, 44:45], sc[:, 7:8])
            eng.tensor_tensor(sc[:, 46:47], sc[:, 42:43], sc[:, 45:46],
                              ALU.mult)
            eng.drain().then_inc(h, 1)
        ph0b["vector"].append(p0v)

        def p0t(eng, h=h_st2, v_sv=v_sv):
            eng.wait_ge(prog.sems["s_vp"][0], v_sv)
            eng.matmul(psT[:, 0:1], onesr[:], sc[0:1, 46:47], start=True,
                       stop=True)
            eng.matmul(psT[:, 1:2], onesr[:], sc[0:1, 44:45], start=True,
                       stop=True).then_inc(h, 1)
        ph0b["tensor"].append(p0t)

        def p0v2(eng, h=h_sv2, v=v_st2):
            eng.wait_ge(prog.sems["s_tp"][0], v)
            eng.tensor_copy(fdcol[:], psT[:, 0:1])
            eng.tensor_copy(s5col[:], psT[:, 1:2])
            eng.drain().then_inc(h, 1)
        ph0b["vector"].append(p0v2)

        ph1 = prog.phase()
        for i in range(8):
            buf = (i % 2) * DF
            ldn = "b_ld%d" % (i % 2)
            h_ld, v_ld = INC(ldn, 16)

            def load(eng, i=i, buf=buf, h=h_ld):
                if i >= 2:
                    eng.wait_ge(prog.sems["b_dq"][0], i - 1)
                eng.dma_start(out=wbuf[:, buf:buf + DF],
                              in_=w_down[i * 128:i * 128 + 128, :]).then_inc(h, 16)
            ph1["sync"].append(load)

            h_dq, v_dq = INC("b_dq", 1)

            def dequant(eng, i=i, buf=buf, v_ld=v_ld, h=h_dq, ldn=ldn):
                eng.wait_ge(prog.sems[ldn][0], v_ld)
                if i >= 2:
                    eng.wait_ge(prog.sems["b_tp"][0], (i - 1) * 24)
                wb3 = wbuf[:, buf:buf + DF].rearrange("p (g c) -> p g c", g=96)
                eng.tensor_reduce(
                    wg[:].rearrange("p (g o) -> p g o", o=1),
                    wb3, AX.X, ALU.max, apply_absolute_value=True)
                eng.drain()
                eng.tensor_scalar(wg[:], wg[:], 1.0 / 7.0, 1e-12,
                                  ALU.mult, ALU.max)
                eng.drain()
                eng.reciprocal(wz[:, 0:96], wg[:])
                eng.drain()
                eng.tensor_tensor(wz[:, 96:192], wg[:], wz[:, 0:96], ALU.mult)
                eng.drain()
                eng.tensor_scalar(wz[:, 96:192], wz[:, 96:192], -1.0, 2.0,
                                  ALU.mult, ALU.add)
                eng.drain()
                eng.tensor_tensor(wz[:, 0:96], wz[:, 0:96], wz[:, 96:192],
                                  ALU.mult)
                eng.drain()
                eng.tensor_tensor(
                    wdq[:].rearrange("p (g c) -> p g c", g=96), wb3,
                    wz[:, 0:96].rearrange("p (g o) -> p g o", o=1).broadcast_to(
                        (128, 96, 32)), ALU.mult)
                eng.drain()
                eng.tensor_copy(wq[:], wdq[:])
                eng.drain()
                eng.tensor_scalar(wq[:], wq[:], 7.0, -8.0, ALU.min, ALU.max)
                eng.drain()
                eng.tensor_tensor(
                    wdq[:].rearrange("p (g c) -> p g c", g=96),
                    wq.rearrange("p (g c) -> p g c", g=96),
                    wg[:].rearrange("p (g o) -> p g o", o=1).broadcast_to(
                        (128, 96, 32)), ALU.mult)
                eng.drain()
                eng.tensor_scalar(wdq[:], wdq[:], s5col[:], None, ALU.mult)
                eng.drain().then_inc(h, 1)
            prog.sem("b_tp")
            ph1["vector"].append(dequant)

            for k in range(24):
                h_tp, v_tp = INC("b_tp", 1)

                def tpose(eng, k=k, v_dq=v_dq, h=h_tp,
                          ntp=prog.sems["b_tp"][1]):
                    eng.wait_ge(prog.sems["b_dq"][0], v_dq)
                    if ntp >= 3:
                        eng.wait_ge(prog.sems["b_sp"][0], ntp - 2)
                    p = psT if ntp % 2 == 1 else psT2
                    eng.transpose(p[:], wdq[:, k * 128:k * 128 + 128],
                                  identf[:]).then_inc(h, 1)
                prog.sem("b_sp")
                ph1["tensor"].append(tpose)

                sbuf2 = (prog.sems["b_sp"][1] % 2) * 128
                gpn = "b_gp%d" % (prog.sems["b_sp"][1] % 2)
                v_gp_need = prog.sem(gpn)[1]
                h_sp, v_sp = INC("b_sp", 1)

                def split(eng, v_tp=v_tp, sbuf2=sbuf2, h=h_sp, gpn=gpn,
                          v_gp_need=v_gp_need, ntp=prog.sems["b_tp"][1]):
                    eng.wait_ge(prog.sems["b_tp"][0], v_tp)
                    if v_gp_need > 0:
                        eng.wait_ge(prog.sems[gpn][0], v_gp_need)
                    p = psT if ntp % 2 == 1 else psT2
                    eng.tensor_copy(wth[:, sbuf2:sbuf2 + 128], p[:])
                    eng.tensor_tensor(wtl[:, sbuf2:sbuf2 + 128], p[:],
                                      wth[:, sbuf2:sbuf2 + 128],
                                      ALU.subtract).then_inc(h, 1)
                ph1["vector"].append(split)

                h_gp, v_gp = INC(gpn, 32)
                col = i * DF + k * 128

                def store(eng, v_sp=v_sp, sbuf2=sbuf2, col=col, h=h_gp):
                    eng.wait_ge(prog.sems["b_sp"][0], v_sp)
                    eng.dma_start(out=dnT_h[:, col:col + 128],
                                  in_=wth[:, sbuf2:sbuf2 + 128]).then_inc(h, 16)
                    eng.dma_start(out=dnT_l[:, col:col + 128],
                                  in_=wtl[:, sbuf2:sbuf2 + 128]).then_inc(h, 16)
                ph1["gpsimd"].append(store)

        _snap = {n: prog.sems[n][1] for n in
                 ("b_ld0", "b_ld1", "b_gp0", "b_gp1") if n in prog.sems}

        def p1fin_s(eng, _snap=_snap):
            for n in ("b_ld0", "b_ld1"):
                if n in _snap:
                    eng.wait_ge(prog.sems[n][0], _snap[n])
        ph1["sync"].append(p1fin_s)

        def p1fin_g(eng, _snap=_snap):
            for n in ("b_gp0", "b_gp1"):
                if n in _snap:
                    eng.wait_ge(prog.sems[n][0], _snap[n])
        ph1["gpsimd"].append(p1fin_g)

        def emit_slice(sl):
            t0 = sl * BT
            b, s0 = divmod(t0, S)
            ph = prog.phase()
            for c in range(24):
                cbuf = (c % 2) * BT
                ldn = "c_ld%d" % (c % 2)
                v_dc_need = max(prog.sem("c_dc")[1] - 1, 0)
                h_ld, v_ld = INC(ldn, 16)

                def ldc(eng, c=c, t0=t0, cbuf=cbuf, h=h_ld,
                        v_dc_need=v_dc_need):
                    if v_dc_need > 0:
                        eng.wait_ge(prog.sems["c_dc"][0], v_dc_need)
                    eng.dma_start(out=cub[:, cbuf:cbuf + BT],
                                  in_=cut[c * 128:c * 128 + 128, t0:t0 + BT]
                                  ).then_inc(h, 16)
                ph["sync"].append(ldc)

                h_dc, v_dc = INC("c_dc", 1)

                def dcvt(eng, c=c, cbuf=cbuf, v_ld=v_ld, h=h_dc, ldn=ldn):
                    eng.wait_ge(prog.sems[ldn][0], v_ld)
                    eng.tensor_scalar(di[:], cub[:, cbuf:cbuf + BT], fdcol[:],
                                      None, ALU.mult)
                    eng.drain()
                    eng.tensor_copy(dh[:, c * BT:c * BT + BT], di[:])
                    eng.drain()
                    eng.tensor_tensor(dl[:, c * BT:c * BT + BT], di[:],
                                      dh[:, c * BT:c * BT + BT],
                                      ALU.subtract).then_inc(h, 1)
                ph["vector"].append(dcvt)
            v_dc_all = prog.sems["c_dc"][1]

            for r in range(8):
                wb = (r % 2) * DF
                wsn = "c_ws%d" % (r % 2)
                v_need = prog.sem("c_mm")[1] - 1 if r >= 2 else 0
                h_ws, v_ws = INC(wsn, 32)

                def wload(eng, r=r, wb=wb, h=h_ws, v_need=v_need):
                    if v_need > 0:
                        eng.wait_ge(prog.sems["c_mm"][0], v_need)
                    eng.dma_start(out=wsh[:, wb:wb + DF],
                                  in_=dnT_h[:, r * DF:r * DF + DF]).then_inc(h, 16)
                    eng.dma_start(out=wsl[:, wb:wb + DF],
                                  in_=dnT_l[:, r * DF:r * DF + DF]).then_inc(h, 16)
                ph["sync"].append(wload)

                xldn = "x_ld%d" % (r % 2)
                v_ev_need = max(prog.sem("c_ev")[1] - 1, 0)
                h_xl, v_xl = INC(xldn, 16)

                def ldxp(eng, r=r, b=b, s0=s0, h=h_xl, v_ev_need=v_ev_need):
                    if v_ev_need > 0:
                        eng.wait_ge(prog.sems["c_ev"][0], v_ev_need)
                    eng.dma_start(out=xpb[:, (r % 2) * BT:(r % 2) * BT + BT],
                                  in_=xpT[b, r * 128:r * 128 + 128, s0:s0 + BT]
                                  ).then_inc(h, 16)
                ph["sync"].append(ldxp)

                bank = psA if r % 2 == 0 else psB
                h_mm, v_mm = INC("c_mm", 1)

                def cmm(eng, r=r, wb=wb, bank=bank, v_ws=v_ws, h=h_mm,
                        wsn=wsn, v_dc_all=v_dc_all, nmm=prog.sems["c_mm"][1]):
                    eng.wait_ge(prog.sems[wsn][0], v_ws)
                    eng.wait_ge(prog.sems["c_dc"][0], v_dc_all)
                    if nmm >= 3:
                        eng.wait_ge(prog.sems["c_ev"][0], nmm - 2)
                    for k in range(24):
                        wcol = wb + k * 128
                        eng.matmul(bank[:, 0:BT], wsh[:, wcol:wcol + 128],
                                   dh[:, k * BT:k * BT + BT],
                                   start=(k == 0), stop=False)
                        eng.matmul(bank[:, 0:BT], wsh[:, wcol:wcol + 128],
                                   dl[:, k * BT:k * BT + BT],
                                   start=False, stop=False)
                        mm = eng.matmul(bank[:, 0:BT], wsl[:, wcol:wcol + 128],
                                        dh[:, k * BT:k * BT + BT],
                                        start=False, stop=(k == 23))
                        if k == 23:
                            mm.then_inc(h, 1)
                prog.sem("c_ev")
                ph["tensor"].append(cmm)

                v_tp_prev = prog.sem("c_tp")[1]
                h_ev, v_ev = INC("c_ev", 1)

                def cev(eng, r=r, bank=bank, v_mm=v_mm, v_xl=v_xl, h=h_ev,
                        xldn=xldn, v_tp_prev=v_tp_prev):
                    eng.wait_ge(prog.sems["c_mm"][0], v_mm)
                    eng.wait_ge(prog.sems[xldn][0], v_xl)
                    if v_tp_prev > 0:
                        eng.wait_ge(prog.sems["c_tp"][0], v_tp_prev)
                    eng.tensor_tensor(fo[:], bank[:, 0:BT],
                                      xpb[:, (r % 2) * BT:(r % 2) * BT + BT],
                                      ALU.add)
                    eng.drain().then_inc(h, 1)
                ph["vector"].append(cev)

                for j in range(4):
                    h_tp, v_tp = INC("c_tp", 1)

                    def ftp(eng, j=j, v_ev=v_ev, h=h_tp,
                            ntp=prog.sems["c_tp"][1]):
                        eng.wait_ge(prog.sems["c_ev"][0], v_ev)
                        if ntp >= 3:
                            eng.wait_ge(prog.sems["c_fe"][0], ntp - 2)
                        p = psT if ntp % 2 == 1 else psT2
                        eng.transpose(p[:], fo[:, j * 128:j * 128 + 128],
                                      identf[:]).then_inc(h, 1)
                    prog.sem("c_fe")
                    ph["tensor"].append(ftp)

                    ssl = (prog.sem("c_fe")[1] % 4) * 128
                    stn = "c_st%d" % (prog.sem("c_fe")[1] % 4)
                    v_st_need = prog.sem(stn)[1]
                    h_fe, v_fe = INC("c_fe", 1)

                    def fev(eng, v_tp=v_tp, ssl=ssl, h=h_fe, stn=stn,
                            v_st_need=v_st_need, ntp=prog.sems["c_tp"][1]):
                        eng.wait_ge(prog.sems["c_tp"][0], v_tp)
                        if v_st_need > 0:
                            eng.wait_ge(prog.sems[stn][0], v_st_need)
                        p = psT if ntp % 2 == 1 else psT2
                        eng.tensor_copy(ost[:, ssl:ssl + 128],
                                        p[:]).then_inc(h, 1)
                    ph["vector"].append(fev)

                    h_st, v_st = INC(stn, 16)

                    def stf(eng, r=r, j=j, b=b, s0=s0, ssl=ssl, v_fe=v_fe,
                            h=h_st):
                        eng.wait_ge(prog.sems["c_fe"][0], v_fe)
                        eng.dma_start(
                            out=out[b, s0 + j * 128:s0 + j * 128 + 128,
                                    r * 128:r * 128 + 128],
                            in_=ost[:, ssl:ssl + 128]).then_inc(h, 16)
                    ph["gpsimd"].append(stf)

            _snap = {n: prog.sems[n][1] for n in
                     ("c_ld0", "c_ld1", "c_ws0", "c_ws1", "x_ld0", "x_ld1")
                     if n in prog.sems}

            def sfin(eng, _snap=_snap):
                for n, v in _snap.items():
                    eng.wait_ge(prog.sems[n][0], v)
            ph["sync"].append(sfin)

            _snap2 = {n: prog.sems[n][1] for n in
                      ("c_st0", "c_st1", "c_st2", "c_st3") if n in prog.sems}

            def gfin(eng, _snap2=_snap2):
                for n, v in _snap2.items():
                    eng.wait_ge(prog.sems[n][0], v)
            ph["gpsimd"].append(gfin)

        for sl in range(NSL):
            emit_slice(sl)

        prog.emit(block)
    return nc



_CACHE = {}


def _get(name, builder):
    if name not in _CACHE:
        _CACHE[name] = builder()
    return _CACHE[name]


def kernel(**inputs):
    xs = np.ascontiguousarray(inputs["L0_attn_x"], dtype=np.float32)
    conv = np.ascontiguousarray(
        inputs["L0_attn_conv_w"], dtype=np.float32).reshape(D, 9)
    pe = np.ascontiguousarray(inputs["L0_attn_pe_w"], dtype=np.float32)
    qkv_w = np.ascontiguousarray(inputs["L0_attn_qkv_w"], dtype=np.float32)
    out_w = np.ascontiguousarray(inputs["L0_attn_out_w"], dtype=np.float32)
    w_gate = np.ascontiguousarray(inputs["L0_ffn_w_gate"], dtype=np.float32)
    w_up = np.ascontiguousarray(inputs["L0_ffn_w_up"], dtype=np.float32)
    w_down = np.ascontiguousarray(inputs["L0_ffn_w_down"], dtype=np.float32)
    cores = list(range(NCORES))

    def run(name, builder, per_core):
        nc = _get(name, builder)
        return run_bass_kernel_spmd(nc, per_core, core_ids=cores).results

    r1 = run("a1", build_a1,
             [{"x": xs[B * c:B * c + B], "conv_w": conv, "pe_w": pe,
               "qkv_w": qkv_w} for c in cores])
    r2 = run("a2", build_a2, [{"qkvT": r1[c]["qkvT"]} for c in cores])
    r3 = run("a3", build_a3,
             [{"onT": r2[c]["onT"], "x": xs[B * c:B * c + B], "out_w": out_w}
              for c in cores])
    m1a = np.concatenate([r3[c]["m1"] for c in cores], axis=1)
    r4 = run("b1", build_b1,
             [{"xpT": r3[c]["xpT"], "m1a": m1a, "w_up": w_up,
               "w_gate": w_gate} for c in cores])
    m2a = np.concatenate([r4[c]["m2"] for c in cores], axis=1)
    m3a = np.concatenate([r4[c]["m3"] for c in cores], axis=1)
    r5 = run("b2", build_b2,
             [{"gt_raw": r4[c]["gt_raw"], "m3a": m3a} for c in cores])
    m4a = np.concatenate([r5[c]["m4"] for c in cores], axis=1)
    r6 = run("b3", build_b3,
             [{"ab": r5[c]["ab"], "up_raw": r4[c]["up_raw"], "m2a": m2a,
               "m3a": m3a, "m4a": m4a} for c in cores])
    m5a = np.concatenate([r6[c]["m5"] for c in cores], axis=1)
    r7 = run("b4", build_b4,
             [{"cu": r6[c]["cu"], "xpT": r3[c]["xpT"], "w_down": w_down,
               "m2a": m2a, "m3a": m3a, "m4a": m4a, "m5a": m5a}
              for c in cores])
    return np.concatenate([r7[c]["out"] for c in cores], axis=0)




# version 06
# speedup vs baseline: 1.0996x; 1.0981x over previous
import contextlib
import numpy as np
import concourse.bass as bass
import concourse.mybir as mybir
from concourse.bass_utils import run_bass_kernel_spmd

F32 = mybir.dt.float32
F32R = mybir.dt.float32r
BF16 = mybir.dt.bfloat16
I16 = mybir.dt.int16
U16 = mybir.dt.uint16
AF = mybir.ActivationFunctionType
ALU = mybir.AluOpType
AX = mybir.AxisListType

NCORES = 8
B = 2
S = 2048
D = 1024
DF = 3072
NH = 16
TOK = B * S
QMAX = 32767.0
EPS = 1e-5

ENGINES = ("sync", "gpsimd", "tensor", "vector", "scalar")


class Prog:

    def __init__(self, nc, ctx):
        self.nc = nc
        self.ctx = ctx
        self.ph = ctx.enter_context(nc.semaphore("phase"))
        self.phases = []
        self.sems = {}

    def sem(self, name):
        if name not in self.sems:
            h = self.ctx.enter_context(self.nc.semaphore(name))
            self.sems[name] = [h, 0]
        return self.sems[name]

    def phase(self):
        d = {e: [] for e in ENGINES}
        self.phases.append(d)
        return d

    def emit(self, block):
        nph = len(self.phases)

        def mk(ename):
            def body(eng):
                for i, ph in enumerate(self.phases):
                    if i > 0:
                        eng.wait_ge(self.ph, 5 * i)
                    for fn in ph[ename]:
                        fn(eng)
                    eng.drain().then_inc(self.ph, 1)
            return body

        for e in ENGINES:
            getattr(block, e)(mk(e))
        return nph



def build_a1():
    nc = bass.Bass("TRN2", target_bir_lowering=False,
                   detect_race_conditions=False)
    x = nc.dram_tensor("x", [B, S, D], F32, kind="ExternalInput").ap()
    conv_w = nc.dram_tensor("conv_w", [D, 9], F32, kind="ExternalInput").ap()
    pe_w = nc.dram_tensor("pe_w", [D, D], F32, kind="ExternalInput").ap()
    qkv_w = nc.dram_tensor("qkv_w", [3 * D, D], F32, kind="ExternalInput").ap()
    qkvT = nc.dram_tensor("qkvT", [B, 3 * D, S], BF16, kind="ExternalOutput").ap()
    qkT_h = nc.dram_tensor("qkT_h", [128, 24 * 1024], F32R, kind="Internal").ap()
    qkT_l = nc.dram_tensor("qkT_l", [128, 24 * 1024], F32R, kind="Internal").ap()

    BT = 512
    NBLK = S // BT

    with contextlib.ExitStack() as ctx:
        E = ctx.enter_context
        wtmp = E(nc.sbuf_tensor("wtmp", [128, 256], F32))
        wth = E(nc.sbuf_tensor("wth", [128, 256], F32R))
        wtl = E(nc.sbuf_tensor("wtl", [128, 256], F32R))
        ident = E(nc.sbuf_tensor("ident", [128, 128], F32))
        cw = E(nc.sbuf_tensor("cw", [128, 8 * 9], F32))
        zcol = E(nc.sbuf_tensor("zcol", [128, 8], F32))
        epsc = E(nc.sbuf_tensor("epsc", [128, 1], F32))
        wpeh = E(nc.sbuf_tensor("wpeh", [128, 8 * 1024], F32R))
        wpel = E(nc.sbuf_tensor("wpel", [128, 8 * 1024], F32R))
        wqh = E(nc.sbuf_tensor("wqh", [128, 2 * 1024], F32R))
        wql = E(nc.sbuf_tensor("wql", [128, 2 * 1024], F32R))
        lnT = E(nc.sbuf_tensor("lnT", [128, 2 * 8 * BT], F32))
        haloL = E(nc.sbuf_tensor("haloL", [128, 8 * NBLK], F32))
        haloF = E(nc.sbuf_tensor("haloF", [128, 8 * NBLK], F32))
        xtm = E(nc.sbuf_tensor("xtm", [128, 2 * D], F32))
        sq = E(nc.sbuf_tensor("sq", [128, D], F32))
        stats = E(nc.sbuf_tensor("stats", [128, 16], F32))
        xcf = E(nc.sbuf_tensor("xcf", [128, BT], F32))
        xch = E(nc.sbuf_tensor("xch", [128, 8 * BT], F32R))
        xcl = E(nc.sbuf_tensor("xcl", [128, 8 * BT], F32R))
        x2t = E(nc.sbuf_tensor("x2t", [128, BT], F32))
        x2h = E(nc.sbuf_tensor("x2h", [128, 8 * BT], F32R))
        x2l = E(nc.sbuf_tensor("x2l", [128, 8 * BT], F32R))
        qstage = E(nc.sbuf_tensor("qstage", [128, 2 * BT], BF16))
        psA = E(nc.psum_tensor("psA", [128, 512], F32))
        psB = E(nc.psum_tensor("psB", [128, 512], F32))
        psT = E(nc.psum_tensor("psT", [128, 128], F32))
        psT2 = E(nc.psum_tensor("psT2", [128, 128], F32))
        block = E(nc.Block())

        prog = Prog(nc, ctx)
        W = lambda name: prog.sem(name)
        def WNOW(name):
            s = prog.sem(name)
            v = s[1]
            return lambda eng, _h=s[0], _v=v: eng.wait_ge(_h, _v)

        def INC(name, n=1):
            s = prog.sem(name)
            s[1] += n
            return s[0], s[1]

        ph0 = prog.phase()

        def p0(g):
            h, _ = INC("g0", 4)
            g.memset(ident[:], 0.0).then_inc(h, 1)
            g.wait_ge(h, 1)
            g.affine_select(out=ident[:], in_=ident[:], compare_op=ALU.not_equal,
                            fill=1.0, base=0, pattern=[[-1, 128]],
                            channel_multiplier=1).then_inc(h, 1)
            g.memset(zcol[:], 0.0).then_inc(h, 1)
            g.memset(epsc[:], EPS).then_inc(h, 1)
            g.wait_ge(h, 4)
        ph0["gpsimd"].append(p0)

        def p0s(sync):
            h, v = INC("cwld", 16)
            sync.dma_start(out=cw.rearrange("p (c n) -> p c n", c=8),
                           in_=conv_w.rearrange("(c p) n -> p c n", p=128)
                           ).then_inc(h, 16)
            sync.wait_ge(h, v)
        ph0["sync"].append(p0s)

        ph1 = prog.phase()
        tiles = []
        for of in range(8):
            for k in range(8):
                tiles.append((pe_w, of, k, True))
        for of in range(24):
            for k in range(8):
                tiles.append((qkv_w, of, k, False))

        for i, (src, of, k, is_pe) in enumerate(tiles):
            buf = (i % 2) * 128
            ldn = "a_ld%d" % (i % 2)
            h_ld, v_ld = INC(ldn, 16)

            def load(eng, i=i, src=src, of=of, k=k, buf=buf, h=h_ld):
                if i >= 2:
                    eng.wait_ge(prog.sems["a_tp"][0], i - 1)
                eng.dma_start(out=wtmp[:, buf:buf + 128],
                              in_=src[of * 128:of * 128 + 128,
                                      k * 128:k * 128 + 128]).then_inc(h, 16)
            ph1["sync"].append(load)

            h_tp, v_tp = INC("a_tp", 1)

            def tpose(eng, i=i, buf=buf, v_ld=v_ld, h=h_tp, ldn=ldn):
                eng.wait_ge(prog.sems[ldn][0], v_ld)
                if i >= 2:
                    eng.wait_ge(prog.sems["a_vp"][0], i - 1)
                p = psT if i % 2 == 0 else psT2
                eng.transpose(p[:], wtmp[:, buf:buf + 128], ident[:]).then_inc(h, 1)
            ph1["tensor"].append(tpose)

            h_vp, v_vp = INC("a_vp", 1)
            col = of * 1024 + k * 128
            if is_pe:
                def split(eng, i=i, v_tp=v_tp, col=col, h=h_vp):
                    eng.wait_ge(prog.sems["a_tp"][0], v_tp)
                    p = psT if i % 2 == 0 else psT2
                    eng.tensor_copy(wpeh[:, col:col + 128], p[:])
                    eng.tensor_tensor(wpel[:, col:col + 128], p[:],
                                      wpeh[:, col:col + 128],
                                      ALU.subtract).then_inc(h, 1)
                ph1["vector"].append(split)
            else:
                gpn = "a_gp%d" % (i % 2)
                v_gp_need = prog.sem(gpn)[1]

                def split(eng, i=i, v_tp=v_tp, buf=buf, h=h_vp, gpn=gpn,
                          v_gp_need=v_gp_need):
                    eng.wait_ge(prog.sems["a_tp"][0], v_tp)
                    if v_gp_need > 0:
                        eng.wait_ge(prog.sems[gpn][0], v_gp_need)
                    p = psT if i % 2 == 0 else psT2
                    eng.tensor_copy(wth[:, buf:buf + 128], p[:])
                    eng.tensor_tensor(wtl[:, buf:buf + 128], p[:],
                                      wth[:, buf:buf + 128],
                                      ALU.subtract).then_inc(h, 1)
                ph1["vector"].append(split)

                h_gp, v_gp = INC(gpn, 32)

                def store(eng, v_vp=v_vp, buf=buf, col=col, h=h_gp):
                    eng.wait_ge(prog.sems["a_vp"][0], v_vp)
                    eng.dma_start(out=qkT_h[:, col:col + 128],
                                  in_=wth[:, buf:buf + 128]).then_inc(h, 16)
                    eng.dma_start(out=qkT_l[:, col:col + 128],
                                  in_=wtl[:, buf:buf + 128]).then_inc(h, 16)
                ph1["gpsimd"].append(store)
        _snap = {n: prog.sems[n][1] for n in ("a_ld0", "a_ld1", "a_gp0", "a_gp1")
                 if n in prog.sems}

        def p1_sync_fin(eng, _snap=_snap):
            for n in ("a_ld0", "a_ld1"):
                if n in _snap:
                    eng.wait_ge(prog.sems[n][0], _snap[n])
        ph1["sync"].append(p1_sync_fin)

        def p1_gp_fin(eng, _snap=_snap):
            for n in ("a_gp0", "a_gp1"):
                if n in _snap:
                    eng.wait_ge(prog.sems[n][0], _snap[n])
        ph1["gpsimd"].append(p1_gp_fin)

        xslot = [0]

        def emit_ln_block(b, blk):
            ph = prog.phase()
            slot = (blk % 2) * 8 * BT
            for i4 in range(BT // 128):
                t0 = blk * BT + i4 * 128
                n_x = xslot[0]
                xslot[0] += 1
                buf = (n_x % 2) * D
                ldn = "x_ld%d" % (n_x % 2)
                h_ld, v_ld = INC(ldn, 16)
                v_tp_need = max(prog.sem("x_tp")[1] - 8, 0)

                def load(eng, b=b, t0=t0, buf=buf, h=h_ld, v_tp_need=v_tp_need):
                    if v_tp_need > 0:
                        eng.wait_ge(prog.sems["x_tp"][0], v_tp_need)
                    eng.dma_start(out=xtm[:, buf:buf + D],
                                  in_=x[b, t0:t0 + 128, :]).then_inc(h, 16)
                ph["sync"].append(load)

                h_sp, v_sp = INC("x_sp", 1)

                v_vp_need = max(prog.sem("x_vp")[1] - 1, 0)

                def dosq(eng, v_ld=v_ld, buf=buf, h=h_sp, v_vp_need=v_vp_need,
                         ldn=ldn):
                    eng.wait_ge(prog.sems[ldn][0], v_ld)
                    if v_vp_need > 0:
                        eng.wait_ge(prog.sems["x_vp"][0], v_vp_need)
                    eng.activation(sq[:], xtm[:, buf:buf + D],
                                   AF.Square).then_inc(h, 1)
                ph["scalar"].append(dosq)

                h_vp, v_vp = INC("x_vp", 1)

                def dostats(eng, v_sp=v_sp, v_ld=v_ld, buf=buf, h=h_vp, ldn=ldn):
                    eng.wait_ge(prog.sems[ldn][0], v_ld)
                    eng.reduce_sum(stats[:, 0:1], xtm[:, buf:buf + D], AX.X)
                    eng.wait_ge(prog.sems["x_sp"][0], v_sp)
                    eng.reduce_sum(stats[:, 1:2], sq[:], AX.X)
                    eng.drain()
                    eng.tensor_scalar(stats[:, 2:3], stats[:, 0:1],
                                      1.0 / D, None, ALU.mult)
                    eng.tensor_scalar(stats[:, 3:4], stats[:, 1:2],
                                      1.0 / D, None, ALU.mult)
                    eng.drain()
                    eng.tensor_tensor(stats[:, 4:5], stats[:, 2:3],
                                      stats[:, 2:3], ALU.mult)
                    eng.drain()
                    eng.tensor_tensor(stats[:, 5:6], stats[:, 3:4],
                                      stats[:, 4:5], ALU.subtract)
                    eng.drain().then_inc(h, 1)
                ph["vector"].append(dostats)

                h_sp2, v_sp2 = INC("x_sp", 1)

                def dosqrt(eng, v_vp=v_vp, h=h_sp2):
                    eng.wait_ge(prog.sems["x_vp"][0], v_vp)
                    eng.activation(stats[:, 6:7], stats[:, 5:6], AF.Sqrt,
                                   bias=epsc[:]).then_inc(h, 1)
                ph["scalar"].append(dosqrt)

                h_vp2, v_vp2 = INC("x_vp", 1)

                def doapply(eng, v_sp2=v_sp2, buf=buf, h=h_vp2):
                    eng.wait_ge(prog.sems["x_sp"][0], v_sp2)
                    eng.reciprocal(stats[:, 7:8], stats[:, 6:7])
                    eng.drain()
                    eng.tensor_tensor(stats[:, 8:9], stats[:, 2:3],
                                      stats[:, 7:8], ALU.mult)
                    eng.drain()
                    eng.tensor_scalar(stats[:, 9:10], stats[:, 8:9],
                                      -1.0, None, ALU.mult)
                    eng.drain()
                    eng.tensor_scalar(xtm[:, buf:buf + D], xtm[:, buf:buf + D],
                                      stats[:, 7:8], stats[:, 9:10],
                                      ALU.mult, ALU.add).then_inc(h, 1)
                ph["vector"].append(doapply)

                for c in range(8):
                    h_tp, v_tp = INC("x_tp", 1)

                    def tpose(eng, v_vp2=v_vp2, buf=buf, c=c, h=h_tp,
                              ntp=prog.sems["x_tp"][1]):
                        eng.wait_ge(prog.sems["x_vp"][0], v_vp2)
                        if ntp >= 3:
                            eng.wait_ge(prog.sems["x_ev"][0], ntp - 2)
                        p = psT if ntp % 2 == 1 else psT2
                        eng.transpose(p[:], xtm[:, buf + c * 128:buf + c * 128 + 128],
                                      ident[:]).then_inc(h, 1)
                    prog.sem("x_ev")
                    ph["tensor"].append(tpose)

                    h_ev, v_ev = INC("x_ev", 1)
                    dst = slot + c * BT + i4 * 128

                    def evac(eng, v_tp=v_tp, dst=dst, h=h_ev,
                             ntp=prog.sems["x_tp"][1]):
                        eng.wait_ge(prog.sems["x_tp"][0], v_tp)
                        p = psT if ntp % 2 == 1 else psT2
                        eng.tensor_copy(lnT[:, dst:dst + 128], p[:]).then_inc(h, 1)
                    ph["vector"].append(evac)

            def halos(eng, b=b, blk=blk, slot=slot):
                for c in range(8):
                    eng.tensor_copy(haloL[:, c * NBLK + blk:c * NBLK + blk + 1],
                                    lnT[:, slot + c * BT + BT - 1:
                                        slot + c * BT + BT])
                    eng.tensor_copy(haloF[:, c * NBLK + blk:c * NBLK + blk + 1],
                                    lnT[:, slot + c * BT:slot + c * BT + 1])
            ph["vector"].append(halos)

            _snap = {n: prog.sems[n][1] for n in ("x_ld0", "x_ld1")
                     if n in prog.sems}

            def lnfin(eng, _snap=_snap):
                for n, v in _snap.items():
                    eng.wait_ge(prog.sems[n][0], v)
            ph["sync"].append(lnfin)

        def emit_main_block(b, blk):
            slot = (blk % 2) * 8 * BT
            t0 = blk * BT

            ph = prog.phase()

            def conv(eng, b=b, blk=blk, slot=slot):
                for c in range(8):
                    w0 = cw[:, 9 * c + 1:9 * c + 2]
                    w1 = cw[:, 9 * c + 4:9 * c + 5]
                    w2 = cw[:, 9 * c + 7:9 * c + 8]
                    mid = lnT[:, slot + c * BT:slot + c * BT + BT]
                    eng.tensor_scalar(xcf[:], mid, w1, None, ALU.mult)
                    if blk == 0:
                        eng.scalar_tensor_tensor(
                            xcf[:, 1:BT], lnT[:, slot + c * BT:slot + c * BT + BT - 1],
                            w0, xcf[:, 1:BT], ALU.mult, ALU.add)
                    else:
                        eng.scalar_tensor_tensor(
                            xcf[:, 0:1], haloL[:, c * NBLK + blk - 1:c * NBLK + blk],
                            w0, xcf[:, 0:1], ALU.mult, ALU.add)
                        eng.scalar_tensor_tensor(
                            xcf[:, 1:BT], lnT[:, slot + c * BT:slot + c * BT + BT - 1],
                            w0, xcf[:, 1:BT], ALU.mult, ALU.add)
                    if blk == NBLK - 1:
                        eng.scalar_tensor_tensor(
                            xcf[:, 0:BT - 1], lnT[:, slot + c * BT + 1:slot + c * BT + BT],
                            w2, xcf[:, 0:BT - 1], ALU.mult, ALU.add)
                    else:
                        eng.scalar_tensor_tensor(
                            xcf[:, 0:BT - 1], lnT[:, slot + c * BT + 1:slot + c * BT + BT],
                            w2, xcf[:, 0:BT - 1], ALU.mult, ALU.add)
                        eng.scalar_tensor_tensor(
                            xcf[:, BT - 1:BT], haloF[:, c * NBLK + blk + 1:c * NBLK + blk + 2],
                            w2, xcf[:, BT - 1:BT], ALU.mult, ALU.add)
                    eng.tensor_copy(xch[:, c * BT:c * BT + BT], xcf[:])
                    eng.tensor_tensor(xcl[:, c * BT:c * BT + BT], xcf[:],
                                      xch[:, c * BT:c * BT + BT], ALU.subtract)
            ph["vector"].append(conv)

            ph = prog.phase()
            for of in range(8):
                bank = psA if of % 2 == 0 else psB
                h_mm, v_mm = INC("m_mm", 1)

                def pemm(eng, of=of, bank=bank, h=h_mm, nmm=prog.sems["m_mm"][1]):
                    if nmm >= 3:
                        eng.wait_ge(prog.sems["m_ev"][0], nmm - 2)
                    for k in range(8):
                        wcol = of * 1024 + k * 128
                        eng.matmul(bank[:, 0:BT], wpeh[:, wcol:wcol + 128],
                                   xch[:, k * BT:k * BT + BT],
                                   start=(k == 0), stop=False)
                        eng.matmul(bank[:, 0:BT], wpeh[:, wcol:wcol + 128],
                                   xcl[:, k * BT:k * BT + BT],
                                   start=False, stop=False)
                        mm = eng.matmul(bank[:, 0:BT], wpel[:, wcol:wcol + 128],
                                        xch[:, k * BT:k * BT + BT],
                                        start=False, stop=(k == 7))
                        if k == 7:
                            mm.then_inc(h, 1)
                prog.sem("m_ev")
                ph["tensor"].append(pemm)

                h_ev, v_ev = INC("m_ev", 1)

                def peev(eng, of=of, bank=bank, v_mm=v_mm, h=h_ev):
                    eng.wait_ge(prog.sems["m_mm"][0], v_mm)
                    eng.tensor_tensor(x2t[:], bank[:, 0:BT],
                                      xch[:, of * BT:of * BT + BT], ALU.add)
                    eng.tensor_tensor(x2t[:], x2t[:],
                                      xcl[:, of * BT:of * BT + BT], ALU.add)
                    eng.tensor_copy(x2h[:, of * BT:of * BT + BT], x2t[:])
                    eng.tensor_tensor(x2l[:, of * BT:of * BT + BT], x2t[:],
                                      x2h[:, of * BT:of * BT + BT],
                                      ALU.subtract).then_inc(h, 1)
                ph["vector"].append(peev)

            ph = prog.phase()
            for of in range(24):
                buf = (of % 2) * 1024
                wsn = "q_ws%d" % (of % 2)
                v_need = prog.sem("q_mm")[1] - 1 if of >= 2 else 0
                h_ws, v_ws = INC(wsn, 32)

                def wload(eng, of=of, buf=buf, h=h_ws, v_need=v_need):
                    if v_need > 0:
                        eng.wait_ge(prog.sems["q_mm"][0], v_need)
                    eng.dma_start(out=wqh[:, buf:buf + 1024],
                                  in_=qkT_h[:, of * 1024:of * 1024 + 1024]).then_inc(h, 16)
                    eng.dma_start(out=wql[:, buf:buf + 1024],
                                  in_=qkT_l[:, of * 1024:of * 1024 + 1024]).then_inc(h, 16)
                ph["sync"].append(wload)

                bank = psA if of % 2 == 0 else psB
                h_mm, v_mm = INC("q_mm", 1)

                def qmm(eng, of=of, buf=buf, bank=bank, v_ws=v_ws,
                        h=h_mm, nmm=prog.sems["q_mm"][1], wsn=wsn):
                    eng.wait_ge(prog.sems[wsn][0], v_ws)
                    if nmm >= 3:
                        eng.wait_ge(prog.sems["q_ev"][0], nmm - 2)
                    for k in range(8):
                        wcol = buf + k * 128
                        eng.matmul(bank[:, 0:BT], wqh[:, wcol:wcol + 128],
                                   x2h[:, k * BT:k * BT + BT],
                                   start=(k == 0), stop=False)
                        eng.matmul(bank[:, 0:BT], wqh[:, wcol:wcol + 128],
                                   x2l[:, k * BT:k * BT + BT],
                                   start=False, stop=False)
                        mm = eng.matmul(bank[:, 0:BT], wql[:, wcol:wcol + 128],
                                        x2h[:, k * BT:k * BT + BT],
                                        start=False, stop=(k == 7))
                        if k == 7:
                            mm.then_inc(h, 1)
                prog.sem("q_ev")
                prog.sem("q_st")
                ph["tensor"].append(qmm)

                h_ev, v_ev = INC("q_ev", 1)
                nev = prog.sems["q_ev"][1]
                stn = "q_st%d" % (nev % 2)
                v_st_need = prog.sem(stn)[1]

                def qev(eng, of=of, bank=bank, v_mm=v_mm, h=h_ev,
                        nev=nev, stn=stn, v_st_need=v_st_need):
                    eng.wait_ge(prog.sems["q_mm"][0], v_mm)
                    if v_st_need > 0:
                        eng.wait_ge(prog.sems[stn][0], v_st_need)
                    qb = (nev % 2) * BT
                    eng.tensor_copy(qstage[:, qb:qb + BT],
                                    bank[:, 0:BT]).then_inc(h, 1)
                ph["vector"].append(qev)

                h_st, v_st = INC(stn, 16)

                def qst(eng, b=b, of=of, t0=t0, v_ev=v_ev, h=h_st,
                        nev=nev):
                    eng.wait_ge(prog.sems["q_ev"][0], v_ev)
                    qb = (nev % 2) * BT
                    eng.dma_start(out=qkvT[b, of * 128:of * 128 + 128,
                                           t0:t0 + BT],
                                  in_=qstage[:, qb:qb + BT]).then_inc(h, 16)
                ph["gpsimd"].append(qst)

            _snap = {n: prog.sems[n][1] for n in
                     ("q_ws0", "q_ws1", "q_st0", "q_st1") if n in prog.sems}

            def qsync_fin(eng, _snap=_snap):
                for n in ("q_ws0", "q_ws1"):
                    if n in _snap:
                        eng.wait_ge(prog.sems[n][0], _snap[n])
            ph["sync"].append(qsync_fin)

            def qgp_fin(eng, _snap=_snap):
                for n in ("q_st0", "q_st1"):
                    if n in _snap:
                        eng.wait_ge(prog.sems[n][0], _snap[n])
            ph["gpsimd"].append(qgp_fin)

        for b in range(B):
            emit_ln_block(b, 0)
            emit_ln_block(b, 1)
            emit_main_block(b, 0)
            emit_ln_block(b, 2)
            emit_main_block(b, 1)
            emit_ln_block(b, 3)
            emit_main_block(b, 2)
            emit_main_block(b, 3)

        prog.emit(block)
    return nc




def build_a2():
    nc = bass.Bass("TRN2", target_bir_lowering=False,
                   detect_race_conditions=False)
    qkvT = nc.dram_tensor("qkvT", [B, 3 * D, S], BF16, kind="ExternalInput").ap()
    onT = nc.dram_tensor("onT", [B, D, S], BF16, kind="ExternalOutput").ap()

    NP = 8
    NBLK = 4
    BQ = 512
    NCH = 16

    with contextlib.ExitStack() as ctx:
        E = ctx.enter_context
        ident = E(nc.sbuf_tensor("ident", [128, 128], BF16))
        identf = E(nc.sbuf_tensor("identf", [128, 128], F32))
        onesr = E(nc.sbuf_tensor("onesr", [1, 128], F32))
        indic = E(nc.sbuf_tensor("indic", [128, 2], BF16))
        qplain = E(nc.sbuf_tensor("qplain", [128, 8 * 256], BF16))
        kplain = E(nc.sbuf_tensor("kplain", [128, 8 * 256], BF16))
        vplain = E(nc.sbuf_tensor("vplain", [128, 8 * 256], BF16))
        qtr = E(nc.sbuf_tensor("qtr", [128, S], BF16))
        ktr = E(nc.sbuf_tensor("ktr", [128, S], BF16))
        vht = E(nc.sbuf_tensor("vht", [64, S], BF16))
        vre = E(nc.sbuf_tensor("vre", [128, 2 * NCH * 65], BF16))
        nstat = E(nc.sbuf_tensor("nstat", [1, 16], F32))
        nstat2 = E(nc.sbuf_tensor("nstat2", [128, 4], F32))
        biasA = E(nc.sbuf_tensor("biasA", [128, 1], F32))
        biasB = E(nc.sbuf_tensor("biasB", [128, 1], F32))
        pt = E(nc.sbuf_tensor("pt", [128, 4 * 1024], BF16))
        rsum = E(nc.sbuf_tensor("rsum", [1, 512], F32))
        bsum = E(nc.sbuf_tensor("bsum", [64, 2 * 512], F32))
        onst = E(nc.sbuf_tensor("onst", [128, 2 * 512], BF16))
        psQA = E(nc.psum_tensor("psQA", [128, 1024], F32))
        psQB = E(nc.psum_tensor("psQB", [128, 1024], F32))
        psVA = E(nc.psum_tensor("psVA", [128, 512], F32))
        psVB = E(nc.psum_tensor("psVB", [128, 512], F32))
        psX = E(nc.psum_tensor("psX", [128, 512], F32))
        psVT = E(nc.psum_tensor("psVT", [128, 128], BF16))
        block = E(nc.Block())

        prog = Prog(nc, ctx)

        def INC(name, n=1):
            s = prog.sem(name)
            s[1] += n
            return s[0], s[1]

        ph0 = prog.phase()

        def p0(g):
            h, _ = INC("g0", 6)
            g.memset(ident[:], 0.0).then_inc(h, 1)
            g.wait_ge(h, 1)
            g.affine_select(out=ident[:], in_=ident[:], compare_op=ALU.not_equal,
                            fill=1.0, base=0, pattern=[[-1, 128]],
                            channel_multiplier=1).then_inc(h, 1)
            g.memset(onesr[:], 1.0).then_inc(h, 1)
            hf, _ = INC("g2", 2)
            g.memset(identf[:], 0.0).then_inc(hf, 1)
            g.wait_ge(hf, 1)
            g.affine_select(out=identf[:], in_=identf[:], compare_op=ALU.not_equal,
                            fill=1.0, base=0, pattern=[[-1, 128]],
                            channel_multiplier=1).then_inc(hf, 1)
            g.wait_ge(hf, 2)
            g.memset(indic[:], 0.0).then_inc(h, 1)
            g.wait_ge(h, 4)
            g.memset(indic[0:64, 0:1], 1.0).then_inc(h, 1)
            g.memset(indic[64:128, 1:2], 1.0).then_inc(h, 1)
            g.wait_ge(h, 6)
            h2, _ = INC("g1", 1)
            g.memset(vre.rearrange("p (s c) -> p s c", c=65)[:, :, 64:65],
                     1.0).then_inc(h2, 1)
            g.wait_ge(h2, 1)
        ph0["gpsimd"].append(p0)

        def emit_pair(b, u):
            ph = prog.phase()
            for name, buf, base in (("q", qplain, 0), ("k", kplain, D),
                                    ("v", vplain, 2 * D)):
                for c8 in range(8):
                    h_ld, v_ld = INC("r_ld", 16)

                    def ld(eng, b=b, u=u, c8=c8, buf=buf, base=base, h=h_ld):
                        eng.dma_start(
                            out=buf[:, c8 * 256:c8 * 256 + 256],
                            in_=qkvT[b, base + c8 * 128: base + c8 * 128 + 128,
                                     256 * u:256 * u + 256]).then_inc(h, 16)
                    ph["sync"].append(ld)
            v_ld_all = prog.sems["r_ld"][1]

            h_re, v_re = INC("r_re", 1)

            def rearr(eng, v_ld_all=v_ld_all, h=h_re):
                eng.wait_ge(prog.sems["r_ld"][0], v_ld_all)
                for dst, srcb in ((qtr, qplain), (ktr, kplain)):
                    for hh in range(2):
                        for g in range(16):
                            eng.tensor_copy(
                                dst.rearrange("p (t g) -> p t g", g=16)
                                   [hh * 64:hh * 64 + 64, :, g],
                                srcb[(g % 2) * 64:(g % 2) * 64 + 64,
                                     (g // 2) * 256 + hh * 128:
                                     (g // 2) * 256 + hh * 128 + 128])
                eng.drain().then_inc(h, 1)
            ph["vector"].append(rearr)

            for hh in range(2):
                h_vh, v_vh = INC("r_vh", 1)

                v_ev_need = prog.sem("r_ev")[1]

                def vre_build(eng, hh=hh, v_ld_all=v_ld_all, h=h_vh,
                              v_ev_need=v_ev_need):
                    eng.wait_ge(prog.sems["r_ld"][0], v_ld_all)
                    if hh == 1:
                        eng.wait_ge(prog.sems["r_ev"][0], v_ev_need)
                    for g in range(16):
                        eng.tensor_copy(
                            vht.rearrange("p (t g) -> p t g", g=16)[:, :, g],
                            vplain[(g % 2) * 64:(g % 2) * 64 + 64,
                                   (g // 2) * 256 + hh * 128:
                                   (g // 2) * 256 + hh * 128 + 128])
                    eng.drain().then_inc(h, 1)
                ph["vector"].append(vre_build)

                for c in range(NCH):
                    h_tp, v_tp = INC("r_tp", 1)

                    def vtp(eng, hh=hh, c=c, v_vh=v_vh, h=h_tp,
                            ntp=prog.sems["r_tp"][1]):
                        eng.wait_ge(prog.sems["r_vh"][0], v_vh)
                        if ntp >= 2:
                            eng.wait_ge(prog.sems["r_ev"][0], ntp - 1)
                        p = psVT
                        eng.transpose(p[:, 0:64], vht[:, c * 128:c * 128 + 128],
                                      ident[0:64, 0:64]).then_inc(h, 1)
                    prog.sem("r_ev")
                    ph["tensor"].append(vtp)

                    h_ev, v_ev = INC("r_ev", 1)
                    col = (hh * NCH + c) * 65

                    def vev(eng, v_tp=v_tp, col=col, h=h_ev):
                        eng.wait_ge(prog.sems["r_tp"][0], v_tp)
                        eng.tensor_copy(vre[:, col:col + 64],
                                        psVT[:, 0:64]).then_inc(h, 1)
                    ph["vector"].append(vev)

            h_cm, v_cm = INC("r_cm", 1)
            v_re_c = prog.sems["r_re"][1]

            def colmax(eng, v_re_c=v_re_c, h=h_cm):
                eng.wait_ge(prog.sems["r_re"][0], v_re_c)
                eng.tensor_reduce(nstat2[:, 0:1], qtr[:], AX.X, ALU.max,
                                  apply_absolute_value=True)
                eng.tensor_reduce(nstat2[:, 1:2], ktr[:], AX.X, ALU.max,
                                  apply_absolute_value=True)
                eng.drain()
                eng.tensor_tensor(nstat2[:, 2:3], nstat2[:, 0:1],
                                  nstat2[:, 1:2], ALU.max)
                eng.drain().then_inc(h, 1)
            ph["vector"].append(colmax)

            h_ct, v_ct = INC("r_ct", 1)

            def coltp(eng, v_cm=v_cm, h=h_ct):
                eng.wait_ge(prog.sems["r_cm"][0], v_cm)
                eng.transpose(psX[0:1, 0:128], nstat2[:, 2:3],
                              identf[:]).then_inc(h, 1)
            ph["tensor"].append(coltp)

            h_nr, v_nr = INC("r_nr", 1)

            def headmax(eng, v_ct=v_ct, h=h_nr):
                eng.wait_ge(prog.sems["r_ct"][0], v_ct)
                eng.tensor_reduce(nstat[:, 0:1], psX[0:1, 0:64], AX.X, ALU.max)
                eng.tensor_reduce(nstat[:, 1:2], psX[0:1, 64:128], AX.X, ALU.max)
                eng.drain()
                eng.tensor_tensor(nstat[:, 2:3], nstat[:, 0:1], nstat[:, 0:1],
                                  ALU.mult)
                eng.tensor_tensor(nstat[:, 3:4], nstat[:, 1:2], nstat[:, 1:2],
                                  ALU.mult)
                eng.drain()
                eng.tensor_scalar(nstat[:, 4:5], nstat[:, 2:3], -8.0, None,
                                  ALU.mult)
                eng.tensor_scalar(nstat[:, 5:6], nstat[:, 3:4], -8.0, None,
                                  ALU.mult)
                eng.drain().then_inc(h, 1)
            ph["vector"].append(headmax)

            h_bc, v_bc = INC("r_bc", 1)

            def bcb(eng, v_nr=v_nr, h=h_bc):
                eng.wait_ge(prog.sems["r_nr"][0], v_nr)
                eng.matmul(psX[:, 0:1], onesr[:], nstat[0:1, 4:5],
                           start=True, stop=True)
                eng.matmul(psX[:, 1:2], onesr[:], nstat[0:1, 5:6],
                           start=True, stop=True).then_inc(h, 1)
            ph["tensor"].append(bcb)

            h_bv, v_bv = INC("r_bv", 1)

            def bcv(eng, v_bc=v_bc, h=h_bv):
                eng.wait_ge(prog.sems["r_bc"][0], v_bc)
                eng.tensor_copy(biasA[:], psX[:, 0:1])
                eng.tensor_copy(biasB[:], psX[:, 1:2])
                eng.drain().then_inc(h, 1)
            ph["vector"].append(bcv)

            def ld_fin(eng, v=v_ld_all):
                eng.wait_ge(prog.sems["r_ld"][0], v)
            ph["sync"].append(ld_fin)

            ph = prog.phase()
            NG = NCH // 2
            for blk in range(NBLK):
                q0 = blk * BQ

                def emit_qk(g):
                    v_ex_need = prog.sem("t_ex")[1]
                    h_qk, v_qk = INC("t_qk", 1)

                    def qk(eng, g=g, q0=q0, h=h_qk, v_ex_need=v_ex_need):
                        if v_ex_need > 0:
                            eng.wait_ge(prog.sems["t_ex"][0], v_ex_need)
                        for cc in range(2):
                            c = 2 * g + cc
                            eng.matmul(psQA[:, cc * 512:cc * 512 + 512],
                                       ktr[0:64, c * 128:c * 128 + 128],
                                       qtr[0:64, q0:q0 + BQ], start=True,
                                       stop=True)
                            mmb = eng.matmul(psQB[:, cc * 512:cc * 512 + 512],
                                             ktr[64:128, c * 128:c * 128 + 128],
                                             qtr[64:128, q0:q0 + BQ],
                                             start=True, stop=True)
                            if cc == 1:
                                mmb.then_inc(h, 1)
                    ph["tensor"].append(qk)
                    return v_qk

                def emit_ex(g, v_qk):
                    v_av_need = prog.sem("t_av")[1]
                    h_ex, v_ex = INC("t_ex", 1)
                    slotA = (g % 2) * 1024
                    slotB = 2048 + (g % 2) * 1024

                    def ex(eng, v_qk=v_qk, slotA=slotA, slotB=slotB, h=h_ex,
                           v_av_need=v_av_need):
                        eng.wait_ge(prog.sems["t_qk"][0], v_qk)
                        if v_av_need > 0:
                            eng.wait_ge(prog.sems["t_av"][0], v_av_need)
                        eng.activation(pt[:, slotA:slotA + 1024], psQA[:],
                                       AF.Exp, bias=biasA[:], scale=0.125)
                        eng.activation(pt[:, slotB:slotB + 1024], psQB[:],
                                       AF.Exp, bias=biasB[:],
                                       scale=0.125).then_inc(h, 1)
                    ph["scalar"].append(ex)
                    return v_ex, slotA, slotB

                def emit_av(g, v_ex, slotA, slotB, v_nrm_need):
                    h_av, v_av = INC("t_av", 1)

                    def av(eng, g=g, v_ex=v_ex, slotA=slotA, slotB=slotB,
                           h=h_av, v_nrm_need=v_nrm_need):
                        eng.wait_ge(prog.sems["t_ex"][0], v_ex)
                        if g == 0 and v_nrm_need > 0:
                            eng.wait_ge(prog.sems["t_n1"][0], v_nrm_need)
                        for cc in range(2):
                            c = 2 * g + cc
                            eng.matmul(psVA[0:65, :],
                                       vre[:, c * 65:c * 65 + 65],
                                       pt[:, slotA + cc * 512:slotA + cc * 512 + 512],
                                       start=(c == 0), stop=(c == NCH - 1))
                            mmb = eng.matmul(
                                psVB[0:65, :],
                                vre[:, (NCH + c) * 65:(NCH + c) * 65 + 65],
                                pt[:, slotB + cc * 512:slotB + cc * 512 + 512],
                                start=(c == 0), stop=(c == NCH - 1))
                            if cc == 1:
                                mmb.then_inc(h, 1)
                    ph["tensor"].append(av)

                v_nrm_need = prog.sem("t_n1")[1]
                pend = []
                for g in range(NG):
                    v_qk = emit_qk(g)
                    v_ex, sA, sB = emit_ex(g, v_qk)
                    pend.append((g, v_ex, sA, sB))
                    if g >= 1:
                        gg, ve, sa, sb = pend.pop(0)
                        emit_av(gg, ve, sa, sb, v_nrm_need)
                gg, ve, sa, sb = pend.pop(0)
                emit_av(gg, ve, sa, sb, v_nrm_need)

                v_av_end = prog.sems["t_av"][1]
                h_n1, v_n1 = INC("t_n1", 1)

                def norm1(eng, v_av_end=v_av_end, h=h_n1):
                    eng.wait_ge(prog.sems["t_av"][0], v_av_end)
                    eng.reciprocal(rsum[:], psVA[64:65, :])
                    eng.drain().then_inc(h, 1)
                ph["vector"].append(norm1)

                h_b1, v_b1 = INC("t_b1", 1)

                def bc1(eng, v_n1=v_n1, h=h_b1):
                    eng.wait_ge(prog.sems["t_n1"][0], v_n1)
                    eng.matmul(psX[0:64, :], onesr[0:1, 0:64], rsum[:],
                               start=True, stop=True).then_inc(h, 1)
                ph["tensor"].append(bc1)

                h_n2, v_n2 = INC("t_n1", 1)

                def norm2(eng, v_b1=v_b1, h=h_n2):
                    eng.wait_ge(prog.sems["t_b1"][0], v_b1)
                    eng.tensor_copy(bsum[:, 0:512], psX[0:64, :])
                    eng.drain()
                    eng.reciprocal(rsum[:], psVB[64:65, :])
                    eng.drain().then_inc(h, 1)
                ph["vector"].append(norm2)

                h_b2, v_b2 = INC("t_b1", 1)

                def bc2(eng, v_n2=v_n2, h=h_b2):
                    eng.wait_ge(prog.sems["t_n1"][0], v_n2)
                    eng.matmul(psX[0:64, :], onesr[0:1, 0:64], rsum[:],
                               start=True, stop=True).then_inc(h, 1)
                ph["tensor"].append(bc2)

                h_n3, v_n3 = INC("t_n1", 1)
                stn = "t_st%d" % (blk % 2)
                v_st_need = prog.sem(stn)[1]

                def norm3(eng, blk=blk, v_b2=v_b2, h=h_n3, v_st_need=v_st_need,
                          stn=stn):
                    eng.wait_ge(prog.sems["t_b1"][0], v_b2)
                    if v_st_need > 0:
                        eng.wait_ge(prog.sems[stn][0], v_st_need)
                    eng.tensor_copy(bsum[:, 512:1024], psX[0:64, :])
                    eng.drain()
                    ob = (blk % 2) * 512
                    eng.tensor_tensor(onst[0:64, ob:ob + 512], psVA[0:64, :],
                                      bsum[:, 0:512], ALU.mult)
                    eng.tensor_tensor(onst[64:128, ob:ob + 512], psVB[0:64, :],
                                      bsum[:, 512:1024], ALU.mult)
                    eng.drain().then_inc(h, 1)
                ph["vector"].append(norm3)

                h_st, v_st = INC(stn, 16)

                def store(eng, b=b, u=u, q0=q0, blk=blk, v_n3=v_n3, h=h_st):
                    eng.wait_ge(prog.sems["t_n1"][0], v_n3)
                    ob = (blk % 2) * 512
                    eng.dma_start(out=onT[b, u * 128:u * 128 + 128, q0:q0 + BQ],
                                  in_=onst[:, ob:ob + 512]).then_inc(h, 16)
                ph["gpsimd"].append(store)

            _snap = {n: prog.sems[n][1] for n in ("t_st0", "t_st1")
                     if n in prog.sems}

            def gp_fin(eng, _snap=_snap):
                for n, v in _snap.items():
                    eng.wait_ge(prog.sems[n][0], v)
            ph["gpsimd"].append(gp_fin)

        for b in range(B):
            for u in range(NP):
                emit_pair(b, u)

        prog.emit(block)
    return nc


_A2 = None


def _get_a2():
    global _A2
    if _A2 is None:
        _A2 = build_a2()
    return _A2


def run_a2(qkvT_list):
    nc = _get_a2()
    in_maps = [{"qkvT": q} for q in qkvT_list]
    res = run_bass_kernel_spmd(nc, in_maps, core_ids=list(range(NCORES)))
    return [r["onT"] for r in res.results]




def build_a3():
    nc = bass.Bass("TRN2", target_bir_lowering=False,
                   detect_race_conditions=False)
    onT = nc.dram_tensor("onT", [B, D, S], BF16, kind="ExternalInput").ap()
    x = nc.dram_tensor("x", [B, S, D], F32, kind="ExternalInput").ap()
    out_w = nc.dram_tensor("out_w", [D, D], F32, kind="ExternalInput").ap()
    xpT = nc.dram_tensor("xpT", [B, D, S], F32, kind="ExternalOutput").ap()
    m1 = nc.dram_tensor("m1", [1, 1], F32, kind="ExternalOutput").ap()
    woT_h = nc.dram_tensor("woT_h", [128, 8 * D], F32R, kind="Internal").ap()
    woT_l = nc.dram_tensor("woT_l", [128, 8 * D], F32R, kind="Internal").ap()

    BT = 512
    NBLK = S // BT

    with contextlib.ExitStack() as ctx:
        E = ctx.enter_context
        wtmp = E(nc.sbuf_tensor("wtmp", [128, 256], F32))
        wth = E(nc.sbuf_tensor("wth", [128, 256], F32R))
        wtl = E(nc.sbuf_tensor("wtl", [128, 256], F32R))
        identf = E(nc.sbuf_tensor("identf", [128, 128], F32))
        onescol = E(nc.sbuf_tensor("onescol", [128, 1], F32))
        onesr = E(nc.sbuf_tensor("onesr", [1, 128], F32))
        epsc = E(nc.sbuf_tensor("epsc", [1, 1], F32))
        obuf = E(nc.sbuf_tensor("obuf", [128, 8 * BT], BF16))
        orr = E(nc.sbuf_tensor("orr", [128, 8 * BT], F32R))
        xtm = E(nc.sbuf_tensor("xtm", [128, 2 * D], F32))
        xT = E(nc.sbuf_tensor("xT", [128, 8 * BT], F32))
        xp = E(nc.sbuf_tensor("xp", [128, 8 * BT], F32))
        sqb = E(nc.sbuf_tensor("sqb", [128, 2 * BT], F32))
        wsh = E(nc.sbuf_tensor("wsh", [128, 2 * 1024], F32R))
        wsl = E(nc.sbuf_tensor("wsl", [128, 2 * 1024], F32R))
        rows = E(nc.sbuf_tensor("rows", [1, 16 * BT], F32))
        mub = E(nc.sbuf_tensor("mub", [128, BT], F32))
        rsb = E(nc.sbuf_tensor("rsb", [128, BT], F32))
        zb = E(nc.sbuf_tensor("zb", [128, BT], F32))
        macc = E(nc.sbuf_tensor("macc", [128, 1], F32))
        mcol = E(nc.sbuf_tensor("mcol", [128, 1], F32))
        m1s = E(nc.sbuf_tensor("m1s", [1, 1], F32))
        psA = E(nc.psum_tensor("psA", [128, 512], F32))
        psB = E(nc.psum_tensor("psB", [128, 512], F32))
        psT = E(nc.psum_tensor("psT", [128, 128], F32))
        psT2 = E(nc.psum_tensor("psT2", [128, 128], F32))
        psS = E(nc.psum_tensor("psS", [1, 512], F32))
        psS2 = E(nc.psum_tensor("psS2", [1, 512], F32))
        psX = E(nc.psum_tensor("psX", [128, 512], F32))
        block = E(nc.Block())

        prog = Prog(nc, ctx)

        def INC(name, n=1):
            s = prog.sem(name)
            s[1] += n
            return s[0], s[1]

        ph0 = prog.phase()

        def p0(g):
            h, _ = INC("g0", 5)
            g.memset(identf[:], 0.0).then_inc(h, 1)
            g.wait_ge(h, 1)
            g.affine_select(out=identf[:], in_=identf[:], compare_op=ALU.not_equal,
                            fill=1.0, base=0, pattern=[[-1, 128]],
                            channel_multiplier=1).then_inc(h, 1)
            g.memset(onescol[:], 1.0).then_inc(h, 1)
            g.memset(onesr[:], 1.0).then_inc(h, 1)
            g.memset(epsc[:], EPS).then_inc(h, 1)
            g.wait_ge(h, 5)
            h2, _ = INC("g1", 1)
            g.memset(macc[:], 0.0).then_inc(h2, 1)
            g.wait_ge(h2, 1)
        ph0["gpsimd"].append(p0)

        ph1 = prog.phase()
        for i in range(64):
            of, k = divmod(i, 8)
            buf = (i % 2) * 128
            ldn = "a_ld%d" % (i % 2)
            h_ld, v_ld = INC(ldn, 16)

            def load(eng, i=i, of=of, k=k, buf=buf, h=h_ld):
                if i >= 2:
                    eng.wait_ge(prog.sems["a_tp"][0], i - 1)
                eng.dma_start(out=wtmp[:, buf:buf + 128],
                              in_=out_w[of * 128:of * 128 + 128,
                                        k * 128:k * 128 + 128]).then_inc(h, 16)
            ph1["sync"].append(load)

            h_tp, v_tp = INC("a_tp", 1)

            def tpose(eng, i=i, buf=buf, v_ld=v_ld, h=h_tp, ldn=ldn):
                eng.wait_ge(prog.sems[ldn][0], v_ld)
                if i >= 2:
                    eng.wait_ge(prog.sems["a_vp"][0], i - 1)
                p = psT if i % 2 == 0 else psT2
                eng.transpose(p[:], wtmp[:, buf:buf + 128],
                              identf[:]).then_inc(h, 1)
            ph1["tensor"].append(tpose)

            gpn = "a_gp%d" % (i % 2)
            v_gp_need = prog.sem(gpn)[1]
            h_vp, v_vp = INC("a_vp", 1)

            def split(eng, i=i, v_tp=v_tp, buf=buf, h=h_vp, gpn=gpn,
                      v_gp_need=v_gp_need):
                eng.wait_ge(prog.sems["a_tp"][0], v_tp)
                if v_gp_need > 0:
                    eng.wait_ge(prog.sems[gpn][0], v_gp_need)
                p = psT if i % 2 == 0 else psT2
                eng.tensor_copy(wth[:, buf:buf + 128], p[:])
                eng.tensor_tensor(wtl[:, buf:buf + 128], p[:],
                                  wth[:, buf:buf + 128],
                                  ALU.subtract).then_inc(h, 1)
            ph1["vector"].append(split)

            h_gp, v_gp = INC(gpn, 32)
            col = of * 1024 + k * 128

            def store(eng, v_vp=v_vp, buf=buf, col=col, h=h_gp):
                eng.wait_ge(prog.sems["a_vp"][0], v_vp)
                eng.dma_start(out=woT_h[:, col:col + 128],
                              in_=wth[:, buf:buf + 128]).then_inc(h, 16)
                eng.dma_start(out=woT_l[:, col:col + 128],
                              in_=wtl[:, buf:buf + 128]).then_inc(h, 16)
            ph1["gpsimd"].append(store)

        _snap = {n: prog.sems[n][1] for n in ("a_ld0", "a_ld1", "a_gp0", "a_gp1")
                 if n in prog.sems}

        def p1fin_s(eng, _snap=_snap):
            for n in ("a_ld0", "a_ld1"):
                if n in _snap:
                    eng.wait_ge(prog.sems[n][0], _snap[n])
        ph1["sync"].append(p1fin_s)

        def p1fin_g(eng, _snap=_snap):
            for n in ("a_gp0", "a_gp1"):
                if n in _snap:
                    eng.wait_ge(prog.sems[n][0], _snap[n])
        ph1["gpsimd"].append(p1fin_g)

        def emit_block(b, blk):
            t0 = blk * BT
            ph = prog.phase()
            for c in range(8):
                h_ld, v_ld = INC("o_ld", 16)

                def ldo(eng, b=b, c=c, t0=t0, h=h_ld):
                    eng.dma_start(out=obuf[:, c * BT:c * BT + BT],
                                  in_=onT[b, c * 128:c * 128 + 128,
                                          t0:t0 + BT]).then_inc(h, 16)
                ph["sync"].append(ldo)
            v_old = prog.sems["o_ld"][1]

            h_or, v_or = INC("o_or", 1)

            def ocvt(eng, v_old=v_old, h=h_or):
                eng.wait_ge(prog.sems["o_ld"][0], v_old)
                for c in range(8):
                    eng.tensor_copy(orr[:, c * BT:c * BT + BT],
                                    obuf[:, c * BT:c * BT + BT])
                eng.drain().then_inc(h, 1)
            ph["vector"].append(ocvt)

            for i4 in range(4):
                xldn = "x_ld%d" % (i4 % 2)
                h_xl, v_xl = INC(xldn, 16)
                v_tp_need = max(prog.sem("x_tp")[1] - 8, 0)

                def ldx(eng, b=b, t0=t0, i4=i4, h=h_xl, v_tp_need=v_tp_need):
                    if v_tp_need > 0:
                        eng.wait_ge(prog.sems["x_tp"][0], v_tp_need)
                    eng.dma_start(out=xtm[:, (i4 % 2) * D:(i4 % 2) * D + D],
                                  in_=x[b, t0 + i4 * 128:t0 + i4 * 128 + 128, :]
                                  ).then_inc(h, 16)
                ph["sync"].append(ldx)

                for c in range(8):
                    h_tp, v_tp = INC("x_tp", 1)

                    def xtp(eng, i4=i4, c=c, v_xl=v_xl, h=h_tp, xldn=xldn,
                            ntp=prog.sems["x_tp"][1]):
                        eng.wait_ge(prog.sems[xldn][0], v_xl)
                        if ntp >= 3:
                            eng.wait_ge(prog.sems["x_ev"][0], ntp - 2)
                        p = psT if ntp % 2 == 1 else psT2
                        eng.transpose(p[:], xtm[:, (i4 % 2) * D + c * 128:
                                                (i4 % 2) * D + c * 128 + 128],
                                      identf[:]).then_inc(h, 1)
                    prog.sem("x_ev")
                    ph["tensor"].append(xtp)

                    h_ev, v_ev = INC("x_ev", 1)
                    dst = c * BT + i4 * 128

                    def xev(eng, v_tp=v_tp, dst=dst, h=h_ev,
                            ntp=prog.sems["x_tp"][1]):
                        eng.wait_ge(prog.sems["x_tp"][0], v_tp)
                        p = psT if ntp % 2 == 1 else psT2
                        eng.tensor_copy(xT[:, dst:dst + 128], p[:]).then_inc(h, 1)
                    ph["vector"].append(xev)

            for of in range(8):
                buf = (of % 2) * 1024
                wsn = "w_ws%d" % (of % 2)
                v_need = prog.sem("w_mm")[1] - 1 if of >= 2 else 0
                h_ws, v_ws = INC(wsn, 32)

                def wload(eng, of=of, buf=buf, h=h_ws, v_need=v_need):
                    if v_need > 0:
                        eng.wait_ge(prog.sems["w_mm"][0], v_need)
                    eng.dma_start(out=wsh[:, buf:buf + 1024],
                                  in_=woT_h[:, of * 1024:of * 1024 + 1024]
                                  ).then_inc(h, 16)
                    eng.dma_start(out=wsl[:, buf:buf + 1024],
                                  in_=woT_l[:, of * 1024:of * 1024 + 1024]
                                  ).then_inc(h, 16)
                ph["sync"].append(wload)

                bank = psA if of % 2 == 0 else psB
                h_mm, v_mm = INC("w_mm", 1)
                v_or_c = prog.sems["o_or"][1]

                def wmm(eng, of=of, buf=buf, bank=bank, v_ws=v_ws, h=h_mm,
                        wsn=wsn, v_or_c=v_or_c, nmm=prog.sems["w_mm"][1]):
                    eng.wait_ge(prog.sems[wsn][0], v_ws)
                    eng.wait_ge(prog.sems["o_or"][0], v_or_c)
                    if nmm >= 3:
                        eng.wait_ge(prog.sems["w_ev"][0], nmm - 2)
                    for k in range(8):
                        wcol = buf + k * 128
                        eng.matmul(bank[:, 0:BT], wsh[:, wcol:wcol + 128],
                                   orr[:, k * BT:k * BT + BT],
                                   start=(k == 0), stop=False)
                        mm = eng.matmul(bank[:, 0:BT], wsl[:, wcol:wcol + 128],
                                        orr[:, k * BT:k * BT + BT],
                                        start=False, stop=(k == 7))
                        if k == 7:
                            mm.then_inc(h, 1)
                prog.sem("w_ev")
                ph["tensor"].append(wmm)

                v_xev_all = prog.sems["x_ev"][1]
                h_ev, v_ev = INC("w_ev", 1)

                def wev(eng, of=of, bank=bank, v_mm=v_mm, v_xev_all=v_xev_all,
                        h=h_ev):
                    eng.wait_ge(prog.sems["w_mm"][0], v_mm)
                    eng.wait_ge(prog.sems["x_ev"][0], v_xev_all)
                    eng.tensor_tensor(xp[:, of * BT:of * BT + BT], bank[:, 0:BT],
                                      xT[:, of * BT:of * BT + BT],
                                      ALU.add).then_inc(h, 1)
                ph["vector"].append(wev)

                v_sm_need = max(prog.sem("o_sm")[1] - 1, 0)
                h_sq, v_sq = INC("o_sq", 1)

                def dosq(eng, of=of, v_ev=v_ev, h=h_sq, v_sm_need=v_sm_need):
                    eng.wait_ge(prog.sems["w_ev"][0], v_ev)
                    if of >= 2 and v_sm_need > 0:
                        eng.wait_ge(prog.sems["o_sm"][0], v_sm_need)
                    eng.activation(sqb[:, (of % 2) * BT:(of % 2) * BT + BT],
                                   xp[:, of * BT:of * BT + BT],
                                   AF.Square).then_inc(h, 1)
                ph["scalar"].append(dosq)

                h_sm, v_sm = INC("o_sm", 1)

                def statmm(eng, of=of, v_sq=v_sq, h=h_sm):
                    eng.wait_ge(prog.sems["o_sq"][0], v_sq)
                    eng.matmul(psS[0:1, 0:BT], onescol[:],
                               xp[:, of * BT:of * BT + BT],
                               start=(of == 0), stop=(of == 7))
                    eng.matmul(psS2[0:1, 0:BT], onescol[:],
                               sqb[:, (of % 2) * BT:(of % 2) * BT + BT],
                               start=(of == 0), stop=(of == 7)).then_inc(h, 1)
                ph["tensor"].append(statmm)

                stn = "o_st%d" % (of % 2)
                h_st, v_st = INC(stn, 16)

                def stx(eng, b=b, of=of, t0=t0, v_ev=v_ev, h=h_st):
                    eng.wait_ge(prog.sems["w_ev"][0], v_ev)
                    eng.dma_start(out=xpT[b, of * 128:of * 128 + 128, t0:t0 + BT],
                                  in_=xp[:, of * BT:of * BT + BT]).then_inc(h, 16)
                ph["gpsimd"].append(stx)

            v_sm_all = prog.sems["o_sm"][1]
            h_r1, v_r1 = INC("o_r1", 1)

            def rowstats(eng, v_sm_all=v_sm_all, h=h_r1):
                eng.wait_ge(prog.sems["o_sm"][0], v_sm_all)
                eng.tensor_scalar(rows[:, 0:BT], psS[0:1, 0:BT], 1.0 / D, None,
                                  ALU.mult)
                eng.tensor_scalar(rows[:, BT:2 * BT], psS2[0:1, 0:BT], 1.0 / D,
                                  None, ALU.mult)
                eng.drain()
                eng.tensor_tensor(rows[:, 2 * BT:3 * BT], rows[:, 0:BT],
                                  rows[:, 0:BT], ALU.mult)
                eng.drain()
                eng.tensor_tensor(rows[:, 3 * BT:4 * BT], rows[:, BT:2 * BT],
                                  rows[:, 2 * BT:3 * BT], ALU.subtract)
                eng.drain().then_inc(h, 1)
            ph["vector"].append(rowstats)

            h_r2, v_r2 = INC("o_r2", 1)

            def rowsqrt(eng, v_r1=v_r1, h=h_r2):
                eng.wait_ge(prog.sems["o_r1"][0], v_r1)
                eng.activation(rows[:, 4 * BT:5 * BT], rows[:, 3 * BT:4 * BT],
                               AF.Sqrt,
                               bias=epsc[:].broadcast_to((1, 1))).then_inc(h, 1)
            ph["scalar"].append(rowsqrt)

            h_r3, v_r3 = INC("o_r1", 1)

            def rowrecip(eng, v_r2=v_r2, h=h_r3):
                eng.wait_ge(prog.sems["o_r2"][0], v_r2)
                eng.reciprocal(rows[:, 5 * BT:6 * BT], rows[:, 4 * BT:5 * BT])
                eng.drain()
                eng.tensor_tensor(rows[:, 6 * BT:7 * BT], rows[:, 4 * BT:5 * BT],
                                  rows[:, 5 * BT:6 * BT], ALU.mult)
                eng.drain()
                eng.tensor_scalar(rows[:, 6 * BT:7 * BT], rows[:, 6 * BT:7 * BT],
                                  -1.0, 2.0, ALU.mult, ALU.add)
                eng.drain()
                eng.tensor_tensor(rows[:, 5 * BT:6 * BT], rows[:, 5 * BT:6 * BT],
                                  rows[:, 6 * BT:7 * BT], ALU.mult)
                eng.drain().then_inc(h, 1)
            ph["vector"].append(rowrecip)

            h_bc, v_bc = INC("o_bc", 1)

            def statbc(eng, v_r3=v_r3, h=h_bc):
                eng.wait_ge(prog.sems["o_r1"][0], v_r3)
                eng.matmul(psX[:, 0:BT], onesr[:], rows[:, 0:BT],
                           start=True, stop=True).then_inc(h, 1)
            ph["tensor"].append(statbc)

            h_bv, v_bv = INC("o_r1", 1)

            def statbv(eng, v_bc=v_bc, h=h_bv):
                eng.wait_ge(prog.sems["o_bc"][0], v_bc)
                eng.tensor_copy(mub[:], psX[:, 0:BT])
                eng.drain().then_inc(h, 1)
            ph["vector"].append(statbv)

            h_bc2, v_bc2 = INC("o_bc", 1)

            def statbc2(eng, v_bv=v_bv, h=h_bc2):
                eng.wait_ge(prog.sems["o_r1"][0], v_bv)
                eng.matmul(psX[:, 0:BT], onesr[:], rows[:, 5 * BT:6 * BT],
                           start=True, stop=True).then_inc(h, 1)
            ph["tensor"].append(statbc2)

            h_bv2, v_bv2 = INC("o_r1", 1)

            def statbv2(eng, v_bc2=v_bc2, h=h_bv2):
                eng.wait_ge(prog.sems["o_bc"][0], v_bc2)
                eng.tensor_copy(rsb[:], psX[:, 0:BT])
                eng.drain().then_inc(h, 1)
            ph["vector"].append(statbv2)

            h_mx, v_mx = INC("o_r1", 1)

            def lnmax(eng, v_bv2=v_bv2, h=h_mx):
                eng.wait_ge(prog.sems["o_r1"][0], v_bv2)
                for c in range(8):
                    eng.tensor_tensor(zb[:], xp[:, c * BT:c * BT + BT], mub[:],
                                      ALU.subtract)
                    eng.tensor_tensor(zb[:], zb[:], rsb[:], ALU.mult)
                    eng.tensor_reduce(mcol[:], zb[:], AX.X, ALU.max,
                                      apply_absolute_value=True)
                    eng.drain()
                    eng.tensor_tensor(macc[:], macc[:], mcol[:], ALU.max)
                    eng.drain()
                eng.nop().then_inc(h, 1)
            ph["vector"].append(lnmax)

            _snap = {n: prog.sems[n][1] for n in
                     ("o_ld", "x_ld0", "x_ld1", "w_ws0", "w_ws1") if n in prog.sems}

            def sfin(eng, _snap=_snap):
                for n, v in _snap.items():
                    eng.wait_ge(prog.sems[n][0], v)
            ph["sync"].append(sfin)

            _snap2 = {n: prog.sems[n][1] for n in ("o_st0", "o_st1")
                      if n in prog.sems}

            def gfin(eng, _snap2=_snap2):
                for n, v in _snap2.items():
                    eng.wait_ge(prog.sems[n][0], v)
            ph["gpsimd"].append(gfin)

        for b in range(B):
            for blk in range(NBLK):
                emit_block(b, blk)

        phf = prog.phase()
        h_ftp, v_ftp = INC("f_tp", 1)
        h_fvp, v_fvp = INC("f_vp", 1)
        h_fst, v_fst = INC("f_st", 16)

        def fin_t(eng, h=h_ftp):
            eng.transpose(psT[0:1, 0:128], macc[:], identf[:]).then_inc(h, 1)
        phf["tensor"].append(fin_t)

        def fin_v(eng, h=h_fvp, v_ftp=v_ftp):
            eng.wait_ge(prog.sems["f_tp"][0], v_ftp)
            eng.tensor_reduce(m1s[:], psT[0:1, 0:128], AX.X, ALU.max)
            eng.drain().then_inc(h, 1)
        phf["vector"].append(fin_v)

        def fin_g(eng, h=h_fst, v_fvp=v_fvp, v_fst=v_fst):
            eng.wait_ge(prog.sems["f_vp"][0], v_fvp)
            eng.dma_start(out=m1[:], in_=m1s[:]).then_inc(h, 16)
            eng.wait_ge(h, v_fst)
        phf["gpsimd"].append(fin_g)

        prog.emit(block)
    return nc



_B1_SCRATCH_KIND = "Internal"
_B1_DEBUG_YQ = False


def build_b1():
    nc = bass.Bass("TRN2", target_bir_lowering=False,
                   detect_race_conditions=False)
    xpT = nc.dram_tensor("xpT", [B, D, S], F32, kind="ExternalInput").ap()
    m1a = nc.dram_tensor("m1a", [1, 8], F32, kind="ExternalInput").ap()
    w_up = nc.dram_tensor("w_up", [DF, D], F32, kind="ExternalInput").ap()
    w_gate = nc.dram_tensor("w_gate", [DF, D], F32, kind="ExternalInput").ap()
    up_raw = nc.dram_tensor("up_raw", [DF, TOK], F32, kind="ExternalOutput").ap()
    gt_raw = nc.dram_tensor("gt_raw", [DF, TOK], F32, kind="ExternalOutput").ap()
    m2 = nc.dram_tensor("m2", [1, 1], F32, kind="ExternalOutput").ap()
    m3 = nc.dram_tensor("m3", [1, 1], F32, kind="ExternalOutput").ap()
    if _B1_DEBUG_YQ:
        yqd = nc.dram_tensor("yqd", [128, 8 * 1024], F32,
                             kind="ExternalOutput").ap()
    upT_h = nc.dram_tensor("upT_h", [128, 24 * 1024], F32R, kind=_B1_SCRATCH_KIND).ap()
    upT_l = nc.dram_tensor("upT_l", [128, 24 * 1024], F32R, kind=_B1_SCRATCH_KIND).ap()
    gtT_h = nc.dram_tensor("gtT_h", [128, 24 * 1024], F32R, kind="Internal").ap()
    gtT_l = nc.dram_tensor("gtT_l", [128, 24 * 1024], F32R, kind="Internal").ap()

    BT = 512
    TB = 1024
    NSL = TB // BT

    with contextlib.ExitStack() as ctx:
        E = ctx.enter_context
        identf = E(nc.sbuf_tensor("identf", [128, 128], F32))
        onescol = E(nc.sbuf_tensor("onescol", [128, 1], F32))
        onesr = E(nc.sbuf_tensor("onesr", [1, 128], F32))
        epsc = E(nc.sbuf_tensor("epsc", [1, 1], F32))
        sc = E(nc.sbuf_tensor("sc", [1, 16], F32))
        s1col = E(nc.sbuf_tensor("s1col", [128, 1], F32))
        wbuf = E(nc.sbuf_tensor("wbuf", [128, 2 * 1024], F32))
        wg = E(nc.sbuf_tensor("wg", [128, 2 * 32], F32))
        wq = E(nc.sbuf_tensor("wq", [128, 1024], I16))
        wdq = E(nc.sbuf_tensor("wdq", [128, 2 * 1024], F32))
        wth = E(nc.sbuf_tensor("wth", [128, 256], F32R))
        wtl = E(nc.sbuf_tensor("wtl", [128, 256], F32R))
        xpb = E(nc.sbuf_tensor("xpb", [128, 8 * BT], F32))
        sqb = E(nc.sbuf_tensor("sqb", [128, 2 * BT], F32))
        rows = E(nc.sbuf_tensor("rows", [1, 16 * BT], F32))
        mub = E(nc.sbuf_tensor("mub", [128, BT], F32))
        rsb = E(nc.sbuf_tensor("rsb", [128, BT], F32))
        zb = E(nc.sbuf_tensor("zb", [128, BT], F32))
        yqi = E(nc.sbuf_tensor("yqi", [128, BT], I16))
        yqh = E(nc.sbuf_tensor("yqh", [128, 8 * TB], F32R))
        yql = E(nc.sbuf_tensor("yql", [128, 8 * TB], F32R))
        wsh = E(nc.sbuf_tensor("wsh", [128, 2 * 1024], F32R))
        wsl = E(nc.sbuf_tensor("wsl", [128, 2 * 1024], F32R))
        est = E(nc.sbuf_tensor("est", [128, 4 * BT], F32))
        macc = E(nc.sbuf_tensor("macc", [128, 2], F32))
        mcol = E(nc.sbuf_tensor("mcol", [128, 1], F32))
        mrow = E(nc.sbuf_tensor("mrow", [1, 2], F32))
        psA = E(nc.psum_tensor("psA", [128, 512], F32))
        psB = E(nc.psum_tensor("psB", [128, 512], F32))
        psC = E(nc.psum_tensor("psC", [128, 512], F32))
        psD = E(nc.psum_tensor("psD", [128, 512], F32))
        psT = E(nc.psum_tensor("psT", [128, 128], F32))
        psT2 = E(nc.psum_tensor("psT2", [128, 128], F32))
        psS = E(nc.psum_tensor("psS", [1, 512], F32))
        psS2 = E(nc.psum_tensor("psS2", [1, 512], F32))
        block = E(nc.Block())

        prog = Prog(nc, ctx)

        def INC(name, n=1):
            s = prog.sem(name)
            s[1] += n
            return s[0], s[1]

        ph0 = prog.phase()

        def p0(g):
            h, _ = INC("g0", 6)
            g.memset(identf[:], 0.0).then_inc(h, 1)
            g.wait_ge(h, 1)
            g.affine_select(out=identf[:], in_=identf[:], compare_op=ALU.not_equal,
                            fill=1.0, base=0, pattern=[[-1, 128]],
                            channel_multiplier=1).then_inc(h, 1)
            g.memset(onescol[:], 1.0).then_inc(h, 1)
            g.memset(onesr[:], 1.0).then_inc(h, 1)
            g.memset(epsc[:], EPS).then_inc(h, 1)
            g.memset(macc[:], 0.0).then_inc(h, 1)
            g.wait_ge(h, 6)
        ph0["gpsimd"].append(p0)

        def p0s(sync):
            h, v = INC("s_ld", 16)
            sync.dma_start(out=sc[:, 8:16], in_=m1a[:]).then_inc(h, 16)
            sync.wait_ge(h, v)
        ph0["sync"].append(p0s)

        ph0b = prog.phase()
        h_svp1, v_svp1 = INC("s_vp", 1)
        h_stp1, v_stp1 = INC("s_tp", 1)
        h_svp2, v_svp2 = INC("s_vp", 1)

        def p0v(eng, h=h_svp1):
            eng.tensor_reduce(sc[:, 0:1], sc[:, 8:16], AX.X, ALU.max)
            eng.drain()
            eng.tensor_scalar(sc[:, 1:2], sc[:, 0:1], 1.0 / QMAX, 1e-12,
                              ALU.mult, ALU.max)
            eng.drain()
            eng.reciprocal(sc[:, 2:3], sc[:, 1:2])
            eng.drain()
            eng.tensor_tensor(sc[:, 3:4], sc[:, 1:2], sc[:, 2:3], ALU.mult)
            eng.drain()
            eng.tensor_scalar(sc[:, 3:4], sc[:, 3:4], -1.0, 2.0, ALU.mult,
                              ALU.add)
            eng.drain()
            eng.tensor_tensor(sc[:, 2:3], sc[:, 2:3], sc[:, 3:4], ALU.mult)
            eng.drain().then_inc(h, 1)
        ph0b["vector"].append(p0v)

        def p0t(eng, h=h_stp1, v_svp1=v_svp1):
            eng.wait_ge(prog.sems["s_vp"][0], v_svp1)
            eng.matmul(psT[:, 0:1], onesr[:], sc[0:1, 1:2],
                       start=True, stop=True).then_inc(h, 1)
        ph0b["tensor"].append(p0t)

        def p0v2(eng, h=h_svp2, v_stp1=v_stp1):
            eng.wait_ge(prog.sems["s_tp"][0], v_stp1)
            eng.tensor_copy(s1col[:], psT[:, 0:1])
            eng.drain().then_inc(h, 1)
        ph0b["vector"].append(p0v2)

        ph1 = prog.phase()
        wplan = []
        for wi, (wsrc, dsth, dstl) in enumerate(
                ((w_up, upT_h, upT_l), (w_gate, gtT_h, gtT_l))):
            for r in range(24):
                wplan.append((wsrc, dsth, dstl, r))
        for i, (wsrc, dsth, dstl, r) in enumerate(wplan):
            buf = (i % 2) * 1024
            ldn = "b_ld%d" % (i % 2)
            h_ld, v_ld = INC(ldn, 16)

            def load(eng, i=i, wsrc=wsrc, r=r, buf=buf, h=h_ld):
                if i >= 2:
                    eng.wait_ge(prog.sems["b_dq"][0], i - 1)
                eng.dma_start(out=wbuf[:, buf:buf + 1024],
                              in_=wsrc[r * 128:r * 128 + 128, :]).then_inc(h, 16)
            ph1["sync"].append(load)

            h_dq, v_dq = INC("b_dq", 1)

            def dequant(eng, i=i, buf=buf, v_ld=v_ld, h=h_dq, ldn=ldn):
                eng.wait_ge(prog.sems[ldn][0], v_ld)
                if i >= 2:
                    eng.wait_ge(prog.sems["b_tp"][0], (i - 1) * 8)
                gb = (i % 2) * 32
                wb3 = wbuf[:, buf:buf + 1024].rearrange("p (g c) -> p g c", g=32)
                eng.tensor_reduce(
                    wg[:, gb:gb + 32].rearrange("p (g o) -> p g o", o=1),
                    wb3, AX.X, ALU.max, apply_absolute_value=True)
                eng.drain()
                eng.tensor_scalar(wg[:, gb:gb + 32], wg[:, gb:gb + 32],
                                  1.0 / 7.0, 1e-12, ALU.mult, ALU.max)
                eng.drain()
                eng.reciprocal(zb[:, 0:32], wg[:, gb:gb + 32])
                eng.drain()
                eng.tensor_tensor(zb[:, 32:64], wg[:, gb:gb + 32], zb[:, 0:32],
                                  ALU.mult)
                eng.drain()
                eng.tensor_scalar(zb[:, 32:64], zb[:, 32:64], -1.0, 2.0,
                                  ALU.mult, ALU.add)
                eng.drain()
                eng.tensor_tensor(zb[:, 0:32], zb[:, 0:32], zb[:, 32:64],
                                  ALU.mult)
                eng.drain()
                eng.tensor_tensor(
                    wdq[:, buf:buf + 1024].rearrange("p (g c) -> p g c", g=32),
                    wb3,
                    zb[:, 0:32].rearrange("p (g o) -> p g o", o=1).broadcast_to(
                        (128, 32, 32)), ALU.mult)
                eng.tensor_copy(wq[:], wdq[:, buf:buf + 1024])
                eng.tensor_scalar(wq[:], wq[:], 7.0, -8.0, ALU.min, ALU.max)
                eng.tensor_tensor(
                    wdq[:, buf:buf + 1024].rearrange("p (g c) -> p g c", g=32),
                    wq.rearrange("p (g c) -> p g c", g=32),
                    wg[:, gb:gb + 32].rearrange("p (g o) -> p g o",
                                                o=1).broadcast_to((128, 32, 32)),
                    ALU.mult)
                eng.tensor_scalar(wdq[:, buf:buf + 1024], wdq[:, buf:buf + 1024],
                                  s1col[:], None, ALU.mult)
                eng.drain().then_inc(h, 1)
            prog.sem("b_tp")
            ph1["vector"].append(dequant)

            for k in range(8):
                h_tp, v_tp = INC("b_tp", 1)

                def tpose(eng, k=k, buf=buf, v_dq=v_dq, h=h_tp,
                          ntp=prog.sems["b_tp"][1]):
                    eng.wait_ge(prog.sems["b_dq"][0], v_dq)
                    if ntp >= 3:
                        eng.wait_ge(prog.sems["b_sp"][0], ntp - 2)
                    p = psT if ntp % 2 == 1 else psT2
                    eng.transpose(p[:], wdq[:, buf + k * 128:buf + k * 128 + 128],
                                  identf[:]).then_inc(h, 1)
                prog.sem("b_sp")
                ph1["tensor"].append(tpose)

                sbuf2 = (prog.sems["b_sp"][1] % 2) * 128
                gpn = "b_gp%d" % (prog.sems["b_sp"][1] % 2)
                v_gp_need = prog.sem(gpn)[1]
                h_sp, v_sp = INC("b_sp", 1)

                def split(eng, v_tp=v_tp, sbuf2=sbuf2, h=h_sp, gpn=gpn,
                          v_gp_need=v_gp_need, ntp=prog.sems["b_tp"][1]):
                    eng.wait_ge(prog.sems["b_tp"][0], v_tp)
                    if v_gp_need > 0:
                        eng.wait_ge(prog.sems[gpn][0], v_gp_need)
                    p = psT if ntp % 2 == 1 else psT2
                    eng.tensor_copy(wth[:, sbuf2:sbuf2 + 128], p[:])
                    eng.tensor_tensor(wtl[:, sbuf2:sbuf2 + 128], p[:],
                                      wth[:, sbuf2:sbuf2 + 128],
                                      ALU.subtract).then_inc(h, 1)
                ph1["vector"].append(split)

                h_gp, v_gp = INC(gpn, 32)
                col = r * 1024 + k * 128

                def store(eng, v_sp=v_sp, sbuf2=sbuf2, col=col, h=h_gp,
                          dsth=dsth, dstl=dstl):
                    eng.wait_ge(prog.sems["b_sp"][0], v_sp)
                    eng.dma_start(out=dsth[:, col:col + 128],
                                  in_=wth[:, sbuf2:sbuf2 + 128]).then_inc(h, 16)
                    eng.dma_start(out=dstl[:, col:col + 128],
                                  in_=wtl[:, sbuf2:sbuf2 + 128]).then_inc(h, 16)
                ph["gpsimd"].append(store) if False else ph1["gpsimd"].append(store)

        _snap = {n: prog.sems[n][1] for n in
                 ("b_ld0", "b_ld1", "b_gp0", "b_gp1") if n in prog.sems}

        def p1fin_s(eng, _snap=_snap):
            for n in ("b_ld0", "b_ld1"):
                if n in _snap:
                    eng.wait_ge(prog.sems[n][0], _snap[n])
        ph1["sync"].append(p1fin_s)

        def p1fin_g(eng, _snap=_snap):
            for n in ("b_gp0", "b_gp1"):
                if n in _snap:
                    eng.wait_ge(prog.sems[n][0], _snap[n])
        ph1["gpsimd"].append(p1fin_g)

        def emit_xq(tb):
            for sl in range(NSL):
                t0 = tb * TB + sl * BT
                b, s0 = divmod(t0, S)
                ph = prog.phase()
                for c in range(8):
                    h_ld, v_ld = INC("q_ld", 16)

                    def ldx(eng, b=b, s0=s0, c=c, h=h_ld):
                        eng.dma_start(out=xpb[:, c * BT:c * BT + BT],
                                      in_=xpT[b, c * 128:c * 128 + 128,
                                              s0:s0 + BT]).then_inc(h, 16)
                    ph["sync"].append(ldx)
                v_ld_all = prog.sems["q_ld"][1]

                for c in range(8):
                    v_sm_need = max(prog.sem("q_sm")[1] - 1, 0)
                    h_sq, v_sq = INC("q_sq", 1)

                    def dosq(eng, c=c, v_ld_all=v_ld_all, h=h_sq,
                             v_sm_need=v_sm_need):
                        eng.wait_ge(prog.sems["q_ld"][0], v_ld_all)
                        if c >= 2 and v_sm_need > 0:
                            eng.wait_ge(prog.sems["q_sm"][0], v_sm_need)
                        eng.activation(sqb[:, (c % 2) * BT:(c % 2) * BT + BT],
                                       xpb[:, c * BT:c * BT + BT],
                                       AF.Square).then_inc(h, 1)
                    ph["scalar"].append(dosq)

                    h_sm, v_sm = INC("q_sm", 1)

                    def statmm(eng, c=c, v_sq=v_sq, v_ld_all=v_ld_all, h=h_sm):
                        eng.wait_ge(prog.sems["q_sq"][0], v_sq)
                        eng.matmul(psS[0:1, 0:BT], onescol[:],
                                   xpb[:, c * BT:c * BT + BT],
                                   start=(c == 0), stop=(c == 7))
                        eng.matmul(psS2[0:1, 0:BT], onescol[:],
                                   sqb[:, (c % 2) * BT:(c % 2) * BT + BT],
                                   start=(c == 0), stop=(c == 7)).then_inc(h, 1)
                    ph["tensor"].append(statmm)

                v_sm_all = prog.sems["q_sm"][1]
                h_r1, v_r1 = INC("q_r1", 1)

                def rowstats(eng, v_sm_all=v_sm_all, h=h_r1):
                    eng.wait_ge(prog.sems["q_sm"][0], v_sm_all)
                    eng.tensor_scalar(rows[:, 0:BT], psS[0:1, 0:BT], 1.0 / D,
                                      None, ALU.mult)
                    eng.tensor_scalar(rows[:, BT:2 * BT], psS2[0:1, 0:BT],
                                      1.0 / D, None, ALU.mult)
                    eng.drain()
                    eng.tensor_tensor(rows[:, 2 * BT:3 * BT], rows[:, 0:BT],
                                      rows[:, 0:BT], ALU.mult)
                    eng.drain()
                    eng.tensor_tensor(rows[:, 3 * BT:4 * BT], rows[:, BT:2 * BT],
                                      rows[:, 2 * BT:3 * BT], ALU.subtract)
                    eng.drain().then_inc(h, 1)
                ph["vector"].append(rowstats)

                h_r2, v_r2 = INC("q_r2", 1)

                def rowsqrt(eng, v_r1=v_r1, h=h_r2):
                    eng.wait_ge(prog.sems["q_r1"][0], v_r1)
                    eng.activation(rows[:, 4 * BT:5 * BT], rows[:, 3 * BT:4 * BT],
                                   AF.Sqrt, bias=epsc[:]).then_inc(h, 1)
                ph["scalar"].append(rowsqrt)

                h_r3, v_r3 = INC("q_r1", 1)

                def rowrecip(eng, v_r2=v_r2, h=h_r3):
                    eng.wait_ge(prog.sems["q_r2"][0], v_r2)
                    eng.reciprocal(rows[:, 5 * BT:6 * BT], rows[:, 4 * BT:5 * BT])
                    eng.drain()
                    eng.tensor_tensor(rows[:, 6 * BT:7 * BT],
                                      rows[:, 4 * BT:5 * BT],
                                      rows[:, 5 * BT:6 * BT], ALU.mult)
                    eng.drain()
                    eng.tensor_scalar(rows[:, 6 * BT:7 * BT],
                                      rows[:, 6 * BT:7 * BT], -1.0, 2.0,
                                      ALU.mult, ALU.add)
                    eng.drain()
                    eng.tensor_tensor(rows[:, 5 * BT:6 * BT],
                                      rows[:, 5 * BT:6 * BT],
                                      rows[:, 6 * BT:7 * BT], ALU.mult)
                    eng.drain()
                    eng.tensor_tensor(rows[:, 5 * BT:6 * BT],
                                      rows[:, 5 * BT:6 * BT],
                                      sc[0:1, 2:3].broadcast_to((1, BT)),
                                      ALU.mult)
                    eng.drain()
                    eng.tensor_tensor(rows[:, 7 * BT:8 * BT], rows[:, 0:BT],
                                      rows[:, 5 * BT:6 * BT], ALU.mult)
                    eng.drain()
                    eng.tensor_scalar(rows[:, 7 * BT:8 * BT],
                                      rows[:, 7 * BT:8 * BT], -1.0, None,
                                      ALU.mult)
                    eng.drain().then_inc(h, 1)
                ph["vector"].append(rowrecip)

                h_bc, v_bc = INC("q_bc", 1)

                def statbc(eng, v_r3=v_r3, h=h_bc):
                    eng.wait_ge(prog.sems["q_r1"][0], v_r3)
                    eng.matmul(psC[:, 0:BT], onesr[:], rows[:, 5 * BT:6 * BT],
                               start=True, stop=True)
                    eng.matmul(psD[:, 0:BT], onesr[:], rows[:, 7 * BT:8 * BT],
                               start=True, stop=True).then_inc(h, 1)
                ph["tensor"].append(statbc)

                h_bv, v_bv = INC("q_r1", 1)

                def statbv(eng, v_bc=v_bc, h=h_bv):
                    eng.wait_ge(prog.sems["q_bc"][0], v_bc)
                    eng.tensor_copy(rsb[:], psC[:, 0:BT])
                    eng.tensor_copy(mub[:], psD[:, 0:BT])
                    eng.drain().then_inc(h, 1)
                ph["vector"].append(statbv)

                h_yq, v_yq = INC("q_r1", 1)

                def doyq(eng, sl=sl, v_bv=v_bv, h=h_yq):
                    eng.wait_ge(prog.sems["q_r1"][0], v_bv)
                    for c in range(8):
                        eng.tensor_tensor(zb[:], xpb[:, c * BT:c * BT + BT],
                                          rsb[:], ALU.mult)
                        eng.tensor_tensor(zb[:], zb[:], mub[:], ALU.add)
                        eng.tensor_copy(yqi[:], zb[:])
                        dst = c * TB + sl * BT
                        eng.tensor_copy(yqh[:, dst:dst + BT], yqi[:])
                        eng.tensor_tensor(yql[:, dst:dst + BT], yqi[:],
                                          yqh[:, dst:dst + BT], ALU.subtract)
                    eng.drain().then_inc(h, 1)
                ph["vector"].append(doyq)

                _snap = {"q_ld": prog.sems["q_ld"][1]}

                def sfin(eng, _snap=_snap):
                    eng.wait_ge(prog.sems["q_ld"][0], _snap["q_ld"])
                ph["sync"].append(sfin)

        def emit_gemm(tb, which):
            srch = upT_h if which == 0 else gtT_h
            srcl = upT_l if which == 0 else gtT_l
            dst = up_raw if which == 0 else gt_raw
            ph = prog.phase()
            for of in range(24):
                buf = (of % 2) * 1024
                wsn = "g_ws%d" % (of % 2)
                v_need = prog.sem("g_mm")[1] - 1 if of >= 2 else 0
                h_ws, v_ws = INC(wsn, 32)

                def wload(eng, of=of, buf=buf, h=h_ws, v_need=v_need,
                          srch=srch, srcl=srcl):
                    if v_need > 0:
                        eng.wait_ge(prog.sems["g_mm"][0], v_need)
                    eng.dma_start(out=wsh[:, buf:buf + 1024],
                                  in_=srch[:, of * 1024:of * 1024 + 1024]
                                  ).then_inc(h, 16)
                    eng.dma_start(out=wsl[:, buf:buf + 1024],
                                  in_=srcl[:, of * 1024:of * 1024 + 1024]
                                  ).then_inc(h, 16)
                ph["sync"].append(wload)

                banks = (psA, psB) if of % 2 == 0 else (psC, psD)
                h_mm, v_mm = INC("g_mm", 1)

                def gmm(eng, of=of, buf=buf, banks=banks, v_ws=v_ws, h=h_mm,
                        wsn=wsn, nmm=prog.sems["g_mm"][1]):
                    eng.wait_ge(prog.sems[wsn][0], v_ws)
                    if nmm >= 3:
                        eng.wait_ge(prog.sems["g_ev"][0], (nmm - 2) * 2)
                    for k in range(8):
                        wcol = buf + k * 128
                        for sl in range(NSL):
                            bank = banks[sl]
                            eng.matmul(bank[:, 0:BT], wsh[:, wcol:wcol + 128],
                                       yqh[:, k * TB + sl * BT:
                                           k * TB + sl * BT + BT],
                                       start=(k == 0), stop=False)
                            eng.matmul(bank[:, 0:BT], wsh[:, wcol:wcol + 128],
                                       yql[:, k * TB + sl * BT:
                                           k * TB + sl * BT + BT],
                                       start=False, stop=False)
                            mm = eng.matmul(bank[:, 0:BT],
                                            wsl[:, wcol:wcol + 128],
                                            yqh[:, k * TB + sl * BT:
                                                k * TB + sl * BT + BT],
                                            start=False, stop=(k == 7))
                            if k == 7 and sl == NSL - 1:
                                mm.then_inc(h, 1)
                prog.sem("g_ev")
                ph["tensor"].append(gmm)

                for sl in range(NSL):
                    h_ev, v_ev = INC("g_ev", 1)
                    stn = "g_st%d" % (prog.sems["g_ev"][1] % 4)
                    v_st_need = prog.sem(stn)[1]

                    def gev(eng, of=of, sl=sl, banks=banks, v_mm=v_mm, h=h_ev,
                            stn=stn, v_st_need=v_st_need,
                            nev=prog.sems["g_ev"][1], which=which):
                        eng.wait_ge(prog.sems["g_mm"][0], v_mm)
                        if v_st_need > 0:
                            eng.wait_ge(prog.sems[stn][0], v_st_need)
                        bank = banks[sl]
                        eb = (nev % 4) * BT
                        eng.tensor_copy(est[:, eb:eb + BT], bank[:, 0:BT])
                        eng.tensor_reduce(mcol[:], est[:, eb:eb + BT], AX.X,
                                          ALU.max, apply_absolute_value=True)
                        eng.drain()
                        eng.tensor_tensor(macc[:, which:which + 1],
                                          macc[:, which:which + 1], mcol[:],
                                          ALU.max)
                        eng.drain().then_inc(h, 1)
                    ph["vector"].append(gev)

                    h_st, v_st = INC(stn, 16)

                    def gst(eng, tb=tb, of=of, sl=sl, v_ev=v_ev, h=h_st,
                            dst=dst, nev=prog.sems["g_ev"][1]):
                        eng.wait_ge(prog.sems["g_ev"][0], v_ev)
                        eb = (nev % 4) * BT
                        eng.dma_start(out=dst[of * 128:of * 128 + 128,
                                              tb * TB + sl * BT:
                                              tb * TB + sl * BT + BT],
                                      in_=est[:, eb:eb + BT]).then_inc(h, 16)
                    ph["gpsimd"].append(gst)

            _snap = {n: prog.sems[n][1] for n in
                     ("g_ws0", "g_ws1") if n in prog.sems}

            def sfin(eng, _snap=_snap):
                for n, v in _snap.items():
                    eng.wait_ge(prog.sems[n][0], v)
            ph["sync"].append(sfin)

            _snap2 = {n: prog.sems[n][1] for n in
                      ("g_st0", "g_st1", "g_st2", "g_st3") if n in prog.sems}

            def gfin(eng, _snap2=_snap2):
                for n, v in _snap2.items():
                    eng.wait_ge(prog.sems[n][0], v)
            ph["gpsimd"].append(gfin)

        for tb in range(TOK // TB):
            emit_xq(tb)
            emit_gemm(tb, 0)
            emit_gemm(tb, 1)

        if _B1_DEBUG_YQ:
            phd = prog.phase()
            h_dv, v_dv = INC("d_vp", 1)
            h_dg, v_dg = INC("d_gp", 8 * 16)

            def dump_v(eng, h=h_dv):
                for c in range(8):
                    eng.tensor_tensor(xpb[:, 0:1024] if False else
                                      yqh[:, c * 1024:c * 1024 + 1024],
                                      yqh[:, c * 1024:c * 1024 + 1024],
                                      yql[:, c * 1024:c * 1024 + 1024],
                                      ALU.add)
                eng.drain().then_inc(h, 1)
            phd["vector"].append(dump_v)

            def dump_g(eng, h=h_dg, v_dv=v_dv, v_dg=v_dg):
                eng.wait_ge(prog.sems["d_vp"][0], v_dv)
                for c in range(8):
                    eng.dma_start(out=yqd[:, c * 1024:c * 1024 + 1024],
                                  in_=yqh[:, c * 1024:c * 1024 + 1024]
                                  ).then_inc(h, 16)
                eng.wait_ge(h, v_dg)
            phd["gpsimd"].append(dump_g)

        phf = prog.phase()
        h_ftp, v_ftp = INC("f_tp", 1)
        h_fvp, v_fvp = INC("f_vp", 1)
        h_fst, v_fst = INC("f_st", 32)

        def fin_t(eng, h=h_ftp):
            eng.transpose(psT[0:1, 0:128], macc[:, 0:1], identf[:])
            eng.transpose(psT2[0:1, 0:128], macc[:, 1:2],
                          identf[:]).then_inc(h, 1)
        phf["tensor"].append(fin_t)

        def fin_v(eng, h=h_fvp, v_ftp=v_ftp):
            eng.wait_ge(prog.sems["f_tp"][0], v_ftp)
            eng.tensor_reduce(mrow[:, 0:1], psT[0:1, 0:128], AX.X, ALU.max)
            eng.tensor_reduce(mrow[:, 1:2], psT2[0:1, 0:128], AX.X, ALU.max)
            eng.drain().then_inc(h, 1)
        phf["vector"].append(fin_v)

        def fin_g(eng, h=h_fst, v_fvp=v_fvp, v_fst=v_fst):
            eng.wait_ge(prog.sems["f_vp"][0], v_fvp)
            eng.dma_start(out=m2[:], in_=mrow[:, 0:1]).then_inc(h, 16)
            eng.dma_start(out=m3[:], in_=mrow[:, 1:2]).then_inc(h, 16)
            eng.wait_ge(h, v_fst)
        phf["gpsimd"].append(fin_g)

        prog.emit(block)
    return nc




def _v_newton_recip(eng, out_ap, in_ap, t_ap):
    eng.reciprocal(out_ap, in_ap)
    eng.drain()
    eng.tensor_tensor(t_ap, in_ap, out_ap, ALU.mult)
    eng.drain()
    eng.tensor_scalar(t_ap, t_ap, -1.0, 2.0, ALU.mult, ALU.add)
    eng.drain()
    eng.tensor_tensor(out_ap, out_ap, t_ap, ALU.mult)
    eng.drain()



def build_b2():
    nc = bass.Bass("TRN2", target_bir_lowering=False,
                   detect_race_conditions=False)
    gt_raw = nc.dram_tensor("gt_raw", [DF, TOK], F32, kind="ExternalInput").ap()
    m3a = nc.dram_tensor("m3a", [1, 8], F32, kind="ExternalInput").ap()
    ab = nc.dram_tensor("ab", [DF, TOK], F32, kind="ExternalOutput").ap()
    m4 = nc.dram_tensor("m4", [1, 1], F32, kind="ExternalOutput").ap()

    W = 2048
    NT = (DF // 128) * (TOK // W)

    with contextlib.ExitStack() as ctx:
        E = ctx.enter_context
        identf = E(nc.sbuf_tensor("identf", [128, 128], F32))
        onesr = E(nc.sbuf_tensor("onesr", [1, 128], F32))
        sc = E(nc.sbuf_tensor("sc", [1, 16], F32))
        r3col = E(nc.sbuf_tensor("r3col", [128, 1], F32))
        s3col = E(nc.sbuf_tensor("s3col", [128, 1], F32))
        gbuf = E(nc.sbuf_tensor("gbuf", [128, 2 * W], F32))
        abuf = E(nc.sbuf_tensor("abuf", [128, 2 * W], F32))
        ai = E(nc.sbuf_tensor("ai", [128, W], I16))
        bu = E(nc.sbuf_tensor("bu", [128, W], U16))
        sg = E(nc.sbuf_tensor("sg", [128, W], F32))
        macc = E(nc.sbuf_tensor("macc", [128, 1], F32))
        mcol = E(nc.sbuf_tensor("mcol", [128, 1], F32))
        m4s = E(nc.sbuf_tensor("m4s", [1, 1], F32))
        psT = E(nc.psum_tensor("psT", [128, 128], F32))
        block = E(nc.Block())
        prog = Prog(nc, ctx)

        def INC(name, n=1):
            s = prog.sem(name)
            s[1] += n
            return s[0], s[1]

        ph0 = prog.phase()
        h_sld, v_sld = INC("s_ld", 16)

        def p0g(g):
            h, _ = INC("g0", 4)
            g.memset(identf[:], 0.0).then_inc(h, 1)
            g.wait_ge(h, 1)
            g.affine_select(out=identf[:], in_=identf[:], compare_op=ALU.not_equal,
                            fill=1.0, base=0, pattern=[[-1, 128]],
                            channel_multiplier=1).then_inc(h, 1)
            g.memset(onesr[:], 1.0).then_inc(h, 1)
            g.memset(macc[:], 0.0).then_inc(h, 1)
            g.wait_ge(h, 4)
        ph0["gpsimd"].append(p0g)

        def p0s(sync, h=h_sld, v=v_sld):
            sync.dma_start(out=sc[:, 8:16], in_=m3a[:]).then_inc(h, 16)
            sync.wait_ge(h, v)
        ph0["sync"].append(p0s)

        ph0b = prog.phase()
        h_sv, v_sv = INC("s_vp", 1)
        h_st2, v_st2 = INC("s_tp", 1)
        h_sv2, v_sv2 = INC("s_vp", 1)

        def p0v(eng, h=h_sv):
            eng.tensor_reduce(sc[:, 0:1], sc[:, 8:16], AX.X, ALU.max)
            eng.drain()
            eng.tensor_scalar(sc[:, 1:2], sc[:, 0:1], 1.0 / QMAX, 1e-12,
                              ALU.mult, ALU.max)
            eng.drain()
            _v_newton_recip(eng, sc[:, 2:3], sc[:, 1:2], sc[:, 3:4])
            eng.nop().then_inc(h, 1)
        ph0b["vector"].append(p0v)

        def p0t(eng, h=h_st2, v_sv=v_sv):
            eng.wait_ge(prog.sems["s_vp"][0], v_sv)
            eng.matmul(psT[:, 0:1], onesr[:], sc[0:1, 2:3], start=True, stop=True)
            eng.matmul(psT[:, 1:2], onesr[:], sc[0:1, 1:2], start=True,
                       stop=True).then_inc(h, 1)
        ph0b["tensor"].append(p0t)

        def p0v2(eng, h=h_sv2, v=v_st2):
            eng.wait_ge(prog.sems["s_tp"][0], v)
            eng.tensor_copy(r3col[:], psT[:, 0:1])
            eng.tensor_copy(s3col[:], psT[:, 1:2])
            eng.drain().then_inc(h, 1)
        ph0b["vector"].append(p0v2)

        ph = prog.phase()
        for i in range(NT):
            r, cb = divmod(i, TOK // W)
            buf = (i % 2) * W
            ldn = "e_ld%d" % (i % 2)
            v_va_need = max(prog.sem("e_va")[1] - 1, 0)
            h_ld, v_ld = INC(ldn, 16)

            def load(eng, r=r, cb=cb, buf=buf, h=h_ld, v_va_need=v_va_need):
                if v_va_need > 0:
                    eng.wait_ge(prog.sems["e_va"][0], v_va_need)
                eng.dma_start(out=gbuf[:, buf:buf + W],
                              in_=gt_raw[r * 128:r * 128 + 128,
                                         cb * W:cb * W + W]).then_inc(h, 16)
            ph["sync"].append(load)

            v_vb_prev = prog.sem("e_vb")[1]
            h_va, v_va = INC("e_va", 1)

            def va(eng, buf=buf, v_ld=v_ld, h=h_va, ldn=ldn,
                   v_vb_prev=v_vb_prev):
                eng.wait_ge(prog.sems[ldn][0], v_ld)
                if v_vb_prev > 0:
                    eng.wait_ge(prog.sems["e_vb"][0], v_vb_prev)
                eng.tensor_scalar(ai[:], gbuf[:, buf:buf + W], r3col[:], None,
                                  ALU.mult)
                eng.drain().then_inc(h, 1)
            ph["vector"].append(va)

            h_sg, v_sg = INC("e_sg", 1)

            def dosig(eng, v_va=v_va, h=h_sg, v_vb_prev=v_vb_prev):
                eng.wait_ge(prog.sems["e_va"][0], v_va)
                if v_vb_prev > 0:
                    eng.wait_ge(prog.sems["e_vb"][0], v_vb_prev)
                eng.activation(sg[:], ai[:], AF.Sigmoid,
                               scale=s3col[:]).then_inc(h, 1)
            ph["scalar"].append(dosig)

            stn = "e_st%d" % (i % 2)
            v_st_need = prog.sem(stn)[1]
            h_vb, v_vb = INC("e_vb", 1)

            def vb(eng, buf=buf, v_sg=v_sg, h=h_vb, stn=stn,
                   v_st_need=v_st_need):
                eng.wait_ge(prog.sems["e_sg"][0], v_sg)
                if v_st_need > 0:
                    eng.wait_ge(prog.sems[stn][0], v_st_need)
                eng.tensor_scalar(bu[:], sg[:], 65536.0, None, ALU.mult)
                eng.tensor_tensor(abuf[:, buf:buf + W], ai[:], bu[:], ALU.mult)
                eng.tensor_reduce(mcol[:], abuf[:, buf:buf + W], AX.X, ALU.max,
                                  apply_absolute_value=True)
                eng.drain()
                eng.tensor_tensor(macc[:], macc[:], mcol[:], ALU.max)
                eng.drain().then_inc(h, 1)
            ph["vector"].append(vb)

            h_stt, v_stt = INC(stn, 16)

            def store(eng, r=r, cb=cb, buf=buf, v_vb=v_vb, h=h_stt):
                eng.wait_ge(prog.sems["e_vb"][0], v_vb)
                eng.dma_start(out=ab[r * 128:r * 128 + 128, cb * W:cb * W + W],
                              in_=abuf[:, buf:buf + W]).then_inc(h, 16)
            ph["gpsimd"].append(store)

        _snap = {n: prog.sems[n][1] for n in ("e_ld0", "e_ld1") if n in prog.sems}

        def sfin(eng, _snap=_snap):
            for n, v in _snap.items():
                eng.wait_ge(prog.sems[n][0], v)
        ph["sync"].append(sfin)

        _snap2 = {n: prog.sems[n][1] for n in ("e_st0", "e_st1") if n in prog.sems}

        def gfin(eng, _snap2=_snap2):
            for n, v in _snap2.items():
                eng.wait_ge(prog.sems[n][0], v)
        ph["gpsimd"].append(gfin)

        phf = prog.phase()
        h_ftp, v_ftp = INC("f_tp", 1)
        h_fvp, v_fvp = INC("f_vp", 1)
        h_fst, v_fst = INC("f_st", 16)

        def fin_t(eng, h=h_ftp):
            eng.transpose(psT[0:1, 0:128], macc[:], identf[:]).then_inc(h, 1)
        phf["tensor"].append(fin_t)

        def fin_v(eng, h=h_fvp, v_ftp=v_ftp):
            eng.wait_ge(prog.sems["f_tp"][0], v_ftp)
            eng.tensor_reduce(m4s[:], psT[0:1, 0:128], AX.X, ALU.max)
            eng.drain().then_inc(h, 1)
        phf["vector"].append(fin_v)

        def fin_g(eng, h=h_fst, v_fvp=v_fvp, v_fst=v_fst):
            eng.wait_ge(prog.sems["f_vp"][0], v_fvp)
            eng.dma_start(out=m4[:], in_=m4s[:]).then_inc(h, 16)
            eng.wait_ge(h, v_fst)
        phf["gpsimd"].append(fin_g)

        prog.emit(block)
    return nc



def build_b3():
    nc = bass.Bass("TRN2", target_bir_lowering=False,
                   detect_race_conditions=False)
    abt = nc.dram_tensor("ab", [DF, TOK], F32, kind="ExternalInput").ap()
    up_raw = nc.dram_tensor("up_raw", [DF, TOK], F32, kind="ExternalInput").ap()
    m2a = nc.dram_tensor("m2a", [1, 8], F32, kind="ExternalInput").ap()
    m3a = nc.dram_tensor("m3a", [1, 8], F32, kind="ExternalInput").ap()
    m4a = nc.dram_tensor("m4a", [1, 8], F32, kind="ExternalInput").ap()
    cu = nc.dram_tensor("cu", [DF, TOK], F32, kind="ExternalOutput").ap()
    m5 = nc.dram_tensor("m5", [1, 1], F32, kind="ExternalOutput").ap()

    W = 2048
    NT = (DF // 128) * (TOK // W)

    with contextlib.ExitStack() as ctx:
        E = ctx.enter_context
        identf = E(nc.sbuf_tensor("identf", [128, 128], F32))
        onesr = E(nc.sbuf_tensor("onesr", [1, 128], F32))
        sc = E(nc.sbuf_tensor("sc", [1, 40], F32))
        fccol = E(nc.sbuf_tensor("fccol", [128, 1], F32))
        r2col = E(nc.sbuf_tensor("r2col", [128, 1], F32))
        abuf = E(nc.sbuf_tensor("abuf", [128, 2 * W], F32))
        ubuf = E(nc.sbuf_tensor("ubuf", [128, 2 * W], F32))
        cbuf = E(nc.sbuf_tensor("cbuf", [128, 2 * W], F32))
        ci = E(nc.sbuf_tensor("ci", [128, W], I16))
        ui = E(nc.sbuf_tensor("ui", [128, W], I16))
        macc = E(nc.sbuf_tensor("macc", [128, 1], F32))
        mcol = E(nc.sbuf_tensor("mcol", [128, 1], F32))
        m5s = E(nc.sbuf_tensor("m5s", [1, 1], F32))
        psT = E(nc.psum_tensor("psT", [128, 128], F32))
        block = E(nc.Block())
        prog = Prog(nc, ctx)

        def INC(name, n=1):
            s = prog.sem(name)
            s[1] += n
            return s[0], s[1]

        ph0 = prog.phase()
        h_sld, v_sld = INC("s_ld", 48)

        def p0g(g):
            h, _ = INC("g0", 4)
            g.memset(identf[:], 0.0).then_inc(h, 1)
            g.wait_ge(h, 1)
            g.affine_select(out=identf[:], in_=identf[:], compare_op=ALU.not_equal,
                            fill=1.0, base=0, pattern=[[-1, 128]],
                            channel_multiplier=1).then_inc(h, 1)
            g.memset(onesr[:], 1.0).then_inc(h, 1)
            g.memset(macc[:], 0.0).then_inc(h, 1)
            g.wait_ge(h, 4)
        ph0["gpsimd"].append(p0g)

        def p0s(sync, h=h_sld, v=v_sld):
            sync.dma_start(out=sc[:, 8:16], in_=m2a[:]).then_inc(h, 16)
            sync.dma_start(out=sc[:, 16:24], in_=m3a[:]).then_inc(h, 16)
            sync.dma_start(out=sc[:, 24:32], in_=m4a[:]).then_inc(h, 16)
            sync.wait_ge(h, v)
        ph0["sync"].append(p0s)

        ph0b = prog.phase()
        h_sv, v_sv = INC("s_vp", 1)
        h_st2, v_st2 = INC("s_tp", 1)
        h_sv2, v_sv2 = INC("s_vp", 1)

        def p0v(eng, h=h_sv):
            eng.tensor_reduce(sc[:, 0:1], sc[:, 8:16], AX.X, ALU.max)
            eng.tensor_reduce(sc[:, 1:2], sc[:, 16:24], AX.X, ALU.max)
            eng.tensor_reduce(sc[:, 2:3], sc[:, 24:32], AX.X, ALU.max)
            eng.drain()
            eng.tensor_scalar(sc[:, 3:4], sc[:, 0:1], 1.0 / QMAX, 1e-12,
                              ALU.mult, ALU.max)
            eng.tensor_scalar(sc[:, 4:5], sc[:, 1:2], 1.0 / QMAX, 1e-12,
                              ALU.mult, ALU.max)
            eng.drain()
            _v_newton_recip(eng, sc[:, 5:6], sc[:, 3:4], sc[:, 6:7])
            eng.tensor_scalar(sc[:, 7:8], sc[:, 4:5], 1.0 / 65536.0, None,
                              ALU.mult)
            eng.drain()
            eng.tensor_tensor(sc[:, 32:33], sc[:, 7:8], sc[:, 2:3], ALU.mult)
            eng.drain()
            eng.tensor_scalar(sc[:, 33:34], sc[:, 32:33], 1.0 / QMAX, 1e-12,
                              ALU.mult, ALU.max)
            eng.drain()
            _v_newton_recip(eng, sc[:, 34:35], sc[:, 33:34], sc[:, 6:7])
            eng.tensor_tensor(sc[:, 35:36], sc[:, 7:8], sc[:, 34:35],
                              ALU.mult)
            eng.drain().then_inc(h, 1)
        ph0b["vector"].append(p0v)

        def p0t(eng, h=h_st2, v_sv=v_sv):
            eng.wait_ge(prog.sems["s_vp"][0], v_sv)
            eng.matmul(psT[:, 0:1], onesr[:], sc[0:1, 35:36], start=True,
                       stop=True)
            eng.matmul(psT[:, 1:2], onesr[:], sc[0:1, 5:6], start=True,
                       stop=True).then_inc(h, 1)
        ph0b["tensor"].append(p0t)

        def p0v2(eng, h=h_sv2, v=v_st2):
            eng.wait_ge(prog.sems["s_tp"][0], v)
            eng.tensor_copy(fccol[:], psT[:, 0:1])
            eng.tensor_copy(r2col[:], psT[:, 1:2])
            eng.drain().then_inc(h, 1)
        ph0b["vector"].append(p0v2)

        ph = prog.phase()
        for i in range(NT):
            r, cb = divmod(i, TOK // W)
            buf = (i % 2) * W
            ldn = "e_ld%d" % (i % 2)
            v_vb_need = max(prog.sem("e_vb")[1] - 1, 0)
            h_ld, v_ld = INC(ldn, 32)

            def load(eng, r=r, cb=cb, buf=buf, h=h_ld, v_vb_need=v_vb_need):
                if v_vb_need > 0:
                    eng.wait_ge(prog.sems["e_vb"][0], v_vb_need)
                eng.dma_start(out=abuf[:, buf:buf + W],
                              in_=abt[r * 128:r * 128 + 128,
                                      cb * W:cb * W + W]).then_inc(h, 16)
                eng.dma_start(out=ubuf[:, buf:buf + W],
                              in_=up_raw[r * 128:r * 128 + 128,
                                         cb * W:cb * W + W]).then_inc(h, 16)
            ph["sync"].append(load)

            stn = "e_st%d" % (i % 2)
            v_st_need = prog.sem(stn)[1]
            h_vb, v_vb = INC("e_vb", 1)

            def vb(eng, buf=buf, v_ld=v_ld, h=h_vb, ldn=ldn, stn=stn,
                   v_st_need=v_st_need):
                eng.wait_ge(prog.sems[ldn][0], v_ld)
                if v_st_need > 0:
                    eng.wait_ge(prog.sems[stn][0], v_st_need)
                eng.tensor_scalar(ci[:], abuf[:, buf:buf + W], fccol[:], None,
                                  ALU.mult)
                eng.tensor_scalar(ui[:], ubuf[:, buf:buf + W], r2col[:], None,
                                  ALU.mult)
                eng.tensor_tensor(cbuf[:, buf:buf + W], ci[:], ui[:], ALU.mult)
                eng.tensor_reduce(mcol[:], cbuf[:, buf:buf + W], AX.X, ALU.max,
                                  apply_absolute_value=True)
                eng.drain()
                eng.tensor_tensor(macc[:], macc[:], mcol[:], ALU.max)
                eng.drain().then_inc(h, 1)
            ph["vector"].append(vb)

            h_stt, v_stt = INC(stn, 16)

            def store(eng, r=r, cb=cb, buf=buf, v_vb=v_vb, h=h_stt):
                eng.wait_ge(prog.sems["e_vb"][0], v_vb)
                eng.dma_start(out=cu[r * 128:r * 128 + 128, cb * W:cb * W + W],
                              in_=cbuf[:, buf:buf + W]).then_inc(h, 16)
            ph["gpsimd"].append(store)

        _snap = {n: prog.sems[n][1] for n in ("e_ld0", "e_ld1") if n in prog.sems}

        def sfin(eng, _snap=_snap):
            for n, v in _snap.items():
                eng.wait_ge(prog.sems[n][0], v)
        ph["sync"].append(sfin)

        _snap2 = {n: prog.sems[n][1] for n in ("e_st0", "e_st1") if n in prog.sems}

        def gfin(eng, _snap2=_snap2):
            for n, v in _snap2.items():
                eng.wait_ge(prog.sems[n][0], v)
        ph["gpsimd"].append(gfin)

        phf = prog.phase()
        h_ftp, v_ftp = INC("f_tp", 1)
        h_fvp, v_fvp = INC("f_vp", 1)
        h_fst, v_fst = INC("f_st", 16)

        def fin_t(eng, h=h_ftp):
            eng.transpose(psT[0:1, 0:128], macc[:], identf[:]).then_inc(h, 1)
        phf["tensor"].append(fin_t)

        def fin_v(eng, h=h_fvp, v_ftp=v_ftp):
            eng.wait_ge(prog.sems["f_tp"][0], v_ftp)
            eng.tensor_reduce(m5s[:], psT[0:1, 0:128], AX.X, ALU.max)
            eng.drain().then_inc(h, 1)
        phf["vector"].append(fin_v)

        def fin_g(eng, h=h_fst, v_fvp=v_fvp, v_fst=v_fst):
            eng.wait_ge(prog.sems["f_vp"][0], v_fvp)
            eng.dma_start(out=m5[:], in_=m5s[:]).then_inc(h, 16)
            eng.wait_ge(h, v_fst)
        phf["gpsimd"].append(fin_g)

        prog.emit(block)
    return nc



def build_b4():
    nc = bass.Bass("TRN2", target_bir_lowering=False,
                   detect_race_conditions=False)
    cut = nc.dram_tensor("cu", [DF, TOK], F32, kind="ExternalInput").ap()
    xpT = nc.dram_tensor("xpT", [B, D, S], F32, kind="ExternalInput").ap()
    w_down = nc.dram_tensor("w_down", [D, DF], F32, kind="ExternalInput").ap()
    m2a = nc.dram_tensor("m2a", [1, 8], F32, kind="ExternalInput").ap()
    m3a = nc.dram_tensor("m3a", [1, 8], F32, kind="ExternalInput").ap()
    m4a = nc.dram_tensor("m4a", [1, 8], F32, kind="ExternalInput").ap()
    m5a = nc.dram_tensor("m5a", [1, 8], F32, kind="ExternalInput").ap()
    out = nc.dram_tensor("out", [B, S, D], F32, kind="ExternalOutput").ap()
    dnT_h = nc.dram_tensor("dnT_h", [128, 8 * DF], F32R, kind="Internal").ap()
    dnT_l = nc.dram_tensor("dnT_l", [128, 8 * DF], F32R, kind="Internal").ap()

    BT = 512
    NSL = TOK // BT

    with contextlib.ExitStack() as ctx:
        E = ctx.enter_context
        identf = E(nc.sbuf_tensor("identf", [128, 128], F32))
        onesr = E(nc.sbuf_tensor("onesr", [1, 128], F32))
        sc = E(nc.sbuf_tensor("sc", [1, 48], F32))
        fdcol = E(nc.sbuf_tensor("fdcol", [128, 1], F32))
        s5col = E(nc.sbuf_tensor("s5col", [128, 1], F32))
        wbuf = E(nc.sbuf_tensor("wbuf", [128, 2 * DF], F32))
        wg = E(nc.sbuf_tensor("wg", [128, 96], F32))
        wq = E(nc.sbuf_tensor("wq", [128, DF], I16))
        wdq = E(nc.sbuf_tensor("wdq", [128, DF], F32))
        wz = E(nc.sbuf_tensor("wz", [128, 192], F32))
        wth = E(nc.sbuf_tensor("wth", [128, 256], F32R))
        wtl = E(nc.sbuf_tensor("wtl", [128, 256], F32R))
        cub = E(nc.sbuf_tensor("cub", [128, 2 * BT], F32))
        di = E(nc.sbuf_tensor("di", [128, BT], I16))
        dh = E(nc.sbuf_tensor("dh", [128, 24 * BT], F32R))
        dl = E(nc.sbuf_tensor("dl", [128, 24 * BT], F32R))
        wsh = E(nc.sbuf_tensor("wsh", [128, 2 * DF], F32R))
        wsl = E(nc.sbuf_tensor("wsl", [128, 2 * DF], F32R))
        xpb = E(nc.sbuf_tensor("xpb", [128, 2 * BT], F32))
        fo = E(nc.sbuf_tensor("fo", [128, BT], F32))
        ost = E(nc.sbuf_tensor("ost", [128, 2 * BT], F32))
        psA = E(nc.psum_tensor("psA", [128, 512], F32))
        psB = E(nc.psum_tensor("psB", [128, 512], F32))
        psT = E(nc.psum_tensor("psT", [128, 128], F32))
        psT2 = E(nc.psum_tensor("psT2", [128, 128], F32))
        block = E(nc.Block())
        prog = Prog(nc, ctx)

        def INC(name, n=1):
            s = prog.sem(name)
            s[1] += n
            return s[0], s[1]

        ph0 = prog.phase()
        h_sld, v_sld = INC("s_ld", 64)

        def p0g(g):
            h, _ = INC("g0", 3)
            g.memset(identf[:], 0.0).then_inc(h, 1)
            g.wait_ge(h, 1)
            g.affine_select(out=identf[:], in_=identf[:], compare_op=ALU.not_equal,
                            fill=1.0, base=0, pattern=[[-1, 128]],
                            channel_multiplier=1).then_inc(h, 1)
            g.memset(onesr[:], 1.0).then_inc(h, 1)
            g.wait_ge(h, 3)
        ph0["gpsimd"].append(p0g)

        def p0s(sync, h=h_sld, v=v_sld):
            sync.dma_start(out=sc[:, 8:16], in_=m2a[:]).then_inc(h, 16)
            sync.dma_start(out=sc[:, 16:24], in_=m3a[:]).then_inc(h, 16)
            sync.dma_start(out=sc[:, 24:32], in_=m4a[:]).then_inc(h, 16)
            sync.dma_start(out=sc[:, 32:40], in_=m5a[:]).then_inc(h, 16)
            sync.wait_ge(h, v)
        ph0["sync"].append(p0s)

        ph0b = prog.phase()
        h_sv, v_sv = INC("s_vp", 1)
        h_st2, v_st2 = INC("s_tp", 1)
        h_sv2, v_sv2 = INC("s_vp", 1)

        def p0v(eng, h=h_sv):
            eng.tensor_reduce(sc[:, 0:1], sc[:, 8:16], AX.X, ALU.max)
            eng.tensor_reduce(sc[:, 1:2], sc[:, 16:24], AX.X, ALU.max)
            eng.tensor_reduce(sc[:, 2:3], sc[:, 24:32], AX.X, ALU.max)
            eng.tensor_reduce(sc[:, 3:4], sc[:, 32:40], AX.X, ALU.max)
            eng.drain()
            eng.tensor_scalar(sc[:, 4:5], sc[:, 0:1], 1.0 / QMAX, 1e-12,
                              ALU.mult, ALU.max)
            eng.tensor_scalar(sc[:, 5:6], sc[:, 1:2], 1.0 / QMAX, 1e-12,
                              ALU.mult, ALU.max)
            eng.drain()
            eng.tensor_scalar(sc[:, 6:7], sc[:, 5:6], 1.0 / 65536.0, None,
                              ALU.mult)
            eng.drain()
            eng.tensor_tensor(sc[:, 40:41], sc[:, 6:7], sc[:, 2:3], ALU.mult)
            eng.drain()
            eng.tensor_scalar(sc[:, 41:42], sc[:, 40:41], 1.0 / QMAX, 1e-12,
                              ALU.mult, ALU.max)
            eng.drain()
            eng.tensor_tensor(sc[:, 42:43], sc[:, 41:42], sc[:, 4:5],
                              ALU.mult)
            eng.drain()
            eng.tensor_tensor(sc[:, 43:44], sc[:, 42:43], sc[:, 3:4], ALU.mult)
            eng.drain()
            eng.tensor_scalar(sc[:, 44:45], sc[:, 43:44], 1.0 / QMAX, 1e-12,
                              ALU.mult, ALU.max)
            eng.drain()
            _v_newton_recip(eng, sc[:, 45:46], sc[:, 44:45], sc[:, 7:8])
            eng.tensor_tensor(sc[:, 46:47], sc[:, 42:43], sc[:, 45:46],
                              ALU.mult)
            eng.drain().then_inc(h, 1)
        ph0b["vector"].append(p0v)

        def p0t(eng, h=h_st2, v_sv=v_sv):
            eng.wait_ge(prog.sems["s_vp"][0], v_sv)
            eng.matmul(psT[:, 0:1], onesr[:], sc[0:1, 46:47], start=True,
                       stop=True)
            eng.matmul(psT[:, 1:2], onesr[:], sc[0:1, 44:45], start=True,
                       stop=True).then_inc(h, 1)
        ph0b["tensor"].append(p0t)

        def p0v2(eng, h=h_sv2, v=v_st2):
            eng.wait_ge(prog.sems["s_tp"][0], v)
            eng.tensor_copy(fdcol[:], psT[:, 0:1])
            eng.tensor_copy(s5col[:], psT[:, 1:2])
            eng.drain().then_inc(h, 1)
        ph0b["vector"].append(p0v2)

        ph1 = prog.phase()
        for i in range(8):
            buf = (i % 2) * DF
            ldn = "b_ld%d" % (i % 2)
            h_ld, v_ld = INC(ldn, 16)

            def load(eng, i=i, buf=buf, h=h_ld):
                if i >= 2:
                    eng.wait_ge(prog.sems["b_dq"][0], i - 1)
                eng.dma_start(out=wbuf[:, buf:buf + DF],
                              in_=w_down[i * 128:i * 128 + 128, :]).then_inc(h, 16)
            ph1["sync"].append(load)

            h_dq, v_dq = INC("b_dq", 1)

            def dequant(eng, i=i, buf=buf, v_ld=v_ld, h=h_dq, ldn=ldn):
                eng.wait_ge(prog.sems[ldn][0], v_ld)
                if i >= 2:
                    eng.wait_ge(prog.sems["b_tp"][0], (i - 1) * 24)
                wb3 = wbuf[:, buf:buf + DF].rearrange("p (g c) -> p g c", g=96)
                eng.tensor_reduce(
                    wg[:].rearrange("p (g o) -> p g o", o=1),
                    wb3, AX.X, ALU.max, apply_absolute_value=True)
                eng.drain()
                eng.tensor_scalar(wg[:], wg[:], 1.0 / 7.0, 1e-12,
                                  ALU.mult, ALU.max)
                eng.drain()
                eng.reciprocal(wz[:, 0:96], wg[:])
                eng.drain()
                eng.tensor_tensor(wz[:, 96:192], wg[:], wz[:, 0:96], ALU.mult)
                eng.drain()
                eng.tensor_scalar(wz[:, 96:192], wz[:, 96:192], -1.0, 2.0,
                                  ALU.mult, ALU.add)
                eng.drain()
                eng.tensor_tensor(wz[:, 0:96], wz[:, 0:96], wz[:, 96:192],
                                  ALU.mult)
                eng.drain()
                eng.tensor_tensor(
                    wdq[:].rearrange("p (g c) -> p g c", g=96), wb3,
                    wz[:, 0:96].rearrange("p (g o) -> p g o", o=1).broadcast_to(
                        (128, 96, 32)), ALU.mult)
                eng.tensor_copy(wq[:], wdq[:])
                eng.tensor_scalar(wq[:], wq[:], 7.0, -8.0, ALU.min, ALU.max)
                eng.tensor_tensor(
                    wdq[:].rearrange("p (g c) -> p g c", g=96),
                    wq.rearrange("p (g c) -> p g c", g=96),
                    wg[:].rearrange("p (g o) -> p g o", o=1).broadcast_to(
                        (128, 96, 32)), ALU.mult)
                eng.tensor_scalar(wdq[:], wdq[:], s5col[:], None, ALU.mult)
                eng.drain().then_inc(h, 1)
            prog.sem("b_tp")
            ph1["vector"].append(dequant)

            for k in range(24):
                h_tp, v_tp = INC("b_tp", 1)

                def tpose(eng, k=k, v_dq=v_dq, h=h_tp,
                          ntp=prog.sems["b_tp"][1]):
                    eng.wait_ge(prog.sems["b_dq"][0], v_dq)
                    if ntp >= 3:
                        eng.wait_ge(prog.sems["b_sp"][0], ntp - 2)
                    p = psT if ntp % 2 == 1 else psT2
                    eng.transpose(p[:], wdq[:, k * 128:k * 128 + 128],
                                  identf[:]).then_inc(h, 1)
                prog.sem("b_sp")
                ph1["tensor"].append(tpose)

                sbuf2 = (prog.sems["b_sp"][1] % 2) * 128
                gpn = "b_gp%d" % (prog.sems["b_sp"][1] % 2)
                v_gp_need = prog.sem(gpn)[1]
                h_sp, v_sp = INC("b_sp", 1)

                def split(eng, v_tp=v_tp, sbuf2=sbuf2, h=h_sp, gpn=gpn,
                          v_gp_need=v_gp_need, ntp=prog.sems["b_tp"][1]):
                    eng.wait_ge(prog.sems["b_tp"][0], v_tp)
                    if v_gp_need > 0:
                        eng.wait_ge(prog.sems[gpn][0], v_gp_need)
                    p = psT if ntp % 2 == 1 else psT2
                    eng.tensor_copy(wth[:, sbuf2:sbuf2 + 128], p[:])
                    eng.tensor_tensor(wtl[:, sbuf2:sbuf2 + 128], p[:],
                                      wth[:, sbuf2:sbuf2 + 128],
                                      ALU.subtract).then_inc(h, 1)
                ph1["vector"].append(split)

                h_gp, v_gp = INC(gpn, 32)
                col = i * DF + k * 128

                def store(eng, v_sp=v_sp, sbuf2=sbuf2, col=col, h=h_gp):
                    eng.wait_ge(prog.sems["b_sp"][0], v_sp)
                    eng.dma_start(out=dnT_h[:, col:col + 128],
                                  in_=wth[:, sbuf2:sbuf2 + 128]).then_inc(h, 16)
                    eng.dma_start(out=dnT_l[:, col:col + 128],
                                  in_=wtl[:, sbuf2:sbuf2 + 128]).then_inc(h, 16)
                ph1["gpsimd"].append(store)

        _snap = {n: prog.sems[n][1] for n in
                 ("b_ld0", "b_ld1", "b_gp0", "b_gp1") if n in prog.sems}

        def p1fin_s(eng, _snap=_snap):
            for n in ("b_ld0", "b_ld1"):
                if n in _snap:
                    eng.wait_ge(prog.sems[n][0], _snap[n])
        ph1["sync"].append(p1fin_s)

        def p1fin_g(eng, _snap=_snap):
            for n in ("b_gp0", "b_gp1"):
                if n in _snap:
                    eng.wait_ge(prog.sems[n][0], _snap[n])
        ph1["gpsimd"].append(p1fin_g)

        def emit_slice(sl):
            t0 = sl * BT
            b, s0 = divmod(t0, S)
            ph = prog.phase()
            for c in range(24):
                cbuf = (c % 2) * BT
                ldn = "c_ld%d" % (c % 2)
                v_dc_need = max(prog.sem("c_dc")[1] - 1, 0)
                h_ld, v_ld = INC(ldn, 16)

                def ldc(eng, c=c, t0=t0, cbuf=cbuf, h=h_ld,
                        v_dc_need=v_dc_need):
                    if v_dc_need > 0:
                        eng.wait_ge(prog.sems["c_dc"][0], v_dc_need)
                    eng.dma_start(out=cub[:, cbuf:cbuf + BT],
                                  in_=cut[c * 128:c * 128 + 128, t0:t0 + BT]
                                  ).then_inc(h, 16)
                ph["sync"].append(ldc)

                h_dc, v_dc = INC("c_dc", 1)

                def dcvt(eng, c=c, cbuf=cbuf, v_ld=v_ld, h=h_dc, ldn=ldn):
                    eng.wait_ge(prog.sems[ldn][0], v_ld)
                    eng.tensor_scalar(di[:], cub[:, cbuf:cbuf + BT], fdcol[:],
                                      None, ALU.mult)
                    eng.drain()
                    eng.tensor_copy(dh[:, c * BT:c * BT + BT], di[:])
                    eng.drain()
                    eng.tensor_tensor(dl[:, c * BT:c * BT + BT], di[:],
                                      dh[:, c * BT:c * BT + BT],
                                      ALU.subtract).then_inc(h, 1)
                ph["vector"].append(dcvt)
            v_dc_all = prog.sems["c_dc"][1]

            for r in range(8):
                wb = (r % 2) * DF
                wsn = "c_ws%d" % (r % 2)
                v_need = prog.sem("c_mm")[1] - 1 if r >= 2 else 0
                h_ws, v_ws = INC(wsn, 32)

                def wload(eng, r=r, wb=wb, h=h_ws, v_need=v_need):
                    if v_need > 0:
                        eng.wait_ge(prog.sems["c_mm"][0], v_need)
                    eng.dma_start(out=wsh[:, wb:wb + DF],
                                  in_=dnT_h[:, r * DF:r * DF + DF]).then_inc(h, 16)
                    eng.dma_start(out=wsl[:, wb:wb + DF],
                                  in_=dnT_l[:, r * DF:r * DF + DF]).then_inc(h, 16)
                ph["sync"].append(wload)

                xldn = "x_ld%d" % (r % 2)
                v_ev_need = max(prog.sem("c_ev")[1] - 1, 0)
                h_xl, v_xl = INC(xldn, 16)

                def ldxp(eng, r=r, b=b, s0=s0, h=h_xl, v_ev_need=v_ev_need):
                    if v_ev_need > 0:
                        eng.wait_ge(prog.sems["c_ev"][0], v_ev_need)
                    eng.dma_start(out=xpb[:, (r % 2) * BT:(r % 2) * BT + BT],
                                  in_=xpT[b, r * 128:r * 128 + 128, s0:s0 + BT]
                                  ).then_inc(h, 16)
                ph["sync"].append(ldxp)

                bank = psA if r % 2 == 0 else psB
                h_mm, v_mm = INC("c_mm", 1)

                def cmm(eng, r=r, wb=wb, bank=bank, v_ws=v_ws, h=h_mm,
                        wsn=wsn, v_dc_all=v_dc_all, nmm=prog.sems["c_mm"][1]):
                    eng.wait_ge(prog.sems[wsn][0], v_ws)
                    eng.wait_ge(prog.sems["c_dc"][0], v_dc_all)
                    if nmm >= 3:
                        eng.wait_ge(prog.sems["c_ev"][0], nmm - 2)
                    for k in range(24):
                        wcol = wb + k * 128
                        eng.matmul(bank[:, 0:BT], wsh[:, wcol:wcol + 128],
                                   dh[:, k * BT:k * BT + BT],
                                   start=(k == 0), stop=False)
                        eng.matmul(bank[:, 0:BT], wsh[:, wcol:wcol + 128],
                                   dl[:, k * BT:k * BT + BT],
                                   start=False, stop=False)
                        mm = eng.matmul(bank[:, 0:BT], wsl[:, wcol:wcol + 128],
                                        dh[:, k * BT:k * BT + BT],
                                        start=False, stop=(k == 23))
                        if k == 23:
                            mm.then_inc(h, 1)
                prog.sem("c_ev")
                ph["tensor"].append(cmm)

                v_tp_prev = prog.sem("c_tp")[1]
                h_ev, v_ev = INC("c_ev", 1)

                def cev(eng, r=r, bank=bank, v_mm=v_mm, v_xl=v_xl, h=h_ev,
                        xldn=xldn, v_tp_prev=v_tp_prev):
                    eng.wait_ge(prog.sems["c_mm"][0], v_mm)
                    eng.wait_ge(prog.sems[xldn][0], v_xl)
                    if v_tp_prev > 0:
                        eng.wait_ge(prog.sems["c_tp"][0], v_tp_prev)
                    eng.tensor_tensor(fo[:], bank[:, 0:BT],
                                      xpb[:, (r % 2) * BT:(r % 2) * BT + BT],
                                      ALU.add)
                    eng.drain().then_inc(h, 1)
                ph["vector"].append(cev)

                for j in range(4):
                    h_tp, v_tp = INC("c_tp", 1)

                    def ftp(eng, j=j, v_ev=v_ev, h=h_tp,
                            ntp=prog.sems["c_tp"][1]):
                        eng.wait_ge(prog.sems["c_ev"][0], v_ev)
                        if ntp >= 3:
                            eng.wait_ge(prog.sems["c_fe"][0], ntp - 2)
                        p = psT if ntp % 2 == 1 else psT2
                        eng.transpose(p[:], fo[:, j * 128:j * 128 + 128],
                                      identf[:]).then_inc(h, 1)
                    prog.sem("c_fe")
                    ph["tensor"].append(ftp)

                    ssl = (prog.sem("c_fe")[1] % 4) * 128
                    stn = "c_st%d" % (prog.sem("c_fe")[1] % 4)
                    v_st_need = prog.sem(stn)[1]
                    h_fe, v_fe = INC("c_fe", 1)

                    def fev(eng, v_tp=v_tp, ssl=ssl, h=h_fe, stn=stn,
                            v_st_need=v_st_need, ntp=prog.sems["c_tp"][1]):
                        eng.wait_ge(prog.sems["c_tp"][0], v_tp)
                        if v_st_need > 0:
                            eng.wait_ge(prog.sems[stn][0], v_st_need)
                        p = psT if ntp % 2 == 1 else psT2
                        eng.tensor_copy(ost[:, ssl:ssl + 128],
                                        p[:]).then_inc(h, 1)
                    ph["vector"].append(fev)

                    h_st, v_st = INC(stn, 16)

                    def stf(eng, r=r, j=j, b=b, s0=s0, ssl=ssl, v_fe=v_fe,
                            h=h_st):
                        eng.wait_ge(prog.sems["c_fe"][0], v_fe)
                        eng.dma_start(
                            out=out[b, s0 + j * 128:s0 + j * 128 + 128,
                                    r * 128:r * 128 + 128],
                            in_=ost[:, ssl:ssl + 128]).then_inc(h, 16)
                    ph["gpsimd"].append(stf)

            _snap = {n: prog.sems[n][1] for n in
                     ("c_ld0", "c_ld1", "c_ws0", "c_ws1", "x_ld0", "x_ld1")
                     if n in prog.sems}

            def sfin(eng, _snap=_snap):
                for n, v in _snap.items():
                    eng.wait_ge(prog.sems[n][0], v)
            ph["sync"].append(sfin)

            _snap2 = {n: prog.sems[n][1] for n in
                      ("c_st0", "c_st1", "c_st2", "c_st3") if n in prog.sems}

            def gfin(eng, _snap2=_snap2):
                for n, v in _snap2.items():
                    eng.wait_ge(prog.sems[n][0], v)
            ph["gpsimd"].append(gfin)

        for sl in range(NSL):
            emit_slice(sl)

        prog.emit(block)
    return nc



_CACHE = {}


def _get(name, builder):
    if name not in _CACHE:
        _CACHE[name] = builder()
    return _CACHE[name]


def kernel(**inputs):
    xs = np.ascontiguousarray(inputs["L0_attn_x"], dtype=np.float32)
    conv = np.ascontiguousarray(
        inputs["L0_attn_conv_w"], dtype=np.float32).reshape(D, 9)
    pe = np.ascontiguousarray(inputs["L0_attn_pe_w"], dtype=np.float32)
    qkv_w = np.ascontiguousarray(inputs["L0_attn_qkv_w"], dtype=np.float32)
    out_w = np.ascontiguousarray(inputs["L0_attn_out_w"], dtype=np.float32)
    w_gate = np.ascontiguousarray(inputs["L0_ffn_w_gate"], dtype=np.float32)
    w_up = np.ascontiguousarray(inputs["L0_ffn_w_up"], dtype=np.float32)
    w_down = np.ascontiguousarray(inputs["L0_ffn_w_down"], dtype=np.float32)
    cores = list(range(NCORES))

    def run(name, builder, per_core):
        nc = _get(name, builder)
        return run_bass_kernel_spmd(nc, per_core, core_ids=cores).results

    r1 = run("a1", build_a1,
             [{"x": xs[B * c:B * c + B], "conv_w": conv, "pe_w": pe,
               "qkv_w": qkv_w} for c in cores])
    r2 = run("a2", build_a2, [{"qkvT": r1[c]["qkvT"]} for c in cores])
    r3 = run("a3", build_a3,
             [{"onT": r2[c]["onT"], "x": xs[B * c:B * c + B], "out_w": out_w}
              for c in cores])
    m1a = np.concatenate([r3[c]["m1"] for c in cores], axis=1)
    r4 = run("b1", build_b1,
             [{"xpT": r3[c]["xpT"], "m1a": m1a, "w_up": w_up,
               "w_gate": w_gate} for c in cores])
    m2a = np.concatenate([r4[c]["m2"] for c in cores], axis=1)
    m3a = np.concatenate([r4[c]["m3"] for c in cores], axis=1)
    r5 = run("b2", build_b2,
             [{"gt_raw": r4[c]["gt_raw"], "m3a": m3a} for c in cores])
    m4a = np.concatenate([r5[c]["m4"] for c in cores], axis=1)
    r6 = run("b3", build_b3,
             [{"ab": r5[c]["ab"], "up_raw": r4[c]["up_raw"], "m2a": m2a,
               "m3a": m3a, "m4a": m4a} for c in cores])
    m5a = np.concatenate([r6[c]["m5"] for c in cores], axis=1)
    r7 = run("b4", build_b4,
             [{"cu": r6[c]["cu"], "xpT": r3[c]["xpT"], "w_down": w_down,
               "m2a": m2a, "m3a": m3a, "m4a": m4a, "m5a": m5a}
              for c in cores])
    return np.concatenate([r7[c]["out"] for c in cores], axis=0)


